# version 28
# baseline (speedup 1.0000x reference)
import numpy as np
import ml_dtypes
from contextlib import ExitStack

import concourse.bass as bass
import concourse.mybir as mybir
from concourse.bass_utils import run_bass_kernel_spmd
from concourse.hw_specs import get_activation_tables


def _act_set_idx(nc, funcs):
    for idx, fs in enumerate(get_activation_tables(nc.m.arch).values()):
        if all(f in fs for f in funcs):
            return idx
    raise ValueError(f"no activation table set contains {funcs}")


def _load_tab(eng, idx):
    return eng.add_instruction(mybir.InstLoadActFuncSet(
        name=eng.bass.get_next_instruction_name(),
        act_func_set_id=idx, ins=[], outs=[]))

F32 = mybir.dt.float32
BF16 = mybir.dt.bfloat16
AF = mybir.ActivationFunctionType
ALU = mybir.AluOpType
AX = mybir.AxisListType

DMA_POOLS = {"sx": ("sy", 6), "sp": ("sy", 3), "so": ("sy", 3), "sg": ("gp", 3)}
SEMS = {"spe": "pe", "sve": "ve", "sse": "se"}
for _p, (_e, _k) in DMA_POOLS.items():
    for _i in range(_k):
        SEMS[f"{_p}{_i}"] = _e


class Plan:

    def __init__(self):
        self.events = {}
        self.dry = True
        self.mode = None
        self.eng = None
        self.sems = None
        self.counts = {k: 0 for k in SEMS}
        self.pool_n = {p: 0 for p in DMA_POOLS}

    def begin(self, mode, eng=None, sems=None):
        self.dry = (mode == "dry")
        self.mode = mode
        self.eng = eng
        self.sems = sems
        self.counts = {k: 0 for k in SEMS}
        self.pool_n = {p: 0 for p in DMA_POOLS}

    def op(self, engine, fn, inc=None, inc_n=1, event=None):
        inst = None
        if self.mode == engine:
            inst = fn(self.eng)
        if inc is not None:
            assert SEMS[inc] == engine, f"sem {inc} not produced by {engine}"
            self.counts[inc] += inc_n
            if inst is not None:
                inst.then_inc(self.sems[inc], inc_n)
            if event is not None:
                val = (inc, self.counts[inc], dict(self.counts))
                if self.dry:
                    assert event not in self.events, f"dup event {event}"
                    self.events[event] = val
                else:
                    assert self.events.get(event)[:2] == val[:2], \
                        f"event drift {event}: {self.events.get(event)[:2]} vs {val[:2]}"
        else:
            assert event is None
        return inst

    def dma(self, engine, fn, pool, event=None):
        eng_of, k = DMA_POOLS[pool]
        assert eng_of == engine
        i = self.pool_n[pool]
        self.pool_n[pool] = i + 1
        sem = f"{pool}{i % k}"
        if i >= k and self.mode == engine:
            self.eng.wait_ge(self.sems[sem], 16 * (i // k))
        return self.op(engine, fn, inc=sem, inc_n=16, event=event)

    def wait(self, engine, event):
        if self.mode == engine:
            sem_name, val, _ = self.events[event]
            self.eng.wait_ge(self.sems[sem_name], val)

    def wait_all_dma(self, engine, event, pool):
        if self.mode == engine:
            _, _, snap = self.events[event]
            _, k = DMA_POOLS[pool]
            for i in range(k):
                sem = f"{pool}{i}"
                if snap.get(sem, 0) > 0:
                    self.eng.wait_ge(self.sems[sem], snap[sem])

    def drain(self, engine):
        if self.mode == engine:
            self.eng.drain()


def build_walk(cfg, t, P):
    TAB_LNEXP = cfg["tab_lnexp"]
    TAB_SIG = cfg["tab_sig"]
    TAB_GELU = cfg["tab_gelu"]
    T = cfg["T"]; Tp = cfg["Tp"]
    NPH = T // Tp
    TPT = Tp // 128
    NHF = TPT // 2
    ND = 8
    NH = 4
    SQ2I = float(1.0 / np.sqrt(2.0))
    DQS = float(256 ** -0.5)

    psH = {
        "A0": t.psA[:, 0:512], "A1": t.psA[:, 512:1024],
        "B0": t.psB[:, 0:512], "B1": t.psB[:, 512:1024],
        "C0": t.psC[:, 0:512], "C1": t.psC[:, 512:1024],
        "T0": t.psT0[:, :], "T1": t.psT1[:, :],
    }
    bank_free = {}

    def pe_claim(*regions):
        for r in regions:
            ev = bank_free.pop(r, None)
            if ev is not None:
                P.wait("pe", ev)

    def release(region, event):
        bank_free[region] = event

    def xslot(ph, h):
        return (2 * ph + h) % 4

    gload = [
        lambda g: g.dma_start(out=t.Wp_sb[:],
            in_=t.i_Wp_pk.rearrange("p (k f) -> p k f", k=8)),
        lambda g: g.dma_start(out=t.Wu_sb[:],
            in_=t.i_Wu_pk.rearrange("p (k f) -> p k f", k=8)),
        lambda g: g.dma_start(out=t.Wo_sb[:],
            in_=t.i_Wo_pk.rearrange("p (k f) -> p k f", k=8)),
        lambda g: g.dma_start(out=t.w1t_sb[:],
            in_=t.i_w1t_pk.rearrange("p (k f) -> p k f", k=8)),
        lambda g: g.dma_start(out=t.w2_sb[:],
            in_=t.i_w2_pk.rearrange("p (k f) -> p k f", k=4)),
        lambda g: g.dma_start(out=t.wq_sb[:],
            in_=t.i_wq_pk.rearrange("p (k f) -> p k f", k=8)),
        lambda g: g.dma_start(out=t.wk_sb[:],
            in_=t.i_wk_pk.rearrange("p (k f) -> p k f", k=8)),
        lambda g: g.dma_start(out=t.s0_bf[:], in_=t.i_slot[:]),
        lambda g: g.dma_start(out=t.b2_r[:], in_=t.i_b2[None, :]),
    ]
    for i, fn in enumerate(gload):
        P.dma("gp", fn, "sg", event=("wload",) if i == len(gload) - 1 else None)

    sload = [
        lambda s: s.dma_start(out=t.ident[:], in_=t.i_ident[:]),
        lambda s: s.dma_start(out=t.ones_r[:], in_=t.i_ones_r[:]),
        lambda s: s.dma_start(out=t.ones_c[:], in_=t.i_ones_c[:]),
        lambda s: s.dma_start(out=t.tau_b[:], in_=t.i_tau_b[:]),
        lambda s: s.dma_start(out=t.bpw[:], in_=t.i_bp0_pk[:]),
        lambda s: s.dma_start(out=t.bpk2[:], in_=t.i_bpk_pk[:]),
        lambda s: s.dma_start(out=t.bu_c[:], in_=t.i_bu_pk[:]),
        lambda s: s.dma_start(out=t.b1_c[:], in_=t.i_b1_pk[:]),
        lambda s: s.dma_start(out=t.bq_c[:], in_=t.i_bq_pk[:]),
        lambda s: s.dma_start(out=t.bk_c[:], in_=t.i_bk_pk[:]),
        lambda s: s.dma_start(out=t.lng_c[:], in_=t.i_lng_pk[:]),
        lambda s: s.dma_start(out=t.lnb_c[:], in_=t.i_lnb_pk[:]),
        lambda s: s.dma_start(out=t.bo1_r[:],
            in_=t.i_bo.rearrange("(a b) -> a b", a=8)),
        lambda s: s.dma_start(out=t.bo2_r[:],
            in_=t.i_bof.rearrange("(a b) -> a b", a=8)),
    ]
    for i, fn in enumerate(sload):
        P.dma("sy", fn, "sx", event=("cload",) if i == len(sload) - 1 else None)

    for ph in range(min(2, NPH)):
        for h in range(NHF):
            P.dma("sy", lambda s, ph=ph, h=h: s.dma_start(
                out=t.x_sl[:, xslot(ph, h), :, :],
                in_=t.i_x[ph * Tp + h * 256: ph * Tp + (h + 1) * 256, :]
                    .rearrange("(n p) d -> p n d", p=128)), "sx",
                event=("xload", ph, h))

    P.op("se", lambda s: _load_tab(s, TAB_SIG))
    P.wait_all_dma("se", ("cload",), "sx")
    P.op("se", lambda s: s.activation(t.invc[:], t.tau_b[:], AF.Abs),
         inc="sse", event=("tauabs",))
    P.wait("ve", ("tauabs",))
    P.wait_all_dma("ve", ("cload",), "sx")
    P.wait_all_dma("ve", ("wload",), "sg")
    P.op("ve", lambda v: v.tensor_scalar(t.invc[:], t.invc[:], 0.1, None, ALU.add))
    P.drain("ve")
    P.op("ve", lambda v: v.reciprocal(t.invc[:], t.invc[:]))
    P.drain("ve")
    P.op("ve", lambda v: v.tensor_scalar(t.bpk2[:], t.bpk2[:], -2.0, None, ALU.mult),
         inc="sve", event=("bpk2",))
    P.op("ve", lambda v: v.tensor_add(t.bo1_r[:], t.bo1_r[:], t.bo2_r[:]),
         inc="sve", event=("bosum",))
    P.wait("gp", ("bosum",))
    P.dma("gp", lambda g: g.dma_start(
        out=t.bo_tot[:], in_=t.bo1_r[:]), "sg", event=("botot",))

    P.wait_all_dma("pe", ("cload",), "sx")
    P.wait_all_dma("pe", ("wload",), "sg")
    pe_claim("T0")
    for k in range(ND):
        P.op("pe", lambda e, k=k: e.transpose(
            t.psT0[:, k * 64:(k + 1) * 64],
            t.s0_bf[0:64, k * 128:(k + 1) * 128], t.ident[0:64, 0:64]),
            inc="spe" if k == ND - 1 else None,
            event=("s0T",) if k == ND - 1 else None)
    P.wait("ve", ("s0T",))
    for k in range(ND):
        P.op("ve", lambda v, k=k: v.tensor_copy(
            t.si0f[:, k, :], t.psT0[:, k * 64:(k + 1) * 64]),
            inc="sve" if k == ND - 1 else None,
            event=("si0f",) if k == ND - 1 else None)
    release("T0", ("si0f",))
    P.drain("ve")
    for k in range(ND):
        P.op("ve", lambda v, k=k: v.reduce_sum(
            t.sm_f[:, k:k + 1], t.si0f[:, k, :], axis=AX.X))
    P.drain("ve")
    P.op("ve", lambda v: v.tensor_scalar(t.sm_fb[:], t.sm_f[:], 1.0 / 64.0, None,
                                         ALU.mult), inc="sve", event=("smf",))
    regs4 = ["A0", "A1", "B0", "B1"]
    pe_claim(*regs4)
    ps4 = [psH[r] for r in regs4]
    for k in range(ND):
        if k >= 2:
            P.wait("gp", ("smwk", k - 2))
        P.dma("gp", lambda g, k=k: g.dma_start(
            out=t.w1s[:, k % 2, :],
            in_=t.i_w1[1024 + k * 128:1024 + (k + 1) * 128, :]), "sg",
            event=("w1s", k))
        P.wait("pe", ("w1s", k))
        if k == 0:
            P.wait("pe", ("smf",))
        for jh in range(NH):
            P.op("pe", lambda e, jh=jh, k=k: e.matmul(
                ps4[jh][:, 0:1], t.w1s[:, k % 2, jh * 128:(jh + 1) * 128],
                t.sm_fb[:, k:k + 1], start=(k == 0), stop=(k == ND - 1)),
                inc="spe" if jh == NH - 1 else None,
                event=("smwk", k) if jh == NH - 1 else None)
    P.wait("ve", ("smwk", ND - 1))
    for jh in range(NH):
        P.op("ve", lambda v, jh=jh: v.tensor_add(
            t.bias_h[:, jh:jh + 1], t.b1_c[:, jh:jh + 1], ps4[jh][:, 0:1]),
            inc="sve" if jh == NH - 1 else None,
            event=("biash",) if jh == NH - 1 else None)
    for r in regs4:
        release(r, ("biash",))

    def stats_tile(ph, i):
        pp = ph % 2
        sl, ii = xslot(ph, i // 2), i % 2
        if i % 2 == 0:
            P.wait("se", ("xload", ph, i // 2))
        if i == 0 and ph >= 2:
            P.wait("se", ("var", ph - 2))
        P.drain("se")
        P.op("se", lambda s, sl=sl, ii=ii, i=i, pp=pp: s.activation(
            t.dump[:], t.x_sl[:, sl, ii, :], AF.Identity,
            accum_out=t.ssum[:, pp, i:i + 1]))
        P.drain("se")
        P.op("se", lambda s, sl=sl, ii=ii, i=i, pp=pp: s.activation(
            t.dump[:], t.x_sl[:, sl, ii, :], AF.Square,
            accum_out=t.ssq[:, pp, i:i + 1]),
            inc="sse" if i == TPT - 1 else None,
            event=("sums", ph) if i == TPT - 1 else None)

    def var_chain(ph):
        pp = ph % 2
        smn, svar, stmp = (t.smn[:, pp, :], t.svar[:, pp, :], t.stmp[:, pp, :])
        P.wait("ve", ("sums", ph))
        if ph >= 2:
            P.wait("ve", ("rstd", ph - 2))
        P.op("ve", lambda v, smn=smn, pp=pp: v.tensor_scalar(
            smn, t.ssum[:, pp, :], 1.0 / 1024.0, None, ALU.mult))
        P.op("ve", lambda v, svar=svar, pp=pp: v.tensor_scalar(
            svar, t.ssq[:, pp, :], 1.0 / 1024.0, 1e-5, ALU.mult, ALU.add))
        P.drain("ve")
        P.op("ve", lambda v, stmp=stmp, smn=smn: v.tensor_mul(stmp, smn, smn))
        P.drain("ve")
        P.op("ve", lambda v, svar=svar, stmp=stmp: v.tensor_sub(svar, svar, stmp),
             inc="sve", event=("var", ph))

    def rstd_window(ph):
        pp = ph % 2
        P.wait("se", ("var", ph))
        P.op("se", lambda s: _load_tab(s, TAB_LNEXP))
        P.op("se", lambda s, pp=pp: s.activation(
            t.stmp[:, pp, :], t.svar[:, pp, :], AF.Ln))
        P.drain("se")
        P.op("se", lambda s, pp=pp: s.activation(
            t.sr[:, pp, :], t.stmp[:, pp, :], AF.Exp, scale=-0.5),
            inc="sse", event=("rstd", ph))
        P.op("se", lambda s: _load_tab(s, TAB_SIG))

    def window(ph):
        if ph < 1:
            return
        P.wait("se", ("max8", ph - 1))
        if ph >= 2:
            P.wait("se", ("alpha", ph - 2))
        P.op("se", lambda s: _load_tab(s, TAB_LNEXP))
        for i in range(TPT):
            P.op("se", lambda s, i=i: s.activation(
                t.e_al[:, i, :], t.l_sb[:, i, :], AF.Exp,
                bias=t.m1n[:, i:i + 1]),
                inc="sse" if i == TPT - 1 else None,
                event=("exps", ph - 1) if i == TPT - 1 else None)

    def alpha_fin(ph):
        P.wait("ve", ("exps", ph))
        if ph >= 1:
            P.wait("ve", ("slotmm", ph - 1))
        for i in range(TPT):
            P.op("ve", lambda v, i=i: v.scalar_tensor_tensor(
                t.e_al[:, i, :], t.l_sb[:, i, :], t.m18[:, i, 7:8],
                t.e_al[:, i, :], ALU.is_ge, ALU.mult))
        P.drain("ve")
        for i in range(TPT):
            P.op("ve", lambda v, i=i: v.reduce_sum(
                t.alns[:, i:i + 1], t.e_al[:, i, :], axis=AX.X))
        P.drain("ve")
        P.op("ve", lambda v: v.reciprocal(t.alns[:], t.alns[:]))
        P.drain("ve")
        for i in range(TPT):
            P.op("ve", lambda v, i=i: v.tensor_scalar(
                t.al_bf[:, i, :], t.e_al[:, i, :], t.alns[:, i:i + 1], None,
                ALU.mult),
                inc="sve" if i == TPT - 1 else None,
                event=("alpha", ph) if i == TPT - 1 else None)

    def wo_block(ph):
        regs = [("A0", "A1", t.psA), ("B0", "B1", t.psB), ("C0", "C1", t.psC),
                ("A0", "A1", t.psA)]

        def wo_tile_k(i, k, ps):
            for f in range(2):
                last = (f == 1 and k == ND - 1)
                P.op("pe", lambda e, i=i, f=f, k=k, ps=ps: e.matmul(
                    ps[:, f * 512:(f + 1) * 512],
                    t.ys_bf[:, k, i * 128:(i + 1) * 128],
                    t.Wo_sb[:, k, f * 512:(f + 1) * 512],
                    start=(k == 0), stop=(k == ND - 1)),
                    inc="spe" if last else None,
                    event=("womm", ph, i) if last else None)

        pe_claim(regs[0][0], regs[0][1])
        for k in range(ND - 1):
            P.wait("pe", ("ysd", ph, k))
            wo_tile_k(0, k, regs[0][2])
        pe_claim(regs[1][0], regs[1][1])
        for k in range(ND - 1):
            wo_tile_k(1, k, regs[1][2])
        P.wait("pe", ("ysd", ph, ND - 1))
        wo_tile_k(0, ND - 1, regs[0][2])
        wo_tile_k(1, ND - 1, regs[1][2])
        for i in range(TPT):
            r0, r1, ps = regs[i]
            if i >= 2:
                pe_claim(r0, r1)
                for k in range(ND):
                    wo_tile_k(i, k, ps)
            P.wait("ve", ("womm", ph, i))
            if i == 0:
                P.wait("ve", ("sums", ph))
            sl, ii = xslot(ph, i // 2), i % 2
            P.op("ve", lambda v, ps=ps, sl=sl, ii=ii: v.tensor_add(
                t.x_sl[:, sl, ii, :], ps[:, 0:1024], t.x_sl[:, sl, ii, :]),
                inc="sve", event=("part", ph, i))
            release(r0, ("part", ph, i))
            release(r1, ("part", ph, i))
        for h in range(NHF):
            P.wait("sy", ("part", ph, 2 * h + 1))
            P.dma("sy", lambda s, ph=ph, h=h: s.dma_start(
                out=t.o_out[ph * Tp + h * 256: ph * Tp + (h + 1) * 256, :]
                    .rearrange("(n p) d -> p n d", p=128),
                in_=t.x_sl[:, xslot(ph, h), :, :]), "sp",
                event=("pstore", ph, h))

    def slot_pe(ph):
        P.wait("pe", ("alpha", ph))
        pe_claim("B0", "B1", "C0", "T1")
        for i in range(TPT):
            P.op("pe", lambda e, i=i, pp=ph % 2: e.matmul(
                t.psB[0:64, 0:512], t.al_bf[:, i, :], t.xn_tm[:, pp, i, 0:512],
                start=(i == 0), stop=(i == TPT - 1)))
            P.op("pe", lambda e, i=i, pp=ph % 2: e.matmul(
                t.psB[0:64, 512:1024], t.al_bf[:, i, :],
                t.xn_tm[:, pp, i, 512:1024],
                start=(i == 0), stop=(i == TPT - 1)))
            P.op("pe", lambda e, i=i: e.matmul(
                t.psC[0:64, 0:1], t.al_bf[:, i, :], t.ones_c[:, 0:1],
                start=(i == 0), stop=(i == TPT - 1)))
        for i in range(TPT):
            P.op("pe", lambda e, i=i: e.transpose(
                t.psT1[0:64, i * 128:(i + 1) * 128],
                t.al_bf[:, i, :], t.ident[:]),
                inc="spe" if i == TPT - 1 else None,
                event=("slotmm", ph) if i == TPT - 1 else None)

    def slot_ve(ph):
        P.wait("ve", ("slotmm", ph))
        if ph == 0:
            P.op("ve", lambda v: v.tensor_copy(t.sl_acc[:, 0:512],
                                               t.psB[0:64, 0:512]))
            P.op("ve", lambda v: v.tensor_copy(t.sl_acc[:, 512:1024],
                                               t.psB[0:64, 512:1024]))
            P.op("ve", lambda v: v.tensor_copy(t.den_acc[:], t.psC[0:64, 0:1]))
        else:
            P.op("ve", lambda v: v.tensor_add(t.sl_acc[:, 0:512],
                                              t.sl_acc[:, 0:512], t.psB[0:64, 0:512]))
            P.op("ve", lambda v: v.tensor_add(t.sl_acc[:, 512:1024],
                                              t.sl_acc[:, 512:1024],
                                              t.psB[0:64, 512:1024]))
            P.op("ve", lambda v: v.tensor_add(t.den_acc[:], t.den_acc[:],
                                              t.psC[0:64, 0:1]))
        P.op("ve", lambda v, ph=ph: v.tensor_copy(
            t.alpha_T[0:64, ph * Tp:(ph + 1) * Tp], t.psT1[0:64, 0:Tp]),
            inc="sve", event=("slotev", ph))
        for r in ("B0", "B1", "C0", "T1"):
            release(r, ("slotev", ph))

    def ln_apply(ph):
        pp = ph % 2
        P.wait("ve", ("rstd", ph))
        if ph >= 2:
            P.wait("ve", ("slotmm", ph - 2))
        P.op("ve", lambda v, pp=pp: v.tensor_mul(
            t.smrn[:, pp, :], t.smn[:, pp, :], t.sr[:, pp, :]))
        P.drain("ve")
        P.op("ve", lambda v, pp=pp: v.tensor_scalar(
            t.smrn[:, pp, :], t.smrn[:, pp, :], -1.0, None, ALU.mult))
        P.drain("ve")
        for i in range(TPT):
            sl, ii = xslot(ph, i // 2), i % 2
            P.op("ve", lambda v, i=i, sl=sl, ii=ii, pp=pp: v.tensor_scalar(
                t.xn_tm[:, pp, i, :], t.x_sl[:, sl, ii, :],
                t.sr[:, pp, i:i + 1], t.smrn[:, pp, i:i + 1],
                ALU.mult, ALU.add),
                inc="sve" if i == TPT - 1 else None,
                event=("ln", ph) if i == TPT - 1 else None)

    def xn_transposes(ph):
        P.wait("pe", ("ln", ph))
        if ph >= 1:
            pass
        for k in range(ND):
            reg = "T0" if k % 2 == 0 else "T1"
            psT = psH[reg]
            pe_claim(reg)
            for i in range(TPT):
                P.op("pe", lambda e, k=k, i=i, psT=psT, pp=ph % 2: e.transpose(
                    psT[:, i * 128:(i + 1) * 128],
                    t.xn_tm[:, pp, i, k * 128:(k + 1) * 128], t.ident[:]),
                    inc="spe" if i == TPT - 1 else None,
                    event=("xnT", ph, k) if i == TPT - 1 else None)
            P.wait("ve", ("xnT", ph, k))
            if ph >= 1 and k < 2:
                P.wait("ve", ("wku", ph - 1, ND - 1))
                P.wait("ve", ("hmm", ph - 1, NH - 1))
            P.op("ve", lambda v, k=k, psT=psT: v.tensor_copy(
                t.xn_f[:, k, :], psT[:, 0:Tp]), inc="sve", event=("xnf", ph, k))
            release(reg, ("xnf", ph, k))

    def wku_loop(ph):
        P.wait("pe", ("xnf", ph, ND - 1))
        for j in range(ND):
            par = j % 2
            if par == 0:
                rw, rk, ru = "A0", "A1", "B0"
            else:
                rw, rk, ru = "B1", "C0", "C1"
            psw, psk, psu = (psH[rw][:, 0:Tp], psH[rk][:, 0:Tp],
                            psH[ru][:, 0:Tp])
            pe_claim(rw, rk, ru)
            for k in range(ND):
                P.op("pe", lambda e, j=j, k=k, psw=psw: e.matmul(
                    psw, t.Wp_sb[:, k, j * 128:(j + 1) * 128], t.xn_f[:, k, :],
                    start=(k == 0), stop=(k == ND - 1)))
            for k in range(ND):
                P.op("pe", lambda e, j=j, k=k, psk=psk: e.matmul(
                    psk, t.Wp_sb[:, k, 1024 + j * 128:1024 + (j + 1) * 128],
                    t.xn_f[:, k, :], start=(k == 0), stop=(k == ND - 1)))
            for k in range(ND):
                last = (k == ND - 1)
                P.op("pe", lambda e, j=j, k=k, psu=psu: e.matmul(
                    psu, t.Wu_sb[:, k, j * 128:(j + 1) * 128], t.xn_f[:, k, :],
                    start=(k == 0), stop=(k == ND - 1)),
                    inc="spe" if last else None,
                    event=("wku", ph, j) if last else None)

            if j == 0:
                P.op("se", lambda s: _load_tab(s, TAB_SIG))
            P.wait("se", ("wku", ph, j))
            if j >= 2:
                P.wait("se", ("absc", ph, j - 2))
            elif ph >= 1:
                P.wait("se", ("absc", ph - 1, j + 6))
            P.drain("se")
            P.op("se", lambda s, j=j, psw=psw: s.activation(
                t.g_sb[:, j % 2, :], psw, AF.Sigmoid, bias=t.bpw[:, j:j + 1]))
            P.op("se", lambda s, j=j, psk=psk: s.activation(
                t.s2_sb[:, j % 2, :], psk, AF.Sigmoid, bias=t.bpk2[:, j:j + 1],
                scale=-2.0), inc="sse", event=("gs", ph, j))
            release(rw, ("gs", ph, j))
            release(rk, ("gs", ph, j))
            if ph + 1 < NPH and j < TPT:
                stats_tile(ph + 1, j)
            if ph + 1 < NPH and j == TPT:
                rstd_window(ph + 1)

            P.wait("ve", ("gs", ph, j))
            if ph >= 1 and j == 0:
                P.wait("ve", ("womm", ph - 1, TPT - 1))
            pp = j % 2
            P.drain("ve")
            P.op("ve", lambda v, pp=pp: v.scalar_tensor_tensor(
                t.a_sb[:], t.g_sb[:, pp, :], -2.0, t.s2_sb[:, pp, :],
                ALU.mult, ALU.mult))
            P.op("ve", lambda v, pp=pp, psu=psu, j=j: v.scalar_tensor_tensor(
                t.b_sb[:], psu, t.bu_c[:, j:j + 1], t.g_sb[:, pp, :],
                ALU.add, ALU.mult), inc="sve", event=("absc", ph, j))
            release(ru, ("absc", ph, j))
            P.drain("ve")
            P.op("ve", lambda v, pp=pp: v.tensor_scalar(
                t.a_sb[:], t.a_sb[:], 1.0, None, ALU.add))
            P.drain("ve")
            init = 0.0 if ph == 0 else t.ys_bf[:, j, Tp - 1:Tp]
            P.op("ve", lambda v, j=j, init=init: v.tensor_tensor_scan(
                t.ys_bf[:, j, :], t.a_sb[:], t.b_sb[:], init,
                ALU.mult, ALU.add),
                inc="sve", event=("ysd", ph, j))
            if ph + 1 < NPH and j == TPT - 1:
                var_chain(ph + 1)
            if ph + 1 < NPH and j == ND - 1:
                ln_apply(ph + 1)

    def h_logits(ph):
        P.wait("pe", ("xnf", ph, ND - 1))
        hr = ["A0", "A1", "B0", "B1"]
        for jh in range(NH):
            reg = hr[jh]
            ps = psH[reg][:, 0:Tp]
            pe_claim(reg)
            for k in range(ND):
                last = (k == ND - 1)
                P.op("pe", lambda e, jh=jh, k=k, ps=ps: e.matmul(
                    ps, t.w1t_sb[:, k, jh * 128:(jh + 1) * 128], t.xn_f[:, k, :],
                    start=(k == 0), stop=(k == ND - 1)),
                    inc="spe" if last else None,
                    event=("hmm", ph, jh) if last else None)
            P.wait("se", ("hmm", ph, jh))
            if jh == 0:
                if ph >= 1:
                    P.wait("se", ("lmm", ph - 1))
                P.op("se", lambda s: _load_tab(s, TAB_GELU))
            P.drain("se")
            P.op("se", lambda s, jh=jh, ps=ps: s.activation(
                t.h_f[:, jh, :], ps, AF.Gelu, bias=t.bias_h[:, jh:jh + 1]),
                inc="sse", event=("hf", ph, jh))
            release(reg, ("hf", ph, jh))
        P.wait("pe", ("hf", ph, NH - 1))
        pe_claim("C0")
        for i in range(TPT):
            for kh in range(NH):
                P.op("pe", lambda e, i=i, kh=kh: e.matmul(
                    t.psC[:, i * 64:(i + 1) * 64],
                    t.h_f[:, kh, i * 128:(i + 1) * 128], t.w2_sb[:, kh, :],
                    start=(kh == 0), stop=False))
            last = (i == TPT - 1)
            P.op("pe", lambda e, i=i: e.matmul(
                t.psC[:, i * 64:(i + 1) * 64], t.ones_r[:], t.b2_r[:],
                start=False, stop=True),
                inc="spe" if last else None,
                event=("lmm", ph) if last else None)
        P.wait("se", ("lmm", ph))
        if ph >= 1:
            P.wait("se", ("alpha", ph - 1))
        P.drain("se")
        P.op("se", lambda s: s.activation(
            t.l_sb[:], t.psC[:, 0:TPT * 64], AF.Identity, scale=t.invc[:]),
            inc="sse", event=("lev", ph))
        release("C0", ("lev", ph))
        P.wait("ve", ("lev", ph))
        for i in range(TPT):
            P.op("ve", lambda v, i=i: v.max(out=t.m18[:, i, :],
                                            in_=t.l_sb[:, i, :]))
        P.drain("ve")
        P.op("ve", lambda v: v.tensor_scalar(
            t.m1n[:], t.m18[:, :, 0], -1.0, None, ALU.mult),
            inc="sve", event=("max8", ph))

    def mid_weight_loads():
        def ihl(jt, h):
            P.dma("gp", lambda g, jt=jt, h=h: g.dma_start(
                out=t.WT[:, jt, h * 4:(h + 1) * 4],
                in_=t.i_Wih_pk.rearrange("p (jt k f) -> p jt k f", jt=6, k=8)
                    [:, jt, h * 4:(h + 1) * 4]), "sg", event=("ihl", jt, h))

        def whl(jt, h):
            P.dma("gp", lambda g, jt=jt, h=h: g.dma_start(
                out=t.whh_pp[:, jt % 2, h * 4:(h + 1) * 4],
                in_=t.i_Whh_pk.rearrange("p (jt k f) -> p jt k f", jt=6, k=8)
                    [:, jt, h * 4:(h + 1) * 4]), "sg", event=("whl", jt, h))

        P.wait("gp", ("wku", NPH - 1, ND - 1))
        P.wait_all_dma("gp", ("pstore", NPH - 1, NHF - 1), "sp")
        P.wait("gp", ("slotev", NPH - 1))
        P.wait("gp", ("exps", NPH - 1))
        P.wait("gp", ("slotmm", NPH - 1))
        P.dma("gp", lambda g: g.dma_start(out=t.s0_f32[:], in_=t.i_slot[:]), "sg")
        P.dma("gp", lambda g: g.dma_start(out=t.bih_r[:], in_=t.i_bih[None, :]), "sg")
        P.dma("gp", lambda g: g.dma_start(out=t.bhh_r[:], in_=t.i_bhh[None, :]), "sg")
        P.dma("gp", lambda g: g.dma_start(out=t.bv_r[:], in_=t.i_bv[None, :]), "sg")
        P.dma("gp", lambda g: g.dma_start(out=t.bvp_r[:], in_=t.i_bvp[None, :]), "sg",
              event=("brows",))
        P.wait("gp", ("womm", NPH - 1, TPT - 1))
        for jt in range(2):
            ihl(jt, 0)
            whl(jt, 0)
            ihl(jt, 1)
            whl(jt, 1)
        for jt in range(2, 6):
            ihl(jt, 0)
            ihl(jt, 1)

    for i in range(TPT):
        stats_tile(0, i)
    var_chain(0)
    rstd_window(0)
    ln_apply(0)
    for ph in range(NPH + 1):
        window(ph)
        if ph >= 1:
            alpha_fin(ph - 1)
        if ph >= 1:
            wo_block(ph - 1)
            if ph + 1 < NPH:
                for h in range(NHF):
                    P.wait("sy", ("pstore", ph - 1, h))
                    P.dma("sy", lambda s, ph=ph, h=h: s.dma_start(
                        out=t.x_sl[:, xslot(ph + 1, h), :, :],
                        in_=t.i_x[(ph + 1) * Tp + h * 256:
                                  (ph + 1) * Tp + (h + 1) * 256, :]
                            .rearrange("(n p) d -> p n d", p=128)), "sx",
                        event=("xload", ph + 1, h))
        if ph >= 1:
            slot_pe(ph - 1)
            slot_ve(ph - 1)
        if ph < NPH:
            xn_transposes(ph)
            h_logits(ph)
            wku_loop(ph)
        if ph == NPH:
            mid_weight_loads()

    P.wait("ve", ("slotev", NPH - 1))
    P.wait("ve", ("exps", NPH - 1))
    P.op("ve", lambda v: v.tensor_scalar(t.dent[:], t.den_acc[:], 1e-8, None,
                                         ALU.add))
    P.drain("ve")
    P.op("ve", lambda v: v.reciprocal(t.dent[:], t.dent[:]))
    P.drain("ve")
    P.op("ve", lambda v: v.tensor_scalar(t.si_bf[:], t.sl_acc[:], t.dent[:],
                                         None, ALU.mult), inc="sve",
         event=("si",))
    P.wait("pe", ("si",))
    pe_claim("T0")
    for k in range(ND):
        P.op("pe", lambda e, k=k: e.transpose(
            t.psT0[:, k * 64:(k + 1) * 64], t.si_bf[0:64, k * 128:(k + 1) * 128],
            t.ident[0:64, 0:64]),
            inc="spe" if k == ND - 1 else None,
            event=("siT",) if k == ND - 1 else None)
    P.wait("ve", ("siT",))
    for k in range(ND):
        P.op("ve", lambda v, k=k: v.tensor_copy(
            t.si_f[:, k, :], t.psT0[:, k * 64:(k + 1) * 64]),
            inc="sve" if k == ND - 1 else None,
            event=("sif",) if k == ND - 1 else None)
    release("T0", ("sif",))

    whhbuf = t.whh_pp
    P.wait("pe", ("sif",))
    P.wait_all_dma("pe", ("brows",), "sg")
    mregs = ["A0", "A1", "B0", "B1", "C0", "C1"]
    for jt in range(6):
        reg = mregs[jt]
        ps = psH[reg]
        pe_claim(reg)
        for k in range(ND):
            if k % 4 == 0:
                P.wait("pe", ("ihl", jt, k // 4))
            P.op("pe", lambda e, k=k, jt=jt, ps=ps: e.matmul(
                ps[0:64, :], t.si_f[:, k, :], t.WT[:, jt, k, :],
                start=(k == 0), stop=False))
        P.op("pe", lambda e, jt=jt, ps=ps: e.matmul(
            ps[0:64, :], t.ones_r[0:1, 0:64], t.bih_r[:, jt * 512:(jt + 1) * 512],
            start=False, stop=True), inc="spe", event=("gmm", 0, jt))
        P.wait("se", ("gmm", 0, jt))
        P.op("se", lambda s, jt=jt, ps=ps: s.activation(
            t.gi_sb[:, jt * 512:(jt + 1) * 512], ps[0:64, :], AF.Identity),
            inc="sse", event=("gev", 0, jt))
        release(reg, ("gev", 0, jt))
        reg = mregs[jt]
        ps = psH[reg]
        pe_claim(reg)
        for k in range(ND):
            if k % 4 == 0:
                P.wait("pe", ("whl", jt, k // 4))
            P.op("pe", lambda e, k=k, jt=jt, ps=ps: e.matmul(
                ps[0:64, :], t.si0f[:, k, :],
                t.whh_pp[:, jt % 2, k, :], start=(k == 0), stop=False))
        P.op("pe", lambda e, jt=jt, ps=ps: e.matmul(
            ps[0:64, :], t.ones_r[0:1, 0:64], t.bhh_r[:, jt * 512:(jt + 1) * 512],
            start=False, stop=True), inc="spe", event=("ghmm", jt))
        if jt + 2 < 6:
            P.wait("gp", ("ghmm", jt))
            for h in range(2):
                P.dma("gp", lambda g, jt=jt, h=h: g.dma_start(
                    out=t.whh_pp[:, jt % 2, h * 4:(h + 1) * 4],
                    in_=t.i_Whh_pk.rearrange("p (jt k f) -> p jt k f",
                                             jt=6, k=8)
                        [:, jt + 2, h * 4:(h + 1) * 4]), "sg",
                    event=("whl", jt + 2, h))
        if jt < 4:
            P.wait("ve", ("ghmm", jt))
            P.wait("ve", ("gev", 0, jt))
            P.op("ve", lambda v, jt=jt, ps=ps: v.tensor_add(
                t.gi_sb[:, jt * 512:(jt + 1) * 512],
                t.gi_sb[:, jt * 512:(jt + 1) * 512], ps[0:64, :]),
                inc="sve", event=("gev", 1, jt))
        else:
            P.wait("ve", ("ghmm", jt))
            P.op("ve", lambda v, jt=jt, ps=ps: v.tensor_copy(
                t.gh_n[:, (jt - 4) * 512:(jt - 3) * 512], ps[0:64, :]),
                inc="sve", event=("gev", 1, jt))
        release(reg, ("gev", 1, jt))
    P.dma("gp", lambda g: g.dma_start(
        out=t.wv_sb[:], in_=t.i_wv_pk.rearrange("p (k f) -> p k f", k=8)), "sg")
    P.dma("gp", lambda g: g.dma_start(
        out=t.wvp_sb[:], in_=t.i_wvp_pk.rearrange("p (k f) -> p k f", k=8)), "sg",
        event=("w2load",))
    P.wait("gp", ("svmm",))
    P.dma("gp", lambda g: g.dma_start(
        out=t.wo2_sb[:], in_=t.i_wo2_pk.rearrange("p (k f) -> p k f", k=8)), "sg",
        event=("w3load",))

    P.wait("se", ("gev", 1, 3))
    P.op("se", lambda s: _load_tab(s, TAB_SIG))
    P.op("se", lambda s: s.activation(t.r_g[:], t.gi_sb[:, 0:1024], AF.Sigmoid))
    P.op("se", lambda s: s.activation(t.z_g[:], t.gi_sb[:, 1024:2048],
                                      AF.Sigmoid), inc="sse", event=("rz",))
    P.wait("ve", ("rz",))
    P.drain("ve")
    P.op("ve", lambda v: v.tensor_mul(t.t64a[:], t.r_g[:], t.gh_n[:, 0:1024]))
    P.drain("ve")
    P.op("ve", lambda v: v.tensor_add(t.t64a[:], t.t64a[:],
                                      t.gi_sb[:, 2048:3072]),
         inc="sve", event=("n_in",))
    P.wait("se", ("n_in",))
    P.op("se", lambda s: s.activation(t.n_g[:], t.t64a[:], AF.Tanh),
         inc="sse", event=("n_g",))
    P.wait("ve", ("n_g",))
    P.op("ve", lambda v: v.tensor_sub(t.t64a[:], t.s0_f32[:], t.n_g[:]))
    P.drain("ve")
    P.op("ve", lambda v: v.tensor_mul(t.t64a[:], t.t64a[:], t.z_g[:]))
    P.drain("ve")
    P.op("ve", lambda v: v.tensor_add(t.snew[:], t.n_g[:], t.t64a[:]))
    P.drain("ve")
    P.op("ve", lambda v: v.tensor_copy(t.snew_bf[:], t.snew[:]), inc="sve",
         event=("snew",))
    P.wait("pe", ("snew",))
    pe_claim("T0")
    for k in range(ND):
        P.op("pe", lambda e, k=k: e.transpose(
            t.psT0[:, k * 64:(k + 1) * 64],
            t.snew_bf[0:64, k * 128:(k + 1) * 128], t.ident[0:64, 0:64]),
            inc="spe" if k == ND - 1 else None,
            event=("snT",) if k == ND - 1 else None)
    P.wait("ve", ("snT",))
    for k in range(ND):
        P.op("ve", lambda v, k=k: v.tensor_copy(
            t.snew_f[:, k, :], t.psT0[:, k * 64:(k + 1) * 64]),
            inc="sve" if k == ND - 1 else None,
            event=("snf",) if k == ND - 1 else None)
    release("T0", ("snf",))
    P.drain("ve")
    P.op("ve", lambda v: v.tensor_scalar(t.bq_c[:], t.bq_c[:], DQS, None,
                                         ALU.mult), inc="sve", event=("bqs",))

    P.wait("pe", ("snf",))
    P.wait("pe", ("bqs",))
    pe_claim("A0", "A1")
    for m in range(2):
        for dqc in range(2):
            for k in range(ND):
                last = (m == 1 and dqc == 1 and k == ND - 1)
                P.op("pe", lambda e, m=m, dqc=dqc, k=k: e.matmul(
                    (t.psA[:, dqc * 64:(dqc + 1) * 64] if m == 0
                     else t.psA[:, 512 + dqc * 64:512 + (dqc + 1) * 64]),
                    (t.wq_sb if m == 0 else t.wk_sb)[:, k, dqc * 128:(dqc + 1) * 128],
                    t.snew_f[:, k, :], start=(k == 0), stop=(k == ND - 1)),
                    inc="spe" if last else None,
                    event=("qkmm",) if last else None)
    P.wait("se", ("qkmm",))
    for dqc in range(2):
        P.op("se", lambda s, dqc=dqc: s.activation(
            t.q_bf[:, dqc, :], t.psA[:, dqc * 64:(dqc + 1) * 64], AF.Identity,
            bias=t.bq_c[:, dqc:dqc + 1], scale=DQS))
        P.op("se", lambda s, dqc=dqc: s.activation(
            t.k_bf[:, dqc, :], t.psA[:, 512 + dqc * 64:512 + (dqc + 1) * 64],
            AF.Identity, bias=t.bk_c[:, dqc:dqc + 1]),
            inc="sse" if dqc == 1 else None,
            event=("qk",) if dqc == 1 else None)
    release("A0", ("qk",))
    release("A1", ("qk",))

    P.wait("pe", ("qk",))
    pe_claim("B0")
    for dqc in range(2):
        P.op("pe", lambda e, dqc=dqc: e.matmul(
            t.psB[0:64, 0:64], t.q_bf[:, dqc, :], t.k_bf[:, dqc, :],
            start=(dqc == 0), stop=(dqc == 1)),
            inc="spe" if dqc == 1 else None,
            event=("scmm",) if dqc == 1 else None)
    P.wait("se", ("scmm",))
    P.op("se", lambda s: s.activation(t.sc_sb[:], t.psB[0:64, 0:64], AF.Identity),
         inc="sse", event=("scev",))
    release("B0", ("scev",))
    P.wait("ve", ("scev",))
    P.op("ve", lambda v: v.reduce_max(t.scm[:], t.sc_sb[:], axis=AX.X))
    P.drain("ve")
    P.op("ve", lambda v: v.tensor_scalar(t.scm[:], t.scm[:], -1.0, None,
                                         ALU.mult), inc="sve", event=("scmax",))
    P.wait("se", ("scmax",))
    P.op("se", lambda s: _load_tab(s, TAB_LNEXP))
    P.drain("se")
    P.op("se", lambda s: s.activation(t.sc_sb[:], t.sc_sb[:], AF.Exp,
                                      bias=t.scm[:], accum_out=t.scs[:]),
         inc="sse", event=("scexp",))
    P.wait("ve", ("scexp",))
    P.op("ve", lambda v: v.reciprocal(t.scs[:], t.scs[:]))
    P.drain("ve")
    P.op("ve", lambda v: v.tensor_scalar(t.attn_bf[:], t.sc_sb[:], t.scs[:],
                                         None, ALU.mult), inc="sve",
         event=("attn",))
    P.wait("pe", ("attn",))
    pe_claim("T0")
    P.op("pe", lambda e: e.transpose(t.psT0[0:64, 0:64], t.attn_bf[0:64, 0:64],
                                     t.ident[0:64, 0:64]),
         inc="spe", event=("atT",))
    P.wait("ve", ("atT",))
    P.op("ve", lambda v: v.tensor_copy(t.attnT_bf[:], t.psT0[0:64, 0:64]),
         inc="sve", event=("atTe",))
    release("T0", ("atTe",))

    P.wait_all_dma("pe", ("w2load",), "sg")
    pe_claim("B0", "B1")
    for jt in range(2):
        ps = psH["B0" if jt == 0 else "B1"]
        for k in range(ND):
            P.op("pe", lambda e, jt=jt, k=k, ps=ps: e.matmul(
                ps[0:64, :], t.snew_f[:, k, :],
                t.wv_sb[:, k, jt * 512:(jt + 1) * 512],
                start=(k == 0), stop=False))
        P.op("pe", lambda e, jt=jt, ps=ps: e.matmul(
            ps[0:64, :], t.ones_r[0:1, 0:64], t.bv_r[:, jt * 512:(jt + 1) * 512],
            start=False, stop=True),
            inc="spe" if jt == 1 else None,
            event=("vmm",) if jt == 1 else None)
    P.wait("se", ("vmm",))
    P.op("se", lambda s: s.activation(t.v_bf[:, 0:512], t.psB[0:64, 0:512],
                                      AF.Identity))
    P.op("se", lambda s: s.activation(t.v_bf[:, 512:1024],
                                      t.psB[0:64, 512:1024], AF.Identity),
         inc="sse", event=("vev",))
    release("B0", ("vev",))
    release("B1", ("vev",))

    P.wait("pe", ("atTe",))
    P.wait("pe", ("vev",))
    pe_claim("C0", "C1")
    for jt in range(2):
        ps = psH["C0" if jt == 0 else "C1"]
        P.op("pe", lambda e, jt=jt, ps=ps: e.matmul(
            ps[0:64, :], t.attnT_bf[:], t.v_bf[0:64, jt * 512:(jt + 1) * 512],
            start=True, stop=True),
            inc="spe" if jt == 1 else None,
            event=("cmm",) if jt == 1 else None)
    P.wait("ve", ("cmm",))
    P.op("ve", lambda v: v.tensor_add(t.y_sb[:, 0:512], t.psC[0:64, 0:512],
                                      t.snew[:, 0:512]))
    P.op("ve", lambda v: v.tensor_add(t.y_sb[:, 512:1024],
                                      t.psC[0:64, 512:1024],
                                      t.snew[:, 512:1024]),
         inc="sve", event=("y",))
    release("C0", ("y",))
    release("C1", ("y",))

    P.wait("se", ("y",))
    P.drain("se")
    P.op("se", lambda s: s.activation(t.dump[0:64, :], t.y_sb[:], AF.Identity,
                                      accum_out=t.ysum[:]))
    P.drain("se")
    P.op("se", lambda s: s.activation(t.dump[0:64, :], t.y_sb[:], AF.Square,
                                      accum_out=t.yssq[:]),
         inc="sse", event=("ysums",))
    P.wait("ve", ("ysums",))
    P.op("ve", lambda v: v.tensor_scalar(t.ymu[:], t.ysum[:], 1.0 / 1024.0,
                                         None, ALU.mult))
    P.op("ve", lambda v: v.tensor_scalar(t.yvar[:], t.yssq[:], 1.0 / 1024.0,
                                         1e-5, ALU.mult, ALU.add))
    P.drain("ve")
    P.op("ve", lambda v: v.tensor_mul(t.ytmp[:], t.ymu[:], t.ymu[:]))
    P.drain("ve")
    P.op("ve", lambda v: v.tensor_sub(t.yvar[:], t.yvar[:], t.ytmp[:]),
         inc="sve", event=("yvar",))
    P.wait("se", ("yvar",))
    P.op("se", lambda s: s.activation(t.ytmp[:], t.yvar[:], AF.Ln))
    P.drain("se")
    P.op("se", lambda s: s.activation(t.yrst[:], t.ytmp[:], AF.Exp, scale=-0.5),
         inc="sse", event=("yrst",))
    P.wait("ve", ("yrst",))
    P.op("ve", lambda v: v.tensor_mul(t.ymu[:], t.ymu[:], t.yrst[:]))
    P.drain("ve")
    P.op("ve", lambda v: v.tensor_scalar(t.ymu[:], t.ymu[:], -1.0, None,
                                         ALU.mult))
    P.drain("ve")
    P.op("ve", lambda v: v.tensor_scalar(t.y0_bf[:], t.y_sb[:], t.yrst[:],
                                         t.ymu[:], ALU.mult, ALU.add),
         inc="sve", event=("y0",))
    P.wait("pe", ("y0",))
    pe_claim("T0")
    for k in range(ND):
        P.op("pe", lambda e, k=k: e.transpose(
            t.psT0[:, k * 64:(k + 1) * 64],
            t.y0_bf[0:64, k * 128:(k + 1) * 128], t.ident[0:64, 0:64]),
            inc="spe" if k == ND - 1 else None,
            event=("yT",) if k == ND - 1 else None)
    P.wait("ve", ("yT",))
    for k in range(ND):
        P.op("ve", lambda v, k=k: v.tensor_scalar(
            t.yln_f[:, k, :], t.psT0[:, k * 64:(k + 1) * 64],
            t.lng_c[:, k:k + 1], t.lnb_c[:, k:k + 1], ALU.mult, ALU.add),
            inc="sve" if k == ND - 1 else None,
            event=("ylnf",) if k == ND - 1 else None)
    release("T0", ("ylnf",))

    P.wait("pe", ("ylnf",))
    pe_claim("A0", "A1")
    for jt in range(2):
        ps = psH["A0" if jt == 0 else "A1"]
        for k in range(ND):
            P.op("pe", lambda e, jt=jt, k=k, ps=ps: e.matmul(
                ps[0:64, :], t.yln_f[:, k, :],
                t.wvp_sb[:, k, jt * 512:(jt + 1) * 512],
                start=(k == 0), stop=False))
        P.op("pe", lambda e, jt=jt, ps=ps: e.matmul(
            ps[0:64, :], t.ones_r[0:1, 0:64],
            t.bvp_r[:, jt * 512:(jt + 1) * 512], start=False, stop=True),
            inc="spe" if jt == 1 else None,
            event=("svmm",) if jt == 1 else None)
    P.wait("se", ("svmm",))
    P.op("se", lambda s: s.activation(t.sv_bf[:, 0:512], t.psA[0:64, 0:512],
                                      AF.Identity))
    P.op("se", lambda s: s.activation(t.sv_bf[:, 512:1024],
                                      t.psA[0:64, 512:1024], AF.Identity),
         inc="sse", event=("sv",))
    release("A0", ("sv",))
    release("A1", ("sv",))

    zregs = ["A0", "A1", "B0", "B1", "C0", "C1", "A0", "A1"]

    def zmm_phase(q):
        if q == 0:
            P.wait("pe", ("sv",))
            P.wait("pe", ("botot",))
            P.wait_all_dma("pe", ("w3load",), "sg")
        for k in range(ND):
            reg = zregs[k]
            pe_claim(reg)
            ps = psH[reg][:, 0:Tp]
            P.op("pe", lambda e, k=k, q=q, ps=ps: e.matmul(
                ps, t.sv_bf[0:64, k * 128:(k + 1) * 128],
                t.alpha_T[0:64, q * Tp:(q + 1) * Tp],
                start=True, stop=True), inc="spe", event=("zmm", q, k))
            P.wait("se", ("zmm", q, k))
            if q >= 2 and k == 0:
                P.wait("se", ("omm", q - 2, TPT - 1))
                P.drain("se")
            P.op("se", lambda s, k=k, q=q, ps=ps: s.activation(
                t.z_f[:, q % 2, k, :], ps, AF.Identity),
                inc="sse", event=("zev", q, k))
            release(reg, ("zev", q, k))

    if not cfg.get("skip_pass2"):
        zmm_phase(0)
    for p2 in ([] if cfg.get("skip_pass2") else range(NPH)):
        pr = p2 % 2
        if p2 == 0:
            P.wait_all_dma("sy", ("pstore", NPH - 1, NHF - 1), "sp")
            P.wait("sy", ("sv",))
        if p2 >= 2:
            P.wait("sy", ("fadd", p2 - 2, TPT - 1))
            P.wait("sy", ("ostore", p2 - 2, NHF - 1))
        P.dma("sy", lambda s, p2=p2, pr=pr: s.dma_start(
            out=t.prd[:, pr, :, :],
            in_=t.o_out[p2 * Tp:(p2 + 1) * Tp, :]
                .rearrange("(n p) d -> p n d", p=128)), "sx",
            event=("pread", p2))
        if p2 + 1 < NPH:
            zmm_phase(p2 + 1)
        oregs = [("A0", "A1", t.psA), ("B0", "B1", t.psB), ("C0", "C1", t.psC),
                 ("A0", "A1", t.psA)]
        for i in range(TPT):
            r0, r1, ps = oregs[i]
            pe_claim(r0, r1)
            for k in range(ND):
                if i == 0:
                    P.wait("pe", ("zev", p2, k))
                for f in range(2):
                    P.op("pe", lambda e, i=i, f=f, k=k, ps=ps, p2=p2: e.matmul(
                        ps[:, f * 512:(f + 1) * 512],
                        t.z_f[:, p2 % 2, k, i * 128:(i + 1) * 128],
                        t.wo2_sb[:, k, f * 512:(f + 1) * 512],
                        start=(k == 0), stop=False))
            for f in range(2):
                last = (f == 1)
                P.op("pe", lambda e, i=i, f=f, ps=ps: e.matmul(
                    ps[:, f * 512:(f + 1) * 512], t.ones_r[:],
                    t.bo_tot[:, f * 512:(f + 1) * 512], start=False, stop=True),
                    inc="spe" if last else None,
                    event=("omm", p2, i) if last else None)
            P.wait("ve", ("omm", p2, i))
            if i == 0:
                P.wait("ve", ("pread", p2))
            P.op("ve", lambda v, ps=ps, pr=pr, i=i: v.tensor_add(
                t.prd[:, pr, i, :], ps[:, 0:1024], t.prd[:, pr, i, :]),
                inc="sve", event=("fadd", p2, i))
            release(r0, ("fadd", p2, i))
            release(r1, ("fadd", p2, i))
        for h in range(NHF):
            P.wait("sy", ("fadd", p2, 2 * h + 1))
            P.dma("sy", lambda s, p2=p2, h=h, pr=pr: s.dma_start(
                out=t.o_out[p2 * Tp + h * 256:p2 * Tp + (h + 1) * 256, :]
                    .rearrange("(n p) d -> p n d", p=128),
                in_=t.prd[:, pr, 2 * h:2 * h + 2, :]), "so",
                event=("ostore", p2, h))

    if cfg.get("taps"):
        P.wait("sy", ("sv",))
        P.wait("sy", ("slotev", NPH - 1))
        P.wait("sy", ("ysd", NPH - 1, ND - 1))
        P.wait("sy", ("biash",))
        P.dma("sy", lambda s: s.dma_start(out=t.o_alpha[:], in_=t.alpha_T[:]), "so")
        P.dma("sy", lambda s: s.dma_start(out=t.o_slacc[:], in_=t.sl_acc[:]), "so")
        P.dma("sy", lambda s: s.dma_start(out=t.o_den[:], in_=t.den_acc[:]), "so")
        P.dma("sy", lambda s: s.dma_start(out=t.o_sv[:], in_=t.sv_bf[:]), "so")
        P.dma("sy", lambda s: s.dma_start(out=t.o_carry[:], in_=t.carry[:]), "so")
        P.dma("sy", lambda s: s.dma_start(out=t.o_biash[:], in_=t.bias_h[:]), "so")
        P.dma("sy", lambda s: s.dma_start(out=t.o_snew[:], in_=t.snew[:]), "so")
        P.dma("sy", lambda s: s.dma_start(out=t.o_s0[:], in_=t.s0_f32[:]), "so")
        P.dma("sy", lambda s: s.dma_start(out=t.o_ghn[:], in_=t.gh_n[:]), "so")
        P.dma("sy", lambda s: s.dma_start(out=t.o_gi[:], in_=t.gi_sb[:]), "so")
        P.dma("sy", lambda s: s.dma_start(out=t.o_ssum[:], in_=t.ssum[:, (NPH - 1) % 2]), "so")
        P.dma("sy", lambda s: s.dma_start(out=t.o_ssq[:], in_=t.ssq[:, (NPH - 1) % 2]), "so")
        P.dma("sy", lambda s: s.dma_start(out=t.o_sr[:], in_=t.sr[:, (NPH - 1) % 2]), "so")
        P.dma("sy", lambda s: s.dma_start(out=t.o_xn[:], in_=t.xn_tm[:, (NPH - 1) % 2]), "so")
        P.dma("sy", lambda s: s.dma_start(out=t.o_xnf[:], in_=t.xn_f[:]), "so")
        P.dma("sy", lambda s: s.dma_start(out=t.o_ysbf[:], in_=t.ys_bf[:]), "so")
        P.dma("sy", lambda s: s.dma_start(out=t.o_lsb[:], in_=t.l_sb[:]), "so")

    if P.mode == "sy":
        _, k = DMA_POOLS["so"]
        n = P.pool_n["so"]
        for i in range(k):
            cnt = 16 * ((n - i - 1) // k + 1) if n > i else 0
            if cnt:
                P.eng.wait_ge(P.sems[f"so{i}"], cnt)


def build_nc(cfg):
    T = cfg["T"]; Tp = cfg["Tp"]
    TPT = Tp // 128
    nc = bass.Bass("TRN2", target_bir_lowering=False)
    cfg = dict(cfg)
    cfg["tab_lnexp"] = _act_set_idx(nc, {AF.Ln, AF.Exp, AF.Identity, AF.Square})
    cfg["tab_sig"] = _act_set_idx(nc, {AF.Sigmoid, AF.Tanh, AF.Identity,
                                       AF.Square, AF.Abs})
    cfg["tab_gelu"] = _act_set_idx(nc, {AF.Gelu, AF.Identity})

    class TB:
        pass
    t = TB()

    def din(name, shape, dtype=F32):
        return nc.dram_tensor(name, shape, dtype, kind="ExternalInput").ap()

    t.i_x = din("x", [T, 1024])
    t.i_Wp_pk = din("Wp_pk", [128, 8 * 2048])
    t.i_bp0_pk = din("bp0_pk", [128, 8])
    t.i_bpk_pk = din("bpk_pk", [128, 8])
    t.i_Wu_pk = din("Wu_pk", [128, 8 * 1024])
    t.i_bu_pk = din("bu_pk", [128, 8])
    t.i_Wo_pk = din("Wo_pk", [128, 8 * 1024])
    t.i_bo = din("bo", [1024])
    t.i_slot = din("slot_init", [64, 1024])
    t.i_w1 = din("w1", [2048, 512])
    t.i_w1t_pk = din("w1t_pk", [128, 8 * 512])
    t.i_b1_pk = din("b1_pk", [128, 4])
    t.i_w2_pk = din("w2_pk", [128, 4 * 64])
    t.i_b2 = din("b2", [64])
    t.i_tau_b = din("tau_b", [128, 1])
    t.i_Wih_pk = din("Wih_pk", [128, 6 * 8 * 512])
    t.i_Whh_pk = din("Whh_pk", [128, 6 * 8 * 512])
    t.i_bih = din("bih", [3072])
    t.i_bhh = din("bhh", [3072])
    t.i_wq_pk = din("wq_pk", [128, 8 * 256])
    t.i_bq_pk = din("bq_pk", [128, 2])
    t.i_wk_pk = din("wk_pk", [128, 8 * 256])
    t.i_bk_pk = din("bk_pk", [128, 2])
    t.i_wv_pk = din("wv_pk", [128, 8 * 1024])
    t.i_bv = din("bv", [1024])
    t.i_lng_pk = din("lng_pk", [128, 8])
    t.i_lnb_pk = din("lnb_pk", [128, 8])
    t.i_wvp_pk = din("wvp_pk", [128, 8 * 1024])
    t.i_bvp = din("bvp", [1024])
    t.i_wo2_pk = din("wo2_pk", [128, 8 * 1024])
    t.i_bof = din("bof", [1024])
    t.i_ident = din("ident", [128, 128], BF16)
    t.i_ones_r = din("ones_r", [1, 128], BF16)
    t.i_ones_c = din("ones_c", [128, 1], BF16)
    t.o_out = nc.dram_tensor("out", [T, 1024], F32, kind="ExternalOutput").ap()
    if cfg.get("taps"):
        t.o_alpha = nc.dram_tensor("o_alpha", [64, T], BF16, kind="ExternalOutput").ap()
        t.o_slacc = nc.dram_tensor("o_slacc", [64, 1024], F32, kind="ExternalOutput").ap()
        t.o_den = nc.dram_tensor("o_den", [64, 1], F32, kind="ExternalOutput").ap()
        t.o_sv = nc.dram_tensor("o_sv", [64, 1024], BF16, kind="ExternalOutput").ap()
        t.o_carry = nc.dram_tensor("o_carry", [128, 8], F32, kind="ExternalOutput").ap()
        t.o_biash = nc.dram_tensor("o_biash", [128, 4], F32, kind="ExternalOutput").ap()
        t.o_snew = nc.dram_tensor("o_snew", [64, 1024], F32, kind="ExternalOutput").ap()
        t.o_gi = nc.dram_tensor("o_gi", [64, 3072], F32, kind="ExternalOutput").ap()
        t.o_s0 = nc.dram_tensor("o_s0", [64, 1024], F32, kind="ExternalOutput").ap()
        t.o_ghn = nc.dram_tensor("o_ghn", [64, 1024], F32, kind="ExternalOutput").ap()
        t.o_ssum = nc.dram_tensor("o_ssum", [128, TPT], F32, kind="ExternalOutput").ap()
        t.o_ssq = nc.dram_tensor("o_ssq", [128, TPT], F32, kind="ExternalOutput").ap()
        t.o_sr = nc.dram_tensor("o_sr", [128, TPT], F32, kind="ExternalOutput").ap()
        t.o_xn = nc.dram_tensor("o_xn", [128, TPT, 1024], BF16, kind="ExternalOutput").ap()
        t.o_xnf = nc.dram_tensor("o_xnf", [128, 8, Tp], BF16, kind="ExternalOutput").ap()
        t.o_ysbf = nc.dram_tensor("o_ysbf", [128, 8, Tp], BF16, kind="ExternalOutput").ap()
        t.o_lsb = nc.dram_tensor("o_lsb", [128, TPT, 64], F32, kind="ExternalOutput").ap()

    with ExitStack() as ctx:
        def sb(name, shape, dtype):
            return ctx.enter_context(nc.sbuf_tensor("sb_" + name, shape, dtype))

        t.ident = sb("ident", [128, 128], BF16)
        t.ones_r = sb("ones_r", [1, 128], BF16)
        t.ones_c = sb("ones_c", [128, 1], BF16)
        t.tau_b = sb("tau_b", [128, 1], F32)
        t.invc = sb("invc", [128, 1], F32)
        t.alpha_T = sb("alpha_T", [64, T], BF16)
        t.sl_acc = sb("sl_acc", [64, 1024], F32)
        t.den_acc = sb("den_acc", [64, 1], F32)
        t.carry = sb("carry", [128, 8], F32)
        t.si0f = sb("si0f", [128, 8, 64], BF16)
        t.sm_f = sb("sm_f", [128, 8], F32)
        t.sm_fb = sb("sm_fb", [128, 8], BF16)
        t.bpw = sb("bpw", [128, 8], F32)
        t.bpk2 = sb("bpk2", [128, 8], F32)
        t.bu_c = sb("bu_c", [128, 8], F32)
        t.b1_c = sb("b1_c", [128, 4], F32)
        t.bias_h = sb("bias_h", [128, 4], F32)
        t.b2_r = sb("b2_r", [1, 64], BF16)
        t.bq_c = sb("bq_c", [128, 2], F32)
        t.bk_c = sb("bk_c", [128, 2], F32)
        t.lng_c = sb("lng_c", [128, 8], F32)
        t.lnb_c = sb("lnb_c", [128, 8], F32)
        t.bo_tot = sb("bo_tot", [1, 1024], BF16)
        t.wq_sb = sb("wq_sb", [128, 8, 256], BF16)
        t.wk_sb = sb("wk_sb", [128, 8, 256], BF16)
        t.w2_sb = sb("w2_sb", [128, 4, 64], BF16)
        t.sv_bf = sb("sv_bf", [64, 1024], BF16)
        t.dump = sb("dump", [128, 1024], BF16)
        s1 = ExitStack()

        def sb(name, shape, dtype, _s=s1):
            return _s.enter_context(nc.sbuf_tensor("sb_" + name, shape, dtype))

        t.s0_bf = sb("s0_bf", [64, 1024], BF16)
        t.bo1_r = sb("bo1_r", [8, 128], F32)
        t.bo2_r = sb("bo2_r", [8, 128], F32)
        t.Wp_sb = sb("Wp_sb", [128, 8, 2048], BF16)
        t.Wu_sb = sb("Wu_sb", [128, 8, 1024], BF16)
        t.Wo_sb = sb("Wo_sb", [128, 8, 1024], BF16)
        t.w1t_sb = sb("w1t_sb", [128, 8, 512], BF16)
        t.w1s = sb("w1s", [128, 2, 512], BF16)
        t.x_sl = sb("x_sl", [128, 4, 2, 1024], F32)
        t.xn_tm = sb("xn_tm", [128, 2, TPT, 1024], BF16)
        t.xn_f = sb("xn_f", [128, 8, Tp], BF16)
        t.g_sb = sb("g_sb", [128, 2, Tp], F32)
        t.s2_sb = sb("s2_sb", [128, 2, Tp], F32)
        t.a_sb = sb("a_sb", [128, Tp], F32)
        t.b_sb = sb("b_sb", [128, Tp], F32)
        t.ys_bf = sb("ys_bf", [128, 8, Tp], BF16)
        t.h_f = sb("h_f", [128, 4, Tp], BF16)
        t.l_sb = sb("l_sb", [128, TPT, 64], F32)
        t.e_al = sb("e_al", [128, TPT, 64], F32)
        t.al_bf = sb("al_bf", [128, TPT, 64], BF16)
        t.m18 = sb("m18", [128, TPT, 8], F32)
        t.m1n = sb("m1n", [128, TPT], F32)
        t.alns = sb("alns", [128, TPT], F32)
        t.ssum = sb("ssum", [128, 2, TPT], F32)
        t.ssq = sb("ssq", [128, 2, TPT], F32)
        t.smn = sb("smn", [128, 2, TPT], F32)
        t.svar = sb("svar", [128, 2, TPT], F32)
        t.sr = sb("sr", [128, 2, TPT], F32)
        t.smrn = sb("smrn", [128, 2, TPT], F32)
        t.stmp = sb("stmp", [128, 2, TPT], F32)
        s1.close()
        s2 = ExitStack()

        def sb(name, shape, dtype, _s=s2):
            return _s.enter_context(nc.sbuf_tensor("sb_" + name, shape, dtype))

        t.WT = sb("WT", [128, 6, 8, 512], BF16)
        t.whh_pp = sb("whh_pp", [128, 2, 8, 512], BF16)
        t.s0_f32 = sb("s0_f32", [64, 1024], F32)
        t.bih_r = sb("bih_r", [1, 3072], BF16)
        t.bhh_r = sb("bhh_r", [1, 3072], BF16)
        t.bv_r = sb("bv_r", [1, 1024], BF16)
        t.bvp_r = sb("bvp_r", [1, 1024], BF16)
        t.wv_sb = sb("wv_sb", [128, 8, 1024], BF16)
        t.wvp_sb = sb("wvp_sb", [128, 8, 1024], BF16)
        t.si_bf = sb("si_bf", [64, 1024], BF16)
        t.si_f = sb("si_f", [128, 8, 64], BF16)
        t.gi_sb = sb("gi_sb", [64, 3072], F32)
        t.gh_n = sb("gh_n", [64, 1024], F32)
        t.t64a = sb("t64a", [64, 1024], F32)
        t.r_g = sb("r_g", [64, 1024], F32)
        t.z_g = sb("z_g", [64, 1024], F32)
        t.n_g = sb("n_g", [64, 1024], F32)
        t.snew = sb("snew", [64, 1024], F32)
        t.snew_bf = sb("snew_bf", [64, 1024], BF16)
        t.snew_f = sb("snew_f", [128, 8, 64], BF16)
        t.dent = sb("dent", [64, 1], F32)
        t.q_bf = sb("q_bf", [128, 2, 64], BF16)
        t.k_bf = sb("k_bf", [128, 2, 64], BF16)
        t.sc_sb = sb("sc_sb", [64, 64], F32)
        t.scm = sb("scm", [64, 1], F32)
        t.scs = sb("scs", [64, 1], F32)
        t.attn_bf = sb("attn_bf", [64, 64], BF16)
        t.attnT_bf = sb("attnT_bf", [64, 64], BF16)
        t.v_bf = sb("v_bf", [64, 1024], BF16)
        t.y_sb = sb("y_sb", [64, 1024], F32)
        t.ysum = sb("ysum", [64, 1], F32)
        t.yssq = sb("yssq", [64, 1], F32)
        t.ymu = sb("ymu", [64, 1], F32)
        t.yvar = sb("yvar", [64, 1], F32)
        t.ytmp = sb("ytmp", [64, 1], F32)
        t.yrst = sb("yrst", [64, 1], F32)
        t.y0_bf = sb("y0_bf", [64, 1024], BF16)
        t.yln_f = sb("yln_f", [128, 8, 64], BF16)
        s2.close()
        s3 = ExitStack()

        def sb(name, shape, dtype, _s=s3):
            return _s.enter_context(nc.sbuf_tensor("sb_" + name, shape, dtype))

        t.wo2_sb = sb("wo2_sb", [128, 8, 1024], BF16)
        t.z_f = sb("z_f", [128, 2, 8, Tp], BF16)
        t.prd = sb("prd", [128, 2, TPT, 1024], F32)
        s3.close()

        with ExitStack() as pctx:
            e2 = pctx.enter_context
            t.psA = e2(nc.psum_tensor("psA", [128, 1024], F32))
            t.psB = e2(nc.psum_tensor("psB", [128, 1024], F32))
            t.psC = e2(nc.psum_tensor("psC", [128, 1024], F32))
            t.psT0 = e2(nc.psum_tensor("psT0", [128, 1024], BF16))
            t.psT1 = e2(nc.psum_tensor("psT1", [128, 1024], BF16))
            sems = {k: e2(nc.semaphore(k)) for k in SEMS}

            P = Plan()
            P.begin("dry")
            build_walk(cfg, t, P)

            with nc.Block() as block:
                @block.sync
                def _(eng):
                    P.begin("sy", eng, sems)
                    build_walk(cfg, t, P)

                @block.gpsimd
                def _(eng):
                    P.begin("gp", eng, sems)
                    build_walk(cfg, t, P)

                @block.tensor
                def _(eng):
                    P.begin("pe", eng, sems)
                    build_walk(cfg, t, P)

                @block.vector
                def _(eng):
                    P.begin("ve", eng, sems)
                    build_walk(cfg, t, P)

                @block.scalar
                def _(eng):
                    P.begin("se", eng, sems)
                    build_walk(cfg, t, P)
    return nc


def make_in_maps(inputs, cfg, n_cores=8):
    T = cfg["T"]
    x = np.ascontiguousarray(np.asarray(inputs["L0_attn_x"], dtype=np.float32))
    ident = np.eye(128, dtype=ml_dtypes.bfloat16)
    ones_r = np.ones((1, 128), dtype=ml_dtypes.bfloat16)
    ones_c = np.ones((128, 1), dtype=ml_dtypes.bfloat16)
    tau_b = np.ascontiguousarray(
        np.broadcast_to(np.asarray(inputs["L0_ffn_tau"], np.float32).reshape(1, 1),
                        (128, 1)))
    names = {
        "bo": "L0_attn_bo", "slot_init": "L0_ffn_slot_init",
        "w1": "L0_ffn_w1", "b2": "L0_ffn_b2",
        "bih": "L0_ffn_bih", "bhh": "L0_ffn_bhh", "bv": "L0_ffn_bv",
        "bvp": "L0_ffn_bvp", "bof": "L0_ffn_bo",
    }
    common = {k: np.ascontiguousarray(np.asarray(inputs[v], dtype=np.float32))
              for k, v in names.items()}
    common.update({"tau_b": tau_b, "ident": ident, "ones_r": ones_r,
                   "ones_c": ones_c})

    def f32(v):
        return np.asarray(inputs[v], np.float32)

    def pk(a, k):
        return np.ascontiguousarray(
            a.reshape(k, 128, -1).transpose(1, 0, 2).reshape(128, -1))

    def cpk(a, k):
        return np.ascontiguousarray(a.reshape(k, 128).T)

    def jtpk(a):
        return np.ascontiguousarray(
            a.reshape(8, 128, 6, 512).transpose(1, 2, 0, 3).reshape(128, -1))

    bp = f32("L0_attn_bp")
    common.update({
        "Wp_pk": pk(f32("L0_attn_Wp")[:, 0:2048], 8),
        "bp0_pk": cpk(bp[0:1024], 8), "bpk_pk": cpk(bp[1024:2048], 8),
        "Wu_pk": pk(f32("L0_attn_Wu"), 8), "bu_pk": cpk(f32("L0_attn_bu"), 8),
        "Wo_pk": pk(f32("L0_attn_Wo"), 8),
        "w1t_pk": pk(f32("L0_ffn_w1")[0:1024], 8),
        "b1_pk": cpk(f32("L0_ffn_b1"), 4),
        "w2_pk": pk(f32("L0_ffn_w2"), 4),
        "Wih_pk": jtpk(np.ascontiguousarray(f32("L0_ffn_Wih").T)),
        "Whh_pk": jtpk(np.ascontiguousarray(f32("L0_ffn_Whh").T)),
        "wq_pk": pk(f32("L0_ffn_wq"), 8), "bq_pk": cpk(f32("L0_ffn_bq"), 2),
        "wk_pk": pk(f32("L0_ffn_wk"), 8), "bk_pk": cpk(f32("L0_ffn_bk"), 2),
        "wv_pk": pk(f32("L0_ffn_wv"), 8),
        "lng_pk": cpk(f32("L0_ffn_ln_g"), 8), "lnb_pk": cpk(f32("L0_ffn_ln_b"), 8),
        "wvp_pk": pk(f32("L0_ffn_wvp"), 8),
        "wo2_pk": pk(f32("L0_ffn_wo"), 8),
    })
    in_maps = []
    for c in range(n_cores):
        m = dict(common)
        m["x"] = np.ascontiguousarray(x[c, :T, :])
        in_maps.append(m)
    return in_maps


CFG = {"T": 8192, "Tp": 512}


def kernel(**inputs):
    cfg = CFG
    nc = build_nc(cfg)
    in_maps = make_in_maps(inputs, cfg, n_cores=8)
    res = run_bass_kernel_spmd(nc, in_maps, core_ids=list(range(8)))
    out = np.stack([r["out"] for r in res.results], axis=0)
    return out.astype(np.float32)


# version 29
# speedup vs baseline: 1.0157x; 1.0157x over previous
import numpy as np
import ml_dtypes
from contextlib import ExitStack

import concourse.bass as bass
import concourse.mybir as mybir
from concourse.bass_utils import run_bass_kernel_spmd
from concourse.hw_specs import get_activation_tables


def _act_set_idx(nc, funcs):
    for idx, fs in enumerate(get_activation_tables(nc.m.arch).values()):
        if all(f in fs for f in funcs):
            return idx
    raise ValueError(f"no activation table set contains {funcs}")


def _load_tab(eng, idx):
    return eng.add_instruction(mybir.InstLoadActFuncSet(
        name=eng.bass.get_next_instruction_name(),
        act_func_set_id=idx, ins=[], outs=[]))

F32 = mybir.dt.float32
BF16 = mybir.dt.bfloat16
AF = mybir.ActivationFunctionType
ALU = mybir.AluOpType
AX = mybir.AxisListType

DMA_POOLS = {"sx": ("sy", 6), "sp": ("sy", 3), "so": ("sy", 3), "sg": ("gp", 3)}
SEMS = {"spe": "pe", "sve": "ve", "sse": "se"}
for _p, (_e, _k) in DMA_POOLS.items():
    for _i in range(_k):
        SEMS[f"{_p}{_i}"] = _e


class Plan:

    def __init__(self):
        self.events = {}
        self.dry = True
        self.mode = None
        self.eng = None
        self.sems = None
        self.counts = {k: 0 for k in SEMS}
        self.pool_n = {p: 0 for p in DMA_POOLS}

    def begin(self, mode, eng=None, sems=None):
        self.dry = (mode == "dry")
        self.mode = mode
        self.eng = eng
        self.sems = sems
        self.counts = {k: 0 for k in SEMS}
        self.pool_n = {p: 0 for p in DMA_POOLS}

    def op(self, engine, fn, inc=None, inc_n=1, event=None):
        inst = None
        if self.mode == engine:
            inst = fn(self.eng)
        if inc is not None:
            assert SEMS[inc] == engine, f"sem {inc} not produced by {engine}"
            self.counts[inc] += inc_n
            if inst is not None:
                inst.then_inc(self.sems[inc], inc_n)
            if event is not None:
                val = (inc, self.counts[inc], dict(self.counts))
                if self.dry:
                    assert event not in self.events, f"dup event {event}"
                    self.events[event] = val
                else:
                    assert self.events.get(event)[:2] == val[:2], \
                        f"event drift {event}: {self.events.get(event)[:2]} vs {val[:2]}"
        else:
            assert event is None
        return inst

    def dma(self, engine, fn, pool, event=None):
        eng_of, k = DMA_POOLS[pool]
        assert eng_of == engine
        i = self.pool_n[pool]
        self.pool_n[pool] = i + 1
        sem = f"{pool}{i % k}"
        if i >= k and self.mode == engine:
            self.eng.wait_ge(self.sems[sem], 16 * (i // k))
        return self.op(engine, fn, inc=sem, inc_n=16, event=event)

    def wait(self, engine, event):
        if self.mode == engine:
            sem_name, val, _ = self.events[event]
            self.eng.wait_ge(self.sems[sem_name], val)

    def wait_all_dma(self, engine, event, pool):
        if self.mode == engine:
            _, _, snap = self.events[event]
            _, k = DMA_POOLS[pool]
            for i in range(k):
                sem = f"{pool}{i}"
                if snap.get(sem, 0) > 0:
                    self.eng.wait_ge(self.sems[sem], snap[sem])

    def drain(self, engine):
        if self.mode == engine:
            self.eng.drain()


def build_walk(cfg, t, P):
    TAB_LNEXP = cfg["tab_lnexp"]
    TAB_SIG = cfg["tab_sig"]
    TAB_GELU = cfg["tab_gelu"]
    T = cfg["T"]; Tp = cfg["Tp"]
    NPH = T // Tp
    TPT = Tp // 128
    NHF = TPT // 2
    ND = 8
    NH = 4
    SQ2I = float(1.0 / np.sqrt(2.0))
    DQS = float(256 ** -0.5)

    psH = {
        "A0": t.psA[:, 0:512], "A1": t.psA[:, 512:1024],
        "B0": t.psB[:, 0:512], "B1": t.psB[:, 512:1024],
        "C0": t.psC[:, 0:512], "C1": t.psC[:, 512:1024],
        "T0": t.psT0[:, :], "T1": t.psT1[:, :],
    }
    bank_free = {}

    def pe_claim(*regions):
        for r in regions:
            ev = bank_free.pop(r, None)
            if ev is not None:
                P.wait("pe", ev)

    def release(region, event):
        bank_free[region] = event

    def xslot(ph, h):
        return (2 * ph + h) % 4

    gload = [
        lambda g: g.dma_start(out=t.Wp_sb[:],
            in_=t.i_Wp_pk.rearrange("p (k f) -> p k f", k=8)),
        lambda g: g.dma_start(out=t.Wu_sb[:],
            in_=t.i_Wu_pk.rearrange("p (k f) -> p k f", k=8)),
        lambda g: g.dma_start(out=t.Wo_sb[:],
            in_=t.i_Wo_pk.rearrange("p (k f) -> p k f", k=8)),
        lambda g: g.dma_start(out=t.w1t_sb[:],
            in_=t.i_w1t_pk.rearrange("p (k f) -> p k f", k=8)),
        lambda g: g.dma_start(out=t.w2_sb[:],
            in_=t.i_w2_pk.rearrange("p (k f) -> p k f", k=4)),
        lambda g: g.dma_start(out=t.wq_sb[:],
            in_=t.i_wq_pk.rearrange("p (k f) -> p k f", k=8)),
        lambda g: g.dma_start(out=t.wk_sb[:],
            in_=t.i_wk_pk.rearrange("p (k f) -> p k f", k=8)),
        lambda g: g.dma_start(out=t.s0_bf[:], in_=t.i_slot[:]),
        lambda g: g.dma_start(out=t.b2_r[:], in_=t.i_b2[None, :]),
    ]
    for i, fn in enumerate(gload):
        P.dma("gp", fn, "sg", event=("wload",) if i == len(gload) - 1 else None)

    sload = [
        lambda s: s.dma_start(out=t.ident[:], in_=t.i_ident[:]),
        lambda s: s.dma_start(out=t.ones_r[:], in_=t.i_ones_r[:]),
        lambda s: s.dma_start(out=t.ones_c[:], in_=t.i_ones_c[:]),
        lambda s: s.dma_start(out=t.tau_b[:], in_=t.i_tau_b[:]),
        lambda s: s.dma_start(out=t.bpw[:], in_=t.i_bp0_pk[:]),
        lambda s: s.dma_start(out=t.bpk2[:], in_=t.i_bpk_pk[:]),
        lambda s: s.dma_start(out=t.bu_c[:], in_=t.i_bu_pk[:]),
        lambda s: s.dma_start(out=t.b1_c[:], in_=t.i_b1_pk[:]),
        lambda s: s.dma_start(out=t.bq_c[:], in_=t.i_bq_pk[:]),
        lambda s: s.dma_start(out=t.bk_c[:], in_=t.i_bk_pk[:]),
        lambda s: s.dma_start(out=t.lng_c[:], in_=t.i_lng_pk[:]),
        lambda s: s.dma_start(out=t.lnb_c[:], in_=t.i_lnb_pk[:]),
        lambda s: s.dma_start(out=t.bo1_r[:],
            in_=t.i_bo.rearrange("(a b) -> a b", a=8)),
        lambda s: s.dma_start(out=t.bo2_r[:],
            in_=t.i_bof.rearrange("(a b) -> a b", a=8)),
    ]
    for i, fn in enumerate(sload):
        P.dma("sy", fn, "sx", event=("cload",) if i == len(sload) - 1 else None)

    for ph in range(min(2, NPH)):
        for h in range(NHF):
            P.dma("sy", lambda s, ph=ph, h=h: s.dma_start(
                out=t.x_sl[:, xslot(ph, h), :, :],
                in_=t.i_x[ph * Tp + h * 256: ph * Tp + (h + 1) * 256, :]
                    .rearrange("(n p) d -> p n d", p=128)), "sx",
                event=("xload", ph, h))

    P.op("se", lambda s: _load_tab(s, TAB_SIG))
    P.wait_all_dma("se", ("cload",), "sx")
    P.op("se", lambda s: s.activation(t.invc[:], t.tau_b[:], AF.Abs),
         inc="sse", event=("tauabs",))
    P.wait("ve", ("tauabs",))
    P.wait_all_dma("ve", ("cload",), "sx")
    P.wait_all_dma("ve", ("wload",), "sg")
    P.op("ve", lambda v: v.tensor_scalar(t.invc[:], t.invc[:], 0.1, None, ALU.add))
    P.drain("ve")
    P.op("ve", lambda v: v.reciprocal(t.invc[:], t.invc[:]))
    P.drain("ve")
    P.op("ve", lambda v: v.tensor_scalar(t.bpk2[:], t.bpk2[:], -2.0, None, ALU.mult),
         inc="sve", event=("bpk2",))
    P.op("ve", lambda v: v.tensor_add(t.bo1_r[:], t.bo1_r[:], t.bo2_r[:]),
         inc="sve", event=("bosum",))
    P.wait("gp", ("bosum",))
    P.dma("gp", lambda g: g.dma_start(
        out=t.bo_tot[:], in_=t.bo1_r[:]), "sg", event=("botot",))

    P.wait_all_dma("pe", ("cload",), "sx")
    P.wait_all_dma("pe", ("wload",), "sg")
    pe_claim("T0")
    for k in range(ND):
        P.op("pe", lambda e, k=k: e.transpose(
            t.psT0[:, k * 64:(k + 1) * 64],
            t.s0_bf[0:64, k * 128:(k + 1) * 128], t.ident[0:64, 0:64]),
            inc="spe" if k == ND - 1 else None,
            event=("s0T",) if k == ND - 1 else None)
    P.wait("ve", ("s0T",))
    for k in range(ND):
        P.op("ve", lambda v, k=k: v.tensor_copy(
            t.si0f[:, k, :], t.psT0[:, k * 64:(k + 1) * 64]),
            inc="sve" if k == ND - 1 else None,
            event=("si0f",) if k == ND - 1 else None)
    release("T0", ("si0f",))
    P.drain("ve")
    for k in range(ND):
        P.op("ve", lambda v, k=k: v.reduce_sum(
            t.sm_f[:, k:k + 1], t.si0f[:, k, :], axis=AX.X))
    P.drain("ve")
    P.op("ve", lambda v: v.tensor_scalar(t.sm_fb[:], t.sm_f[:], 1.0 / 64.0, None,
                                         ALU.mult), inc="sve", event=("smf",))
    regs4 = ["A0", "A1", "B0", "B1"]
    pe_claim(*regs4)
    ps4 = [psH[r] for r in regs4]
    for k in range(ND):
        if k >= 2:
            P.wait("gp", ("smwk", k - 2))
        P.dma("gp", lambda g, k=k: g.dma_start(
            out=t.w1s[:, k % 2, :],
            in_=t.i_w1[1024 + k * 128:1024 + (k + 1) * 128, :]), "sg",
            event=("w1s", k))
        P.wait("pe", ("w1s", k))
        if k == 0:
            P.wait("pe", ("smf",))
        for jh in range(NH):
            P.op("pe", lambda e, jh=jh, k=k: e.matmul(
                ps4[jh][:, 0:1], t.w1s[:, k % 2, jh * 128:(jh + 1) * 128],
                t.sm_fb[:, k:k + 1], start=(k == 0), stop=(k == ND - 1)),
                inc="spe" if jh == NH - 1 else None,
                event=("smwk", k) if jh == NH - 1 else None)
    P.wait("ve", ("smwk", ND - 1))
    for jh in range(NH):
        P.op("ve", lambda v, jh=jh: v.tensor_add(
            t.bias_h[:, jh:jh + 1], t.b1_c[:, jh:jh + 1], ps4[jh][:, 0:1]),
            inc="sve" if jh == NH - 1 else None,
            event=("biash",) if jh == NH - 1 else None)
    for r in regs4:
        release(r, ("biash",))

    def stats_tile(ph, i):
        pp = ph % 2
        sl, ii = xslot(ph, i // 2), i % 2
        if i % 2 == 0:
            P.wait("se", ("xload", ph, i // 2))
        if i == 0 and ph >= 2:
            P.wait("se", ("var", ph - 2))
        P.drain("se")
        P.op("se", lambda s, sl=sl, ii=ii, i=i, pp=pp: s.activation(
            t.dump[:], t.x_sl[:, sl, ii, :], AF.Identity,
            accum_out=t.ssum[:, pp, i:i + 1]))
        P.drain("se")
        P.op("se", lambda s, sl=sl, ii=ii, i=i, pp=pp: s.activation(
            t.dump[:], t.x_sl[:, sl, ii, :], AF.Square,
            accum_out=t.ssq[:, pp, i:i + 1]),
            inc="sse" if i == TPT - 1 else None,
            event=("sums", ph) if i == TPT - 1 else None)

    def var_chain(ph):
        pp = ph % 2
        smn, svar, stmp = (t.smn[:, pp, :], t.svar[:, pp, :], t.stmp[:, pp, :])
        P.wait("ve", ("sums", ph))
        if ph >= 2:
            P.wait("ve", ("rstd", ph - 2))
        P.op("ve", lambda v, smn=smn, pp=pp: v.tensor_scalar(
            smn, t.ssum[:, pp, :], 1.0 / 1024.0, None, ALU.mult))
        P.op("ve", lambda v, svar=svar, pp=pp: v.tensor_scalar(
            svar, t.ssq[:, pp, :], 1.0 / 1024.0, 1e-5, ALU.mult, ALU.add))
        P.drain("ve")
        P.op("ve", lambda v, stmp=stmp, smn=smn: v.tensor_mul(stmp, smn, smn))
        P.drain("ve")
        P.op("ve", lambda v, svar=svar, stmp=stmp: v.tensor_sub(svar, svar, stmp),
             inc="sve", event=("var", ph))

    def rstd_window(ph):
        pp = ph % 2
        P.wait("se", ("var", ph))
        P.op("se", lambda s: _load_tab(s, TAB_LNEXP))
        P.op("se", lambda s, pp=pp: s.activation(
            t.stmp[:, pp, :], t.svar[:, pp, :], AF.Ln))
        P.drain("se")
        P.op("se", lambda s, pp=pp: s.activation(
            t.sr[:, pp, :], t.stmp[:, pp, :], AF.Exp, scale=-0.5),
            inc="sse", event=("rstd", ph))
        P.op("se", lambda s: _load_tab(s, TAB_SIG))

    def window(ph):
        if ph < 1:
            return
        P.wait("se", ("max8", ph - 1))
        if ph >= 2:
            P.wait("se", ("alpha", ph - 2))
        P.op("se", lambda s: _load_tab(s, TAB_LNEXP))
        for i in range(TPT):
            P.op("se", lambda s, i=i: s.activation(
                t.e_al[:, i, :], t.l_sb[:, i, :], AF.Exp,
                bias=t.m1n[:, i:i + 1]),
                inc="sse" if i == TPT - 1 else None,
                event=("exps", ph - 1) if i == TPT - 1 else None)

    def alpha_fin(ph):
        P.wait("ve", ("exps", ph))
        if ph >= 1:
            P.wait("ve", ("slotmm", ph - 1))
        for i in range(TPT):
            P.op("ve", lambda v, i=i: v.scalar_tensor_tensor(
                t.e_al[:, i, :], t.l_sb[:, i, :], t.m18[:, i, 7:8],
                t.e_al[:, i, :], ALU.is_ge, ALU.mult))
        P.drain("ve")
        for i in range(TPT):
            P.op("ve", lambda v, i=i: v.reduce_sum(
                t.alns[:, i:i + 1], t.e_al[:, i, :], axis=AX.X))
        P.drain("ve")
        P.op("ve", lambda v: v.reciprocal(t.alns[:], t.alns[:]))
        P.drain("ve")
        for i in range(TPT):
            P.op("ve", lambda v, i=i: v.tensor_scalar(
                t.al_bf[:, i, :], t.e_al[:, i, :], t.alns[:, i:i + 1], None,
                ALU.mult),
                inc="sve" if i == TPT - 1 else None,
                event=("alpha", ph) if i == TPT - 1 else None)

    def wo_block(ph):
        regs = [("A0", "A1", t.psA), ("B0", "B1", t.psB), ("C0", "C1", t.psC),
                ("A0", "A1", t.psA)]

        def wo_tile_k(i, k, ps):
            for f in range(2):
                last = (f == 1 and k == ND - 1)
                P.op("pe", lambda e, i=i, f=f, k=k, ps=ps: e.matmul(
                    ps[:, f * 512:(f + 1) * 512],
                    t.ys_bf[:, k, i * 128:(i + 1) * 128],
                    t.Wo_sb[:, k, f * 512:(f + 1) * 512],
                    start=(k == 0), stop=(k == ND - 1)),
                    inc="spe" if last else None,
                    event=("womm", ph, i) if last else None)

        pe_claim(regs[0][0], regs[0][1])
        for k in range(ND - 1):
            P.wait("pe", ("ysd", ph, k))
            wo_tile_k(0, k, regs[0][2])
        pe_claim(regs[1][0], regs[1][1])
        for k in range(ND - 1):
            wo_tile_k(1, k, regs[1][2])
        P.wait("pe", ("ysd", ph, ND - 1))
        wo_tile_k(0, ND - 1, regs[0][2])
        wo_tile_k(1, ND - 1, regs[1][2])
        for i in range(TPT):
            r0, r1, ps = regs[i]
            if i >= 2:
                pe_claim(r0, r1)
                for k in range(ND):
                    wo_tile_k(i, k, ps)
            P.wait("ve", ("womm", ph, i))
            if i == 0:
                P.wait("ve", ("sums", ph))
            sl, ii = xslot(ph, i // 2), i % 2
            P.op("ve", lambda v, ps=ps, sl=sl, ii=ii: v.tensor_add(
                t.x_sl[:, sl, ii, :], ps[:, 0:1024], t.x_sl[:, sl, ii, :]),
                inc="sve", event=("part", ph, i))
            release(r0, ("part", ph, i))
            release(r1, ("part", ph, i))
        for h in range(NHF):
            P.wait("sy", ("part", ph, 2 * h + 1))
            P.dma("sy", lambda s, ph=ph, h=h: s.dma_start(
                out=t.o_out[ph * Tp + h * 256: ph * Tp + (h + 1) * 256, :]
                    .rearrange("(n p) d -> p n d", p=128),
                in_=t.x_sl[:, xslot(ph, h), :, :]), "sp",
                event=("pstore", ph, h))

    def slot_pe(ph):
        P.wait("pe", ("alpha", ph))
        pe_claim("B0", "B1", "C0", "T1")
        for i in range(TPT):
            P.op("pe", lambda e, i=i, pp=ph % 2: e.matmul(
                t.psB[0:64, 0:512], t.al_bf[:, i, :], t.xn_tm[:, pp, i, 0:512],
                start=(i == 0), stop=(i == TPT - 1)))
            P.op("pe", lambda e, i=i, pp=ph % 2: e.matmul(
                t.psB[0:64, 512:1024], t.al_bf[:, i, :],
                t.xn_tm[:, pp, i, 512:1024],
                start=(i == 0), stop=(i == TPT - 1)))
            P.op("pe", lambda e, i=i: e.matmul(
                t.psC[0:64, 0:1], t.al_bf[:, i, :], t.ones_c[:, 0:1],
                start=(i == 0), stop=(i == TPT - 1)))
        for i in range(TPT):
            P.op("pe", lambda e, i=i: e.transpose(
                t.psT1[0:64, i * 128:(i + 1) * 128],
                t.al_bf[:, i, :], t.ident[:]),
                inc="spe" if i == TPT - 1 else None,
                event=("slotmm", ph) if i == TPT - 1 else None)

    def slot_ve(ph):
        P.wait("ve", ("slotmm", ph))
        if ph == 0:
            P.op("ve", lambda v: v.tensor_copy(t.sl_acc[:, 0:512],
                                               t.psB[0:64, 0:512]))
            P.op("ve", lambda v: v.tensor_copy(t.sl_acc[:, 512:1024],
                                               t.psB[0:64, 512:1024]))
            P.op("ve", lambda v: v.tensor_copy(t.den_acc[:], t.psC[0:64, 0:1]))
        else:
            P.op("ve", lambda v: v.tensor_add(t.sl_acc[:, 0:512],
                                              t.sl_acc[:, 0:512], t.psB[0:64, 0:512]))
            P.op("ve", lambda v: v.tensor_add(t.sl_acc[:, 512:1024],
                                              t.sl_acc[:, 512:1024],
                                              t.psB[0:64, 512:1024]))
            P.op("ve", lambda v: v.tensor_add(t.den_acc[:], t.den_acc[:],
                                              t.psC[0:64, 0:1]))
        P.op("ve", lambda v, ph=ph: v.tensor_copy(
            t.alpha_T[0:64, ph * Tp:(ph + 1) * Tp], t.psT1[0:64, 0:Tp]),
            inc="sve", event=("slotev", ph))
        for r in ("B0", "B1", "C0", "T1"):
            release(r, ("slotev", ph))

    def ln_apply(ph):
        pp = ph % 2
        P.wait("ve", ("rstd", ph))
        if ph >= 2:
            P.wait("ve", ("slotmm", ph - 2))
        P.op("ve", lambda v, pp=pp: v.tensor_mul(
            t.smrn[:, pp, :], t.smn[:, pp, :], t.sr[:, pp, :]))
        P.drain("ve")
        P.op("ve", lambda v, pp=pp: v.tensor_scalar(
            t.smrn[:, pp, :], t.smrn[:, pp, :], -1.0, None, ALU.mult))
        P.drain("ve")
        for i in range(TPT):
            sl, ii = xslot(ph, i // 2), i % 2
            P.op("ve", lambda v, i=i, sl=sl, ii=ii, pp=pp: v.tensor_scalar(
                t.xn_tm[:, pp, i, :], t.x_sl[:, sl, ii, :],
                t.sr[:, pp, i:i + 1], t.smrn[:, pp, i:i + 1],
                ALU.mult, ALU.add),
                inc="sve" if i == TPT - 1 else None,
                event=("ln", ph) if i == TPT - 1 else None)

    def xn_transposes(ph):
        P.wait("pe", ("ln", ph))
        if ph >= 1:
            pass
        for k in range(ND):
            reg = "T0" if k % 2 == 0 else "T1"
            psT = psH[reg]
            pe_claim(reg)
            for i in range(TPT):
                P.op("pe", lambda e, k=k, i=i, psT=psT, pp=ph % 2: e.transpose(
                    psT[:, i * 128:(i + 1) * 128],
                    t.xn_tm[:, pp, i, k * 128:(k + 1) * 128], t.ident[:]),
                    inc="spe" if i == TPT - 1 else None,
                    event=("xnT", ph, k) if i == TPT - 1 else None)
            P.wait("ve", ("xnT", ph, k))
            if ph >= 1 and k < 2:
                P.wait("ve", ("wku", ph - 1, ND - 1))
                P.wait("ve", ("hmm", ph - 1, NH - 1))
            P.op("ve", lambda v, k=k, psT=psT: v.tensor_copy(
                t.xn_f[:, k, :], psT[:, 0:Tp]), inc="sve", event=("xnf", ph, k))
            release(reg, ("xnf", ph, k))

    def wku_loop(ph):
        P.wait("pe", ("xnf", ph, ND - 1))
        for j in range(ND):
            par = j % 2
            if par == 0:
                rw, rk, ru = "A0", "A1", "B0"
            else:
                rw, rk, ru = "B1", "C0", "C1"
            psw, psk, psu = (psH[rw][:, 0:Tp], psH[rk][:, 0:Tp],
                            psH[ru][:, 0:Tp])
            pe_claim(rw, rk, ru)
            for k in range(ND):
                P.op("pe", lambda e, j=j, k=k, psw=psw: e.matmul(
                    psw, t.Wp_sb[:, k, j * 128:(j + 1) * 128], t.xn_f[:, k, :],
                    start=(k == 0), stop=(k == ND - 1)))
            for k in range(ND):
                P.op("pe", lambda e, j=j, k=k, psk=psk: e.matmul(
                    psk, t.Wp_sb[:, k, 1024 + j * 128:1024 + (j + 1) * 128],
                    t.xn_f[:, k, :], start=(k == 0), stop=(k == ND - 1)))
            for k in range(ND):
                last = (k == ND - 1)
                P.op("pe", lambda e, j=j, k=k, psu=psu: e.matmul(
                    psu, t.Wu_sb[:, k, j * 128:(j + 1) * 128], t.xn_f[:, k, :],
                    start=(k == 0), stop=(k == ND - 1)),
                    inc="spe" if last else None,
                    event=("wku", ph, j) if last else None)

            if j == 0:
                P.op("se", lambda s: _load_tab(s, TAB_SIG))
            P.wait("se", ("wku", ph, j))
            if j >= 2:
                P.wait("se", ("absc", ph, j - 2))
            elif ph >= 1:
                P.wait("se", ("absc", ph - 1, j + 6))
            P.drain("se")
            P.op("se", lambda s, j=j, psw=psw: s.activation(
                t.g_sb[:, j % 2, :], psw, AF.Sigmoid, bias=t.bpw[:, j:j + 1]))
            P.op("se", lambda s, j=j, psk=psk: s.activation(
                t.s2_sb[:, j % 2, :], psk, AF.Sigmoid, bias=t.bpk2[:, j:j + 1],
                scale=-2.0), inc="sse", event=("gs", ph, j))
            release(rw, ("gs", ph, j))
            release(rk, ("gs", ph, j))
            if ph + 1 < NPH and j < TPT:
                stats_tile(ph + 1, j)
            if ph + 1 < NPH and j == TPT:
                rstd_window(ph + 1)

            P.wait("ve", ("gs", ph, j))
            if ph >= 1 and j == 0:
                P.wait("ve", ("womm", ph - 1, TPT - 1))
            pp = j % 2
            P.drain("ve")
            P.op("ve", lambda v, pp=pp: v.scalar_tensor_tensor(
                t.a_sb[:], t.g_sb[:, pp, :], -2.0, t.s2_sb[:, pp, :],
                ALU.mult, ALU.mult))
            P.op("ve", lambda v, pp=pp, psu=psu, j=j: v.scalar_tensor_tensor(
                t.b_sb[:], psu, t.bu_c[:, j:j + 1], t.g_sb[:, pp, :],
                ALU.add, ALU.mult), inc="sve", event=("absc", ph, j))
            release(ru, ("absc", ph, j))
            P.drain("ve")
            P.op("ve", lambda v, pp=pp: v.tensor_scalar(
                t.a_sb[:], t.a_sb[:], 1.0, None, ALU.add))
            P.drain("ve")
            init = 0.0 if ph == 0 else t.ys_bf[:, j, Tp - 1:Tp]
            P.op("ve", lambda v, j=j, init=init: v.tensor_tensor_scan(
                t.ys_bf[:, j, :], t.a_sb[:], t.b_sb[:], init,
                ALU.mult, ALU.add),
                inc="sve", event=("ysd", ph, j))
            if ph + 1 < NPH and j == TPT - 1:
                var_chain(ph + 1)
            if ph + 1 < NPH and j == ND - 1:
                ln_apply(ph + 1)

    def h_logits(ph):
        P.wait("pe", ("xnf", ph, ND - 1))
        hr = ["A0", "A1", "B0", "B1"]
        for jh in range(NH):
            reg = hr[jh]
            ps = psH[reg][:, 0:Tp]
            pe_claim(reg)
            for k in range(ND):
                last = (k == ND - 1)
                P.op("pe", lambda e, jh=jh, k=k, ps=ps: e.matmul(
                    ps, t.w1t_sb[:, k, jh * 128:(jh + 1) * 128], t.xn_f[:, k, :],
                    start=(k == 0), stop=(k == ND - 1)),
                    inc="spe" if last else None,
                    event=("hmm", ph, jh) if last else None)
            P.wait("se", ("hmm", ph, jh))
            if jh == 0:
                if ph >= 1:
                    P.wait("se", ("lmm", ph - 1))
                P.op("se", lambda s: _load_tab(s, TAB_GELU))
            P.drain("se")
            P.op("se", lambda s, jh=jh, ps=ps: s.activation(
                t.h_f[:, jh, :], ps, AF.Gelu, bias=t.bias_h[:, jh:jh + 1]),
                inc="sse", event=("hf", ph, jh))
            release(reg, ("hf", ph, jh))
        P.wait("pe", ("hf", ph, NH - 1))
        pe_claim("C0")
        for i in range(TPT):
            for kh in range(NH):
                P.op("pe", lambda e, i=i, kh=kh: e.matmul(
                    t.psC[:, i * 64:(i + 1) * 64],
                    t.h_f[:, kh, i * 128:(i + 1) * 128], t.w2_sb[:, kh, :],
                    start=(kh == 0), stop=False))
            last = (i == TPT - 1)
            P.op("pe", lambda e, i=i: e.matmul(
                t.psC[:, i * 64:(i + 1) * 64], t.ones_r[:], t.b2_r[:],
                start=False, stop=True),
                inc="spe" if last else None,
                event=("lmm", ph) if last else None)
        P.wait("se", ("lmm", ph))
        if ph >= 1:
            P.wait("se", ("alpha", ph - 1))
        P.drain("se")
        P.op("se", lambda s: s.activation(
            t.l_sb[:], t.psC[:, 0:TPT * 64], AF.Identity, scale=t.invc[:]),
            inc="sse", event=("lev", ph))
        release("C0", ("lev", ph))
        P.wait("ve", ("lev", ph))
        for i in range(TPT):
            P.op("ve", lambda v, i=i: v.max(out=t.m18[:, i, :],
                                            in_=t.l_sb[:, i, :]))
        P.drain("ve")
        P.op("ve", lambda v: v.tensor_scalar(
            t.m1n[:], t.m18[:, :, 0], -1.0, None, ALU.mult),
            inc="sve", event=("max8", ph))

    def mid_weight_loads():
        def ihl(jt, h):
            P.dma("gp", lambda g, jt=jt, h=h: g.dma_start(
                out=t.WT[:, jt, h * 4:(h + 1) * 4],
                in_=t.i_Wih_pk.rearrange("p (jt k f) -> p jt k f", jt=6, k=8)
                    [:, jt, h * 4:(h + 1) * 4]), "sg", event=("ihl", jt, h))

        def whl(jt, h):
            P.dma("gp", lambda g, jt=jt, h=h: g.dma_start(
                out=t.whh_pp[:, jt % 2, h * 4:(h + 1) * 4],
                in_=t.i_Whh_pk.rearrange("p (jt k f) -> p jt k f", jt=6, k=8)
                    [:, jt, h * 4:(h + 1) * 4]), "sg", event=("whl", jt, h))

        P.wait("gp", ("wku", NPH - 1, ND - 1))
        P.wait_all_dma("gp", ("pstore", NPH - 1, NHF - 1), "sp")
        P.wait("gp", ("slotev", NPH - 1))
        P.wait("gp", ("exps", NPH - 1))
        P.wait("gp", ("slotmm", NPH - 1))
        P.dma("gp", lambda g: g.dma_start(out=t.s0_f32[:], in_=t.i_slot[:]), "sg")
        P.dma("gp", lambda g: g.dma_start(out=t.bih_r[:], in_=t.i_bih[None, :]), "sg")
        P.dma("gp", lambda g: g.dma_start(out=t.bhh_r[:], in_=t.i_bhh[None, :]), "sg")
        P.dma("gp", lambda g: g.dma_start(out=t.bv_r[:], in_=t.i_bv[None, :]), "sg")
        P.dma("gp", lambda g: g.dma_start(out=t.bvp_r[:], in_=t.i_bvp[None, :]), "sg",
              event=("brows",))
        P.wait("gp", ("womm", NPH - 1, TPT - 1))
        for jt in range(2):
            ihl(jt, 0)
            whl(jt, 0)
            ihl(jt, 1)
            whl(jt, 1)
        for jt in range(2, 6):
            ihl(jt, 0)
            ihl(jt, 1)

    for i in range(TPT):
        stats_tile(0, i)
    var_chain(0)
    rstd_window(0)
    ln_apply(0)
    for ph in range(NPH + 1):
        window(ph)
        if ph >= 1:
            alpha_fin(ph - 1)
        if ph >= 1:
            wo_block(ph - 1)
            if ph + 1 < NPH:
                for h in range(NHF):
                    P.wait("sy", ("pstore", ph - 1, h))
                    P.dma("sy", lambda s, ph=ph, h=h: s.dma_start(
                        out=t.x_sl[:, xslot(ph + 1, h), :, :],
                        in_=t.i_x[(ph + 1) * Tp + h * 256:
                                  (ph + 1) * Tp + (h + 1) * 256, :]
                            .rearrange("(n p) d -> p n d", p=128)), "sx",
                        event=("xload", ph + 1, h))
        if ph >= 1:
            slot_pe(ph - 1)
            slot_ve(ph - 1)
        if ph < NPH:
            xn_transposes(ph)
            h_logits(ph)
            wku_loop(ph)
        if ph == NPH:
            mid_weight_loads()

    P.wait("ve", ("slotev", NPH - 1))
    P.wait("ve", ("exps", NPH - 1))
    P.op("ve", lambda v: v.tensor_scalar(t.dent[:], t.den_acc[:], 1e-8, None,
                                         ALU.add))
    P.drain("ve")
    P.op("ve", lambda v: v.reciprocal(t.dent[:], t.dent[:]))
    P.drain("ve")
    P.op("ve", lambda v: v.tensor_scalar(t.si_bf[:], t.sl_acc[:], t.dent[:],
                                         None, ALU.mult), inc="sve",
         event=("si",))
    P.wait("pe", ("si",))
    pe_claim("T0")
    for k in range(ND):
        P.op("pe", lambda e, k=k: e.transpose(
            t.psT0[:, k * 64:(k + 1) * 64], t.si_bf[0:64, k * 128:(k + 1) * 128],
            t.ident[0:64, 0:64]),
            inc="spe" if k == ND - 1 else None,
            event=("siT",) if k == ND - 1 else None)
    P.wait("ve", ("siT",))
    for k in range(ND):
        P.op("ve", lambda v, k=k: v.tensor_copy(
            t.si_f[:, k, :], t.psT0[:, k * 64:(k + 1) * 64]),
            inc="sve" if k == ND - 1 else None,
            event=("sif",) if k == ND - 1 else None)
    release("T0", ("sif",))

    whhbuf = t.whh_pp
    P.wait("pe", ("sif",))
    P.wait_all_dma("pe", ("brows",), "sg")
    mregs = ["A0", "A1", "B0", "B1", "C0", "C1"]
    for jt in range(6):
        reg = mregs[jt]
        ps = psH[reg]
        pe_claim(reg)
        for k in range(ND):
            if k % 4 == 0:
                P.wait("pe", ("ihl", jt, k // 4))
            P.op("pe", lambda e, k=k, jt=jt, ps=ps: e.matmul(
                ps[0:64, :], t.si_f[:, k, :], t.WT[:, jt, k, :],
                start=(k == 0), stop=False))
        P.op("pe", lambda e, jt=jt, ps=ps: e.matmul(
            ps[0:64, :], t.ones_r[0:1, 0:64], t.bih_r[:, jt * 512:(jt + 1) * 512],
            start=False, stop=True), inc="spe", event=("gmm", 0, jt))
        P.wait("se", ("gmm", 0, jt))
        P.op("se", lambda s, jt=jt, ps=ps: s.activation(
            t.gi_sb[:, jt * 512:(jt + 1) * 512], ps[0:64, :], AF.Identity),
            inc="sse", event=("gev", 0, jt))
        release(reg, ("gev", 0, jt))
        reg = mregs[jt]
        ps = psH[reg]
        pe_claim(reg)
        for k in range(ND):
            if k % 4 == 0:
                P.wait("pe", ("whl", jt, k // 4))
            P.op("pe", lambda e, k=k, jt=jt, ps=ps: e.matmul(
                ps[0:64, :], t.si0f[:, k, :],
                t.whh_pp[:, jt % 2, k, :], start=(k == 0), stop=False))
        P.op("pe", lambda e, jt=jt, ps=ps: e.matmul(
            ps[0:64, :], t.ones_r[0:1, 0:64], t.bhh_r[:, jt * 512:(jt + 1) * 512],
            start=False, stop=True), inc="spe", event=("ghmm", jt))
        if jt + 2 < 6:
            P.wait("gp", ("ghmm", jt))
            for h in range(2):
                P.dma("gp", lambda g, jt=jt, h=h: g.dma_start(
                    out=t.whh_pp[:, jt % 2, h * 4:(h + 1) * 4],
                    in_=t.i_Whh_pk.rearrange("p (jt k f) -> p jt k f",
                                             jt=6, k=8)
                        [:, jt + 2, h * 4:(h + 1) * 4]), "sg",
                    event=("whl", jt + 2, h))
        if jt < 4:
            P.wait("ve", ("ghmm", jt))
            P.wait("ve", ("gev", 0, jt))
            P.op("ve", lambda v, jt=jt, ps=ps: v.tensor_add(
                t.gi_sb[:, jt * 512:(jt + 1) * 512],
                t.gi_sb[:, jt * 512:(jt + 1) * 512], ps[0:64, :]),
                inc="sve", event=("gev", 1, jt))
        else:
            P.wait("ve", ("ghmm", jt))
            P.op("ve", lambda v, jt=jt, ps=ps: v.tensor_copy(
                t.gh_n[:, (jt - 4) * 512:(jt - 3) * 512], ps[0:64, :]),
                inc="sve", event=("gev", 1, jt))
        release(reg, ("gev", 1, jt))
    P.dma("gp", lambda g: g.dma_start(
        out=t.wv_sb[:], in_=t.i_wv_pk.rearrange("p (k f) -> p k f", k=8)), "sg")
    P.dma("gp", lambda g: g.dma_start(
        out=t.wvp_sb[:], in_=t.i_wvp_pk.rearrange("p (k f) -> p k f", k=8)), "sg",
        event=("w2load",))
    P.wait("gp", ("svmm",))
    P.dma("gp", lambda g: g.dma_start(
        out=t.wo2_sb[:], in_=t.i_wo2_pk.rearrange("p (k f) -> p k f", k=8)), "sg",
        event=("w3load",))

    P.wait("se", ("gev", 1, 3))
    P.op("se", lambda s: _load_tab(s, TAB_SIG))
    P.op("se", lambda s: s.activation(t.r_g[:], t.gi_sb[:, 0:1024], AF.Sigmoid))
    P.op("se", lambda s: s.activation(t.z_g[:], t.gi_sb[:, 1024:2048],
                                      AF.Sigmoid), inc="sse", event=("rz",))
    P.wait("ve", ("rz",))
    P.drain("ve")
    P.op("ve", lambda v: v.tensor_mul(t.t64a[:], t.r_g[:], t.gh_n[:, 0:1024]))
    P.drain("ve")
    P.op("ve", lambda v: v.tensor_add(t.t64a[:], t.t64a[:],
                                      t.gi_sb[:, 2048:3072]),
         inc="sve", event=("n_in",))
    P.wait("se", ("n_in",))
    P.op("se", lambda s: s.activation(t.n_g[:], t.t64a[:], AF.Tanh),
         inc="sse", event=("n_g",))
    P.wait("ve", ("n_g",))
    P.op("ve", lambda v: v.tensor_sub(t.t64a[:], t.s0_f32[:], t.n_g[:]))
    P.drain("ve")
    P.op("ve", lambda v: v.tensor_mul(t.t64a[:], t.t64a[:], t.z_g[:]))
    P.drain("ve")
    P.op("ve", lambda v: v.tensor_add(t.snew[:], t.n_g[:], t.t64a[:]))
    P.drain("ve")
    P.op("ve", lambda v: v.tensor_copy(t.snew_bf[:], t.snew[:]), inc="sve",
         event=("snew",))
    P.wait("pe", ("snew",))
    pe_claim("T0")
    for k in range(ND):
        P.op("pe", lambda e, k=k: e.transpose(
            t.psT0[:, k * 64:(k + 1) * 64],
            t.snew_bf[0:64, k * 128:(k + 1) * 128], t.ident[0:64, 0:64]),
            inc="spe" if k == ND - 1 else None,
            event=("snT",) if k == ND - 1 else None)
    P.wait("ve", ("snT",))
    for k in range(ND):
        P.op("ve", lambda v, k=k: v.tensor_copy(
            t.snew_f[:, k, :], t.psT0[:, k * 64:(k + 1) * 64]),
            inc="sve" if k == ND - 1 else None,
            event=("snf",) if k == ND - 1 else None)
    release("T0", ("snf",))
    P.drain("ve")
    P.op("ve", lambda v: v.tensor_scalar(t.bq_c[:], t.bq_c[:], DQS, None,
                                         ALU.mult), inc="sve", event=("bqs",))

    P.wait("pe", ("snf",))
    P.wait("pe", ("bqs",))
    pe_claim("A0", "A1")
    for m in range(2):
        for dqc in range(2):
            for k in range(ND):
                last = (m == 1 and dqc == 1 and k == ND - 1)
                P.op("pe", lambda e, m=m, dqc=dqc, k=k: e.matmul(
                    (t.psA[:, dqc * 64:(dqc + 1) * 64] if m == 0
                     else t.psA[:, 512 + dqc * 64:512 + (dqc + 1) * 64]),
                    (t.wq_sb if m == 0 else t.wk_sb)[:, k, dqc * 128:(dqc + 1) * 128],
                    t.snew_f[:, k, :], start=(k == 0), stop=(k == ND - 1)),
                    inc="spe" if last else None,
                    event=("qkmm",) if last else None)
    P.wait("se", ("qkmm",))
    for dqc in range(2):
        P.op("se", lambda s, dqc=dqc: s.activation(
            t.q_bf[:, dqc, :], t.psA[:, dqc * 64:(dqc + 1) * 64], AF.Identity,
            bias=t.bq_c[:, dqc:dqc + 1], scale=DQS))
        P.op("se", lambda s, dqc=dqc: s.activation(
            t.k_bf[:, dqc, :], t.psA[:, 512 + dqc * 64:512 + (dqc + 1) * 64],
            AF.Identity, bias=t.bk_c[:, dqc:dqc + 1]),
            inc="sse" if dqc == 1 else None,
            event=("qk",) if dqc == 1 else None)
    release("A0", ("qk",))
    release("A1", ("qk",))

    P.wait("pe", ("qk",))
    pe_claim("B0")
    for dqc in range(2):
        P.op("pe", lambda e, dqc=dqc: e.matmul(
            t.psB[0:64, 0:64], t.q_bf[:, dqc, :], t.k_bf[:, dqc, :],
            start=(dqc == 0), stop=(dqc == 1)),
            inc="spe" if dqc == 1 else None,
            event=("scmm",) if dqc == 1 else None)
    P.wait("se", ("scmm",))
    P.op("se", lambda s: s.activation(t.sc_sb[:], t.psB[0:64, 0:64], AF.Identity),
         inc="sse", event=("scev",))
    release("B0", ("scev",))
    P.wait("ve", ("scev",))
    P.op("ve", lambda v: v.reduce_max(t.scm[:], t.sc_sb[:], axis=AX.X))
    P.drain("ve")
    P.op("ve", lambda v: v.tensor_scalar(t.scm[:], t.scm[:], -1.0, None,
                                         ALU.mult), inc="sve", event=("scmax",))
    P.wait("se", ("scmax",))
    P.op("se", lambda s: _load_tab(s, TAB_LNEXP))
    P.drain("se")
    P.op("se", lambda s: s.activation(t.sc_sb[:], t.sc_sb[:], AF.Exp,
                                      bias=t.scm[:], accum_out=t.scs[:]),
         inc="sse", event=("scexp",))
    P.wait("ve", ("scexp",))
    P.op("ve", lambda v: v.reciprocal(t.scs[:], t.scs[:]))
    P.drain("ve")
    P.op("ve", lambda v: v.tensor_scalar(t.attn_bf[:], t.sc_sb[:], t.scs[:],
                                         None, ALU.mult), inc="sve",
         event=("attn",))
    P.wait("pe", ("attn",))
    pe_claim("T0")
    P.op("pe", lambda e: e.transpose(t.psT0[0:64, 0:64], t.attn_bf[0:64, 0:64],
                                     t.ident[0:64, 0:64]),
         inc="spe", event=("atT",))
    P.wait("ve", ("atT",))
    P.op("ve", lambda v: v.tensor_copy(t.attnT_bf[:], t.psT0[0:64, 0:64]),
         inc="sve", event=("atTe",))
    release("T0", ("atTe",))

    P.wait_all_dma("pe", ("w2load",), "sg")
    pe_claim("B0", "B1")
    for jt in range(2):
        ps = psH["B0" if jt == 0 else "B1"]
        for k in range(ND):
            P.op("pe", lambda e, jt=jt, k=k, ps=ps: e.matmul(
                ps[0:64, :], t.snew_f[:, k, :],
                t.wv_sb[:, k, jt * 512:(jt + 1) * 512],
                start=(k == 0), stop=False))
        P.op("pe", lambda e, jt=jt, ps=ps: e.matmul(
            ps[0:64, :], t.ones_r[0:1, 0:64], t.bv_r[:, jt * 512:(jt + 1) * 512],
            start=False, stop=True),
            inc="spe" if jt == 1 else None,
            event=("vmm",) if jt == 1 else None)
    P.wait("se", ("vmm",))
    P.op("se", lambda s: s.activation(t.v_bf[:, 0:512], t.psB[0:64, 0:512],
                                      AF.Identity))
    P.op("se", lambda s: s.activation(t.v_bf[:, 512:1024],
                                      t.psB[0:64, 512:1024], AF.Identity),
         inc="sse", event=("vev",))
    release("B0", ("vev",))
    release("B1", ("vev",))

    P.wait("pe", ("atTe",))
    P.wait("pe", ("vev",))
    pe_claim("C0", "C1")
    for jt in range(2):
        ps = psH["C0" if jt == 0 else "C1"]
        P.op("pe", lambda e, jt=jt, ps=ps: e.matmul(
            ps[0:64, :], t.attnT_bf[:], t.v_bf[0:64, jt * 512:(jt + 1) * 512],
            start=True, stop=True),
            inc="spe" if jt == 1 else None,
            event=("cmm",) if jt == 1 else None)
    P.wait("ve", ("cmm",))
    P.op("ve", lambda v: v.tensor_add(t.y_sb[:, 0:512], t.psC[0:64, 0:512],
                                      t.snew[:, 0:512]))
    P.op("ve", lambda v: v.tensor_add(t.y_sb[:, 512:1024],
                                      t.psC[0:64, 512:1024],
                                      t.snew[:, 512:1024]),
         inc="sve", event=("y",))
    release("C0", ("y",))
    release("C1", ("y",))

    P.wait("se", ("y",))
    P.drain("se")
    P.op("se", lambda s: s.activation(t.dump[0:64, :], t.y_sb[:], AF.Identity,
                                      accum_out=t.ysum[:]))
    P.drain("se")
    P.op("se", lambda s: s.activation(t.dump[0:64, :], t.y_sb[:], AF.Square,
                                      accum_out=t.yssq[:]),
         inc="sse", event=("ysums",))
    P.wait("ve", ("ysums",))
    P.op("ve", lambda v: v.tensor_scalar(t.ymu[:], t.ysum[:], 1.0 / 1024.0,
                                         None, ALU.mult))
    P.op("ve", lambda v: v.tensor_scalar(t.yvar[:], t.yssq[:], 1.0 / 1024.0,
                                         1e-5, ALU.mult, ALU.add))
    P.drain("ve")
    P.op("ve", lambda v: v.tensor_mul(t.ytmp[:], t.ymu[:], t.ymu[:]))
    P.drain("ve")
    P.op("ve", lambda v: v.tensor_sub(t.yvar[:], t.yvar[:], t.ytmp[:]),
         inc="sve", event=("yvar",))
    P.wait("se", ("yvar",))
    P.op("se", lambda s: s.activation(t.ytmp[:], t.yvar[:], AF.Ln))
    P.drain("se")
    P.op("se", lambda s: s.activation(t.yrst[:], t.ytmp[:], AF.Exp, scale=-0.5),
         inc="sse", event=("yrst",))
    P.wait("ve", ("yrst",))
    P.op("ve", lambda v: v.tensor_mul(t.ymu[:], t.ymu[:], t.yrst[:]))
    P.drain("ve")
    P.op("ve", lambda v: v.tensor_scalar(t.ymu[:], t.ymu[:], -1.0, None,
                                         ALU.mult))
    P.drain("ve")
    P.op("ve", lambda v: v.tensor_scalar(t.y0_bf[:], t.y_sb[:], t.yrst[:],
                                         t.ymu[:], ALU.mult, ALU.add),
         inc="sve", event=("y0",))
    P.wait("pe", ("y0",))
    pe_claim("T0")
    for k in range(ND):
        P.op("pe", lambda e, k=k: e.transpose(
            t.psT0[:, k * 64:(k + 1) * 64],
            t.y0_bf[0:64, k * 128:(k + 1) * 128], t.ident[0:64, 0:64]),
            inc="spe" if k == ND - 1 else None,
            event=("yT",) if k == ND - 1 else None)
    P.wait("ve", ("yT",))
    for k in range(ND):
        P.op("ve", lambda v, k=k: v.tensor_scalar(
            t.yln_f[:, k, :], t.psT0[:, k * 64:(k + 1) * 64],
            t.lng_c[:, k:k + 1], t.lnb_c[:, k:k + 1], ALU.mult, ALU.add),
            inc="sve" if k == ND - 1 else None,
            event=("ylnf",) if k == ND - 1 else None)
    release("T0", ("ylnf",))

    P.wait("pe", ("ylnf",))
    pe_claim("A0", "A1")
    for jt in range(2):
        ps = psH["A0" if jt == 0 else "A1"]
        for k in range(ND):
            P.op("pe", lambda e, jt=jt, k=k, ps=ps: e.matmul(
                ps[0:64, :], t.yln_f[:, k, :],
                t.wvp_sb[:, k, jt * 512:(jt + 1) * 512],
                start=(k == 0), stop=False))
        P.op("pe", lambda e, jt=jt, ps=ps: e.matmul(
            ps[0:64, :], t.ones_r[0:1, 0:64],
            t.bvp_r[:, jt * 512:(jt + 1) * 512], start=False, stop=True),
            inc="spe" if jt == 1 else None,
            event=("svmm",) if jt == 1 else None)
    P.wait("se", ("svmm",))
    P.op("se", lambda s: s.activation(t.sv_bf[:, 0:512], t.psA[0:64, 0:512],
                                      AF.Identity))
    P.op("se", lambda s: s.activation(t.sv_bf[:, 512:1024],
                                      t.psA[0:64, 512:1024], AF.Identity),
         inc="sse", event=("sv",))
    release("A0", ("sv",))
    release("A1", ("sv",))

    zregs = ["C0", "C1", "B0", "B1", "A0", "A1", "C0", "C1"]

    def zmm_phase(q):
        if q == 0:
            P.wait("pe", ("sv",))
            P.wait("pe", ("botot",))
            P.wait_all_dma("pe", ("w3load",), "sg")
        for k in range(ND):
            reg = zregs[k]
            pe_claim(reg)
            ps = psH[reg][:, 0:Tp]
            P.op("pe", lambda e, k=k, q=q, ps=ps: e.matmul(
                ps, t.sv_bf[0:64, k * 128:(k + 1) * 128],
                t.alpha_T[0:64, q * Tp:(q + 1) * Tp],
                start=True, stop=True), inc="spe", event=("zmm", q, k))
            P.wait("se", ("zmm", q, k))
            if q >= 2 and k == 0:
                P.wait("se", ("omm", q - 2, TPT - 1))
                P.drain("se")
            P.op("se", lambda s, k=k, q=q, ps=ps: s.activation(
                t.z_f[:, q % 2, k, :], ps, AF.Identity),
                inc="sse", event=("zev", q, k))
            release(reg, ("zev", q, k))

    if not cfg.get("skip_pass2"):
        zmm_phase(0)
    for p2 in ([] if cfg.get("skip_pass2") else range(NPH)):
        pr = p2 % 2
        if p2 == 0:
            P.wait_all_dma("sy", ("pstore", NPH - 1, NHF - 1), "sp")
            P.wait("sy", ("sv",))
        if p2 >= 2:
            P.wait("sy", ("fadd", p2 - 2, TPT - 1))
            P.wait("sy", ("ostore", p2 - 2, NHF - 1))
        P.dma("sy", lambda s, p2=p2, pr=pr: s.dma_start(
            out=t.prd[:, pr, :, :],
            in_=t.o_out[p2 * Tp:(p2 + 1) * Tp, :]
                .rearrange("(n p) d -> p n d", p=128)), "sx",
            event=("pread", p2))
        if p2 + 1 < NPH:
            zmm_phase(p2 + 1)
        oregs = [("A0", "A1", t.psA), ("B0", "B1", t.psB), ("C0", "C1", t.psC),
                 ("A0", "A1", t.psA)]
        for i in range(TPT):
            r0, r1, ps = oregs[i]
            pe_claim(r0, r1)
            for k in range(ND):
                if i == 0:
                    P.wait("pe", ("zev", p2, k))
                for f in range(2):
                    P.op("pe", lambda e, i=i, f=f, k=k, ps=ps, p2=p2: e.matmul(
                        ps[:, f * 512:(f + 1) * 512],
                        t.z_f[:, p2 % 2, k, i * 128:(i + 1) * 128],
                        t.wo2_sb[:, k, f * 512:(f + 1) * 512],
                        start=(k == 0), stop=False))
            for f in range(2):
                last = (f == 1)
                P.op("pe", lambda e, i=i, f=f, ps=ps: e.matmul(
                    ps[:, f * 512:(f + 1) * 512], t.ones_r[:],
                    t.bo_tot[:, f * 512:(f + 1) * 512], start=False, stop=True),
                    inc="spe" if last else None,
                    event=("omm", p2, i) if last else None)
            P.wait("ve", ("omm", p2, i))
            if i == 0:
                P.wait("ve", ("pread", p2))
            P.op("ve", lambda v, ps=ps, pr=pr, i=i: v.tensor_add(
                t.prd[:, pr, i, :], ps[:, 0:1024], t.prd[:, pr, i, :]),
                inc="sve", event=("fadd", p2, i))
            release(r0, ("fadd", p2, i))
            release(r1, ("fadd", p2, i))
        for h in range(NHF):
            P.wait("sy", ("fadd", p2, 2 * h + 1))
            P.dma("sy", lambda s, p2=p2, h=h, pr=pr: s.dma_start(
                out=t.o_out[p2 * Tp + h * 256:p2 * Tp + (h + 1) * 256, :]
                    .rearrange("(n p) d -> p n d", p=128),
                in_=t.prd[:, pr, 2 * h:2 * h + 2, :]), "so",
                event=("ostore", p2, h))

    if cfg.get("taps"):
        P.wait("sy", ("sv",))
        P.wait("sy", ("slotev", NPH - 1))
        P.wait("sy", ("ysd", NPH - 1, ND - 1))
        P.wait("sy", ("biash",))
        P.dma("sy", lambda s: s.dma_start(out=t.o_alpha[:], in_=t.alpha_T[:]), "so")
        P.dma("sy", lambda s: s.dma_start(out=t.o_slacc[:], in_=t.sl_acc[:]), "so")
        P.dma("sy", lambda s: s.dma_start(out=t.o_den[:], in_=t.den_acc[:]), "so")
        P.dma("sy", lambda s: s.dma_start(out=t.o_sv[:], in_=t.sv_bf[:]), "so")
        P.dma("sy", lambda s: s.dma_start(out=t.o_carry[:], in_=t.carry[:]), "so")
        P.dma("sy", lambda s: s.dma_start(out=t.o_biash[:], in_=t.bias_h[:]), "so")
        P.dma("sy", lambda s: s.dma_start(out=t.o_snew[:], in_=t.snew[:]), "so")
        P.dma("sy", lambda s: s.dma_start(out=t.o_s0[:], in_=t.s0_f32[:]), "so")
        P.dma("sy", lambda s: s.dma_start(out=t.o_ghn[:], in_=t.gh_n[:]), "so")
        P.dma("sy", lambda s: s.dma_start(out=t.o_gi[:], in_=t.gi_sb[:]), "so")
        P.dma("sy", lambda s: s.dma_start(out=t.o_ssum[:], in_=t.ssum[:, (NPH - 1) % 2]), "so")
        P.dma("sy", lambda s: s.dma_start(out=t.o_ssq[:], in_=t.ssq[:, (NPH - 1) % 2]), "so")
        P.dma("sy", lambda s: s.dma_start(out=t.o_sr[:], in_=t.sr[:, (NPH - 1) % 2]), "so")
        P.dma("sy", lambda s: s.dma_start(out=t.o_xn[:], in_=t.xn_tm[:, (NPH - 1) % 2]), "so")
        P.dma("sy", lambda s: s.dma_start(out=t.o_xnf[:], in_=t.xn_f[:]), "so")
        P.dma("sy", lambda s: s.dma_start(out=t.o_ysbf[:], in_=t.ys_bf[:]), "so")
        P.dma("sy", lambda s: s.dma_start(out=t.o_lsb[:], in_=t.l_sb[:]), "so")

    if P.mode == "sy":
        _, k = DMA_POOLS["so"]
        n = P.pool_n["so"]
        for i in range(k):
            cnt = 16 * ((n - i - 1) // k + 1) if n > i else 0
            if cnt:
                P.eng.wait_ge(P.sems[f"so{i}"], cnt)


def build_nc(cfg):
    T = cfg["T"]; Tp = cfg["Tp"]
    TPT = Tp // 128
    nc = bass.Bass("TRN2", target_bir_lowering=False)
    cfg = dict(cfg)
    cfg["tab_lnexp"] = _act_set_idx(nc, {AF.Ln, AF.Exp, AF.Identity, AF.Square})
    cfg["tab_sig"] = _act_set_idx(nc, {AF.Sigmoid, AF.Tanh, AF.Identity,
                                       AF.Square, AF.Abs})
    cfg["tab_gelu"] = _act_set_idx(nc, {AF.Gelu, AF.Identity})

    class TB:
        pass
    t = TB()

    def din(name, shape, dtype=F32):
        return nc.dram_tensor(name, shape, dtype, kind="ExternalInput").ap()

    t.i_x = din("x", [T, 1024])
    t.i_Wp_pk = din("Wp_pk", [128, 8 * 2048])
    t.i_bp0_pk = din("bp0_pk", [128, 8])
    t.i_bpk_pk = din("bpk_pk", [128, 8])
    t.i_Wu_pk = din("Wu_pk", [128, 8 * 1024])
    t.i_bu_pk = din("bu_pk", [128, 8])
    t.i_Wo_pk = din("Wo_pk", [128, 8 * 1024])
    t.i_bo = din("bo", [1024])
    t.i_slot = din("slot_init", [64, 1024])
    t.i_w1 = din("w1", [2048, 512])
    t.i_w1t_pk = din("w1t_pk", [128, 8 * 512])
    t.i_b1_pk = din("b1_pk", [128, 4])
    t.i_w2_pk = din("w2_pk", [128, 4 * 64])
    t.i_b2 = din("b2", [64])
    t.i_tau_b = din("tau_b", [128, 1])
    t.i_Wih_pk = din("Wih_pk", [128, 6 * 8 * 512])
    t.i_Whh_pk = din("Whh_pk", [128, 6 * 8 * 512])
    t.i_bih = din("bih", [3072])
    t.i_bhh = din("bhh", [3072])
    t.i_wq_pk = din("wq_pk", [128, 8 * 256])
    t.i_bq_pk = din("bq_pk", [128, 2])
    t.i_wk_pk = din("wk_pk", [128, 8 * 256])
    t.i_bk_pk = din("bk_pk", [128, 2])
    t.i_wv_pk = din("wv_pk", [128, 8 * 1024])
    t.i_bv = din("bv", [1024])
    t.i_lng_pk = din("lng_pk", [128, 8])
    t.i_lnb_pk = din("lnb_pk", [128, 8])
    t.i_wvp_pk = din("wvp_pk", [128, 8 * 1024])
    t.i_bvp = din("bvp", [1024])
    t.i_wo2_pk = din("wo2_pk", [128, 8 * 1024])
    t.i_bof = din("bof", [1024])
    t.i_ident = din("ident", [128, 128], BF16)
    t.i_ones_r = din("ones_r", [1, 128], BF16)
    t.i_ones_c = din("ones_c", [128, 1], BF16)
    t.o_out = nc.dram_tensor("out", [T, 1024], F32, kind="ExternalOutput").ap()
    if cfg.get("taps"):
        t.o_alpha = nc.dram_tensor("o_alpha", [64, T], BF16, kind="ExternalOutput").ap()
        t.o_slacc = nc.dram_tensor("o_slacc", [64, 1024], F32, kind="ExternalOutput").ap()
        t.o_den = nc.dram_tensor("o_den", [64, 1], F32, kind="ExternalOutput").ap()
        t.o_sv = nc.dram_tensor("o_sv", [64, 1024], BF16, kind="ExternalOutput").ap()
        t.o_carry = nc.dram_tensor("o_carry", [128, 8], F32, kind="ExternalOutput").ap()
        t.o_biash = nc.dram_tensor("o_biash", [128, 4], F32, kind="ExternalOutput").ap()
        t.o_snew = nc.dram_tensor("o_snew", [64, 1024], F32, kind="ExternalOutput").ap()
        t.o_gi = nc.dram_tensor("o_gi", [64, 3072], F32, kind="ExternalOutput").ap()
        t.o_s0 = nc.dram_tensor("o_s0", [64, 1024], F32, kind="ExternalOutput").ap()
        t.o_ghn = nc.dram_tensor("o_ghn", [64, 1024], F32, kind="ExternalOutput").ap()
        t.o_ssum = nc.dram_tensor("o_ssum", [128, TPT], F32, kind="ExternalOutput").ap()
        t.o_ssq = nc.dram_tensor("o_ssq", [128, TPT], F32, kind="ExternalOutput").ap()
        t.o_sr = nc.dram_tensor("o_sr", [128, TPT], F32, kind="ExternalOutput").ap()
        t.o_xn = nc.dram_tensor("o_xn", [128, TPT, 1024], BF16, kind="ExternalOutput").ap()
        t.o_xnf = nc.dram_tensor("o_xnf", [128, 8, Tp], BF16, kind="ExternalOutput").ap()
        t.o_ysbf = nc.dram_tensor("o_ysbf", [128, 8, Tp], BF16, kind="ExternalOutput").ap()
        t.o_lsb = nc.dram_tensor("o_lsb", [128, TPT, 64], F32, kind="ExternalOutput").ap()

    with ExitStack() as ctx:
        def sb(name, shape, dtype):
            return ctx.enter_context(nc.sbuf_tensor("sb_" + name, shape, dtype))

        t.ident = sb("ident", [128, 128], BF16)
        t.ones_r = sb("ones_r", [1, 128], BF16)
        t.ones_c = sb("ones_c", [128, 1], BF16)
        t.tau_b = sb("tau_b", [128, 1], F32)
        t.invc = sb("invc", [128, 1], F32)
        t.alpha_T = sb("alpha_T", [64, T], BF16)
        t.sl_acc = sb("sl_acc", [64, 1024], F32)
        t.den_acc = sb("den_acc", [64, 1], F32)
        t.carry = sb("carry", [128, 8], F32)
        t.si0f = sb("si0f", [128, 8, 64], BF16)
        t.sm_f = sb("sm_f", [128, 8], F32)
        t.sm_fb = sb("sm_fb", [128, 8], BF16)
        t.bpw = sb("bpw", [128, 8], F32)
        t.bpk2 = sb("bpk2", [128, 8], F32)
        t.bu_c = sb("bu_c", [128, 8], F32)
        t.b1_c = sb("b1_c", [128, 4], F32)
        t.bias_h = sb("bias_h", [128, 4], F32)
        t.b2_r = sb("b2_r", [1, 64], BF16)
        t.bq_c = sb("bq_c", [128, 2], F32)
        t.bk_c = sb("bk_c", [128, 2], F32)
        t.lng_c = sb("lng_c", [128, 8], F32)
        t.lnb_c = sb("lnb_c", [128, 8], F32)
        t.bo_tot = sb("bo_tot", [1, 1024], BF16)
        t.wq_sb = sb("wq_sb", [128, 8, 256], BF16)
        t.wk_sb = sb("wk_sb", [128, 8, 256], BF16)
        t.w2_sb = sb("w2_sb", [128, 4, 64], BF16)
        t.sv_bf = sb("sv_bf", [64, 1024], BF16)
        t.dump = sb("dump", [128, 1024], BF16)
        s1 = ExitStack()

        def sb(name, shape, dtype, _s=s1):
            return _s.enter_context(nc.sbuf_tensor("sb_" + name, shape, dtype))

        t.s0_bf = sb("s0_bf", [64, 1024], BF16)
        t.bo1_r = sb("bo1_r", [8, 128], F32)
        t.bo2_r = sb("bo2_r", [8, 128], F32)
        t.Wp_sb = sb("Wp_sb", [128, 8, 2048], BF16)
        t.Wu_sb = sb("Wu_sb", [128, 8, 1024], BF16)
        t.Wo_sb = sb("Wo_sb", [128, 8, 1024], BF16)
        t.w1t_sb = sb("w1t_sb", [128, 8, 512], BF16)
        t.w1s = sb("w1s", [128, 2, 512], BF16)
        t.x_sl = sb("x_sl", [128, 4, 2, 1024], F32)
        t.xn_tm = sb("xn_tm", [128, 2, TPT, 1024], BF16)
        t.xn_f = sb("xn_f", [128, 8, Tp], BF16)
        t.g_sb = sb("g_sb", [128, 2, Tp], F32)
        t.s2_sb = sb("s2_sb", [128, 2, Tp], F32)
        t.a_sb = sb("a_sb", [128, Tp], F32)
        t.b_sb = sb("b_sb", [128, Tp], F32)
        t.ys_bf = sb("ys_bf", [128, 8, Tp], BF16)
        t.h_f = sb("h_f", [128, 4, Tp], BF16)
        t.l_sb = sb("l_sb", [128, TPT, 64], F32)
        t.e_al = sb("e_al", [128, TPT, 64], F32)
        t.al_bf = sb("al_bf", [128, TPT, 64], BF16)
        t.m18 = sb("m18", [128, TPT, 8], F32)
        t.m1n = sb("m1n", [128, TPT], F32)
        t.alns = sb("alns", [128, TPT], F32)
        t.ssum = sb("ssum", [128, 2, TPT], F32)
        t.ssq = sb("ssq", [128, 2, TPT], F32)
        t.smn = sb("smn", [128, 2, TPT], F32)
        t.svar = sb("svar", [128, 2, TPT], F32)
        t.sr = sb("sr", [128, 2, TPT], F32)
        t.smrn = sb("smrn", [128, 2, TPT], F32)
        t.stmp = sb("stmp", [128, 2, TPT], F32)
        s1.close()
        s2 = ExitStack()

        def sb(name, shape, dtype, _s=s2):
            return _s.enter_context(nc.sbuf_tensor("sb_" + name, shape, dtype))

        t.WT = sb("WT", [128, 6, 8, 512], BF16)
        t.whh_pp = sb("whh_pp", [128, 2, 8, 512], BF16)
        t.s0_f32 = sb("s0_f32", [64, 1024], F32)
        t.bih_r = sb("bih_r", [1, 3072], BF16)
        t.bhh_r = sb("bhh_r", [1, 3072], BF16)
        t.bv_r = sb("bv_r", [1, 1024], BF16)
        t.bvp_r = sb("bvp_r", [1, 1024], BF16)
        t.wv_sb = sb("wv_sb", [128, 8, 1024], BF16)
        t.wvp_sb = sb("wvp_sb", [128, 8, 1024], BF16)
        t.si_bf = sb("si_bf", [64, 1024], BF16)
        t.si_f = sb("si_f", [128, 8, 64], BF16)
        t.gi_sb = sb("gi_sb", [64, 3072], F32)
        t.gh_n = sb("gh_n", [64, 1024], F32)
        t.t64a = sb("t64a", [64, 1024], F32)
        t.r_g = sb("r_g", [64, 1024], F32)
        t.z_g = sb("z_g", [64, 1024], F32)
        t.n_g = sb("n_g", [64, 1024], F32)
        t.snew = sb("snew", [64, 1024], F32)
        t.snew_bf = sb("snew_bf", [64, 1024], BF16)
        t.snew_f = sb("snew_f", [128, 8, 64], BF16)
        t.dent = sb("dent", [64, 1], F32)
        t.q_bf = sb("q_bf", [128, 2, 64], BF16)
        t.k_bf = sb("k_bf", [128, 2, 64], BF16)
        t.sc_sb = sb("sc_sb", [64, 64], F32)
        t.scm = sb("scm", [64, 1], F32)
        t.scs = sb("scs", [64, 1], F32)
        t.attn_bf = sb("attn_bf", [64, 64], BF16)
        t.attnT_bf = sb("attnT_bf", [64, 64], BF16)
        t.v_bf = sb("v_bf", [64, 1024], BF16)
        t.y_sb = sb("y_sb", [64, 1024], F32)
        t.ysum = sb("ysum", [64, 1], F32)
        t.yssq = sb("yssq", [64, 1], F32)
        t.ymu = sb("ymu", [64, 1], F32)
        t.yvar = sb("yvar", [64, 1], F32)
        t.ytmp = sb("ytmp", [64, 1], F32)
        t.yrst = sb("yrst", [64, 1], F32)
        t.y0_bf = sb("y0_bf", [64, 1024], BF16)
        t.yln_f = sb("yln_f", [128, 8, 64], BF16)
        s2.close()
        s3 = ExitStack()

        def sb(name, shape, dtype, _s=s3):
            return _s.enter_context(nc.sbuf_tensor("sb_" + name, shape, dtype))

        t.wo2_sb = sb("wo2_sb", [128, 8, 1024], BF16)
        t.z_f = sb("z_f", [128, 2, 8, Tp], BF16)
        t.prd = sb("prd", [128, 2, TPT, 1024], F32)
        s3.close()

        with ExitStack() as pctx:
            e2 = pctx.enter_context
            t.psA = e2(nc.psum_tensor("psA", [128, 1024], F32))
            t.psB = e2(nc.psum_tensor("psB", [128, 1024], F32))
            t.psC = e2(nc.psum_tensor("psC", [128, 1024], F32))
            t.psT0 = e2(nc.psum_tensor("psT0", [128, 1024], BF16))
            t.psT1 = e2(nc.psum_tensor("psT1", [128, 1024], BF16))
            sems = {k: e2(nc.semaphore(k)) for k in SEMS}

            P = Plan()
            P.begin("dry")
            build_walk(cfg, t, P)

            with nc.Block() as block:
                @block.sync
                def _(eng):
                    P.begin("sy", eng, sems)
                    build_walk(cfg, t, P)

                @block.gpsimd
                def _(eng):
                    P.begin("gp", eng, sems)
                    build_walk(cfg, t, P)

                @block.tensor
                def _(eng):
                    P.begin("pe", eng, sems)
                    build_walk(cfg, t, P)

                @block.vector
                def _(eng):
                    P.begin("ve", eng, sems)
                    build_walk(cfg, t, P)

                @block.scalar
                def _(eng):
                    P.begin("se", eng, sems)
                    build_walk(cfg, t, P)
    return nc


def make_in_maps(inputs, cfg, n_cores=8):
    T = cfg["T"]
    x = np.ascontiguousarray(np.asarray(inputs["L0_attn_x"], dtype=np.float32))
    ident = np.eye(128, dtype=ml_dtypes.bfloat16)
    ones_r = np.ones((1, 128), dtype=ml_dtypes.bfloat16)
    ones_c = np.ones((128, 1), dtype=ml_dtypes.bfloat16)
    tau_b = np.ascontiguousarray(
        np.broadcast_to(np.asarray(inputs["L0_ffn_tau"], np.float32).reshape(1, 1),
                        (128, 1)))
    names = {
        "bo": "L0_attn_bo", "slot_init": "L0_ffn_slot_init",
        "w1": "L0_ffn_w1", "b2": "L0_ffn_b2",
        "bih": "L0_ffn_bih", "bhh": "L0_ffn_bhh", "bv": "L0_ffn_bv",
        "bvp": "L0_ffn_bvp", "bof": "L0_ffn_bo",
    }
    common = {k: np.ascontiguousarray(np.asarray(inputs[v], dtype=np.float32))
              for k, v in names.items()}
    common.update({"tau_b": tau_b, "ident": ident, "ones_r": ones_r,
                   "ones_c": ones_c})

    def f32(v):
        return np.asarray(inputs[v], np.float32)

    def pk(a, k):
        return np.ascontiguousarray(
            a.reshape(k, 128, -1).transpose(1, 0, 2).reshape(128, -1))

    def cpk(a, k):
        return np.ascontiguousarray(a.reshape(k, 128).T)

    def jtpk(a):
        return np.ascontiguousarray(
            a.reshape(8, 128, 6, 512).transpose(1, 2, 0, 3).reshape(128, -1))

    bp = f32("L0_attn_bp")
    common.update({
        "Wp_pk": pk(f32("L0_attn_Wp")[:, 0:2048], 8),
        "bp0_pk": cpk(bp[0:1024], 8), "bpk_pk": cpk(bp[1024:2048], 8),
        "Wu_pk": pk(f32("L0_attn_Wu"), 8), "bu_pk": cpk(f32("L0_attn_bu"), 8),
        "Wo_pk": pk(f32("L0_attn_Wo"), 8),
        "w1t_pk": pk(f32("L0_ffn_w1")[0:1024], 8),
        "b1_pk": cpk(f32("L0_ffn_b1"), 4),
        "w2_pk": pk(f32("L0_ffn_w2"), 4),
        "Wih_pk": jtpk(np.ascontiguousarray(f32("L0_ffn_Wih").T)),
        "Whh_pk": jtpk(np.ascontiguousarray(f32("L0_ffn_Whh").T)),
        "wq_pk": pk(f32("L0_ffn_wq"), 8), "bq_pk": cpk(f32("L0_ffn_bq"), 2),
        "wk_pk": pk(f32("L0_ffn_wk"), 8), "bk_pk": cpk(f32("L0_ffn_bk"), 2),
        "wv_pk": pk(f32("L0_ffn_wv"), 8),
        "lng_pk": cpk(f32("L0_ffn_ln_g"), 8), "lnb_pk": cpk(f32("L0_ffn_ln_b"), 8),
        "wvp_pk": pk(f32("L0_ffn_wvp"), 8),
        "wo2_pk": pk(f32("L0_ffn_wo"), 8),
    })
    in_maps = []
    for c in range(n_cores):
        m = dict(common)
        m["x"] = np.ascontiguousarray(x[c, :T, :])
        in_maps.append(m)
    return in_maps


CFG = {"T": 8192, "Tp": 512}


def kernel(**inputs):
    cfg = CFG
    nc = build_nc(cfg)
    in_maps = make_in_maps(inputs, cfg, n_cores=8)
    res = run_bass_kernel_spmd(nc, in_maps, core_ids=list(range(8)))
    out = np.stack([r["out"] for r in res.results], axis=0)
    return out.astype(np.float32)


# version 31
# speedup vs baseline: 1.0214x; 1.0056x over previous
import numpy as np
import ml_dtypes
from contextlib import ExitStack

import concourse.bass as bass
import concourse.mybir as mybir
from concourse.bass_utils import run_bass_kernel_spmd
from concourse.hw_specs import get_activation_tables


def _act_set_idx(nc, funcs):
    for idx, fs in enumerate(get_activation_tables(nc.m.arch).values()):
        if all(f in fs for f in funcs):
            return idx
    raise ValueError(f"no activation table set contains {funcs}")


def _load_tab(eng, idx):
    return eng.add_instruction(mybir.InstLoadActFuncSet(
        name=eng.bass.get_next_instruction_name(),
        act_func_set_id=idx, ins=[], outs=[]))

F32 = mybir.dt.float32
BF16 = mybir.dt.bfloat16
AF = mybir.ActivationFunctionType
ALU = mybir.AluOpType
AX = mybir.AxisListType

DMA_POOLS = {"sx": ("sy", 6), "sp": ("sy", 3), "so": ("sy", 3), "sg": ("gp", 3)}
SEMS = {"spe": "pe", "sve": "ve", "sse": "se"}
for _p, (_e, _k) in DMA_POOLS.items():
    for _i in range(_k):
        SEMS[f"{_p}{_i}"] = _e


class Plan:

    def __init__(self):
        self.events = {}
        self.dry = True
        self.mode = None
        self.eng = None
        self.sems = None
        self.counts = {k: 0 for k in SEMS}
        self.pool_n = {p: 0 for p in DMA_POOLS}

    def begin(self, mode, eng=None, sems=None):
        self.dry = (mode == "dry")
        self.mode = mode
        self.eng = eng
        self.sems = sems
        self.counts = {k: 0 for k in SEMS}
        self.pool_n = {p: 0 for p in DMA_POOLS}

    def op(self, engine, fn, inc=None, inc_n=1, event=None):
        inst = None
        if self.mode == engine:
            inst = fn(self.eng)
        if inc is not None:
            assert SEMS[inc] == engine, f"sem {inc} not produced by {engine}"
            self.counts[inc] += inc_n
            if inst is not None:
                inst.then_inc(self.sems[inc], inc_n)
            if event is not None:
                val = (inc, self.counts[inc], dict(self.counts))
                if self.dry:
                    assert event not in self.events, f"dup event {event}"
                    self.events[event] = val
                else:
                    assert self.events.get(event)[:2] == val[:2], \
                        f"event drift {event}: {self.events.get(event)[:2]} vs {val[:2]}"
        else:
            assert event is None
        return inst

    def dma(self, engine, fn, pool, event=None):
        eng_of, k = DMA_POOLS[pool]
        assert eng_of == engine
        i = self.pool_n[pool]
        self.pool_n[pool] = i + 1
        sem = f"{pool}{i % k}"
        if i >= k and self.mode == engine:
            self.eng.wait_ge(self.sems[sem], 16 * (i // k))
        return self.op(engine, fn, inc=sem, inc_n=16, event=event)

    def wait(self, engine, event):
        if self.mode == engine:
            sem_name, val, _ = self.events[event]
            self.eng.wait_ge(self.sems[sem_name], val)

    def wait_all_dma(self, engine, event, pool):
        if self.mode == engine:
            _, _, snap = self.events[event]
            _, k = DMA_POOLS[pool]
            for i in range(k):
                sem = f"{pool}{i}"
                if snap.get(sem, 0) > 0:
                    self.eng.wait_ge(self.sems[sem], snap[sem])

    def drain(self, engine):
        if self.mode == engine:
            self.eng.drain()


def build_walk(cfg, t, P):
    TAB_LNEXP = cfg["tab_lnexp"]
    TAB_SIG = cfg["tab_sig"]
    TAB_GELU = cfg["tab_gelu"]
    T = cfg["T"]; Tp = cfg["Tp"]
    NPH = T // Tp
    TPT = Tp // 128
    NHF = TPT // 2
    ND = 8
    NH = 4
    SQ2I = float(1.0 / np.sqrt(2.0))
    DQS = float(256 ** -0.5)

    psH = {
        "A0": t.psA[:, 0:512], "A1": t.psA[:, 512:1024],
        "B0": t.psB[:, 0:512], "B1": t.psB[:, 512:1024],
        "C0": t.psC[:, 0:512], "C1": t.psC[:, 512:1024],
        "T0": t.psT0[:, :], "T1": t.psT1[:, :],
    }
    bank_free = {}

    def pe_claim(*regions):
        for r in regions:
            ev = bank_free.pop(r, None)
            if ev is not None:
                P.wait("pe", ev)

    def release(region, event):
        bank_free[region] = event

    def xslot(ph, h):
        return (2 * ph + h) % 4

    gload = [
        lambda g: g.dma_start(out=t.s0_bf[:], in_=t.i_slot[:]),
        lambda g: g.dma_start(out=t.b2_r[:], in_=t.i_b2[None, :]),
        lambda g: g.dma_start(out=t.w1t_sb[:],
            in_=t.i_w1t_pk.rearrange("p (k f) -> p k f", k=8)),
        lambda g: g.dma_start(out=t.w2_sb[:],
            in_=t.i_w2_pk.rearrange("p (k f) -> p k f", k=4)),
        lambda g: g.dma_start(out=t.Wp_sb[:],
            in_=t.i_Wp_pk.rearrange("p (k f) -> p k f", k=8)),
        lambda g: g.dma_start(out=t.Wu_sb[:],
            in_=t.i_Wu_pk.rearrange("p (k f) -> p k f", k=8)),
    ]
    for i, fn in enumerate(gload):
        P.dma("gp", fn, "sg", event=("wload",) if i == len(gload) - 1 else None)
    gload2 = [
        lambda g: g.dma_start(out=t.Wo_sb[:],
            in_=t.i_Wo_pk.rearrange("p (k f) -> p k f", k=8)),
        lambda g: g.dma_start(out=t.wq_sb[:],
            in_=t.i_wq_pk.rearrange("p (k f) -> p k f", k=8)),
        lambda g: g.dma_start(out=t.wk_sb[:],
            in_=t.i_wk_pk.rearrange("p (k f) -> p k f", k=8)),
    ]

    sload = [
        lambda s: s.dma_start(out=t.ident[:], in_=t.i_ident[:]),
        lambda s: s.dma_start(out=t.ones_r[:], in_=t.i_ones_r[:]),
        lambda s: s.dma_start(out=t.ones_c[:], in_=t.i_ones_c[:]),
        lambda s: s.dma_start(out=t.tau_b[:], in_=t.i_tau_b[:]),
        lambda s: s.dma_start(out=t.bpw[:], in_=t.i_bp0_pk[:]),
        lambda s: s.dma_start(out=t.bpk2[:], in_=t.i_bpk_pk[:]),
        lambda s: s.dma_start(out=t.bu_c[:], in_=t.i_bu_pk[:]),
        lambda s: s.dma_start(out=t.b1_c[:], in_=t.i_b1_pk[:]),
        lambda s: s.dma_start(out=t.bq_c[:], in_=t.i_bq_pk[:]),
        lambda s: s.dma_start(out=t.bk_c[:], in_=t.i_bk_pk[:]),
        lambda s: s.dma_start(out=t.lng_c[:], in_=t.i_lng_pk[:]),
        lambda s: s.dma_start(out=t.lnb_c[:], in_=t.i_lnb_pk[:]),
        lambda s: s.dma_start(out=t.bo1_r[:],
            in_=t.i_bo.rearrange("(a b) -> a b", a=8)),
        lambda s: s.dma_start(out=t.bo2_r[:],
            in_=t.i_bof.rearrange("(a b) -> a b", a=8)),
    ]
    for i, fn in enumerate(sload):
        P.dma("sy", fn, "sx", event=("cload",) if i == len(sload) - 1 else None)

    for ph in range(min(2, NPH)):
        for h in range(NHF):
            P.dma("sy", lambda s, ph=ph, h=h: s.dma_start(
                out=t.x_sl[:, xslot(ph, h), :, :],
                in_=t.i_x[ph * Tp + h * 256: ph * Tp + (h + 1) * 256, :]
                    .rearrange("(n p) d -> p n d", p=128)), "sx",
                event=("xload", ph, h))

    P.op("se", lambda s: _load_tab(s, TAB_SIG))
    P.wait_all_dma("se", ("cload",), "sx")
    P.op("se", lambda s: s.activation(t.invc[:], t.tau_b[:], AF.Abs),
         inc="sse", event=("tauabs",))
    P.wait("ve", ("tauabs",))
    P.wait_all_dma("ve", ("cload",), "sx")
    P.op("ve", lambda v: v.tensor_scalar(t.invc[:], t.invc[:], 0.1, None, ALU.add))
    P.drain("ve")
    P.op("ve", lambda v: v.reciprocal(t.invc[:], t.invc[:]))
    P.drain("ve")
    P.op("ve", lambda v: v.tensor_scalar(t.bpk2[:], t.bpk2[:], -2.0, None, ALU.mult),
         inc="sve", event=("bpk2",))
    P.op("ve", lambda v: v.tensor_add(t.bo1_r[:], t.bo1_r[:], t.bo2_r[:]),
         inc="sve", event=("bosum",))
    P.wait("gp", ("bosum",))
    P.dma("gp", lambda g: g.dma_start(
        out=t.bo_tot[:], in_=t.bo1_r[:]), "sg", event=("botot",))

    P.wait_all_dma("pe", ("cload",), "sx")
    P.wait_all_dma("pe", ("wload",), "sg")
    pe_claim("T0")
    for k in range(ND):
        P.op("pe", lambda e, k=k: e.transpose(
            t.psT0[:, k * 64:(k + 1) * 64],
            t.s0_bf[0:64, k * 128:(k + 1) * 128], t.ident[0:64, 0:64]),
            inc="spe" if k == ND - 1 else None,
            event=("s0T",) if k == ND - 1 else None)
    P.wait("ve", ("s0T",))
    for k in range(ND):
        P.op("ve", lambda v, k=k: v.tensor_copy(
            t.si0f[:, k, :], t.psT0[:, k * 64:(k + 1) * 64]),
            inc="sve" if k == ND - 1 else None,
            event=("si0f",) if k == ND - 1 else None)
    release("T0", ("si0f",))
    P.drain("ve")
    for k in range(ND):
        P.op("ve", lambda v, k=k: v.reduce_sum(
            t.sm_f[:, k:k + 1], t.si0f[:, k, :], axis=AX.X))
    P.drain("ve")
    P.op("ve", lambda v: v.tensor_scalar(t.sm_fb[:], t.sm_f[:], 1.0 / 64.0, None,
                                         ALU.mult), inc="sve", event=("smf",))
    regs4 = ["A0", "A1", "B0", "B1"]
    pe_claim(*regs4)
    ps4 = [psH[r] for r in regs4]
    for k in range(ND):
        if k >= 2:
            P.wait("gp", ("smwk", k - 2))
        P.dma("gp", lambda g, k=k: g.dma_start(
            out=t.w1s[:, k % 2, :],
            in_=t.i_w1[1024 + k * 128:1024 + (k + 1) * 128, :]), "sg",
            event=("w1s", k))
        P.wait("pe", ("w1s", k))
        if k == 0:
            P.wait("pe", ("smf",))
        for jh in range(NH):
            P.op("pe", lambda e, jh=jh, k=k: e.matmul(
                ps4[jh][:, 0:1], t.w1s[:, k % 2, jh * 128:(jh + 1) * 128],
                t.sm_fb[:, k:k + 1], start=(k == 0), stop=(k == ND - 1)),
                inc="spe" if jh == NH - 1 else None,
                event=("smwk", k) if jh == NH - 1 else None)
    P.wait("ve", ("smwk", ND - 1))
    for jh in range(NH):
        P.op("ve", lambda v, jh=jh: v.tensor_add(
            t.bias_h[:, jh:jh + 1], t.b1_c[:, jh:jh + 1], ps4[jh][:, 0:1]),
            inc="sve" if jh == NH - 1 else None,
            event=("biash",) if jh == NH - 1 else None)
    for r in regs4:
        release(r, ("biash",))
    for i, fn in enumerate(gload2):
        P.dma("gp", fn, "sg", event=("wload2",) if i == len(gload2) - 1 else None)

    def stats_tile(ph, i):
        pp = ph % 2
        sl, ii = xslot(ph, i // 2), i % 2
        if i % 2 == 0:
            P.wait("se", ("xload", ph, i // 2))
        if i == 0 and ph >= 2:
            P.wait("se", ("var", ph - 2))
        P.drain("se")
        P.op("se", lambda s, sl=sl, ii=ii, i=i, pp=pp: s.activation(
            t.dump[:], t.x_sl[:, sl, ii, :], AF.Identity,
            accum_out=t.ssum[:, pp, i:i + 1]))
        P.drain("se")
        P.op("se", lambda s, sl=sl, ii=ii, i=i, pp=pp: s.activation(
            t.dump[:], t.x_sl[:, sl, ii, :], AF.Square,
            accum_out=t.ssq[:, pp, i:i + 1]),
            inc="sse" if i == TPT - 1 else None,
            event=("sums", ph) if i == TPT - 1 else None)

    def var_chain(ph):
        pp = ph % 2
        smn, svar, stmp = (t.smn[:, pp, :], t.svar[:, pp, :], t.stmp[:, pp, :])
        P.wait("ve", ("sums", ph))
        if ph >= 2:
            P.wait("ve", ("rstd", ph - 2))
        P.op("ve", lambda v, smn=smn, pp=pp: v.tensor_scalar(
            smn, t.ssum[:, pp, :], 1.0 / 1024.0, None, ALU.mult))
        P.op("ve", lambda v, svar=svar, pp=pp: v.tensor_scalar(
            svar, t.ssq[:, pp, :], 1.0 / 1024.0, 1e-5, ALU.mult, ALU.add))
        P.drain("ve")
        P.op("ve", lambda v, stmp=stmp, smn=smn: v.tensor_mul(stmp, smn, smn))
        P.drain("ve")
        P.op("ve", lambda v, svar=svar, stmp=stmp: v.tensor_sub(svar, svar, stmp),
             inc="sve", event=("var", ph))

    def rstd_window(ph):
        pp = ph % 2
        P.wait("se", ("var", ph))
        P.op("se", lambda s: _load_tab(s, TAB_LNEXP))
        P.op("se", lambda s, pp=pp: s.activation(
            t.stmp[:, pp, :], t.svar[:, pp, :], AF.Ln))
        P.drain("se")
        P.op("se", lambda s, pp=pp: s.activation(
            t.sr[:, pp, :], t.stmp[:, pp, :], AF.Exp, scale=-0.5),
            inc="sse", event=("rstd", ph))
        P.op("se", lambda s: _load_tab(s, TAB_SIG))

    def window(ph):
        if ph < 1:
            return
        P.wait("se", ("max8", ph - 1))
        if ph >= 2:
            P.wait("se", ("alpha", ph - 2))
        P.op("se", lambda s: _load_tab(s, TAB_LNEXP))
        for i in range(TPT):
            P.op("se", lambda s, i=i: s.activation(
                t.e_al[:, i, :], t.l_sb[:, i, :], AF.Exp,
                bias=t.m1n[:, i:i + 1]),
                inc="sse" if i == TPT - 1 else None,
                event=("exps", ph - 1) if i == TPT - 1 else None)

    def alpha_fin(ph):
        P.wait("ve", ("exps", ph))
        if ph >= 1:
            P.wait("ve", ("slotmm", ph - 1))
        for i in range(TPT):
            P.op("ve", lambda v, i=i: v.scalar_tensor_tensor(
                t.e_al[:, i, :], t.l_sb[:, i, :], t.m18[:, i, 7:8],
                t.e_al[:, i, :], ALU.is_ge, ALU.mult))
        P.drain("ve")
        for i in range(TPT):
            P.op("ve", lambda v, i=i: v.reduce_sum(
                t.alns[:, i:i + 1], t.e_al[:, i, :], axis=AX.X))
        P.drain("ve")
        P.op("ve", lambda v: v.reciprocal(t.alns[:], t.alns[:]))
        P.drain("ve")
        for i in range(TPT):
            P.op("ve", lambda v, i=i: v.tensor_scalar(
                t.al_bf[:, i, :], t.e_al[:, i, :], t.alns[:, i:i + 1], None,
                ALU.mult),
                inc="sve" if i == TPT - 1 else None,
                event=("alpha", ph) if i == TPT - 1 else None)

    def wo_block(ph):
        if ph == 0:
            P.wait_all_dma("pe", ("wload2",), "sg")
        regs = [("A0", "A1", t.psA), ("B0", "B1", t.psB), ("C0", "C1", t.psC),
                ("A0", "A1", t.psA)]

        def wo_tile_k(i, k, ps):
            for f in range(2):
                last = (f == 1 and k == ND - 1)
                P.op("pe", lambda e, i=i, f=f, k=k, ps=ps: e.matmul(
                    ps[:, f * 512:(f + 1) * 512],
                    t.ys_bf[:, k, i * 128:(i + 1) * 128],
                    t.Wo_sb[:, k, f * 512:(f + 1) * 512],
                    start=(k == 0), stop=(k == ND - 1)),
                    inc="spe" if last else None,
                    event=("womm", ph, i) if last else None)

        pe_claim(regs[0][0], regs[0][1])
        for k in range(ND - 1):
            P.wait("pe", ("ysd", ph, k))
            wo_tile_k(0, k, regs[0][2])
        pe_claim(regs[1][0], regs[1][1])
        for k in range(ND - 1):
            wo_tile_k(1, k, regs[1][2])
        P.wait("pe", ("ysd", ph, ND - 1))
        wo_tile_k(0, ND - 1, regs[0][2])
        wo_tile_k(1, ND - 1, regs[1][2])
        for i in range(TPT):
            r0, r1, ps = regs[i]
            if i >= 2:
                pe_claim(r0, r1)
                for k in range(ND):
                    wo_tile_k(i, k, ps)
            P.wait("ve", ("womm", ph, i))
            if i == 0:
                P.wait("ve", ("sums", ph))
            sl, ii = xslot(ph, i // 2), i % 2
            P.op("ve", lambda v, ps=ps, sl=sl, ii=ii: v.tensor_add(
                t.x_sl[:, sl, ii, :], ps[:, 0:1024], t.x_sl[:, sl, ii, :]),
                inc="sve", event=("part", ph, i))
            release(r0, ("part", ph, i))
            release(r1, ("part", ph, i))
        for h in range(NHF):
            P.wait("sy", ("part", ph, 2 * h + 1))
            P.dma("sy", lambda s, ph=ph, h=h: s.dma_start(
                out=t.o_out[ph * Tp + h * 256: ph * Tp + (h + 1) * 256, :]
                    .rearrange("(n p) d -> p n d", p=128),
                in_=t.x_sl[:, xslot(ph, h), :, :]), "sp",
                event=("pstore", ph, h))

    def slot_pe(ph):
        P.wait("pe", ("alpha", ph))
        pe_claim("B0", "B1", "C0", "T1")
        for i in range(TPT):
            P.op("pe", lambda e, i=i, pp=ph % 2: e.matmul(
                t.psB[0:64, 0:512], t.al_bf[:, i, :], t.xn_tm[:, pp, i, 0:512],
                start=(i == 0), stop=(i == TPT - 1)))
            P.op("pe", lambda e, i=i, pp=ph % 2: e.matmul(
                t.psB[0:64, 512:1024], t.al_bf[:, i, :],
                t.xn_tm[:, pp, i, 512:1024],
                start=(i == 0), stop=(i == TPT - 1)))
            P.op("pe", lambda e, i=i: e.matmul(
                t.psC[0:64, 0:1], t.al_bf[:, i, :], t.ones_c[:, 0:1],
                start=(i == 0), stop=(i == TPT - 1)))
        for i in range(TPT):
            P.op("pe", lambda e, i=i: e.transpose(
                t.psT1[0:64, i * 128:(i + 1) * 128],
                t.al_bf[:, i, :], t.ident[:]),
                inc="spe" if i == TPT - 1 else None,
                event=("slotmm", ph) if i == TPT - 1 else None)

    def slot_ve(ph):
        P.wait("ve", ("slotmm", ph))
        if ph == 0:
            P.op("ve", lambda v: v.tensor_copy(t.sl_acc[:, 0:512],
                                               t.psB[0:64, 0:512]))
            P.op("ve", lambda v: v.tensor_copy(t.sl_acc[:, 512:1024],
                                               t.psB[0:64, 512:1024]))
            P.op("ve", lambda v: v.tensor_copy(t.den_acc[:], t.psC[0:64, 0:1]))
        else:
            P.op("ve", lambda v: v.tensor_add(t.sl_acc[:, 0:512],
                                              t.sl_acc[:, 0:512], t.psB[0:64, 0:512]))
            P.op("ve", lambda v: v.tensor_add(t.sl_acc[:, 512:1024],
                                              t.sl_acc[:, 512:1024],
                                              t.psB[0:64, 512:1024]))
            P.op("ve", lambda v: v.tensor_add(t.den_acc[:], t.den_acc[:],
                                              t.psC[0:64, 0:1]))
        P.op("ve", lambda v, ph=ph: v.tensor_copy(
            t.alpha_T[0:64, ph * Tp:(ph + 1) * Tp], t.psT1[0:64, 0:Tp]),
            inc="sve", event=("slotev", ph))
        for r in ("B0", "B1", "C0", "T1"):
            release(r, ("slotev", ph))

    def ln_apply(ph):
        pp = ph % 2
        P.wait("ve", ("rstd", ph))
        if ph >= 2:
            P.wait("ve", ("slotmm", ph - 2))
        P.op("ve", lambda v, pp=pp: v.tensor_mul(
            t.smrn[:, pp, :], t.smn[:, pp, :], t.sr[:, pp, :]))
        P.drain("ve")
        P.op("ve", lambda v, pp=pp: v.tensor_scalar(
            t.smrn[:, pp, :], t.smrn[:, pp, :], -1.0, None, ALU.mult))
        P.drain("ve")
        for i in range(TPT):
            sl, ii = xslot(ph, i // 2), i % 2
            P.op("ve", lambda v, i=i, sl=sl, ii=ii, pp=pp: v.tensor_scalar(
                t.xn_tm[:, pp, i, :], t.x_sl[:, sl, ii, :],
                t.sr[:, pp, i:i + 1], t.smrn[:, pp, i:i + 1],
                ALU.mult, ALU.add),
                inc="sve" if i == TPT - 1 else None,
                event=("ln", ph) if i == TPT - 1 else None)

    def xn_transposes(ph):
        P.wait("pe", ("ln", ph))
        if ph >= 1:
            pass
        for k in range(ND):
            reg = "T0" if k % 2 == 0 else "T1"
            psT = psH[reg]
            pe_claim(reg)
            for i in range(TPT):
                P.op("pe", lambda e, k=k, i=i, psT=psT, pp=ph % 2: e.transpose(
                    psT[:, i * 128:(i + 1) * 128],
                    t.xn_tm[:, pp, i, k * 128:(k + 1) * 128], t.ident[:]),
                    inc="spe" if i == TPT - 1 else None,
                    event=("xnT", ph, k) if i == TPT - 1 else None)
            P.wait("ve", ("xnT", ph, k))
            if ph >= 1 and k < 2:
                P.wait("ve", ("wku", ph - 1, ND - 1))
                P.wait("ve", ("hmm", ph - 1, NH - 1))
            P.op("ve", lambda v, k=k, psT=psT: v.tensor_copy(
                t.xn_f[:, k, :], psT[:, 0:Tp]), inc="sve", event=("xnf", ph, k))
            release(reg, ("xnf", ph, k))

    def wku_loop(ph):
        P.wait("pe", ("xnf", ph, ND - 1))
        for j in range(ND):
            par = j % 2
            if par == 0:
                rw, rk, ru = "A0", "A1", "B0"
            else:
                rw, rk, ru = "B1", "C0", "C1"
            psw, psk, psu = (psH[rw][:, 0:Tp], psH[rk][:, 0:Tp],
                            psH[ru][:, 0:Tp])
            pe_claim(rw, rk, ru)
            for k in range(ND):
                P.op("pe", lambda e, j=j, k=k, psw=psw: e.matmul(
                    psw, t.Wp_sb[:, k, j * 128:(j + 1) * 128], t.xn_f[:, k, :],
                    start=(k == 0), stop=(k == ND - 1)))
            for k in range(ND):
                P.op("pe", lambda e, j=j, k=k, psk=psk: e.matmul(
                    psk, t.Wp_sb[:, k, 1024 + j * 128:1024 + (j + 1) * 128],
                    t.xn_f[:, k, :], start=(k == 0), stop=(k == ND - 1)))
            for k in range(ND):
                last = (k == ND - 1)
                P.op("pe", lambda e, j=j, k=k, psu=psu: e.matmul(
                    psu, t.Wu_sb[:, k, j * 128:(j + 1) * 128], t.xn_f[:, k, :],
                    start=(k == 0), stop=(k == ND - 1)),
                    inc="spe" if last else None,
                    event=("wku", ph, j) if last else None)

            if j == 0:
                P.op("se", lambda s: _load_tab(s, TAB_SIG))
            P.wait("se", ("wku", ph, j))
            if j >= 2:
                P.wait("se", ("absc", ph, j - 2))
            elif ph >= 1:
                P.wait("se", ("absc", ph - 1, j + 6))
            P.drain("se")
            P.op("se", lambda s, j=j, psw=psw: s.activation(
                t.g_sb[:, j % 2, :], psw, AF.Sigmoid, bias=t.bpw[:, j:j + 1]))
            P.op("se", lambda s, j=j, psk=psk: s.activation(
                t.s2_sb[:, j % 2, :], psk, AF.Sigmoid, bias=t.bpk2[:, j:j + 1],
                scale=-2.0), inc="sse", event=("gs", ph, j))
            release(rw, ("gs", ph, j))
            release(rk, ("gs", ph, j))
            if ph + 1 < NPH and j < TPT:
                stats_tile(ph + 1, j)
            if ph + 1 < NPH and j == TPT:
                rstd_window(ph + 1)

            P.wait("ve", ("gs", ph, j))
            if ph >= 1 and j == 0:
                P.wait("ve", ("womm", ph - 1, TPT - 1))
            pp = j % 2
            P.drain("ve")
            P.op("ve", lambda v, pp=pp: v.scalar_tensor_tensor(
                t.a_sb[:], t.g_sb[:, pp, :], -2.0, t.s2_sb[:, pp, :],
                ALU.mult, ALU.mult))
            P.op("ve", lambda v, pp=pp, psu=psu, j=j: v.scalar_tensor_tensor(
                t.b_sb[:], psu, t.bu_c[:, j:j + 1], t.g_sb[:, pp, :],
                ALU.add, ALU.mult), inc="sve", event=("absc", ph, j))
            release(ru, ("absc", ph, j))
            P.drain("ve")
            P.op("ve", lambda v, pp=pp: v.tensor_scalar(
                t.a_sb[:], t.a_sb[:], 1.0, None, ALU.add))
            P.drain("ve")
            init = 0.0 if ph == 0 else t.ys_bf[:, j, Tp - 1:Tp]
            P.op("ve", lambda v, j=j, init=init: v.tensor_tensor_scan(
                t.ys_bf[:, j, :], t.a_sb[:], t.b_sb[:], init,
                ALU.mult, ALU.add),
                inc="sve", event=("ysd", ph, j))
            if ph + 1 < NPH and j == TPT - 1:
                var_chain(ph + 1)
            if ph + 1 < NPH and j == ND - 1:
                ln_apply(ph + 1)

    def h_logits(ph):
        P.wait("pe", ("xnf", ph, ND - 1))
        hr = ["A0", "A1", "B0", "B1"]
        for jh in range(NH):
            reg = hr[jh]
            ps = psH[reg][:, 0:Tp]
            pe_claim(reg)
            for k in range(ND):
                last = (k == ND - 1)
                P.op("pe", lambda e, jh=jh, k=k, ps=ps: e.matmul(
                    ps, t.w1t_sb[:, k, jh * 128:(jh + 1) * 128], t.xn_f[:, k, :],
                    start=(k == 0), stop=(k == ND - 1)),
                    inc="spe" if last else None,
                    event=("hmm", ph, jh) if last else None)
            P.wait("se", ("hmm", ph, jh))
            if jh == 0:
                if ph >= 1:
                    P.wait("se", ("lmm", ph - 1))
                P.op("se", lambda s: _load_tab(s, TAB_GELU))
            P.drain("se")
            P.op("se", lambda s, jh=jh, ps=ps: s.activation(
                t.h_f[:, jh, :], ps, AF.Gelu, bias=t.bias_h[:, jh:jh + 1]),
                inc="sse", event=("hf", ph, jh))
            release(reg, ("hf", ph, jh))
        P.wait("pe", ("hf", ph, NH - 1))
        pe_claim("C0")
        for i in range(TPT):
            for kh in range(NH):
                P.op("pe", lambda e, i=i, kh=kh: e.matmul(
                    t.psC[:, i * 64:(i + 1) * 64],
                    t.h_f[:, kh, i * 128:(i + 1) * 128], t.w2_sb[:, kh, :],
                    start=(kh == 0), stop=False))
            last = (i == TPT - 1)
            P.op("pe", lambda e, i=i: e.matmul(
                t.psC[:, i * 64:(i + 1) * 64], t.ones_r[:], t.b2_r[:],
                start=False, stop=True),
                inc="spe" if last else None,
                event=("lmm", ph) if last else None)
        P.wait("se", ("lmm", ph))
        if ph >= 1:
            P.wait("se", ("alpha", ph - 1))
        P.drain("se")
        P.op("se", lambda s: s.activation(
            t.l_sb[:], t.psC[:, 0:TPT * 64], AF.Identity, scale=t.invc[:]),
            inc="sse", event=("lev", ph))
        release("C0", ("lev", ph))
        P.wait("ve", ("lev", ph))
        for i in range(TPT):
            P.op("ve", lambda v, i=i: v.max(out=t.m18[:, i, :],
                                            in_=t.l_sb[:, i, :]))
        P.drain("ve")
        P.op("ve", lambda v: v.tensor_scalar(
            t.m1n[:], t.m18[:, :, 0], -1.0, None, ALU.mult),
            inc="sve", event=("max8", ph))

    def mid_weight_loads():
        def ihl(jt, h):
            P.dma("gp", lambda g, jt=jt, h=h: g.dma_start(
                out=t.WT[:, jt, h * 4:(h + 1) * 4],
                in_=t.i_Wih_pk.rearrange("p (jt k f) -> p jt k f", jt=6, k=8)
                    [:, jt, h * 4:(h + 1) * 4]), "sg", event=("ihl", jt, h))

        def whl(jt, h):
            P.dma("gp", lambda g, jt=jt, h=h: g.dma_start(
                out=t.whh_pp[:, jt % 2, h * 4:(h + 1) * 4],
                in_=t.i_Whh_pk.rearrange("p (jt k f) -> p jt k f", jt=6, k=8)
                    [:, jt, h * 4:(h + 1) * 4]), "sg", event=("whl", jt, h))

        P.wait("gp", ("wku", NPH - 1, ND - 1))
        P.wait_all_dma("gp", ("pstore", NPH - 1, NHF - 1), "sp")
        P.wait("gp", ("slotev", NPH - 1))
        P.wait("gp", ("exps", NPH - 1))
        P.wait("gp", ("slotmm", NPH - 1))
        P.dma("gp", lambda g: g.dma_start(out=t.s0_f32[:], in_=t.i_slot[:]), "sg")
        P.dma("gp", lambda g: g.dma_start(out=t.bih_r[:], in_=t.i_bih[None, :]), "sg")
        P.dma("gp", lambda g: g.dma_start(out=t.bhh_r[:], in_=t.i_bhh[None, :]), "sg")
        P.dma("gp", lambda g: g.dma_start(out=t.bv_r[:], in_=t.i_bv[None, :]), "sg")
        P.dma("gp", lambda g: g.dma_start(out=t.bvp_r[:], in_=t.i_bvp[None, :]), "sg",
              event=("brows",))
        P.wait("gp", ("womm", NPH - 1, TPT - 1))
        for jt in range(2):
            ihl(jt, 0)
            whl(jt, 0)
            ihl(jt, 1)
            whl(jt, 1)
        for jt in range(2, 6):
            ihl(jt, 0)
            ihl(jt, 1)

    for i in range(TPT):
        stats_tile(0, i)
    var_chain(0)
    rstd_window(0)
    ln_apply(0)
    for ph in range(NPH + 1):
        window(ph)
        if ph >= 1:
            alpha_fin(ph - 1)
        if ph >= 1:
            wo_block(ph - 1)
            if ph + 1 < NPH:
                for h in range(NHF):
                    P.wait("sy", ("pstore", ph - 1, h))
                    P.dma("sy", lambda s, ph=ph, h=h: s.dma_start(
                        out=t.x_sl[:, xslot(ph + 1, h), :, :],
                        in_=t.i_x[(ph + 1) * Tp + h * 256:
                                  (ph + 1) * Tp + (h + 1) * 256, :]
                            .rearrange("(n p) d -> p n d", p=128)), "sx",
                        event=("xload", ph + 1, h))
        if ph >= 1:
            slot_pe(ph - 1)
            slot_ve(ph - 1)
        if ph < NPH:
            xn_transposes(ph)
            h_logits(ph)
            wku_loop(ph)
        if ph == NPH:
            mid_weight_loads()

    P.wait("ve", ("slotev", NPH - 1))
    P.wait("ve", ("exps", NPH - 1))
    P.op("ve", lambda v: v.tensor_scalar(t.dent[:], t.den_acc[:], 1e-8, None,
                                         ALU.add))
    P.drain("ve")
    P.op("ve", lambda v: v.reciprocal(t.dent[:], t.dent[:]))
    P.drain("ve")
    P.op("ve", lambda v: v.tensor_scalar(t.si_bf[:], t.sl_acc[:], t.dent[:],
                                         None, ALU.mult), inc="sve",
         event=("si",))
    P.wait("pe", ("si",))
    pe_claim("T0")
    for k in range(ND):
        P.op("pe", lambda e, k=k: e.transpose(
            t.psT0[:, k * 64:(k + 1) * 64], t.si_bf[0:64, k * 128:(k + 1) * 128],
            t.ident[0:64, 0:64]),
            inc="spe" if k == ND - 1 else None,
            event=("siT",) if k == ND - 1 else None)
    P.wait("ve", ("siT",))
    for k in range(ND):
        P.op("ve", lambda v, k=k: v.tensor_copy(
            t.si_f[:, k, :], t.psT0[:, k * 64:(k + 1) * 64]),
            inc="sve" if k == ND - 1 else None,
            event=("sif",) if k == ND - 1 else None)
    release("T0", ("sif",))

    whhbuf = t.whh_pp
    P.wait("pe", ("sif",))
    P.wait_all_dma("pe", ("brows",), "sg")
    mregs = ["A0", "A1", "B0", "B1", "C0", "C1"]
    for jt in range(6):
        reg = mregs[jt]
        ps = psH[reg]
        pe_claim(reg)
        for k in range(ND):
            if k % 4 == 0:
                P.wait("pe", ("ihl", jt, k // 4))
            P.op("pe", lambda e, k=k, jt=jt, ps=ps: e.matmul(
                ps[0:64, :], t.si_f[:, k, :], t.WT[:, jt, k, :],
                start=(k == 0), stop=False))
        P.op("pe", lambda e, jt=jt, ps=ps: e.matmul(
            ps[0:64, :], t.ones_r[0:1, 0:64], t.bih_r[:, jt * 512:(jt + 1) * 512],
            start=False, stop=True), inc="spe", event=("gmm", 0, jt))
        P.wait("se", ("gmm", 0, jt))
        P.op("se", lambda s, jt=jt, ps=ps: s.activation(
            t.gi_sb[:, jt * 512:(jt + 1) * 512], ps[0:64, :], AF.Identity),
            inc="sse", event=("gev", 0, jt))
        release(reg, ("gev", 0, jt))
        reg = mregs[jt]
        ps = psH[reg]
        pe_claim(reg)
        for k in range(ND):
            if k % 4 == 0:
                P.wait("pe", ("whl", jt, k // 4))
            P.op("pe", lambda e, k=k, jt=jt, ps=ps: e.matmul(
                ps[0:64, :], t.si0f[:, k, :],
                t.whh_pp[:, jt % 2, k, :], start=(k == 0), stop=False))
        P.op("pe", lambda e, jt=jt, ps=ps: e.matmul(
            ps[0:64, :], t.ones_r[0:1, 0:64], t.bhh_r[:, jt * 512:(jt + 1) * 512],
            start=False, stop=True), inc="spe", event=("ghmm", jt))
        if jt + 2 < 6:
            P.wait("gp", ("ghmm", jt))
            for h in range(2):
                P.dma("gp", lambda g, jt=jt, h=h: g.dma_start(
                    out=t.whh_pp[:, jt % 2, h * 4:(h + 1) * 4],
                    in_=t.i_Whh_pk.rearrange("p (jt k f) -> p jt k f",
                                             jt=6, k=8)
                        [:, jt + 2, h * 4:(h + 1) * 4]), "sg",
                    event=("whl", jt + 2, h))
        if jt < 4:
            P.wait("ve", ("ghmm", jt))
            P.wait("ve", ("gev", 0, jt))
            P.op("ve", lambda v, jt=jt, ps=ps: v.tensor_add(
                t.gi_sb[:, jt * 512:(jt + 1) * 512],
                t.gi_sb[:, jt * 512:(jt + 1) * 512], ps[0:64, :]),
                inc="sve", event=("gev", 1, jt))
        else:
            P.wait("ve", ("ghmm", jt))
            P.op("ve", lambda v, jt=jt, ps=ps: v.tensor_copy(
                t.gh_n[:, (jt - 4) * 512:(jt - 3) * 512], ps[0:64, :]),
                inc="sve", event=("gev", 1, jt))
        release(reg, ("gev", 1, jt))
    P.dma("gp", lambda g: g.dma_start(
        out=t.wv_sb[:], in_=t.i_wv_pk.rearrange("p (k f) -> p k f", k=8)), "sg")
    P.dma("gp", lambda g: g.dma_start(
        out=t.wvp_sb[:], in_=t.i_wvp_pk.rearrange("p (k f) -> p k f", k=8)), "sg",
        event=("w2load",))
    P.wait("gp", ("svmm",))
    P.dma("gp", lambda g: g.dma_start(
        out=t.wo2_sb[:], in_=t.i_wo2_pk.rearrange("p (k f) -> p k f", k=8)), "sg",
        event=("w3load",))

    P.wait("se", ("gev", 1, 3))
    P.op("se", lambda s: _load_tab(s, TAB_SIG))
    P.op("se", lambda s: s.activation(t.r_g[:], t.gi_sb[:, 0:1024], AF.Sigmoid))
    P.op("se", lambda s: s.activation(t.z_g[:], t.gi_sb[:, 1024:2048],
                                      AF.Sigmoid), inc="sse", event=("rz",))
    P.wait("ve", ("rz",))
    P.drain("ve")
    P.op("ve", lambda v: v.tensor_mul(t.t64a[:], t.r_g[:], t.gh_n[:, 0:1024]))
    P.drain("ve")
    P.op("ve", lambda v: v.tensor_add(t.t64a[:], t.t64a[:],
                                      t.gi_sb[:, 2048:3072]),
         inc="sve", event=("n_in",))
    P.wait("se", ("n_in",))
    P.op("se", lambda s: s.activation(t.n_g[:], t.t64a[:], AF.Tanh),
         inc="sse", event=("n_g",))
    P.wait("ve", ("n_g",))
    P.op("ve", lambda v: v.tensor_sub(t.t64a[:], t.s0_f32[:], t.n_g[:]))
    P.drain("ve")
    P.op("ve", lambda v: v.tensor_mul(t.t64a[:], t.t64a[:], t.z_g[:]))
    P.drain("ve")
    P.op("ve", lambda v: v.tensor_add(t.snew[:], t.n_g[:], t.t64a[:]))
    P.drain("ve")
    P.op("ve", lambda v: v.tensor_copy(t.snew_bf[:], t.snew[:]), inc="sve",
         event=("snew",))
    P.wait("pe", ("snew",))
    pe_claim("T0")
    for k in range(ND):
        P.op("pe", lambda e, k=k: e.transpose(
            t.psT0[:, k * 64:(k + 1) * 64],
            t.snew_bf[0:64, k * 128:(k + 1) * 128], t.ident[0:64, 0:64]),
            inc="spe" if k == ND - 1 else None,
            event=("snT",) if k == ND - 1 else None)
    P.wait("ve", ("snT",))
    for k in range(ND):
        P.op("ve", lambda v, k=k: v.tensor_copy(
            t.snew_f[:, k, :], t.psT0[:, k * 64:(k + 1) * 64]),
            inc="sve" if k == ND - 1 else None,
            event=("snf",) if k == ND - 1 else None)
    release("T0", ("snf",))
    P.drain("ve")
    P.op("ve", lambda v: v.tensor_scalar(t.bq_c[:], t.bq_c[:], DQS, None,
                                         ALU.mult), inc="sve", event=("bqs",))

    P.wait("pe", ("snf",))
    P.wait("pe", ("bqs",))
    pe_claim("A0", "A1")
    for m in range(2):
        for dqc in range(2):
            for k in range(ND):
                last = (m == 1 and dqc == 1 and k == ND - 1)
                P.op("pe", lambda e, m=m, dqc=dqc, k=k: e.matmul(
                    (t.psA[:, dqc * 64:(dqc + 1) * 64] if m == 0
                     else t.psA[:, 512 + dqc * 64:512 + (dqc + 1) * 64]),
                    (t.wq_sb if m == 0 else t.wk_sb)[:, k, dqc * 128:(dqc + 1) * 128],
                    t.snew_f[:, k, :], start=(k == 0), stop=(k == ND - 1)),
                    inc="spe" if last else None,
                    event=("qkmm",) if last else None)
    P.wait("se", ("qkmm",))
    for dqc in range(2):
        P.op("se", lambda s, dqc=dqc: s.activation(
            t.q_bf[:, dqc, :], t.psA[:, dqc * 64:(dqc + 1) * 64], AF.Identity,
            bias=t.bq_c[:, dqc:dqc + 1], scale=DQS))
        P.op("se", lambda s, dqc=dqc: s.activation(
            t.k_bf[:, dqc, :], t.psA[:, 512 + dqc * 64:512 + (dqc + 1) * 64],
            AF.Identity, bias=t.bk_c[:, dqc:dqc + 1]),
            inc="sse" if dqc == 1 else None,
            event=("qk",) if dqc == 1 else None)
    release("A0", ("qk",))
    release("A1", ("qk",))

    P.wait("pe", ("qk",))
    pe_claim("B0")
    for dqc in range(2):
        P.op("pe", lambda e, dqc=dqc: e.matmul(
            t.psB[0:64, 0:64], t.q_bf[:, dqc, :], t.k_bf[:, dqc, :],
            start=(dqc == 0), stop=(dqc == 1)),
            inc="spe" if dqc == 1 else None,
            event=("scmm",) if dqc == 1 else None)
    P.wait("se", ("scmm",))
    P.op("se", lambda s: s.activation(t.sc_sb[:], t.psB[0:64, 0:64], AF.Identity),
         inc="sse", event=("scev",))
    release("B0", ("scev",))
    P.wait("ve", ("scev",))
    P.op("ve", lambda v: v.reduce_max(t.scm[:], t.sc_sb[:], axis=AX.X))
    P.drain("ve")
    P.op("ve", lambda v: v.tensor_scalar(t.scm[:], t.scm[:], -1.0, None,
                                         ALU.mult), inc="sve", event=("scmax",))
    P.wait("se", ("scmax",))
    P.op("se", lambda s: _load_tab(s, TAB_LNEXP))
    P.drain("se")
    P.op("se", lambda s: s.activation(t.sc_sb[:], t.sc_sb[:], AF.Exp,
                                      bias=t.scm[:], accum_out=t.scs[:]),
         inc="sse", event=("scexp",))
    P.wait("ve", ("scexp",))
    P.op("ve", lambda v: v.reciprocal(t.scs[:], t.scs[:]))
    P.drain("ve")
    P.op("ve", lambda v: v.tensor_scalar(t.attn_bf[:], t.sc_sb[:], t.scs[:],
                                         None, ALU.mult), inc="sve",
         event=("attn",))
    P.wait("pe", ("attn",))
    pe_claim("T0")
    P.op("pe", lambda e: e.transpose(t.psT0[0:64, 0:64], t.attn_bf[0:64, 0:64],
                                     t.ident[0:64, 0:64]),
         inc="spe", event=("atT",))
    P.wait("ve", ("atT",))
    P.op("ve", lambda v: v.tensor_copy(t.attnT_bf[:], t.psT0[0:64, 0:64]),
         inc="sve", event=("atTe",))
    release("T0", ("atTe",))

    P.wait_all_dma("pe", ("w2load",), "sg")
    pe_claim("B0", "B1")
    for jt in range(2):
        ps = psH["B0" if jt == 0 else "B1"]
        for k in range(ND):
            P.op("pe", lambda e, jt=jt, k=k, ps=ps: e.matmul(
                ps[0:64, :], t.snew_f[:, k, :],
                t.wv_sb[:, k, jt * 512:(jt + 1) * 512],
                start=(k == 0), stop=False))
        P.op("pe", lambda e, jt=jt, ps=ps: e.matmul(
            ps[0:64, :], t.ones_r[0:1, 0:64], t.bv_r[:, jt * 512:(jt + 1) * 512],
            start=False, stop=True),
            inc="spe" if jt == 1 else None,
            event=("vmm",) if jt == 1 else None)
    P.wait("se", ("vmm",))
    P.op("se", lambda s: s.activation(t.v_bf[:, 0:512], t.psB[0:64, 0:512],
                                      AF.Identity))
    P.op("se", lambda s: s.activation(t.v_bf[:, 512:1024],
                                      t.psB[0:64, 512:1024], AF.Identity),
         inc="sse", event=("vev",))
    release("B0", ("vev",))
    release("B1", ("vev",))

    P.wait("pe", ("atTe",))
    P.wait("pe", ("vev",))
    pe_claim("C0", "C1")
    for jt in range(2):
        ps = psH["C0" if jt == 0 else "C1"]
        P.op("pe", lambda e, jt=jt, ps=ps: e.matmul(
            ps[0:64, :], t.attnT_bf[:], t.v_bf[0:64, jt * 512:(jt + 1) * 512],
            start=True, stop=True),
            inc="spe" if jt == 1 else None,
            event=("cmm",) if jt == 1 else None)
    P.wait("ve", ("cmm",))
    P.op("ve", lambda v: v.tensor_add(t.y_sb[:, 0:512], t.psC[0:64, 0:512],
                                      t.snew[:, 0:512]))
    P.op("ve", lambda v: v.tensor_add(t.y_sb[:, 512:1024],
                                      t.psC[0:64, 512:1024],
                                      t.snew[:, 512:1024]),
         inc="sve", event=("y",))
    release("C0", ("y",))
    release("C1", ("y",))

    P.wait("se", ("y",))
    P.drain("se")
    P.op("se", lambda s: s.activation(t.dump[0:64, :], t.y_sb[:], AF.Identity,
                                      accum_out=t.ysum[:]))
    P.drain("se")
    P.op("se", lambda s: s.activation(t.dump[0:64, :], t.y_sb[:], AF.Square,
                                      accum_out=t.yssq[:]),
         inc="sse", event=("ysums",))
    P.wait("ve", ("ysums",))
    P.op("ve", lambda v: v.tensor_scalar(t.ymu[:], t.ysum[:], 1.0 / 1024.0,
                                         None, ALU.mult))
    P.op("ve", lambda v: v.tensor_scalar(t.yvar[:], t.yssq[:], 1.0 / 1024.0,
                                         1e-5, ALU.mult, ALU.add))
    P.drain("ve")
    P.op("ve", lambda v: v.tensor_mul(t.ytmp[:], t.ymu[:], t.ymu[:]))
    P.drain("ve")
    P.op("ve", lambda v: v.tensor_sub(t.yvar[:], t.yvar[:], t.ytmp[:]),
         inc="sve", event=("yvar",))
    P.wait("se", ("yvar",))
    P.op("se", lambda s: s.activation(t.ytmp[:], t.yvar[:], AF.Ln))
    P.drain("se")
    P.op("se", lambda s: s.activation(t.yrst[:], t.ytmp[:], AF.Exp, scale=-0.5),
         inc="sse", event=("yrst",))
    P.wait("ve", ("yrst",))
    P.op("ve", lambda v: v.tensor_mul(t.ymu[:], t.ymu[:], t.yrst[:]))
    P.drain("ve")
    P.op("ve", lambda v: v.tensor_scalar(t.ymu[:], t.ymu[:], -1.0, None,
                                         ALU.mult))
    P.drain("ve")
    P.op("ve", lambda v: v.tensor_scalar(t.y0_bf[:], t.y_sb[:], t.yrst[:],
                                         t.ymu[:], ALU.mult, ALU.add),
         inc="sve", event=("y0",))
    P.wait("pe", ("y0",))
    pe_claim("T0")
    for k in range(ND):
        P.op("pe", lambda e, k=k: e.transpose(
            t.psT0[:, k * 64:(k + 1) * 64],
            t.y0_bf[0:64, k * 128:(k + 1) * 128], t.ident[0:64, 0:64]),
            inc="spe" if k == ND - 1 else None,
            event=("yT",) if k == ND - 1 else None)
    P.wait("ve", ("yT",))
    for k in range(ND):
        P.op("ve", lambda v, k=k: v.tensor_scalar(
            t.yln_f[:, k, :], t.psT0[:, k * 64:(k + 1) * 64],
            t.lng_c[:, k:k + 1], t.lnb_c[:, k:k + 1], ALU.mult, ALU.add),
            inc="sve" if k == ND - 1 else None,
            event=("ylnf",) if k == ND - 1 else None)
    release("T0", ("ylnf",))

    P.wait("pe", ("ylnf",))
    pe_claim("A0", "A1")
    for jt in range(2):
        ps = psH["A0" if jt == 0 else "A1"]
        for k in range(ND):
            P.op("pe", lambda e, jt=jt, k=k, ps=ps: e.matmul(
                ps[0:64, :], t.yln_f[:, k, :],
                t.wvp_sb[:, k, jt * 512:(jt + 1) * 512],
                start=(k == 0), stop=False))
        P.op("pe", lambda e, jt=jt, ps=ps: e.matmul(
            ps[0:64, :], t.ones_r[0:1, 0:64],
            t.bvp_r[:, jt * 512:(jt + 1) * 512], start=False, stop=True),
            inc="spe" if jt == 1 else None,
            event=("svmm",) if jt == 1 else None)
    P.wait("se", ("svmm",))
    P.op("se", lambda s: s.activation(t.sv_bf[:, 0:512], t.psA[0:64, 0:512],
                                      AF.Identity))
    P.op("se", lambda s: s.activation(t.sv_bf[:, 512:1024],
                                      t.psA[0:64, 512:1024], AF.Identity),
         inc="sse", event=("sv",))
    release("A0", ("sv",))
    release("A1", ("sv",))

    zregs = ["C0", "C1", "B0", "B1", "A0", "A1", "C0", "C1"]

    def zmm_phase(q):
        if q == 0:
            P.wait("pe", ("sv",))
            P.wait("pe", ("botot",))
            P.wait_all_dma("pe", ("w3load",), "sg")
        for k in range(ND):
            reg = zregs[k]
            pe_claim(reg)
            ps = psH[reg][:, 0:Tp]
            P.op("pe", lambda e, k=k, q=q, ps=ps: e.matmul(
                ps, t.sv_bf[0:64, k * 128:(k + 1) * 128],
                t.alpha_T[0:64, q * Tp:(q + 1) * Tp],
                start=True, stop=True), inc="spe", event=("zmm", q, k))
            P.wait("se", ("zmm", q, k))
            if q >= 2 and k == 0:
                P.wait("se", ("omm", q - 2, TPT - 1))
                P.drain("se")
            P.op("se", lambda s, k=k, q=q, ps=ps: s.activation(
                t.z_f[:, q % 2, k, :], ps, AF.Identity),
                inc="sse", event=("zev", q, k))
            release(reg, ("zev", q, k))

    if not cfg.get("skip_pass2"):
        zmm_phase(0)
    for p2 in ([] if cfg.get("skip_pass2") else range(NPH)):
        pr = p2 % 2
        if p2 == 0:
            P.wait_all_dma("sy", ("pstore", NPH - 1, NHF - 1), "sp")
            P.wait("sy", ("sv",))
        if p2 >= 2:
            P.wait("sy", ("fadd", p2 - 2, TPT - 1))
            P.wait("sy", ("ostore", p2 - 2, NHF - 1))
        P.dma("sy", lambda s, p2=p2, pr=pr: s.dma_start(
            out=t.prd[:, pr, :, :],
            in_=t.o_out[p2 * Tp:(p2 + 1) * Tp, :]
                .rearrange("(n p) d -> p n d", p=128)), "sx",
            event=("pread", p2))
        if p2 + 1 < NPH:
            zmm_phase(p2 + 1)
        oregs = [("A0", "A1", t.psA), ("B0", "B1", t.psB), ("C0", "C1", t.psC),
                 ("A0", "A1", t.psA)]
        for i in range(TPT):
            r0, r1, ps = oregs[i]
            pe_claim(r0, r1)
            for k in range(ND):
                if i == 0:
                    P.wait("pe", ("zev", p2, k))
                for f in range(2):
                    P.op("pe", lambda e, i=i, f=f, k=k, ps=ps, p2=p2: e.matmul(
                        ps[:, f * 512:(f + 1) * 512],
                        t.z_f[:, p2 % 2, k, i * 128:(i + 1) * 128],
                        t.wo2_sb[:, k, f * 512:(f + 1) * 512],
                        start=(k == 0), stop=False))
            for f in range(2):
                last = (f == 1)
                P.op("pe", lambda e, i=i, f=f, ps=ps: e.matmul(
                    ps[:, f * 512:(f + 1) * 512], t.ones_r[:],
                    t.bo_tot[:, f * 512:(f + 1) * 512], start=False, stop=True),
                    inc="spe" if last else None,
                    event=("omm", p2, i) if last else None)
            P.wait("ve", ("omm", p2, i))
            if i == 0:
                P.wait("ve", ("pread", p2))
            P.op("ve", lambda v, ps=ps, pr=pr, i=i: v.tensor_add(
                t.prd[:, pr, i, :], ps[:, 0:1024], t.prd[:, pr, i, :]),
                inc="sve", event=("fadd", p2, i))
            release(r0, ("fadd", p2, i))
            release(r1, ("fadd", p2, i))
        for h in range(NHF):
            P.wait("sy", ("fadd", p2, 2 * h + 1))
            P.dma("sy", lambda s, p2=p2, h=h, pr=pr: s.dma_start(
                out=t.o_out[p2 * Tp + h * 256:p2 * Tp + (h + 1) * 256, :]
                    .rearrange("(n p) d -> p n d", p=128),
                in_=t.prd[:, pr, 2 * h:2 * h + 2, :]), "so",
                event=("ostore", p2, h))

    if cfg.get("taps"):
        P.wait("sy", ("sv",))
        P.wait("sy", ("slotev", NPH - 1))
        P.wait("sy", ("ysd", NPH - 1, ND - 1))
        P.wait("sy", ("biash",))
        P.dma("sy", lambda s: s.dma_start(out=t.o_alpha[:], in_=t.alpha_T[:]), "so")
        P.dma("sy", lambda s: s.dma_start(out=t.o_slacc[:], in_=t.sl_acc[:]), "so")
        P.dma("sy", lambda s: s.dma_start(out=t.o_den[:], in_=t.den_acc[:]), "so")
        P.dma("sy", lambda s: s.dma_start(out=t.o_sv[:], in_=t.sv_bf[:]), "so")
        P.dma("sy", lambda s: s.dma_start(out=t.o_carry[:], in_=t.carry[:]), "so")
        P.dma("sy", lambda s: s.dma_start(out=t.o_biash[:], in_=t.bias_h[:]), "so")
        P.dma("sy", lambda s: s.dma_start(out=t.o_snew[:], in_=t.snew[:]), "so")
        P.dma("sy", lambda s: s.dma_start(out=t.o_s0[:], in_=t.s0_f32[:]), "so")
        P.dma("sy", lambda s: s.dma_start(out=t.o_ghn[:], in_=t.gh_n[:]), "so")
        P.dma("sy", lambda s: s.dma_start(out=t.o_gi[:], in_=t.gi_sb[:]), "so")
        P.dma("sy", lambda s: s.dma_start(out=t.o_ssum[:], in_=t.ssum[:, (NPH - 1) % 2]), "so")
        P.dma("sy", lambda s: s.dma_start(out=t.o_ssq[:], in_=t.ssq[:, (NPH - 1) % 2]), "so")
        P.dma("sy", lambda s: s.dma_start(out=t.o_sr[:], in_=t.sr[:, (NPH - 1) % 2]), "so")
        P.dma("sy", lambda s: s.dma_start(out=t.o_xn[:], in_=t.xn_tm[:, (NPH - 1) % 2]), "so")
        P.dma("sy", lambda s: s.dma_start(out=t.o_xnf[:], in_=t.xn_f[:]), "so")
        P.dma("sy", lambda s: s.dma_start(out=t.o_ysbf[:], in_=t.ys_bf[:]), "so")
        P.dma("sy", lambda s: s.dma_start(out=t.o_lsb[:], in_=t.l_sb[:]), "so")

    if P.mode == "sy":
        _, k = DMA_POOLS["so"]
        n = P.pool_n["so"]
        for i in range(k):
            cnt = 16 * ((n - i - 1) // k + 1) if n > i else 0
            if cnt:
                P.eng.wait_ge(P.sems[f"so{i}"], cnt)


def build_nc(cfg):
    T = cfg["T"]; Tp = cfg["Tp"]
    TPT = Tp // 128
    nc = bass.Bass("TRN2", target_bir_lowering=False)
    cfg = dict(cfg)
    cfg["tab_lnexp"] = _act_set_idx(nc, {AF.Ln, AF.Exp, AF.Identity, AF.Square})
    cfg["tab_sig"] = _act_set_idx(nc, {AF.Sigmoid, AF.Tanh, AF.Identity,
                                       AF.Square, AF.Abs})
    cfg["tab_gelu"] = _act_set_idx(nc, {AF.Gelu, AF.Identity})

    class TB:
        pass
    t = TB()

    def din(name, shape, dtype=F32):
        return nc.dram_tensor(name, shape, dtype, kind="ExternalInput").ap()

    t.i_x = din("x", [T, 1024])
    t.i_Wp_pk = din("Wp_pk", [128, 8 * 2048])
    t.i_bp0_pk = din("bp0_pk", [128, 8])
    t.i_bpk_pk = din("bpk_pk", [128, 8])
    t.i_Wu_pk = din("Wu_pk", [128, 8 * 1024])
    t.i_bu_pk = din("bu_pk", [128, 8])
    t.i_Wo_pk = din("Wo_pk", [128, 8 * 1024])
    t.i_bo = din("bo", [1024])
    t.i_slot = din("slot_init", [64, 1024])
    t.i_w1 = din("w1", [2048, 512])
    t.i_w1t_pk = din("w1t_pk", [128, 8 * 512])
    t.i_b1_pk = din("b1_pk", [128, 4])
    t.i_w2_pk = din("w2_pk", [128, 4 * 64])
    t.i_b2 = din("b2", [64])
    t.i_tau_b = din("tau_b", [128, 1])
    t.i_Wih_pk = din("Wih_pk", [128, 6 * 8 * 512])
    t.i_Whh_pk = din("Whh_pk", [128, 6 * 8 * 512])
    t.i_bih = din("bih", [3072])
    t.i_bhh = din("bhh", [3072])
    t.i_wq_pk = din("wq_pk", [128, 8 * 256])
    t.i_bq_pk = din("bq_pk", [128, 2])
    t.i_wk_pk = din("wk_pk", [128, 8 * 256])
    t.i_bk_pk = din("bk_pk", [128, 2])
    t.i_wv_pk = din("wv_pk", [128, 8 * 1024])
    t.i_bv = din("bv", [1024])
    t.i_lng_pk = din("lng_pk", [128, 8])
    t.i_lnb_pk = din("lnb_pk", [128, 8])
    t.i_wvp_pk = din("wvp_pk", [128, 8 * 1024])
    t.i_bvp = din("bvp", [1024])
    t.i_wo2_pk = din("wo2_pk", [128, 8 * 1024])
    t.i_bof = din("bof", [1024])
    t.i_ident = din("ident", [128, 128], BF16)
    t.i_ones_r = din("ones_r", [1, 128], BF16)
    t.i_ones_c = din("ones_c", [128, 1], BF16)
    t.o_out = nc.dram_tensor("out", [T, 1024], F32, kind="ExternalOutput").ap()
    if cfg.get("taps"):
        t.o_alpha = nc.dram_tensor("o_alpha", [64, T], BF16, kind="ExternalOutput").ap()
        t.o_slacc = nc.dram_tensor("o_slacc", [64, 1024], F32, kind="ExternalOutput").ap()
        t.o_den = nc.dram_tensor("o_den", [64, 1], F32, kind="ExternalOutput").ap()
        t.o_sv = nc.dram_tensor("o_sv", [64, 1024], BF16, kind="ExternalOutput").ap()
        t.o_carry = nc.dram_tensor("o_carry", [128, 8], F32, kind="ExternalOutput").ap()
        t.o_biash = nc.dram_tensor("o_biash", [128, 4], F32, kind="ExternalOutput").ap()
        t.o_snew = nc.dram_tensor("o_snew", [64, 1024], F32, kind="ExternalOutput").ap()
        t.o_gi = nc.dram_tensor("o_gi", [64, 3072], F32, kind="ExternalOutput").ap()
        t.o_s0 = nc.dram_tensor("o_s0", [64, 1024], F32, kind="ExternalOutput").ap()
        t.o_ghn = nc.dram_tensor("o_ghn", [64, 1024], F32, kind="ExternalOutput").ap()
        t.o_ssum = nc.dram_tensor("o_ssum", [128, TPT], F32, kind="ExternalOutput").ap()
        t.o_ssq = nc.dram_tensor("o_ssq", [128, TPT], F32, kind="ExternalOutput").ap()
        t.o_sr = nc.dram_tensor("o_sr", [128, TPT], F32, kind="ExternalOutput").ap()
        t.o_xn = nc.dram_tensor("o_xn", [128, TPT, 1024], BF16, kind="ExternalOutput").ap()
        t.o_xnf = nc.dram_tensor("o_xnf", [128, 8, Tp], BF16, kind="ExternalOutput").ap()
        t.o_ysbf = nc.dram_tensor("o_ysbf", [128, 8, Tp], BF16, kind="ExternalOutput").ap()
        t.o_lsb = nc.dram_tensor("o_lsb", [128, TPT, 64], F32, kind="ExternalOutput").ap()

    with ExitStack() as ctx:
        def sb(name, shape, dtype):
            return ctx.enter_context(nc.sbuf_tensor("sb_" + name, shape, dtype))

        t.ident = sb("ident", [128, 128], BF16)
        t.ones_r = sb("ones_r", [1, 128], BF16)
        t.ones_c = sb("ones_c", [128, 1], BF16)
        t.tau_b = sb("tau_b", [128, 1], F32)
        t.invc = sb("invc", [128, 1], F32)
        t.alpha_T = sb("alpha_T", [64, T], BF16)
        t.sl_acc = sb("sl_acc", [64, 1024], F32)
        t.den_acc = sb("den_acc", [64, 1], F32)
        t.carry = sb("carry", [128, 8], F32)
        t.si0f = sb("si0f", [128, 8, 64], BF16)
        t.sm_f = sb("sm_f", [128, 8], F32)
        t.sm_fb = sb("sm_fb", [128, 8], BF16)
        t.bpw = sb("bpw", [128, 8], F32)
        t.bpk2 = sb("bpk2", [128, 8], F32)
        t.bu_c = sb("bu_c", [128, 8], F32)
        t.b1_c = sb("b1_c", [128, 4], F32)
        t.bias_h = sb("bias_h", [128, 4], F32)
        t.b2_r = sb("b2_r", [1, 64], BF16)
        t.bq_c = sb("bq_c", [128, 2], F32)
        t.bk_c = sb("bk_c", [128, 2], F32)
        t.lng_c = sb("lng_c", [128, 8], F32)
        t.lnb_c = sb("lnb_c", [128, 8], F32)
        t.bo_tot = sb("bo_tot", [1, 1024], BF16)
        t.wq_sb = sb("wq_sb", [128, 8, 256], BF16)
        t.wk_sb = sb("wk_sb", [128, 8, 256], BF16)
        t.w2_sb = sb("w2_sb", [128, 4, 64], BF16)
        t.sv_bf = sb("sv_bf", [64, 1024], BF16)
        t.dump = sb("dump", [128, 1024], BF16)
        s1 = ExitStack()

        def sb(name, shape, dtype, _s=s1):
            return _s.enter_context(nc.sbuf_tensor("sb_" + name, shape, dtype))

        t.s0_bf = sb("s0_bf", [64, 1024], BF16)
        t.bo1_r = sb("bo1_r", [8, 128], F32)
        t.bo2_r = sb("bo2_r", [8, 128], F32)
        t.Wp_sb = sb("Wp_sb", [128, 8, 2048], BF16)
        t.Wu_sb = sb("Wu_sb", [128, 8, 1024], BF16)
        t.Wo_sb = sb("Wo_sb", [128, 8, 1024], BF16)
        t.w1t_sb = sb("w1t_sb", [128, 8, 512], BF16)
        t.w1s = sb("w1s", [128, 2, 512], BF16)
        t.x_sl = sb("x_sl", [128, 4, 2, 1024], F32)
        t.xn_tm = sb("xn_tm", [128, 2, TPT, 1024], BF16)
        t.xn_f = sb("xn_f", [128, 8, Tp], BF16)
        t.g_sb = sb("g_sb", [128, 2, Tp], F32)
        t.s2_sb = sb("s2_sb", [128, 2, Tp], F32)
        t.a_sb = sb("a_sb", [128, Tp], F32)
        t.b_sb = sb("b_sb", [128, Tp], F32)
        t.ys_bf = sb("ys_bf", [128, 8, Tp], BF16)
        t.h_f = sb("h_f", [128, 4, Tp], BF16)
        t.l_sb = sb("l_sb", [128, TPT, 64], F32)
        t.e_al = sb("e_al", [128, TPT, 64], F32)
        t.al_bf = sb("al_bf", [128, TPT, 64], BF16)
        t.m18 = sb("m18", [128, TPT, 8], F32)
        t.m1n = sb("m1n", [128, TPT], F32)
        t.alns = sb("alns", [128, TPT], F32)
        t.ssum = sb("ssum", [128, 2, TPT], F32)
        t.ssq = sb("ssq", [128, 2, TPT], F32)
        t.smn = sb("smn", [128, 2, TPT], F32)
        t.svar = sb("svar", [128, 2, TPT], F32)
        t.sr = sb("sr", [128, 2, TPT], F32)
        t.smrn = sb("smrn", [128, 2, TPT], F32)
        t.stmp = sb("stmp", [128, 2, TPT], F32)
        s1.close()
        s2 = ExitStack()

        def sb(name, shape, dtype, _s=s2):
            return _s.enter_context(nc.sbuf_tensor("sb_" + name, shape, dtype))

        t.WT = sb("WT", [128, 6, 8, 512], BF16)
        t.whh_pp = sb("whh_pp", [128, 2, 8, 512], BF16)
        t.s0_f32 = sb("s0_f32", [64, 1024], F32)
        t.bih_r = sb("bih_r", [1, 3072], BF16)
        t.bhh_r = sb("bhh_r", [1, 3072], BF16)
        t.bv_r = sb("bv_r", [1, 1024], BF16)
        t.bvp_r = sb("bvp_r", [1, 1024], BF16)
        t.wv_sb = sb("wv_sb", [128, 8, 1024], BF16)
        t.wvp_sb = sb("wvp_sb", [128, 8, 1024], BF16)
        t.si_bf = sb("si_bf", [64, 1024], BF16)
        t.si_f = sb("si_f", [128, 8, 64], BF16)
        t.gi_sb = sb("gi_sb", [64, 3072], F32)
        t.gh_n = sb("gh_n", [64, 1024], F32)
        t.t64a = sb("t64a", [64, 1024], F32)
        t.r_g = sb("r_g", [64, 1024], F32)
        t.z_g = sb("z_g", [64, 1024], F32)
        t.n_g = sb("n_g", [64, 1024], F32)
        t.snew = sb("snew", [64, 1024], F32)
        t.snew_bf = sb("snew_bf", [64, 1024], BF16)
        t.snew_f = sb("snew_f", [128, 8, 64], BF16)
        t.dent = sb("dent", [64, 1], F32)
        t.q_bf = sb("q_bf", [128, 2, 64], BF16)
        t.k_bf = sb("k_bf", [128, 2, 64], BF16)
        t.sc_sb = sb("sc_sb", [64, 64], F32)
        t.scm = sb("scm", [64, 1], F32)
        t.scs = sb("scs", [64, 1], F32)
        t.attn_bf = sb("attn_bf", [64, 64], BF16)
        t.attnT_bf = sb("attnT_bf", [64, 64], BF16)
        t.v_bf = sb("v_bf", [64, 1024], BF16)
        t.y_sb = sb("y_sb", [64, 1024], F32)
        t.ysum = sb("ysum", [64, 1], F32)
        t.yssq = sb("yssq", [64, 1], F32)
        t.ymu = sb("ymu", [64, 1], F32)
        t.yvar = sb("yvar", [64, 1], F32)
        t.ytmp = sb("ytmp", [64, 1], F32)
        t.yrst = sb("yrst", [64, 1], F32)
        t.y0_bf = sb("y0_bf", [64, 1024], BF16)
        t.yln_f = sb("yln_f", [128, 8, 64], BF16)
        s2.close()
        s3 = ExitStack()

        def sb(name, shape, dtype, _s=s3):
            return _s.enter_context(nc.sbuf_tensor("sb_" + name, shape, dtype))

        t.wo2_sb = sb("wo2_sb", [128, 8, 1024], BF16)
        t.z_f = sb("z_f", [128, 2, 8, Tp], BF16)
        t.prd = sb("prd", [128, 2, TPT, 1024], F32)
        s3.close()

        with ExitStack() as pctx:
            e2 = pctx.enter_context
            t.psA = e2(nc.psum_tensor("psA", [128, 1024], F32))
            t.psB = e2(nc.psum_tensor("psB", [128, 1024], F32))
            t.psC = e2(nc.psum_tensor("psC", [128, 1024], F32))
            t.psT0 = e2(nc.psum_tensor("psT0", [128, 1024], BF16))
            t.psT1 = e2(nc.psum_tensor("psT1", [128, 1024], BF16))
            sems = {k: e2(nc.semaphore(k)) for k in SEMS}

            P = Plan()
            P.begin("dry")
            build_walk(cfg, t, P)

            with nc.Block() as block:
                @block.sync
                def _(eng):
                    P.begin("sy", eng, sems)
                    build_walk(cfg, t, P)

                @block.gpsimd
                def _(eng):
                    P.begin("gp", eng, sems)
                    build_walk(cfg, t, P)

                @block.tensor
                def _(eng):
                    P.begin("pe", eng, sems)
                    build_walk(cfg, t, P)

                @block.vector
                def _(eng):
                    P.begin("ve", eng, sems)
                    build_walk(cfg, t, P)

                @block.scalar
                def _(eng):
                    P.begin("se", eng, sems)
                    build_walk(cfg, t, P)
    return nc


def make_in_maps(inputs, cfg, n_cores=8):
    T = cfg["T"]
    x = np.ascontiguousarray(np.asarray(inputs["L0_attn_x"], dtype=np.float32))
    ident = np.eye(128, dtype=ml_dtypes.bfloat16)
    ones_r = np.ones((1, 128), dtype=ml_dtypes.bfloat16)
    ones_c = np.ones((128, 1), dtype=ml_dtypes.bfloat16)
    tau_b = np.ascontiguousarray(
        np.broadcast_to(np.asarray(inputs["L0_ffn_tau"], np.float32).reshape(1, 1),
                        (128, 1)))
    names = {
        "bo": "L0_attn_bo", "slot_init": "L0_ffn_slot_init",
        "w1": "L0_ffn_w1", "b2": "L0_ffn_b2",
        "bih": "L0_ffn_bih", "bhh": "L0_ffn_bhh", "bv": "L0_ffn_bv",
        "bvp": "L0_ffn_bvp", "bof": "L0_ffn_bo",
    }
    common = {k: np.ascontiguousarray(np.asarray(inputs[v], dtype=np.float32))
              for k, v in names.items()}
    common.update({"tau_b": tau_b, "ident": ident, "ones_r": ones_r,
                   "ones_c": ones_c})

    def f32(v):
        return np.asarray(inputs[v], np.float32)

    def pk(a, k):
        return np.ascontiguousarray(
            a.reshape(k, 128, -1).transpose(1, 0, 2).reshape(128, -1))

    def cpk(a, k):
        return np.ascontiguousarray(a.reshape(k, 128).T)

    def jtpk(a):
        return np.ascontiguousarray(
            a.reshape(8, 128, 6, 512).transpose(1, 2, 0, 3).reshape(128, -1))

    bp = f32("L0_attn_bp")
    common.update({
        "Wp_pk": pk(f32("L0_attn_Wp")[:, 0:2048], 8),
        "bp0_pk": cpk(bp[0:1024], 8), "bpk_pk": cpk(bp[1024:2048], 8),
        "Wu_pk": pk(f32("L0_attn_Wu"), 8), "bu_pk": cpk(f32("L0_attn_bu"), 8),
        "Wo_pk": pk(f32("L0_attn_Wo"), 8),
        "w1t_pk": pk(f32("L0_ffn_w1")[0:1024], 8),
        "b1_pk": cpk(f32("L0_ffn_b1"), 4),
        "w2_pk": pk(f32("L0_ffn_w2"), 4),
        "Wih_pk": jtpk(np.ascontiguousarray(f32("L0_ffn_Wih").T)),
        "Whh_pk": jtpk(np.ascontiguousarray(f32("L0_ffn_Whh").T)),
        "wq_pk": pk(f32("L0_ffn_wq"), 8), "bq_pk": cpk(f32("L0_ffn_bq"), 2),
        "wk_pk": pk(f32("L0_ffn_wk"), 8), "bk_pk": cpk(f32("L0_ffn_bk"), 2),
        "wv_pk": pk(f32("L0_ffn_wv"), 8),
        "lng_pk": cpk(f32("L0_ffn_ln_g"), 8), "lnb_pk": cpk(f32("L0_ffn_ln_b"), 8),
        "wvp_pk": pk(f32("L0_ffn_wvp"), 8),
        "wo2_pk": pk(f32("L0_ffn_wo"), 8),
    })
    in_maps = []
    for c in range(n_cores):
        m = dict(common)
        m["x"] = np.ascontiguousarray(x[c, :T, :])
        in_maps.append(m)
    return in_maps


CFG = {"T": 8192, "Tp": 512}


def kernel(**inputs):
    cfg = CFG
    nc = build_nc(cfg)
    in_maps = make_in_maps(inputs, cfg, n_cores=8)
    res = run_bass_kernel_spmd(nc, in_maps, core_ids=list(range(8)))
    out = np.stack([r["out"] for r in res.results], axis=0)
    return out.astype(np.float32)


# version 32
# speedup vs baseline: 1.0274x; 1.0058x over previous
import numpy as np
import ml_dtypes
from contextlib import ExitStack

import concourse.bass as bass
import concourse.mybir as mybir
from concourse.bass_utils import run_bass_kernel_spmd
from concourse.hw_specs import get_activation_tables


def _act_set_idx(nc, funcs):
    for idx, fs in enumerate(get_activation_tables(nc.m.arch).values()):
        if all(f in fs for f in funcs):
            return idx
    raise ValueError(f"no activation table set contains {funcs}")


def _load_tab(eng, idx):
    return eng.add_instruction(mybir.InstLoadActFuncSet(
        name=eng.bass.get_next_instruction_name(),
        act_func_set_id=idx, ins=[], outs=[]))

F32 = mybir.dt.float32
BF16 = mybir.dt.bfloat16
AF = mybir.ActivationFunctionType
ALU = mybir.AluOpType
AX = mybir.AxisListType

DMA_POOLS = {"sx": ("sy", 6), "sp": ("sy", 3), "so": ("sy", 3), "sg": ("gp", 3)}
SEMS = {"spe": "pe", "sve": "ve", "sse": "se"}
for _p, (_e, _k) in DMA_POOLS.items():
    for _i in range(_k):
        SEMS[f"{_p}{_i}"] = _e


class Plan:

    def __init__(self):
        self.events = {}
        self.dry = True
        self.mode = None
        self.eng = None
        self.sems = None
        self.counts = {k: 0 for k in SEMS}
        self.pool_n = {p: 0 for p in DMA_POOLS}

    def begin(self, mode, eng=None, sems=None):
        self.dry = (mode == "dry")
        self.mode = mode
        self.eng = eng
        self.sems = sems
        self.counts = {k: 0 for k in SEMS}
        self.pool_n = {p: 0 for p in DMA_POOLS}

    def op(self, engine, fn, inc=None, inc_n=1, event=None):
        inst = None
        if self.mode == engine:
            inst = fn(self.eng)
        if inc is not None:
            assert SEMS[inc] == engine, f"sem {inc} not produced by {engine}"
            self.counts[inc] += inc_n
            if inst is not None:
                inst.then_inc(self.sems[inc], inc_n)
            if event is not None:
                val = (inc, self.counts[inc], dict(self.counts))
                if self.dry:
                    assert event not in self.events, f"dup event {event}"
                    self.events[event] = val
                else:
                    assert self.events.get(event)[:2] == val[:2], \
                        f"event drift {event}: {self.events.get(event)[:2]} vs {val[:2]}"
        else:
            assert event is None
        return inst

    def dma(self, engine, fn, pool, event=None):
        eng_of, k = DMA_POOLS[pool]
        assert eng_of == engine
        i = self.pool_n[pool]
        self.pool_n[pool] = i + 1
        sem = f"{pool}{i % k}"
        if i >= k and self.mode == engine:
            self.eng.wait_ge(self.sems[sem], 16 * (i // k))
        return self.op(engine, fn, inc=sem, inc_n=16, event=event)

    def wait(self, engine, event):
        if self.mode == engine:
            sem_name, val, _ = self.events[event]
            self.eng.wait_ge(self.sems[sem_name], val)

    def wait_all_dma(self, engine, event, pool):
        if self.mode == engine:
            _, _, snap = self.events[event]
            _, k = DMA_POOLS[pool]
            for i in range(k):
                sem = f"{pool}{i}"
                if snap.get(sem, 0) > 0:
                    self.eng.wait_ge(self.sems[sem], snap[sem])

    def drain(self, engine):
        if self.mode == engine:
            self.eng.drain()


def build_walk(cfg, t, P):
    TAB_LNEXP = cfg["tab_lnexp"]
    TAB_SIG = cfg["tab_sig"]
    TAB_GELU = cfg["tab_gelu"]
    T = cfg["T"]; Tp = cfg["Tp"]
    NPH = T // Tp
    TPT = Tp // 128
    NHF = TPT // 2
    ND = 8
    NH = 4
    SQ2I = float(1.0 / np.sqrt(2.0))
    DQS = float(256 ** -0.5)

    psH = {
        "A0": t.psA[:, 0:512], "A1": t.psA[:, 512:1024],
        "B0": t.psB[:, 0:512], "B1": t.psB[:, 512:1024],
        "C0": t.psC[:, 0:512], "C1": t.psC[:, 512:1024],
        "T0": t.psT0[:, :], "T1": t.psT1[:, :],
    }
    bank_free = {}

    def pe_claim(*regions):
        for r in regions:
            ev = bank_free.pop(r, None)
            if ev is not None:
                P.wait("pe", ev)

    def release(region, event):
        bank_free[region] = event

    def xslot(ph, h):
        return (2 * ph + h) % 4

    gload = [
        lambda g: g.dma_start(out=t.s0_bf[:], in_=t.i_slot[:]),
        lambda g: g.dma_start(out=t.b2_r[:], in_=t.i_b2[None, :]),
        lambda g: g.dma_start(out=t.w1t_sb[:],
            in_=t.i_w1t_pk.rearrange("p (k f) -> p k f", k=8)),
        lambda g: g.dma_start(out=t.w2_sb[:],
            in_=t.i_w2_pk.rearrange("p (k f) -> p k f", k=4)),
        lambda g: g.dma_start(out=t.Wp_sb[:],
            in_=t.i_Wp_pk.rearrange("p (k f) -> p k f", k=8)),
        lambda g: g.dma_start(out=t.Wu_sb[:],
            in_=t.i_Wu_pk.rearrange("p (k f) -> p k f", k=8)),
    ]
    for i, fn in enumerate(gload):
        P.dma("gp", fn, "sg", event=("wload",) if i == len(gload) - 1 else None)
    gload2 = [
        lambda g: g.dma_start(out=t.Wo_sb[:],
            in_=t.i_Wo_pk.rearrange("p (k f) -> p k f", k=8)),
        lambda g: g.dma_start(out=t.wq_sb[:],
            in_=t.i_wq_pk.rearrange("p (k f) -> p k f", k=8)),
        lambda g: g.dma_start(out=t.wk_sb[:],
            in_=t.i_wk_pk.rearrange("p (k f) -> p k f", k=8)),
    ]

    sload = [
        lambda s: s.dma_start(out=t.ident[:], in_=t.i_ident[:]),
        lambda s: s.dma_start(out=t.ones_r[:], in_=t.i_ones_r[:]),
        lambda s: s.dma_start(out=t.ones_c[:], in_=t.i_ones_c[:]),
        lambda s: s.dma_start(out=t.tau_b[:], in_=t.i_tau_b[:]),
        lambda s: s.dma_start(out=t.bpw[:], in_=t.i_bp0_pk[:]),
        lambda s: s.dma_start(out=t.bpk2[:], in_=t.i_bpk_pk[:]),
        lambda s: s.dma_start(out=t.bu_c[:], in_=t.i_bu_pk[:]),
        lambda s: s.dma_start(out=t.b1_c[:], in_=t.i_b1_pk[:]),
        lambda s: s.dma_start(out=t.bq_c[:], in_=t.i_bq_pk[:]),
        lambda s: s.dma_start(out=t.bk_c[:], in_=t.i_bk_pk[:]),
        lambda s: s.dma_start(out=t.lng_c[:], in_=t.i_lng_pk[:]),
        lambda s: s.dma_start(out=t.lnb_c[:], in_=t.i_lnb_pk[:]),
        lambda s: s.dma_start(out=t.bo1_r[:],
            in_=t.i_bo.rearrange("(a b) -> a b", a=8)),
        lambda s: s.dma_start(out=t.bo2_r[:],
            in_=t.i_bof.rearrange("(a b) -> a b", a=8)),
    ]
    for i, fn in enumerate(sload):
        P.dma("sy", fn, "sx", event=("cload",) if i == len(sload) - 1 else None)

    for ph in range(min(2, NPH)):
        for h in range(NHF):
            P.dma("sy", lambda s, ph=ph, h=h: s.dma_start(
                out=t.x_sl[:, xslot(ph, h), :, :],
                in_=t.i_x[ph * Tp + h * 256: ph * Tp + (h + 1) * 256, :]
                    .rearrange("(n p) d -> p n d", p=128)), "sx",
                event=("xload", ph, h))

    P.op("se", lambda s: _load_tab(s, TAB_SIG))
    P.wait_all_dma("se", ("cload",), "sx")
    P.op("se", lambda s: s.activation(t.invc[:], t.tau_b[:], AF.Abs),
         inc="sse", event=("tauabs",))
    P.wait("ve", ("tauabs",))
    P.wait_all_dma("ve", ("cload",), "sx")
    P.op("ve", lambda v: v.tensor_scalar(t.invc[:], t.invc[:], 0.1, None, ALU.add))
    P.drain("ve")
    P.op("ve", lambda v: v.reciprocal(t.invc[:], t.invc[:]))
    P.drain("ve")
    P.op("ve", lambda v: v.tensor_scalar(t.bpk2[:], t.bpk2[:], -2.0, None, ALU.mult),
         inc="sve", event=("bpk2",))
    P.op("ve", lambda v: v.tensor_add(t.bo1_r[:], t.bo1_r[:], t.bo2_r[:]),
         inc="sve", event=("bosum",))
    P.wait("gp", ("bosum",))
    P.dma("gp", lambda g: g.dma_start(
        out=t.bo_tot[:], in_=t.bo1_r[:]), "sg", event=("botot",))

    def stats_tile(ph, i):
        pp = ph % 2
        sl, ii = xslot(ph, i // 2), i % 2
        if i % 2 == 0:
            P.wait("se", ("xload", ph, i // 2))
        if i == 0 and ph >= 2:
            P.wait("se", ("var", ph - 2))
        P.drain("se")
        P.op("se", lambda s, sl=sl, ii=ii, i=i, pp=pp: s.activation(
            t.dump[:], t.x_sl[:, sl, ii, :], AF.Identity,
            accum_out=t.ssum[:, pp, i:i + 1]))
        P.drain("se")
        P.op("se", lambda s, sl=sl, ii=ii, i=i, pp=pp: s.activation(
            t.dump[:], t.x_sl[:, sl, ii, :], AF.Square,
            accum_out=t.ssq[:, pp, i:i + 1]),
            inc="sse" if i == TPT - 1 else None,
            event=("sums", ph) if i == TPT - 1 else None)

    def var_chain(ph):
        pp = ph % 2
        smn, svar, stmp = (t.smn[:, pp, :], t.svar[:, pp, :], t.stmp[:, pp, :])
        P.wait("ve", ("sums", ph))
        if ph >= 2:
            P.wait("ve", ("rstd", ph - 2))
        P.op("ve", lambda v, smn=smn, pp=pp: v.tensor_scalar(
            smn, t.ssum[:, pp, :], 1.0 / 1024.0, None, ALU.mult))
        P.op("ve", lambda v, svar=svar, pp=pp: v.tensor_scalar(
            svar, t.ssq[:, pp, :], 1.0 / 1024.0, 1e-5, ALU.mult, ALU.add))
        P.drain("ve")
        P.op("ve", lambda v, stmp=stmp, smn=smn: v.tensor_mul(stmp, smn, smn))
        P.drain("ve")
        P.op("ve", lambda v, svar=svar, stmp=stmp: v.tensor_sub(svar, svar, stmp),
             inc="sve", event=("var", ph))

    def rstd_window(ph):
        pp = ph % 2
        P.wait("se", ("var", ph))
        P.op("se", lambda s: _load_tab(s, TAB_LNEXP))
        P.op("se", lambda s, pp=pp: s.activation(
            t.stmp[:, pp, :], t.svar[:, pp, :], AF.Ln))
        P.drain("se")
        P.op("se", lambda s, pp=pp: s.activation(
            t.sr[:, pp, :], t.stmp[:, pp, :], AF.Exp, scale=-0.5),
            inc="sse", event=("rstd", ph))
        P.op("se", lambda s: _load_tab(s, TAB_SIG))

    def window(ph):
        if ph < 1:
            return
        P.wait("se", ("max8", ph - 1))
        if ph >= 2:
            P.wait("se", ("alpha", ph - 2))
        P.op("se", lambda s: _load_tab(s, TAB_LNEXP))
        for i in range(TPT):
            P.op("se", lambda s, i=i: s.activation(
                t.e_al[:, i, :], t.l_sb[:, i, :], AF.Exp,
                bias=t.m1n[:, i:i + 1]),
                inc="sse" if i == TPT - 1 else None,
                event=("exps", ph - 1) if i == TPT - 1 else None)

    def alpha_fin(ph):
        P.wait("ve", ("exps", ph))
        if ph >= 1:
            P.wait("ve", ("slotmm", ph - 1))
        for i in range(TPT):
            P.op("ve", lambda v, i=i: v.scalar_tensor_tensor(
                t.e_al[:, i, :], t.l_sb[:, i, :], t.m18[:, i, 7:8],
                t.e_al[:, i, :], ALU.is_ge, ALU.mult))
        P.drain("ve")
        for i in range(TPT):
            P.op("ve", lambda v, i=i: v.reduce_sum(
                t.alns[:, i:i + 1], t.e_al[:, i, :], axis=AX.X))
        P.drain("ve")
        P.op("ve", lambda v: v.reciprocal(t.alns[:], t.alns[:]))
        P.drain("ve")
        for i in range(TPT):
            P.op("ve", lambda v, i=i: v.tensor_scalar(
                t.al_bf[:, i, :], t.e_al[:, i, :], t.alns[:, i:i + 1], None,
                ALU.mult),
                inc="sve" if i == TPT - 1 else None,
                event=("alpha", ph) if i == TPT - 1 else None)

    def wo_block(ph):
        if ph == 0:
            P.wait_all_dma("pe", ("wload2",), "sg")
        regs = [("A0", "A1", t.psA), ("B0", "B1", t.psB), ("C0", "C1", t.psC),
                ("A0", "A1", t.psA)]

        def wo_tile_k(i, k, ps):
            for f in range(2):
                last = (f == 1 and k == ND - 1)
                P.op("pe", lambda e, i=i, f=f, k=k, ps=ps: e.matmul(
                    ps[:, f * 512:(f + 1) * 512],
                    t.ys_bf[:, k, i * 128:(i + 1) * 128],
                    t.Wo_sb[:, k, f * 512:(f + 1) * 512],
                    start=(k == 0), stop=(k == ND - 1)),
                    inc="spe" if last else None,
                    event=("womm", ph, i) if last else None)

        pe_claim(regs[0][0], regs[0][1])
        for k in range(ND - 1):
            P.wait("pe", ("ysd", ph, k))
            wo_tile_k(0, k, regs[0][2])
        pe_claim(regs[1][0], regs[1][1])
        for k in range(ND - 1):
            wo_tile_k(1, k, regs[1][2])
        P.wait("pe", ("ysd", ph, ND - 1))
        wo_tile_k(0, ND - 1, regs[0][2])
        wo_tile_k(1, ND - 1, regs[1][2])
        for i in range(TPT):
            r0, r1, ps = regs[i]
            if i >= 2:
                pe_claim(r0, r1)
                for k in range(ND):
                    wo_tile_k(i, k, ps)
            P.wait("ve", ("womm", ph, i))
            if i == 0:
                P.wait("ve", ("sums", ph))
            sl, ii = xslot(ph, i // 2), i % 2
            P.op("ve", lambda v, ps=ps, sl=sl, ii=ii: v.tensor_add(
                t.x_sl[:, sl, ii, :], ps[:, 0:1024], t.x_sl[:, sl, ii, :]),
                inc="sve", event=("part", ph, i))
            release(r0, ("part", ph, i))
            release(r1, ("part", ph, i))
        for h in range(NHF):
            P.wait("sy", ("part", ph, 2 * h + 1))
            P.dma("sy", lambda s, ph=ph, h=h: s.dma_start(
                out=t.o_out[ph * Tp + h * 256: ph * Tp + (h + 1) * 256, :]
                    .rearrange("(n p) d -> p n d", p=128),
                in_=t.x_sl[:, xslot(ph, h), :, :]), "sp",
                event=("pstore", ph, h))

    def slot_pe(ph):
        P.wait("pe", ("alpha", ph))
        pe_claim("B0", "B1", "C0", "T1")
        for i in range(TPT):
            P.op("pe", lambda e, i=i, pp=ph % 2: e.matmul(
                t.psB[0:64, 0:512], t.al_bf[:, i, :], t.xn_tm[:, pp, i, 0:512],
                start=(i == 0), stop=(i == TPT - 1)))
            P.op("pe", lambda e, i=i, pp=ph % 2: e.matmul(
                t.psB[0:64, 512:1024], t.al_bf[:, i, :],
                t.xn_tm[:, pp, i, 512:1024],
                start=(i == 0), stop=(i == TPT - 1)))
            P.op("pe", lambda e, i=i: e.matmul(
                t.psC[0:64, 0:1], t.al_bf[:, i, :], t.ones_c[:, 0:1],
                start=(i == 0), stop=(i == TPT - 1)))
        for i in range(TPT):
            P.op("pe", lambda e, i=i: e.transpose(
                t.psT1[0:64, i * 128:(i + 1) * 128],
                t.al_bf[:, i, :], t.ident[:]),
                inc="spe" if i == TPT - 1 else None,
                event=("slotmm", ph) if i == TPT - 1 else None)

    def slot_ve(ph):
        P.wait("ve", ("slotmm", ph))
        if ph == 0:
            P.op("ve", lambda v: v.tensor_copy(t.sl_acc[:, 0:512],
                                               t.psB[0:64, 0:512]))
            P.op("ve", lambda v: v.tensor_copy(t.sl_acc[:, 512:1024],
                                               t.psB[0:64, 512:1024]))
            P.op("ve", lambda v: v.tensor_copy(t.den_acc[:], t.psC[0:64, 0:1]))
        else:
            P.op("ve", lambda v: v.tensor_add(t.sl_acc[:, 0:512],
                                              t.sl_acc[:, 0:512], t.psB[0:64, 0:512]))
            P.op("ve", lambda v: v.tensor_add(t.sl_acc[:, 512:1024],
                                              t.sl_acc[:, 512:1024],
                                              t.psB[0:64, 512:1024]))
            P.op("ve", lambda v: v.tensor_add(t.den_acc[:], t.den_acc[:],
                                              t.psC[0:64, 0:1]))
        P.op("ve", lambda v, ph=ph: v.tensor_copy(
            t.alpha_T[0:64, ph * Tp:(ph + 1) * Tp], t.psT1[0:64, 0:Tp]),
            inc="sve", event=("slotev", ph))
        for r in ("B0", "B1", "C0", "T1"):
            release(r, ("slotev", ph))

    def ln_apply(ph):
        pp = ph % 2
        P.wait("ve", ("rstd", ph))
        if ph >= 2:
            P.wait("ve", ("slotmm", ph - 2))
        P.op("ve", lambda v, pp=pp: v.tensor_mul(
            t.smrn[:, pp, :], t.smn[:, pp, :], t.sr[:, pp, :]))
        P.drain("ve")
        P.op("ve", lambda v, pp=pp: v.tensor_scalar(
            t.smrn[:, pp, :], t.smrn[:, pp, :], -1.0, None, ALU.mult))
        P.drain("ve")
        for i in range(TPT):
            sl, ii = xslot(ph, i // 2), i % 2
            P.op("ve", lambda v, i=i, sl=sl, ii=ii, pp=pp: v.tensor_scalar(
                t.xn_tm[:, pp, i, :], t.x_sl[:, sl, ii, :],
                t.sr[:, pp, i:i + 1], t.smrn[:, pp, i:i + 1],
                ALU.mult, ALU.add),
                inc="sve" if i == TPT - 1 else None,
                event=("ln", ph) if i == TPT - 1 else None)

    def xn_transposes(ph):
        P.wait("pe", ("ln", ph))
        if ph >= 1:
            pass
        for k in range(ND):
            reg = "T0" if k % 2 == 0 else "T1"
            psT = psH[reg]
            pe_claim(reg)
            for i in range(TPT):
                P.op("pe", lambda e, k=k, i=i, psT=psT, pp=ph % 2: e.transpose(
                    psT[:, i * 128:(i + 1) * 128],
                    t.xn_tm[:, pp, i, k * 128:(k + 1) * 128], t.ident[:]),
                    inc="spe" if i == TPT - 1 else None,
                    event=("xnT", ph, k) if i == TPT - 1 else None)
            P.wait("ve", ("xnT", ph, k))
            if ph >= 1 and k < 2:
                P.wait("ve", ("wku", ph - 1, ND - 1))
                P.wait("ve", ("hmm", ph - 1, NH - 1))
            P.op("ve", lambda v, k=k, psT=psT: v.tensor_copy(
                t.xn_f[:, k, :], psT[:, 0:Tp]), inc="sve", event=("xnf", ph, k))
            release(reg, ("xnf", ph, k))

    def wku_loop(ph):
        P.wait("pe", ("xnf", ph, ND - 1))
        for j in range(ND):
            par = j % 2
            if par == 0:
                rw, rk, ru = "A0", "A1", "B0"
            else:
                rw, rk, ru = "B1", "C0", "C1"
            psw, psk, psu = (psH[rw][:, 0:Tp], psH[rk][:, 0:Tp],
                            psH[ru][:, 0:Tp])
            pe_claim(rw, rk, ru)
            for k in range(ND):
                P.op("pe", lambda e, j=j, k=k, psw=psw: e.matmul(
                    psw, t.Wp_sb[:, k, j * 128:(j + 1) * 128], t.xn_f[:, k, :],
                    start=(k == 0), stop=(k == ND - 1)))
            for k in range(ND):
                P.op("pe", lambda e, j=j, k=k, psk=psk: e.matmul(
                    psk, t.Wp_sb[:, k, 1024 + j * 128:1024 + (j + 1) * 128],
                    t.xn_f[:, k, :], start=(k == 0), stop=(k == ND - 1)))
            for k in range(ND):
                last = (k == ND - 1)
                P.op("pe", lambda e, j=j, k=k, psu=psu: e.matmul(
                    psu, t.Wu_sb[:, k, j * 128:(j + 1) * 128], t.xn_f[:, k, :],
                    start=(k == 0), stop=(k == ND - 1)),
                    inc="spe" if last else None,
                    event=("wku", ph, j) if last else None)

            if j == 0:
                P.op("se", lambda s: _load_tab(s, TAB_SIG))
            P.wait("se", ("wku", ph, j))
            if j >= 2:
                P.wait("se", ("absc", ph, j - 2))
            elif ph >= 1:
                P.wait("se", ("absc", ph - 1, j + 6))
            P.drain("se")
            P.op("se", lambda s, j=j, psw=psw: s.activation(
                t.g_sb[:, j % 2, :], psw, AF.Sigmoid, bias=t.bpw[:, j:j + 1]))
            P.op("se", lambda s, j=j, psk=psk: s.activation(
                t.s2_sb[:, j % 2, :], psk, AF.Sigmoid, bias=t.bpk2[:, j:j + 1],
                scale=-2.0), inc="sse", event=("gs", ph, j))
            release(rw, ("gs", ph, j))
            release(rk, ("gs", ph, j))
            if ph + 1 < NPH and j < TPT:
                stats_tile(ph + 1, j)
            if ph + 1 < NPH and j == TPT:
                rstd_window(ph + 1)

            P.wait("ve", ("gs", ph, j))
            if ph >= 1 and j == 0:
                P.wait("ve", ("womm", ph - 1, TPT - 1))
            pp = j % 2
            P.drain("ve")
            P.op("ve", lambda v, pp=pp: v.scalar_tensor_tensor(
                t.a_sb[:], t.g_sb[:, pp, :], -2.0, t.s2_sb[:, pp, :],
                ALU.mult, ALU.mult))
            P.op("ve", lambda v, pp=pp, psu=psu, j=j: v.scalar_tensor_tensor(
                t.b_sb[:], psu, t.bu_c[:, j:j + 1], t.g_sb[:, pp, :],
                ALU.add, ALU.mult), inc="sve", event=("absc", ph, j))
            release(ru, ("absc", ph, j))
            P.drain("ve")
            P.op("ve", lambda v, pp=pp: v.tensor_scalar(
                t.a_sb[:], t.a_sb[:], 1.0, None, ALU.add))
            P.drain("ve")
            init = 0.0 if ph == 0 else t.ys_bf[:, j, Tp - 1:Tp]
            P.op("ve", lambda v, j=j, init=init: v.tensor_tensor_scan(
                t.ys_bf[:, j, :], t.a_sb[:], t.b_sb[:], init,
                ALU.mult, ALU.add),
                inc="sve", event=("ysd", ph, j))
            if ph + 1 < NPH and j == TPT - 1:
                var_chain(ph + 1)
            if ph + 1 < NPH and j == ND - 1:
                ln_apply(ph + 1)

    def h_logits(ph):
        P.wait("pe", ("xnf", ph, ND - 1))
        hr = ["A0", "A1", "B0", "B1"]
        for jh in range(NH):
            reg = hr[jh]
            ps = psH[reg][:, 0:Tp]
            pe_claim(reg)
            for k in range(ND):
                last = (k == ND - 1)
                P.op("pe", lambda e, jh=jh, k=k, ps=ps: e.matmul(
                    ps, t.w1t_sb[:, k, jh * 128:(jh + 1) * 128], t.xn_f[:, k, :],
                    start=(k == 0), stop=(k == ND - 1)),
                    inc="spe" if last else None,
                    event=("hmm", ph, jh) if last else None)
            P.wait("se", ("hmm", ph, jh))
            if jh == 0:
                if ph >= 1:
                    P.wait("se", ("lmm", ph - 1))
                P.op("se", lambda s: _load_tab(s, TAB_GELU))
            P.drain("se")
            P.op("se", lambda s, jh=jh, ps=ps: s.activation(
                t.h_f[:, jh, :], ps, AF.Gelu, bias=t.bias_h[:, jh:jh + 1]),
                inc="sse", event=("hf", ph, jh))
            release(reg, ("hf", ph, jh))
        P.wait("pe", ("hf", ph, NH - 1))
        pe_claim("C0")
        for i in range(TPT):
            for kh in range(NH):
                P.op("pe", lambda e, i=i, kh=kh: e.matmul(
                    t.psC[:, i * 64:(i + 1) * 64],
                    t.h_f[:, kh, i * 128:(i + 1) * 128], t.w2_sb[:, kh, :],
                    start=(kh == 0), stop=False))
            last = (i == TPT - 1)
            P.op("pe", lambda e, i=i: e.matmul(
                t.psC[:, i * 64:(i + 1) * 64], t.ones_r[:], t.b2_r[:],
                start=False, stop=True),
                inc="spe" if last else None,
                event=("lmm", ph) if last else None)
        P.wait("se", ("lmm", ph))
        if ph >= 1:
            P.wait("se", ("alpha", ph - 1))
        P.drain("se")
        P.op("se", lambda s: s.activation(
            t.l_sb[:], t.psC[:, 0:TPT * 64], AF.Identity, scale=t.invc[:]),
            inc="sse", event=("lev", ph))
        release("C0", ("lev", ph))
        P.wait("ve", ("lev", ph))
        for i in range(TPT):
            P.op("ve", lambda v, i=i: v.max(out=t.m18[:, i, :],
                                            in_=t.l_sb[:, i, :]))
        P.drain("ve")
        P.op("ve", lambda v: v.tensor_scalar(
            t.m1n[:], t.m18[:, :, 0], -1.0, None, ALU.mult),
            inc="sve", event=("max8", ph))

    def mid_weight_loads():
        def ihl(jt, h):
            P.dma("gp", lambda g, jt=jt, h=h: g.dma_start(
                out=t.WT[:, jt, h * 4:(h + 1) * 4],
                in_=t.i_Wih_pk.rearrange("p (jt k f) -> p jt k f", jt=6, k=8)
                    [:, jt, h * 4:(h + 1) * 4]), "sg", event=("ihl", jt, h))

        def whl(jt, h):
            P.dma("gp", lambda g, jt=jt, h=h: g.dma_start(
                out=t.whh_pp[:, jt % 2, h * 4:(h + 1) * 4],
                in_=t.i_Whh_pk.rearrange("p (jt k f) -> p jt k f", jt=6, k=8)
                    [:, jt, h * 4:(h + 1) * 4]), "sg", event=("whl", jt, h))

        P.wait("gp", ("wku", NPH - 1, ND - 1))
        P.wait_all_dma("gp", ("pstore", NPH - 1, NHF - 1), "sp")
        P.wait("gp", ("slotev", NPH - 1))
        P.wait("gp", ("exps", NPH - 1))
        P.wait("gp", ("slotmm", NPH - 1))
        P.dma("gp", lambda g: g.dma_start(out=t.s0_f32[:], in_=t.i_slot[:]), "sg")
        P.dma("gp", lambda g: g.dma_start(out=t.bih_r[:], in_=t.i_bih[None, :]), "sg")
        P.dma("gp", lambda g: g.dma_start(out=t.bhh_r[:], in_=t.i_bhh[None, :]), "sg")
        P.dma("gp", lambda g: g.dma_start(out=t.bv_r[:], in_=t.i_bv[None, :]), "sg")
        P.dma("gp", lambda g: g.dma_start(out=t.bvp_r[:], in_=t.i_bvp[None, :]), "sg",
              event=("brows",))
        P.wait("gp", ("womm", NPH - 1, TPT - 1))
        for jt in range(2):
            ihl(jt, 0)
            whl(jt, 0)
            ihl(jt, 1)
            whl(jt, 1)
        for jt in range(2, 6):
            ihl(jt, 0)
            ihl(jt, 1)

    for i in range(TPT):
        stats_tile(0, i)
    var_chain(0)
    rstd_window(0)
    ln_apply(0)

    P.wait_all_dma("pe", ("cload",), "sx")
    P.wait_all_dma("pe", ("wload",), "sg")
    pe_claim("T0")
    for k in range(ND):
        P.op("pe", lambda e, k=k: e.transpose(
            t.psT0[:, k * 64:(k + 1) * 64],
            t.s0_bf[0:64, k * 128:(k + 1) * 128], t.ident[0:64, 0:64]),
            inc="spe" if k == ND - 1 else None,
            event=("s0T",) if k == ND - 1 else None)
    P.wait("ve", ("s0T",))
    for k in range(ND):
        P.op("ve", lambda v, k=k: v.tensor_copy(
            t.si0f[:, k, :], t.psT0[:, k * 64:(k + 1) * 64]),
            inc="sve" if k == ND - 1 else None,
            event=("si0f",) if k == ND - 1 else None)
    release("T0", ("si0f",))
    P.drain("ve")
    for k in range(ND):
        P.op("ve", lambda v, k=k: v.reduce_sum(
            t.sm_f[:, k:k + 1], t.si0f[:, k, :], axis=AX.X))
    P.drain("ve")
    P.op("ve", lambda v: v.tensor_scalar(t.sm_fb[:], t.sm_f[:], 1.0 / 64.0, None,
                                         ALU.mult), inc="sve", event=("smf",))
    regs4 = ["A0", "A1", "B0", "B1"]
    pe_claim(*regs4)
    ps4 = [psH[r] for r in regs4]
    for k in range(ND):
        if k >= 2:
            P.wait("gp", ("smwk", k - 2))
        P.dma("gp", lambda g, k=k: g.dma_start(
            out=t.w1s[:, k % 2, :],
            in_=t.i_w1[1024 + k * 128:1024 + (k + 1) * 128, :]), "sg",
            event=("w1s", k))
        P.wait("pe", ("w1s", k))
        if k == 0:
            P.wait("pe", ("smf",))
        for jh in range(NH):
            P.op("pe", lambda e, jh=jh, k=k: e.matmul(
                ps4[jh][:, 0:1], t.w1s[:, k % 2, jh * 128:(jh + 1) * 128],
                t.sm_fb[:, k:k + 1], start=(k == 0), stop=(k == ND - 1)),
                inc="spe" if jh == NH - 1 else None,
                event=("smwk", k) if jh == NH - 1 else None)
    P.wait("ve", ("smwk", ND - 1))
    for jh in range(NH):
        P.op("ve", lambda v, jh=jh: v.tensor_add(
            t.bias_h[:, jh:jh + 1], t.b1_c[:, jh:jh + 1], ps4[jh][:, 0:1]),
            inc="sve" if jh == NH - 1 else None,
            event=("biash",) if jh == NH - 1 else None)
    for r in regs4:
        release(r, ("biash",))
    for i, fn in enumerate(gload2):
        P.dma("gp", fn, "sg", event=("wload2",) if i == len(gload2) - 1 else None)

    for ph in range(NPH + 1):
        window(ph)
        if ph >= 1:
            alpha_fin(ph - 1)
        if ph >= 1:
            wo_block(ph - 1)
            if ph + 1 < NPH:
                for h in range(NHF):
                    P.wait("sy", ("pstore", ph - 1, h))
                    P.dma("sy", lambda s, ph=ph, h=h: s.dma_start(
                        out=t.x_sl[:, xslot(ph + 1, h), :, :],
                        in_=t.i_x[(ph + 1) * Tp + h * 256:
                                  (ph + 1) * Tp + (h + 1) * 256, :]
                            .rearrange("(n p) d -> p n d", p=128)), "sx",
                        event=("xload", ph + 1, h))
        if ph >= 1:
            slot_pe(ph - 1)
            slot_ve(ph - 1)
        if ph < NPH:
            xn_transposes(ph)
            h_logits(ph)
            wku_loop(ph)
        if ph == NPH:
            mid_weight_loads()

    P.wait("ve", ("slotev", NPH - 1))
    P.wait("ve", ("exps", NPH - 1))
    P.op("ve", lambda v: v.tensor_scalar(t.dent[:], t.den_acc[:], 1e-8, None,
                                         ALU.add))
    P.drain("ve")
    P.op("ve", lambda v: v.reciprocal(t.dent[:], t.dent[:]))
    P.drain("ve")
    P.op("ve", lambda v: v.tensor_scalar(t.si_bf[:], t.sl_acc[:], t.dent[:],
                                         None, ALU.mult), inc="sve",
         event=("si",))
    P.wait("pe", ("si",))
    pe_claim("T0")
    for k in range(ND):
        P.op("pe", lambda e, k=k: e.transpose(
            t.psT0[:, k * 64:(k + 1) * 64], t.si_bf[0:64, k * 128:(k + 1) * 128],
            t.ident[0:64, 0:64]),
            inc="spe" if k == ND - 1 else None,
            event=("siT",) if k == ND - 1 else None)
    P.wait("ve", ("siT",))
    for k in range(ND):
        P.op("ve", lambda v, k=k: v.tensor_copy(
            t.si_f[:, k, :], t.psT0[:, k * 64:(k + 1) * 64]),
            inc="sve" if k == ND - 1 else None,
            event=("sif",) if k == ND - 1 else None)
    release("T0", ("sif",))

    whhbuf = t.whh_pp
    P.wait("pe", ("sif",))
    P.wait_all_dma("pe", ("brows",), "sg")
    mregs = ["A0", "A1", "B0", "B1", "C0", "C1"]
    for jt in range(6):
        reg = mregs[jt]
        ps = psH[reg]
        pe_claim(reg)
        for k in range(ND):
            if k % 4 == 0:
                P.wait("pe", ("ihl", jt, k // 4))
            P.op("pe", lambda e, k=k, jt=jt, ps=ps: e.matmul(
                ps[0:64, :], t.si_f[:, k, :], t.WT[:, jt, k, :],
                start=(k == 0), stop=False))
        P.op("pe", lambda e, jt=jt, ps=ps: e.matmul(
            ps[0:64, :], t.ones_r[0:1, 0:64], t.bih_r[:, jt * 512:(jt + 1) * 512],
            start=False, stop=True), inc="spe", event=("gmm", 0, jt))
        P.wait("se", ("gmm", 0, jt))
        P.op("se", lambda s, jt=jt, ps=ps: s.activation(
            t.gi_sb[:, jt * 512:(jt + 1) * 512], ps[0:64, :], AF.Identity),
            inc="sse", event=("gev", 0, jt))
        release(reg, ("gev", 0, jt))
        reg = mregs[jt]
        ps = psH[reg]
        pe_claim(reg)
        for k in range(ND):
            if k % 4 == 0:
                P.wait("pe", ("whl", jt, k // 4))
            P.op("pe", lambda e, k=k, jt=jt, ps=ps: e.matmul(
                ps[0:64, :], t.si0f[:, k, :],
                t.whh_pp[:, jt % 2, k, :], start=(k == 0), stop=False))
        P.op("pe", lambda e, jt=jt, ps=ps: e.matmul(
            ps[0:64, :], t.ones_r[0:1, 0:64], t.bhh_r[:, jt * 512:(jt + 1) * 512],
            start=False, stop=True), inc="spe", event=("ghmm", jt))
        if jt + 2 < 6:
            P.wait("gp", ("ghmm", jt))
            for h in range(2):
                P.dma("gp", lambda g, jt=jt, h=h: g.dma_start(
                    out=t.whh_pp[:, jt % 2, h * 4:(h + 1) * 4],
                    in_=t.i_Whh_pk.rearrange("p (jt k f) -> p jt k f",
                                             jt=6, k=8)
                        [:, jt + 2, h * 4:(h + 1) * 4]), "sg",
                    event=("whl", jt + 2, h))
        if jt < 4:
            P.wait("ve", ("ghmm", jt))
            P.wait("ve", ("gev", 0, jt))
            P.op("ve", lambda v, jt=jt, ps=ps: v.tensor_add(
                t.gi_sb[:, jt * 512:(jt + 1) * 512],
                t.gi_sb[:, jt * 512:(jt + 1) * 512], ps[0:64, :]),
                inc="sve", event=("gev", 1, jt))
        else:
            P.wait("ve", ("ghmm", jt))
            P.op("ve", lambda v, jt=jt, ps=ps: v.tensor_copy(
                t.gh_n[:, (jt - 4) * 512:(jt - 3) * 512], ps[0:64, :]),
                inc="sve", event=("gev", 1, jt))
        release(reg, ("gev", 1, jt))
    P.dma("gp", lambda g: g.dma_start(
        out=t.wv_sb[:], in_=t.i_wv_pk.rearrange("p (k f) -> p k f", k=8)), "sg")
    P.dma("gp", lambda g: g.dma_start(
        out=t.wvp_sb[:], in_=t.i_wvp_pk.rearrange("p (k f) -> p k f", k=8)), "sg",
        event=("w2load",))
    P.wait("gp", ("svmm",))
    P.dma("gp", lambda g: g.dma_start(
        out=t.wo2_sb[:], in_=t.i_wo2_pk.rearrange("p (k f) -> p k f", k=8)), "sg",
        event=("w3load",))

    P.wait("se", ("gev", 1, 3))
    P.op("se", lambda s: _load_tab(s, TAB_SIG))
    P.op("se", lambda s: s.activation(t.r_g[:], t.gi_sb[:, 0:1024], AF.Sigmoid))
    P.op("se", lambda s: s.activation(t.z_g[:], t.gi_sb[:, 1024:2048],
                                      AF.Sigmoid), inc="sse", event=("rz",))
    P.wait("ve", ("rz",))
    P.drain("ve")
    P.op("ve", lambda v: v.tensor_mul(t.t64a[:], t.r_g[:], t.gh_n[:, 0:1024]))
    P.drain("ve")
    P.op("ve", lambda v: v.tensor_add(t.t64a[:], t.t64a[:],
                                      t.gi_sb[:, 2048:3072]),
         inc="sve", event=("n_in",))
    P.wait("se", ("n_in",))
    P.op("se", lambda s: s.activation(t.n_g[:], t.t64a[:], AF.Tanh),
         inc="sse", event=("n_g",))
    P.wait("ve", ("n_g",))
    P.op("ve", lambda v: v.tensor_sub(t.t64a[:], t.s0_f32[:], t.n_g[:]))
    P.drain("ve")
    P.op("ve", lambda v: v.tensor_mul(t.t64a[:], t.t64a[:], t.z_g[:]))
    P.drain("ve")
    P.op("ve", lambda v: v.tensor_add(t.snew[:], t.n_g[:], t.t64a[:]))
    P.drain("ve")
    P.op("ve", lambda v: v.tensor_copy(t.snew_bf[:], t.snew[:]), inc="sve",
         event=("snew",))
    P.wait("pe", ("snew",))
    pe_claim("T0")
    for k in range(ND):
        P.op("pe", lambda e, k=k: e.transpose(
            t.psT0[:, k * 64:(k + 1) * 64],
            t.snew_bf[0:64, k * 128:(k + 1) * 128], t.ident[0:64, 0:64]),
            inc="spe" if k == ND - 1 else None,
            event=("snT",) if k == ND - 1 else None)
    P.wait("ve", ("snT",))
    for k in range(ND):
        P.op("ve", lambda v, k=k: v.tensor_copy(
            t.snew_f[:, k, :], t.psT0[:, k * 64:(k + 1) * 64]),
            inc="sve" if k == ND - 1 else None,
            event=("snf",) if k == ND - 1 else None)
    release("T0", ("snf",))
    P.drain("ve")
    P.op("ve", lambda v: v.tensor_scalar(t.bq_c[:], t.bq_c[:], DQS, None,
                                         ALU.mult), inc="sve", event=("bqs",))

    P.wait("pe", ("snf",))
    P.wait("pe", ("bqs",))
    pe_claim("A0", "A1")
    for m in range(2):
        for dqc in range(2):
            for k in range(ND):
                last = (m == 1 and dqc == 1 and k == ND - 1)
                P.op("pe", lambda e, m=m, dqc=dqc, k=k: e.matmul(
                    (t.psA[:, dqc * 64:(dqc + 1) * 64] if m == 0
                     else t.psA[:, 512 + dqc * 64:512 + (dqc + 1) * 64]),
                    (t.wq_sb if m == 0 else t.wk_sb)[:, k, dqc * 128:(dqc + 1) * 128],
                    t.snew_f[:, k, :], start=(k == 0), stop=(k == ND - 1)),
                    inc="spe" if last else None,
                    event=("qkmm",) if last else None)
    P.wait("se", ("qkmm",))
    for dqc in range(2):
        P.op("se", lambda s, dqc=dqc: s.activation(
            t.q_bf[:, dqc, :], t.psA[:, dqc * 64:(dqc + 1) * 64], AF.Identity,
            bias=t.bq_c[:, dqc:dqc + 1], scale=DQS))
        P.op("se", lambda s, dqc=dqc: s.activation(
            t.k_bf[:, dqc, :], t.psA[:, 512 + dqc * 64:512 + (dqc + 1) * 64],
            AF.Identity, bias=t.bk_c[:, dqc:dqc + 1]),
            inc="sse" if dqc == 1 else None,
            event=("qk",) if dqc == 1 else None)
    release("A0", ("qk",))
    release("A1", ("qk",))

    P.wait("pe", ("qk",))
    pe_claim("B0")
    for dqc in range(2):
        P.op("pe", lambda e, dqc=dqc: e.matmul(
            t.psB[0:64, 0:64], t.q_bf[:, dqc, :], t.k_bf[:, dqc, :],
            start=(dqc == 0), stop=(dqc == 1)),
            inc="spe" if dqc == 1 else None,
            event=("scmm",) if dqc == 1 else None)
    P.wait("se", ("scmm",))
    P.op("se", lambda s: s.activation(t.sc_sb[:], t.psB[0:64, 0:64], AF.Identity),
         inc="sse", event=("scev",))
    release("B0", ("scev",))
    P.wait("ve", ("scev",))
    P.op("ve", lambda v: v.reduce_max(t.scm[:], t.sc_sb[:], axis=AX.X))
    P.drain("ve")
    P.op("ve", lambda v: v.tensor_scalar(t.scm[:], t.scm[:], -1.0, None,
                                         ALU.mult), inc="sve", event=("scmax",))
    P.wait("se", ("scmax",))
    P.op("se", lambda s: _load_tab(s, TAB_LNEXP))
    P.drain("se")
    P.op("se", lambda s: s.activation(t.sc_sb[:], t.sc_sb[:], AF.Exp,
                                      bias=t.scm[:], accum_out=t.scs[:]),
         inc="sse", event=("scexp",))
    P.wait("ve", ("scexp",))
    P.op("ve", lambda v: v.reciprocal(t.scs[:], t.scs[:]))
    P.drain("ve")
    P.op("ve", lambda v: v.tensor_scalar(t.attn_bf[:], t.sc_sb[:], t.scs[:],
                                         None, ALU.mult), inc="sve",
         event=("attn",))
    P.wait("pe", ("attn",))
    pe_claim("T0")
    P.op("pe", lambda e: e.transpose(t.psT0[0:64, 0:64], t.attn_bf[0:64, 0:64],
                                     t.ident[0:64, 0:64]),
         inc="spe", event=("atT",))
    P.wait("ve", ("atT",))
    P.op("ve", lambda v: v.tensor_copy(t.attnT_bf[:], t.psT0[0:64, 0:64]),
         inc="sve", event=("atTe",))
    release("T0", ("atTe",))

    P.wait_all_dma("pe", ("w2load",), "sg")
    pe_claim("B0", "B1")
    for jt in range(2):
        ps = psH["B0" if jt == 0 else "B1"]
        for k in range(ND):
            P.op("pe", lambda e, jt=jt, k=k, ps=ps: e.matmul(
                ps[0:64, :], t.snew_f[:, k, :],
                t.wv_sb[:, k, jt * 512:(jt + 1) * 512],
                start=(k == 0), stop=False))
        P.op("pe", lambda e, jt=jt, ps=ps: e.matmul(
            ps[0:64, :], t.ones_r[0:1, 0:64], t.bv_r[:, jt * 512:(jt + 1) * 512],
            start=False, stop=True),
            inc="spe" if jt == 1 else None,
            event=("vmm",) if jt == 1 else None)
    P.wait("se", ("vmm",))
    P.op("se", lambda s: s.activation(t.v_bf[:, 0:512], t.psB[0:64, 0:512],
                                      AF.Identity))
    P.op("se", lambda s: s.activation(t.v_bf[:, 512:1024],
                                      t.psB[0:64, 512:1024], AF.Identity),
         inc="sse", event=("vev",))
    release("B0", ("vev",))
    release("B1", ("vev",))

    P.wait("pe", ("atTe",))
    P.wait("pe", ("vev",))
    pe_claim("C0", "C1")
    for jt in range(2):
        ps = psH["C0" if jt == 0 else "C1"]
        P.op("pe", lambda e, jt=jt, ps=ps: e.matmul(
            ps[0:64, :], t.attnT_bf[:], t.v_bf[0:64, jt * 512:(jt + 1) * 512],
            start=True, stop=True),
            inc="spe" if jt == 1 else None,
            event=("cmm",) if jt == 1 else None)
    P.wait("ve", ("cmm",))
    P.op("ve", lambda v: v.tensor_add(t.y_sb[:, 0:512], t.psC[0:64, 0:512],
                                      t.snew[:, 0:512]))
    P.op("ve", lambda v: v.tensor_add(t.y_sb[:, 512:1024],
                                      t.psC[0:64, 512:1024],
                                      t.snew[:, 512:1024]),
         inc="sve", event=("y",))
    release("C0", ("y",))
    release("C1", ("y",))

    P.wait("se", ("y",))
    P.drain("se")
    P.op("se", lambda s: s.activation(t.dump[0:64, :], t.y_sb[:], AF.Identity,
                                      accum_out=t.ysum[:]))
    P.drain("se")
    P.op("se", lambda s: s.activation(t.dump[0:64, :], t.y_sb[:], AF.Square,
                                      accum_out=t.yssq[:]),
         inc="sse", event=("ysums",))
    P.wait("ve", ("ysums",))
    P.op("ve", lambda v: v.tensor_scalar(t.ymu[:], t.ysum[:], 1.0 / 1024.0,
                                         None, ALU.mult))
    P.op("ve", lambda v: v.tensor_scalar(t.yvar[:], t.yssq[:], 1.0 / 1024.0,
                                         1e-5, ALU.mult, ALU.add))
    P.drain("ve")
    P.op("ve", lambda v: v.tensor_mul(t.ytmp[:], t.ymu[:], t.ymu[:]))
    P.drain("ve")
    P.op("ve", lambda v: v.tensor_sub(t.yvar[:], t.yvar[:], t.ytmp[:]),
         inc="sve", event=("yvar",))
    P.wait("se", ("yvar",))
    P.op("se", lambda s: s.activation(t.ytmp[:], t.yvar[:], AF.Ln))
    P.drain("se")
    P.op("se", lambda s: s.activation(t.yrst[:], t.ytmp[:], AF.Exp, scale=-0.5),
         inc="sse", event=("yrst",))
    P.wait("ve", ("yrst",))
    P.op("ve", lambda v: v.tensor_mul(t.ymu[:], t.ymu[:], t.yrst[:]))
    P.drain("ve")
    P.op("ve", lambda v: v.tensor_scalar(t.ymu[:], t.ymu[:], -1.0, None,
                                         ALU.mult))
    P.drain("ve")
    P.op("ve", lambda v: v.tensor_scalar(t.y0_bf[:], t.y_sb[:], t.yrst[:],
                                         t.ymu[:], ALU.mult, ALU.add),
         inc="sve", event=("y0",))
    P.wait("pe", ("y0",))
    pe_claim("T0")
    for k in range(ND):
        P.op("pe", lambda e, k=k: e.transpose(
            t.psT0[:, k * 64:(k + 1) * 64],
            t.y0_bf[0:64, k * 128:(k + 1) * 128], t.ident[0:64, 0:64]),
            inc="spe" if k == ND - 1 else None,
            event=("yT",) if k == ND - 1 else None)
    P.wait("ve", ("yT",))
    for k in range(ND):
        P.op("ve", lambda v, k=k: v.tensor_scalar(
            t.yln_f[:, k, :], t.psT0[:, k * 64:(k + 1) * 64],
            t.lng_c[:, k:k + 1], t.lnb_c[:, k:k + 1], ALU.mult, ALU.add),
            inc="sve" if k == ND - 1 else None,
            event=("ylnf",) if k == ND - 1 else None)
    release("T0", ("ylnf",))

    P.wait("pe", ("ylnf",))
    pe_claim("A0", "A1")
    for jt in range(2):
        ps = psH["A0" if jt == 0 else "A1"]
        for k in range(ND):
            P.op("pe", lambda e, jt=jt, k=k, ps=ps: e.matmul(
                ps[0:64, :], t.yln_f[:, k, :],
                t.wvp_sb[:, k, jt * 512:(jt + 1) * 512],
                start=(k == 0), stop=False))
        P.op("pe", lambda e, jt=jt, ps=ps: e.matmul(
            ps[0:64, :], t.ones_r[0:1, 0:64],
            t.bvp_r[:, jt * 512:(jt + 1) * 512], start=False, stop=True),
            inc="spe" if jt == 1 else None,
            event=("svmm",) if jt == 1 else None)
    P.wait("se", ("svmm",))
    P.op("se", lambda s: s.activation(t.sv_bf[:, 0:512], t.psA[0:64, 0:512],
                                      AF.Identity))
    P.op("se", lambda s: s.activation(t.sv_bf[:, 512:1024],
                                      t.psA[0:64, 512:1024], AF.Identity),
         inc="sse", event=("sv",))
    release("A0", ("sv",))
    release("A1", ("sv",))

    zregs = ["C0", "C1", "B0", "B1", "A0", "A1", "C0", "C1"]

    def zmm_phase(q):
        if q == 0:
            P.wait("pe", ("sv",))
            P.wait("pe", ("botot",))
            P.wait_all_dma("pe", ("w3load",), "sg")
        for k in range(ND):
            reg = zregs[k]
            pe_claim(reg)
            ps = psH[reg][:, 0:Tp]
            P.op("pe", lambda e, k=k, q=q, ps=ps: e.matmul(
                ps, t.sv_bf[0:64, k * 128:(k + 1) * 128],
                t.alpha_T[0:64, q * Tp:(q + 1) * Tp],
                start=True, stop=True), inc="spe", event=("zmm", q, k))
            P.wait("se", ("zmm", q, k))
            if q >= 2 and k == 0:
                P.wait("se", ("omm", q - 2, TPT - 1))
                P.drain("se")
            P.op("se", lambda s, k=k, q=q, ps=ps: s.activation(
                t.z_f[:, q % 2, k, :], ps, AF.Identity),
                inc="sse", event=("zev", q, k))
            release(reg, ("zev", q, k))

    if not cfg.get("skip_pass2"):
        zmm_phase(0)
    for p2 in ([] if cfg.get("skip_pass2") else range(NPH)):
        pr = p2 % 2
        if p2 == 0:
            P.wait_all_dma("sy", ("pstore", NPH - 1, NHF - 1), "sp")
            P.wait("sy", ("sv",))
        if p2 >= 2:
            P.wait("sy", ("fadd", p2 - 2, TPT - 1))
            P.wait("sy", ("ostore", p2 - 2, NHF - 1))
        P.dma("sy", lambda s, p2=p2, pr=pr: s.dma_start(
            out=t.prd[:, pr, :, :],
            in_=t.o_out[p2 * Tp:(p2 + 1) * Tp, :]
                .rearrange("(n p) d -> p n d", p=128)), "sx",
            event=("pread", p2))
        if p2 + 1 < NPH:
            zmm_phase(p2 + 1)
        oregs = [("A0", "A1", t.psA), ("B0", "B1", t.psB), ("C0", "C1", t.psC),
                 ("A0", "A1", t.psA)]
        for i in range(TPT):
            r0, r1, ps = oregs[i]
            pe_claim(r0, r1)
            for k in range(ND):
                if i == 0:
                    P.wait("pe", ("zev", p2, k))
                for f in range(2):
                    P.op("pe", lambda e, i=i, f=f, k=k, ps=ps, p2=p2: e.matmul(
                        ps[:, f * 512:(f + 1) * 512],
                        t.z_f[:, p2 % 2, k, i * 128:(i + 1) * 128],
                        t.wo2_sb[:, k, f * 512:(f + 1) * 512],
                        start=(k == 0), stop=False))
            for f in range(2):
                last = (f == 1)
                P.op("pe", lambda e, i=i, f=f, ps=ps: e.matmul(
                    ps[:, f * 512:(f + 1) * 512], t.ones_r[:],
                    t.bo_tot[:, f * 512:(f + 1) * 512], start=False, stop=True),
                    inc="spe" if last else None,
                    event=("omm", p2, i) if last else None)
            P.wait("ve", ("omm", p2, i))
            if i == 0:
                P.wait("ve", ("pread", p2))
            P.op("ve", lambda v, ps=ps, pr=pr, i=i: v.tensor_add(
                t.prd[:, pr, i, :], ps[:, 0:1024], t.prd[:, pr, i, :]),
                inc="sve", event=("fadd", p2, i))
            release(r0, ("fadd", p2, i))
            release(r1, ("fadd", p2, i))
        for h in range(NHF):
            P.wait("sy", ("fadd", p2, 2 * h + 1))
            P.dma("sy", lambda s, p2=p2, h=h, pr=pr: s.dma_start(
                out=t.o_out[p2 * Tp + h * 256:p2 * Tp + (h + 1) * 256, :]
                    .rearrange("(n p) d -> p n d", p=128),
                in_=t.prd[:, pr, 2 * h:2 * h + 2, :]), "so",
                event=("ostore", p2, h))

    if cfg.get("taps"):
        P.wait("sy", ("sv",))
        P.wait("sy", ("slotev", NPH - 1))
        P.wait("sy", ("ysd", NPH - 1, ND - 1))
        P.wait("sy", ("biash",))
        P.dma("sy", lambda s: s.dma_start(out=t.o_alpha[:], in_=t.alpha_T[:]), "so")
        P.dma("sy", lambda s: s.dma_start(out=t.o_slacc[:], in_=t.sl_acc[:]), "so")
        P.dma("sy", lambda s: s.dma_start(out=t.o_den[:], in_=t.den_acc[:]), "so")
        P.dma("sy", lambda s: s.dma_start(out=t.o_sv[:], in_=t.sv_bf[:]), "so")
        P.dma("sy", lambda s: s.dma_start(out=t.o_carry[:], in_=t.carry[:]), "so")
        P.dma("sy", lambda s: s.dma_start(out=t.o_biash[:], in_=t.bias_h[:]), "so")
        P.dma("sy", lambda s: s.dma_start(out=t.o_snew[:], in_=t.snew[:]), "so")
        P.dma("sy", lambda s: s.dma_start(out=t.o_s0[:], in_=t.s0_f32[:]), "so")
        P.dma("sy", lambda s: s.dma_start(out=t.o_ghn[:], in_=t.gh_n[:]), "so")
        P.dma("sy", lambda s: s.dma_start(out=t.o_gi[:], in_=t.gi_sb[:]), "so")
        P.dma("sy", lambda s: s.dma_start(out=t.o_ssum[:], in_=t.ssum[:, (NPH - 1) % 2]), "so")
        P.dma("sy", lambda s: s.dma_start(out=t.o_ssq[:], in_=t.ssq[:, (NPH - 1) % 2]), "so")
        P.dma("sy", lambda s: s.dma_start(out=t.o_sr[:], in_=t.sr[:, (NPH - 1) % 2]), "so")
        P.dma("sy", lambda s: s.dma_start(out=t.o_xn[:], in_=t.xn_tm[:, (NPH - 1) % 2]), "so")
        P.dma("sy", lambda s: s.dma_start(out=t.o_xnf[:], in_=t.xn_f[:]), "so")
        P.dma("sy", lambda s: s.dma_start(out=t.o_ysbf[:], in_=t.ys_bf[:]), "so")
        P.dma("sy", lambda s: s.dma_start(out=t.o_lsb[:], in_=t.l_sb[:]), "so")

    if P.mode == "sy":
        _, k = DMA_POOLS["so"]
        n = P.pool_n["so"]
        for i in range(k):
            cnt = 16 * ((n - i - 1) // k + 1) if n > i else 0
            if cnt:
                P.eng.wait_ge(P.sems[f"so{i}"], cnt)


def build_nc(cfg):
    T = cfg["T"]; Tp = cfg["Tp"]
    TPT = Tp // 128
    nc = bass.Bass("TRN2", target_bir_lowering=False)
    cfg = dict(cfg)
    cfg["tab_lnexp"] = _act_set_idx(nc, {AF.Ln, AF.Exp, AF.Identity, AF.Square})
    cfg["tab_sig"] = _act_set_idx(nc, {AF.Sigmoid, AF.Tanh, AF.Identity,
                                       AF.Square, AF.Abs})
    cfg["tab_gelu"] = _act_set_idx(nc, {AF.Gelu, AF.Identity})

    class TB:
        pass
    t = TB()

    def din(name, shape, dtype=F32):
        return nc.dram_tensor(name, shape, dtype, kind="ExternalInput").ap()

    t.i_x = din("x", [T, 1024])
    t.i_Wp_pk = din("Wp_pk", [128, 8 * 2048])
    t.i_bp0_pk = din("bp0_pk", [128, 8])
    t.i_bpk_pk = din("bpk_pk", [128, 8])
    t.i_Wu_pk = din("Wu_pk", [128, 8 * 1024])
    t.i_bu_pk = din("bu_pk", [128, 8])
    t.i_Wo_pk = din("Wo_pk", [128, 8 * 1024])
    t.i_bo = din("bo", [1024])
    t.i_slot = din("slot_init", [64, 1024])
    t.i_w1 = din("w1", [2048, 512])
    t.i_w1t_pk = din("w1t_pk", [128, 8 * 512])
    t.i_b1_pk = din("b1_pk", [128, 4])
    t.i_w2_pk = din("w2_pk", [128, 4 * 64])
    t.i_b2 = din("b2", [64])
    t.i_tau_b = din("tau_b", [128, 1])
    t.i_Wih_pk = din("Wih_pk", [128, 6 * 8 * 512])
    t.i_Whh_pk = din("Whh_pk", [128, 6 * 8 * 512])
    t.i_bih = din("bih", [3072])
    t.i_bhh = din("bhh", [3072])
    t.i_wq_pk = din("wq_pk", [128, 8 * 256])
    t.i_bq_pk = din("bq_pk", [128, 2])
    t.i_wk_pk = din("wk_pk", [128, 8 * 256])
    t.i_bk_pk = din("bk_pk", [128, 2])
    t.i_wv_pk = din("wv_pk", [128, 8 * 1024])
    t.i_bv = din("bv", [1024])
    t.i_lng_pk = din("lng_pk", [128, 8])
    t.i_lnb_pk = din("lnb_pk", [128, 8])
    t.i_wvp_pk = din("wvp_pk", [128, 8 * 1024])
    t.i_bvp = din("bvp", [1024])
    t.i_wo2_pk = din("wo2_pk", [128, 8 * 1024])
    t.i_bof = din("bof", [1024])
    t.i_ident = din("ident", [128, 128], BF16)
    t.i_ones_r = din("ones_r", [1, 128], BF16)
    t.i_ones_c = din("ones_c", [128, 1], BF16)
    t.o_out = nc.dram_tensor("out", [T, 1024], F32, kind="ExternalOutput").ap()
    if cfg.get("taps"):
        t.o_alpha = nc.dram_tensor("o_alpha", [64, T], BF16, kind="ExternalOutput").ap()
        t.o_slacc = nc.dram_tensor("o_slacc", [64, 1024], F32, kind="ExternalOutput").ap()
        t.o_den = nc.dram_tensor("o_den", [64, 1], F32, kind="ExternalOutput").ap()
        t.o_sv = nc.dram_tensor("o_sv", [64, 1024], BF16, kind="ExternalOutput").ap()
        t.o_carry = nc.dram_tensor("o_carry", [128, 8], F32, kind="ExternalOutput").ap()
        t.o_biash = nc.dram_tensor("o_biash", [128, 4], F32, kind="ExternalOutput").ap()
        t.o_snew = nc.dram_tensor("o_snew", [64, 1024], F32, kind="ExternalOutput").ap()
        t.o_gi = nc.dram_tensor("o_gi", [64, 3072], F32, kind="ExternalOutput").ap()
        t.o_s0 = nc.dram_tensor("o_s0", [64, 1024], F32, kind="ExternalOutput").ap()
        t.o_ghn = nc.dram_tensor("o_ghn", [64, 1024], F32, kind="ExternalOutput").ap()
        t.o_ssum = nc.dram_tensor("o_ssum", [128, TPT], F32, kind="ExternalOutput").ap()
        t.o_ssq = nc.dram_tensor("o_ssq", [128, TPT], F32, kind="ExternalOutput").ap()
        t.o_sr = nc.dram_tensor("o_sr", [128, TPT], F32, kind="ExternalOutput").ap()
        t.o_xn = nc.dram_tensor("o_xn", [128, TPT, 1024], BF16, kind="ExternalOutput").ap()
        t.o_xnf = nc.dram_tensor("o_xnf", [128, 8, Tp], BF16, kind="ExternalOutput").ap()
        t.o_ysbf = nc.dram_tensor("o_ysbf", [128, 8, Tp], BF16, kind="ExternalOutput").ap()
        t.o_lsb = nc.dram_tensor("o_lsb", [128, TPT, 64], F32, kind="ExternalOutput").ap()

    with ExitStack() as ctx:
        def sb(name, shape, dtype):
            return ctx.enter_context(nc.sbuf_tensor("sb_" + name, shape, dtype))

        t.ident = sb("ident", [128, 128], BF16)
        t.ones_r = sb("ones_r", [1, 128], BF16)
        t.ones_c = sb("ones_c", [128, 1], BF16)
        t.tau_b = sb("tau_b", [128, 1], F32)
        t.invc = sb("invc", [128, 1], F32)
        t.alpha_T = sb("alpha_T", [64, T], BF16)
        t.sl_acc = sb("sl_acc", [64, 1024], F32)
        t.den_acc = sb("den_acc", [64, 1], F32)
        t.carry = sb("carry", [128, 8], F32)
        t.si0f = sb("si0f", [128, 8, 64], BF16)
        t.sm_f = sb("sm_f", [128, 8], F32)
        t.sm_fb = sb("sm_fb", [128, 8], BF16)
        t.bpw = sb("bpw", [128, 8], F32)
        t.bpk2 = sb("bpk2", [128, 8], F32)
        t.bu_c = sb("bu_c", [128, 8], F32)
        t.b1_c = sb("b1_c", [128, 4], F32)
        t.bias_h = sb("bias_h", [128, 4], F32)
        t.b2_r = sb("b2_r", [1, 64], BF16)
        t.bq_c = sb("bq_c", [128, 2], F32)
        t.bk_c = sb("bk_c", [128, 2], F32)
        t.lng_c = sb("lng_c", [128, 8], F32)
        t.lnb_c = sb("lnb_c", [128, 8], F32)
        t.bo_tot = sb("bo_tot", [1, 1024], BF16)
        t.wq_sb = sb("wq_sb", [128, 8, 256], BF16)
        t.wk_sb = sb("wk_sb", [128, 8, 256], BF16)
        t.w2_sb = sb("w2_sb", [128, 4, 64], BF16)
        t.sv_bf = sb("sv_bf", [64, 1024], BF16)
        t.dump = sb("dump", [128, 1024], BF16)
        s1 = ExitStack()

        def sb(name, shape, dtype, _s=s1):
            return _s.enter_context(nc.sbuf_tensor("sb_" + name, shape, dtype))

        t.s0_bf = sb("s0_bf", [64, 1024], BF16)
        t.bo1_r = sb("bo1_r", [8, 128], F32)
        t.bo2_r = sb("bo2_r", [8, 128], F32)
        t.Wp_sb = sb("Wp_sb", [128, 8, 2048], BF16)
        t.Wu_sb = sb("Wu_sb", [128, 8, 1024], BF16)
        t.Wo_sb = sb("Wo_sb", [128, 8, 1024], BF16)
        t.w1t_sb = sb("w1t_sb", [128, 8, 512], BF16)
        t.w1s = sb("w1s", [128, 2, 512], BF16)
        t.x_sl = sb("x_sl", [128, 4, 2, 1024], F32)
        t.xn_tm = sb("xn_tm", [128, 2, TPT, 1024], BF16)
        t.xn_f = sb("xn_f", [128, 8, Tp], BF16)
        t.g_sb = sb("g_sb", [128, 2, Tp], F32)
        t.s2_sb = sb("s2_sb", [128, 2, Tp], F32)
        t.a_sb = sb("a_sb", [128, Tp], F32)
        t.b_sb = sb("b_sb", [128, Tp], F32)
        t.ys_bf = sb("ys_bf", [128, 8, Tp], BF16)
        t.h_f = sb("h_f", [128, 4, Tp], BF16)
        t.l_sb = sb("l_sb", [128, TPT, 64], F32)
        t.e_al = sb("e_al", [128, TPT, 64], F32)
        t.al_bf = sb("al_bf", [128, TPT, 64], BF16)
        t.m18 = sb("m18", [128, TPT, 8], F32)
        t.m1n = sb("m1n", [128, TPT], F32)
        t.alns = sb("alns", [128, TPT], F32)
        t.ssum = sb("ssum", [128, 2, TPT], F32)
        t.ssq = sb("ssq", [128, 2, TPT], F32)
        t.smn = sb("smn", [128, 2, TPT], F32)
        t.svar = sb("svar", [128, 2, TPT], F32)
        t.sr = sb("sr", [128, 2, TPT], F32)
        t.smrn = sb("smrn", [128, 2, TPT], F32)
        t.stmp = sb("stmp", [128, 2, TPT], F32)
        s1.close()
        s2 = ExitStack()

        def sb(name, shape, dtype, _s=s2):
            return _s.enter_context(nc.sbuf_tensor("sb_" + name, shape, dtype))

        t.WT = sb("WT", [128, 6, 8, 512], BF16)
        t.whh_pp = sb("whh_pp", [128, 2, 8, 512], BF16)
        t.s0_f32 = sb("s0_f32", [64, 1024], F32)
        t.bih_r = sb("bih_r", [1, 3072], BF16)
        t.bhh_r = sb("bhh_r", [1, 3072], BF16)
        t.bv_r = sb("bv_r", [1, 1024], BF16)
        t.bvp_r = sb("bvp_r", [1, 1024], BF16)
        t.wv_sb = sb("wv_sb", [128, 8, 1024], BF16)
        t.wvp_sb = sb("wvp_sb", [128, 8, 1024], BF16)
        t.si_bf = sb("si_bf", [64, 1024], BF16)
        t.si_f = sb("si_f", [128, 8, 64], BF16)
        t.gi_sb = sb("gi_sb", [64, 3072], F32)
        t.gh_n = sb("gh_n", [64, 1024], F32)
        t.t64a = sb("t64a", [64, 1024], F32)
        t.r_g = sb("r_g", [64, 1024], F32)
        t.z_g = sb("z_g", [64, 1024], F32)
        t.n_g = sb("n_g", [64, 1024], F32)
        t.snew = sb("snew", [64, 1024], F32)
        t.snew_bf = sb("snew_bf", [64, 1024], BF16)
        t.snew_f = sb("snew_f", [128, 8, 64], BF16)
        t.dent = sb("dent", [64, 1], F32)
        t.q_bf = sb("q_bf", [128, 2, 64], BF16)
        t.k_bf = sb("k_bf", [128, 2, 64], BF16)
        t.sc_sb = sb("sc_sb", [64, 64], F32)
        t.scm = sb("scm", [64, 1], F32)
        t.scs = sb("scs", [64, 1], F32)
        t.attn_bf = sb("attn_bf", [64, 64], BF16)
        t.attnT_bf = sb("attnT_bf", [64, 64], BF16)
        t.v_bf = sb("v_bf", [64, 1024], BF16)
        t.y_sb = sb("y_sb", [64, 1024], F32)
        t.ysum = sb("ysum", [64, 1], F32)
        t.yssq = sb("yssq", [64, 1], F32)
        t.ymu = sb("ymu", [64, 1], F32)
        t.yvar = sb("yvar", [64, 1], F32)
        t.ytmp = sb("ytmp", [64, 1], F32)
        t.yrst = sb("yrst", [64, 1], F32)
        t.y0_bf = sb("y0_bf", [64, 1024], BF16)
        t.yln_f = sb("yln_f", [128, 8, 64], BF16)
        s2.close()
        s3 = ExitStack()

        def sb(name, shape, dtype, _s=s3):
            return _s.enter_context(nc.sbuf_tensor("sb_" + name, shape, dtype))

        t.wo2_sb = sb("wo2_sb", [128, 8, 1024], BF16)
        t.z_f = sb("z_f", [128, 2, 8, Tp], BF16)
        t.prd = sb("prd", [128, 2, TPT, 1024], F32)
        s3.close()

        with ExitStack() as pctx:
            e2 = pctx.enter_context
            t.psA = e2(nc.psum_tensor("psA", [128, 1024], F32))
            t.psB = e2(nc.psum_tensor("psB", [128, 1024], F32))
            t.psC = e2(nc.psum_tensor("psC", [128, 1024], F32))
            t.psT0 = e2(nc.psum_tensor("psT0", [128, 1024], BF16))
            t.psT1 = e2(nc.psum_tensor("psT1", [128, 1024], BF16))
            sems = {k: e2(nc.semaphore(k)) for k in SEMS}

            P = Plan()
            P.begin("dry")
            build_walk(cfg, t, P)

            with nc.Block() as block:
                @block.sync
                def _(eng):
                    P.begin("sy", eng, sems)
                    build_walk(cfg, t, P)

                @block.gpsimd
                def _(eng):
                    P.begin("gp", eng, sems)
                    build_walk(cfg, t, P)

                @block.tensor
                def _(eng):
                    P.begin("pe", eng, sems)
                    build_walk(cfg, t, P)

                @block.vector
                def _(eng):
                    P.begin("ve", eng, sems)
                    build_walk(cfg, t, P)

                @block.scalar
                def _(eng):
                    P.begin("se", eng, sems)
                    build_walk(cfg, t, P)
    return nc


def make_in_maps(inputs, cfg, n_cores=8):
    T = cfg["T"]
    x = np.ascontiguousarray(np.asarray(inputs["L0_attn_x"], dtype=np.float32))
    ident = np.eye(128, dtype=ml_dtypes.bfloat16)
    ones_r = np.ones((1, 128), dtype=ml_dtypes.bfloat16)
    ones_c = np.ones((128, 1), dtype=ml_dtypes.bfloat16)
    tau_b = np.ascontiguousarray(
        np.broadcast_to(np.asarray(inputs["L0_ffn_tau"], np.float32).reshape(1, 1),
                        (128, 1)))
    names = {
        "bo": "L0_attn_bo", "slot_init": "L0_ffn_slot_init",
        "w1": "L0_ffn_w1", "b2": "L0_ffn_b2",
        "bih": "L0_ffn_bih", "bhh": "L0_ffn_bhh", "bv": "L0_ffn_bv",
        "bvp": "L0_ffn_bvp", "bof": "L0_ffn_bo",
    }
    common = {k: np.ascontiguousarray(np.asarray(inputs[v], dtype=np.float32))
              for k, v in names.items()}
    common.update({"tau_b": tau_b, "ident": ident, "ones_r": ones_r,
                   "ones_c": ones_c})

    def f32(v):
        return np.asarray(inputs[v], np.float32)

    def pk(a, k):
        return np.ascontiguousarray(
            a.reshape(k, 128, -1).transpose(1, 0, 2).reshape(128, -1))

    def cpk(a, k):
        return np.ascontiguousarray(a.reshape(k, 128).T)

    def jtpk(a):
        return np.ascontiguousarray(
            a.reshape(8, 128, 6, 512).transpose(1, 2, 0, 3).reshape(128, -1))

    bp = f32("L0_attn_bp")
    common.update({
        "Wp_pk": pk(f32("L0_attn_Wp")[:, 0:2048], 8),
        "bp0_pk": cpk(bp[0:1024], 8), "bpk_pk": cpk(bp[1024:2048], 8),
        "Wu_pk": pk(f32("L0_attn_Wu"), 8), "bu_pk": cpk(f32("L0_attn_bu"), 8),
        "Wo_pk": pk(f32("L0_attn_Wo"), 8),
        "w1t_pk": pk(f32("L0_ffn_w1")[0:1024], 8),
        "b1_pk": cpk(f32("L0_ffn_b1"), 4),
        "w2_pk": pk(f32("L0_ffn_w2"), 4),
        "Wih_pk": jtpk(np.ascontiguousarray(f32("L0_ffn_Wih").T)),
        "Whh_pk": jtpk(np.ascontiguousarray(f32("L0_ffn_Whh").T)),
        "wq_pk": pk(f32("L0_ffn_wq"), 8), "bq_pk": cpk(f32("L0_ffn_bq"), 2),
        "wk_pk": pk(f32("L0_ffn_wk"), 8), "bk_pk": cpk(f32("L0_ffn_bk"), 2),
        "wv_pk": pk(f32("L0_ffn_wv"), 8),
        "lng_pk": cpk(f32("L0_ffn_ln_g"), 8), "lnb_pk": cpk(f32("L0_ffn_ln_b"), 8),
        "wvp_pk": pk(f32("L0_ffn_wvp"), 8),
        "wo2_pk": pk(f32("L0_ffn_wo"), 8),
    })
    in_maps = []
    for c in range(n_cores):
        m = dict(common)
        m["x"] = np.ascontiguousarray(x[c, :T, :])
        in_maps.append(m)
    return in_maps


CFG = {"T": 8192, "Tp": 512}


def kernel(**inputs):
    cfg = CFG
    nc = build_nc(cfg)
    in_maps = make_in_maps(inputs, cfg, n_cores=8)
    res = run_bass_kernel_spmd(nc, in_maps, core_ids=list(range(8)))
    out = np.stack([r["out"] for r in res.results], axis=0)
    return out.astype(np.float32)


# version 33
# speedup vs baseline: 1.0464x; 1.0185x over previous
import numpy as np
import ml_dtypes
from contextlib import ExitStack

import concourse.bass as bass
import concourse.mybir as mybir
from concourse.bass_utils import run_bass_kernel_spmd
from concourse.hw_specs import get_activation_tables


def _act_set_idx(nc, funcs):
    for idx, fs in enumerate(get_activation_tables(nc.m.arch).values()):
        if all(f in fs for f in funcs):
            return idx
    raise ValueError(f"no activation table set contains {funcs}")


def _load_tab(eng, idx):
    return eng.add_instruction(mybir.InstLoadActFuncSet(
        name=eng.bass.get_next_instruction_name(),
        act_func_set_id=idx, ins=[], outs=[]))

F32 = mybir.dt.float32
BF16 = mybir.dt.bfloat16
AF = mybir.ActivationFunctionType
ALU = mybir.AluOpType
AX = mybir.AxisListType

DMA_POOLS = {"sx": ("sy", 6), "sp": ("sy", 3), "so": ("sy", 3), "sg": ("gp", 3)}
SEMS = {"spe": "pe", "sve": "ve", "sse": "se"}
for _p, (_e, _k) in DMA_POOLS.items():
    for _i in range(_k):
        SEMS[f"{_p}{_i}"] = _e


class Plan:

    def __init__(self):
        self.events = {}
        self.dry = True
        self.mode = None
        self.eng = None
        self.sems = None
        self.counts = {k: 0 for k in SEMS}
        self.pool_n = {p: 0 for p in DMA_POOLS}

    def begin(self, mode, eng=None, sems=None):
        self.dry = (mode == "dry")
        self.mode = mode
        self.eng = eng
        self.sems = sems
        self.counts = {k: 0 for k in SEMS}
        self.pool_n = {p: 0 for p in DMA_POOLS}

    def op(self, engine, fn, inc=None, inc_n=1, event=None):
        inst = None
        if self.mode == engine:
            inst = fn(self.eng)
        if inc is not None:
            assert SEMS[inc] == engine, f"sem {inc} not produced by {engine}"
            self.counts[inc] += inc_n
            if inst is not None:
                inst.then_inc(self.sems[inc], inc_n)
            if event is not None:
                val = (inc, self.counts[inc], dict(self.counts))
                if self.dry:
                    assert event not in self.events, f"dup event {event}"
                    self.events[event] = val
                else:
                    assert self.events.get(event)[:2] == val[:2], \
                        f"event drift {event}: {self.events.get(event)[:2]} vs {val[:2]}"
        else:
            assert event is None
        return inst

    def dma(self, engine, fn, pool, event=None):
        eng_of, k = DMA_POOLS[pool]
        assert eng_of == engine
        i = self.pool_n[pool]
        self.pool_n[pool] = i + 1
        sem = f"{pool}{i % k}"
        if i >= k and self.mode == engine:
            self.eng.wait_ge(self.sems[sem], 16 * (i // k))
        return self.op(engine, fn, inc=sem, inc_n=16, event=event)

    def wait(self, engine, event):
        if self.mode == engine:
            sem_name, val, _ = self.events[event]
            self.eng.wait_ge(self.sems[sem_name], val)

    def wait_all_dma(self, engine, event, pool):
        if self.mode == engine:
            _, _, snap = self.events[event]
            _, k = DMA_POOLS[pool]
            for i in range(k):
                sem = f"{pool}{i}"
                if snap.get(sem, 0) > 0:
                    self.eng.wait_ge(self.sems[sem], snap[sem])

    def drain(self, engine):
        if self.mode == engine:
            self.eng.drain()


def build_walk(cfg, t, P):
    TAB_LNEXP = cfg["tab_lnexp"]
    TAB_SIG = cfg["tab_sig"]
    TAB_GELU = cfg["tab_gelu"]
    T = cfg["T"]; Tp = cfg["Tp"]
    NPH = T // Tp
    TPT = Tp // 128
    NHF = TPT // 2
    ND = 8
    NH = 4
    SQ2I = float(1.0 / np.sqrt(2.0))
    DQS = float(256 ** -0.5)

    psH = {
        "A0": t.psA[:, 0:512], "A1": t.psA[:, 512:1024],
        "B0": t.psB[:, 0:512], "B1": t.psB[:, 512:1024],
        "C0": t.psC[:, 0:512], "C1": t.psC[:, 512:1024],
        "T0": t.psT0[:, :], "T1": t.psT1[:, :],
    }
    bank_free = {}

    def pe_claim(*regions):
        for r in regions:
            ev = bank_free.pop(r, None)
            if ev is not None:
                P.wait("pe", ev)

    def release(region, event):
        bank_free[region] = event

    def xslot(ph, h):
        return (2 * ph + h) % 4

    gload = [
        lambda g: g.dma_start(out=t.s0_bf[:], in_=t.i_slot[:]),
        lambda g: g.dma_start(out=t.b2_r[:], in_=t.i_b2[None, :]),
        lambda g: g.dma_start(out=t.w1t_sb[:],
            in_=t.i_w1t_pk.rearrange("p (k f) -> p k f", k=8)),
        lambda g: g.dma_start(out=t.w2_sb[:],
            in_=t.i_w2_pk.rearrange("p (k f) -> p k f", k=4)),
        lambda g: g.dma_start(out=t.Wp_sb[:],
            in_=t.i_Wp_pk.rearrange("p (k f) -> p k f", k=8)),
        lambda g: g.dma_start(out=t.Wu_sb[:],
            in_=t.i_Wu_pk.rearrange("p (k f) -> p k f", k=8)),
    ]
    for i, fn in enumerate(gload):
        P.dma("gp", fn, "sg", event=("wload",) if i == len(gload) - 1 else None)
    gload2 = [
        lambda g: g.dma_start(out=t.Wo_sb[:],
            in_=t.i_Wo_pk.rearrange("p (k f) -> p k f", k=8)),
        lambda g: g.dma_start(out=t.wq_sb[:],
            in_=t.i_wq_pk.rearrange("p (k f) -> p k f", k=8)),
        lambda g: g.dma_start(out=t.wk_sb[:],
            in_=t.i_wk_pk.rearrange("p (k f) -> p k f", k=8)),
    ]

    sload = [
        lambda s: s.dma_start(out=t.ident[:], in_=t.i_ident[:]),
        lambda s: s.dma_start(out=t.ones_r[:], in_=t.i_ones_r[:]),
        lambda s: s.dma_start(out=t.ones_c[:], in_=t.i_ones_c[:]),
        lambda s: s.dma_start(out=t.tau_b[:], in_=t.i_tau_b[:]),
        lambda s: s.dma_start(out=t.bpw[:], in_=t.i_bp0_pk[:]),
        lambda s: s.dma_start(out=t.bpk2[:], in_=t.i_bpk_pk[:]),
        lambda s: s.dma_start(out=t.bu_c[:], in_=t.i_bu_pk[:]),
        lambda s: s.dma_start(out=t.b1_c[:], in_=t.i_b1_pk[:]),
        lambda s: s.dma_start(out=t.bq_c[:], in_=t.i_bq_pk[:]),
        lambda s: s.dma_start(out=t.bk_c[:], in_=t.i_bk_pk[:]),
        lambda s: s.dma_start(out=t.lng_c[:], in_=t.i_lng_pk[:]),
        lambda s: s.dma_start(out=t.lnb_c[:], in_=t.i_lnb_pk[:]),
        lambda s: s.dma_start(out=t.bo1_r[:],
            in_=t.i_bo.rearrange("(a b) -> a b", a=8)),
        lambda s: s.dma_start(out=t.bo2_r[:],
            in_=t.i_bof.rearrange("(a b) -> a b", a=8)),
    ]
    for i, fn in enumerate(sload):
        P.dma("sy", fn, "sx", event=("cload",) if i == len(sload) - 1 else None)

    for ph in range(min(2, NPH)):
        for h in range(NHF):
            P.dma("sy", lambda s, ph=ph, h=h: s.dma_start(
                out=t.x_sl[:, xslot(ph, h), :, :],
                in_=t.i_x[ph * Tp + h * 256: ph * Tp + (h + 1) * 256, :]
                    .rearrange("(n p) d -> p n d", p=128)), "sx",
                event=("xload", ph, h))

    P.op("se", lambda s: _load_tab(s, TAB_SIG))
    P.wait_all_dma("se", ("cload",), "sx")
    P.op("se", lambda s: s.activation(t.invc[:], t.tau_b[:], AF.Abs),
         inc="sse", event=("tauabs",))
    P.wait("ve", ("tauabs",))
    P.wait_all_dma("ve", ("cload",), "sx")
    P.op("ve", lambda v: v.tensor_scalar(t.invc[:], t.invc[:], 0.1, None, ALU.add))
    P.drain("ve")
    P.op("ve", lambda v: v.reciprocal(t.invc[:], t.invc[:]))
    P.drain("ve")
    P.op("ve", lambda v: v.tensor_scalar(t.bpk2[:], t.bpk2[:], -2.0, None, ALU.mult),
         inc="sve", event=("bpk2",))
    P.op("ve", lambda v: v.tensor_add(t.bo1_r[:], t.bo1_r[:], t.bo2_r[:]),
         inc="sve", event=("bosum",))
    P.wait("gp", ("bosum",))
    P.dma("gp", lambda g: g.dma_start(
        out=t.bo_tot[:], in_=t.bo1_r[:]), "sg", event=("botot",))

    def stats_tile(ph, i):
        pp = ph % 2
        sl, ii = xslot(ph, i // 2), i % 2
        if i % 2 == 0:
            P.wait("se", ("xload", ph, i // 2))
        if i == 0 and ph >= 2:
            P.wait("se", ("var", ph - 2))
        P.drain("se")
        P.op("se", lambda s, sl=sl, ii=ii, i=i, pp=pp: s.activation(
            t.dump[:], t.x_sl[:, sl, ii, :], AF.Identity,
            accum_out=t.ssum[:, pp, i:i + 1]))
        P.drain("se")
        P.op("se", lambda s, sl=sl, ii=ii, i=i, pp=pp: s.activation(
            t.dump[:], t.x_sl[:, sl, ii, :], AF.Square,
            accum_out=t.ssq[:, pp, i:i + 1]),
            inc="sse" if i == TPT - 1 else None,
            event=("sums", ph) if i == TPT - 1 else None)

    def var_chain(ph):
        pp = ph % 2
        smn, svar, stmp = (t.smn[:, pp, :], t.svar[:, pp, :], t.stmp[:, pp, :])
        P.wait("ve", ("sums", ph))
        if ph >= 2:
            P.wait("ve", ("rstd", ph - 2))
        P.op("ve", lambda v, smn=smn, pp=pp: v.tensor_scalar(
            smn, t.ssum[:, pp, :], 1.0 / 1024.0, None, ALU.mult))
        P.op("ve", lambda v, svar=svar, pp=pp: v.tensor_scalar(
            svar, t.ssq[:, pp, :], 1.0 / 1024.0, 1e-5, ALU.mult, ALU.add))
        P.drain("ve")
        P.op("ve", lambda v, stmp=stmp, smn=smn: v.tensor_mul(stmp, smn, smn))
        P.drain("ve")
        P.op("ve", lambda v, svar=svar, stmp=stmp: v.tensor_sub(svar, svar, stmp),
             inc="sve", event=("var", ph))

    def rstd_window(ph):
        pp = ph % 2
        P.wait("se", ("var", ph))
        P.op("se", lambda s: _load_tab(s, TAB_LNEXP))
        P.op("se", lambda s, pp=pp: s.activation(
            t.stmp[:, pp, :], t.svar[:, pp, :], AF.Ln))
        P.drain("se")
        P.op("se", lambda s, pp=pp: s.activation(
            t.sr[:, pp, :], t.stmp[:, pp, :], AF.Exp, scale=-0.5),
            inc="sse", event=("rstd", ph))
        P.op("se", lambda s: _load_tab(s, TAB_SIG))

    def window(ph):
        if ph < 1:
            return
        P.wait("se", ("max8", ph - 1))
        if ph >= 2:
            P.wait("se", ("alpha", ph - 2))
        P.op("se", lambda s: _load_tab(s, TAB_LNEXP))
        for i in range(TPT):
            P.op("se", lambda s, i=i: s.activation(
                t.e_al[:, i, :], t.l_sb[:, i, :], AF.Exp,
                bias=t.m1n[:, i:i + 1]),
                inc="sse" if i == TPT - 1 else None,
                event=("exps", ph - 1) if i == TPT - 1 else None)

    def alpha_fin(ph):
        P.wait("ve", ("exps", ph))
        if ph >= 1:
            P.wait("ve", ("slotmm", ph - 1))
        for i in range(TPT):
            P.op("ve", lambda v, i=i: v.scalar_tensor_tensor(
                t.e_al[:, i, :], t.l_sb[:, i, :], t.m18[:, i, 7:8],
                t.e_al[:, i, :], ALU.is_ge, ALU.mult))
        P.drain("ve")
        for i in range(TPT):
            P.op("ve", lambda v, i=i: v.reduce_sum(
                t.alns[:, i:i + 1], t.e_al[:, i, :], axis=AX.X))
        P.drain("ve")
        P.op("ve", lambda v: v.reciprocal(t.alns[:], t.alns[:]))
        P.drain("ve")
        for i in range(TPT):
            P.op("ve", lambda v, i=i: v.tensor_scalar(
                t.al_bf[:, i, :], t.e_al[:, i, :], t.alns[:, i:i + 1], None,
                ALU.mult),
                inc="sve" if i == TPT - 1 else None,
                event=("alpha", ph) if i == TPT - 1 else None)

    def wo_block(ph):
        if ph == 0:
            P.wait_all_dma("pe", ("wload2",), "sg")
        regs = [("A0", "A1", t.psA), ("B0", "B1", t.psB), ("C0", "C1", t.psC),
                ("A0", "A1", t.psA)]

        def wo_tile_k(i, k, ps):
            for f in range(2):
                last = (f == 1 and k == ND - 1)
                P.op("pe", lambda e, i=i, f=f, k=k, ps=ps: e.matmul(
                    ps[:, f * 512:(f + 1) * 512],
                    t.ys_bf[:, k, i * 128:(i + 1) * 128],
                    t.Wo_sb[:, k, f * 512:(f + 1) * 512],
                    start=(k == 0), stop=(k == ND - 1)),
                    inc="spe" if last else None,
                    event=("womm", ph, i) if last else None)

        pe_claim(regs[0][0], regs[0][1])
        for k in range(ND - 1):
            P.wait("pe", ("ysd", ph, k))
            wo_tile_k(0, k, regs[0][2])
        pe_claim(regs[1][0], regs[1][1])
        for k in range(ND - 1):
            wo_tile_k(1, k, regs[1][2])
        P.wait("pe", ("ysd", ph, ND - 1))
        wo_tile_k(0, ND - 1, regs[0][2])
        wo_tile_k(1, ND - 1, regs[1][2])
        for i in range(TPT):
            r0, r1, ps = regs[i]
            if i >= 2:
                pe_claim(r0, r1)
                for k in range(ND):
                    wo_tile_k(i, k, ps)
            P.wait("ve", ("womm", ph, i))
            if i == 0:
                P.wait("ve", ("sums", ph))
            sl, ii = xslot(ph, i // 2), i % 2
            P.op("ve", lambda v, ps=ps, sl=sl, ii=ii: v.tensor_add(
                t.x_sl[:, sl, ii, :], ps[:, 0:1024], t.x_sl[:, sl, ii, :]),
                inc="sve", event=("part", ph, i))
            release(r0, ("part", ph, i))
            release(r1, ("part", ph, i))
        for h in range(NHF):
            P.wait("sy", ("part", ph, 2 * h + 1))
            P.dma("sy", lambda s, ph=ph, h=h: s.dma_start(
                out=t.o_out[ph * Tp + h * 256: ph * Tp + (h + 1) * 256, :]
                    .rearrange("(n p) d -> p n d", p=128),
                in_=t.x_sl[:, xslot(ph, h), :, :]), "sp",
                event=("pstore", ph, h))

    def slot_pe(ph):
        P.wait("pe", ("alpha", ph))
        pe_claim("B0", "B1", "C0", "T1")
        for i in range(TPT):
            P.op("pe", lambda e, i=i, pp=ph % 2: e.matmul(
                t.psB[0:64, 0:512], t.al_bf[:, i, :], t.xn_tm[:, pp, i, 0:512],
                start=(i == 0), stop=(i == TPT - 1)))
            P.op("pe", lambda e, i=i, pp=ph % 2: e.matmul(
                t.psB[0:64, 512:1024], t.al_bf[:, i, :],
                t.xn_tm[:, pp, i, 512:1024],
                start=(i == 0), stop=(i == TPT - 1)))
            P.op("pe", lambda e, i=i: e.matmul(
                t.psC[0:64, 0:1], t.al_bf[:, i, :], t.ones_c[:, 0:1],
                start=(i == 0), stop=(i == TPT - 1)))
        for i in range(TPT):
            P.op("pe", lambda e, i=i: e.transpose(
                t.psT1[0:64, i * 128:(i + 1) * 128],
                t.al_bf[:, i, :], t.ident[:]),
                inc="spe" if i == TPT - 1 else None,
                event=("slotmm", ph) if i == TPT - 1 else None)

    def slot_ve(ph):
        P.wait("ve", ("slotmm", ph))
        if ph == 0:
            P.op("ve", lambda v: v.tensor_copy(t.sl_acc[:, 0:512],
                                               t.psB[0:64, 0:512]))
            P.op("ve", lambda v: v.tensor_copy(t.sl_acc[:, 512:1024],
                                               t.psB[0:64, 512:1024]))
            P.op("ve", lambda v: v.tensor_copy(t.den_acc[:], t.psC[0:64, 0:1]))
        else:
            P.op("ve", lambda v: v.tensor_add(t.sl_acc[:, 0:512],
                                              t.sl_acc[:, 0:512], t.psB[0:64, 0:512]))
            P.op("ve", lambda v: v.tensor_add(t.sl_acc[:, 512:1024],
                                              t.sl_acc[:, 512:1024],
                                              t.psB[0:64, 512:1024]))
            P.op("ve", lambda v: v.tensor_add(t.den_acc[:], t.den_acc[:],
                                              t.psC[0:64, 0:1]))
        P.op("ve", lambda v, ph=ph: v.tensor_copy(
            t.alpha_T[0:64, ph * Tp:(ph + 1) * Tp], t.psT1[0:64, 0:Tp]),
            inc="sve", event=("slotev", ph))
        for r in ("B0", "B1", "C0", "T1"):
            release(r, ("slotev", ph))

    def ln_apply(ph):
        pp = ph % 2
        P.wait("ve", ("rstd", ph))
        if ph >= 2:
            P.wait("ve", ("slotmm", ph - 2))
        P.op("ve", lambda v, pp=pp: v.tensor_mul(
            t.smrn[:, pp, :], t.smn[:, pp, :], t.sr[:, pp, :]))
        P.drain("ve")
        P.op("ve", lambda v, pp=pp: v.tensor_scalar(
            t.smrn[:, pp, :], t.smrn[:, pp, :], -1.0, None, ALU.mult))
        P.drain("ve")
        for i in range(TPT):
            sl, ii = xslot(ph, i // 2), i % 2
            P.op("ve", lambda v, i=i, sl=sl, ii=ii, pp=pp: v.tensor_scalar(
                t.xn_tm[:, pp, i, :], t.x_sl[:, sl, ii, :],
                t.sr[:, pp, i:i + 1], t.smrn[:, pp, i:i + 1],
                ALU.mult, ALU.add),
                inc="sve" if i == TPT - 1 else None,
                event=("ln", ph) if i == TPT - 1 else None)

    def xn_transposes(ph):
        P.wait("pe", ("ln", ph))
        if ph >= 1:
            pass
        for k in range(ND):
            reg = "T0" if k % 2 == 0 else "T1"
            psT = psH[reg]
            pe_claim(reg)
            for i in range(TPT):
                P.op("pe", lambda e, k=k, i=i, psT=psT, pp=ph % 2: e.transpose(
                    psT[:, i * 128:(i + 1) * 128],
                    t.xn_tm[:, pp, i, k * 128:(k + 1) * 128], t.ident[:]),
                    inc="spe" if i == TPT - 1 else None,
                    event=("xnT", ph, k) if i == TPT - 1 else None)
            P.wait("ve", ("xnT", ph, k))
            if ph >= 1 and k < 2:
                P.wait("ve", ("wku", ph - 1, ND - 1))
                P.wait("ve", ("hmm", ph - 1, NH - 1))
            P.op("ve", lambda v, k=k, psT=psT: v.tensor_copy(
                t.xn_f[:, k, :], psT[:, 0:Tp]), inc="sve", event=("xnf", ph, k))
            release(reg, ("xnf", ph, k))

    def wku_loop(ph):
        P.wait("pe", ("xnf", ph, ND - 1))
        for j in range(ND):
            par = j % 2
            if par == 0:
                rw, rk, ru = "A0", "A1", "B0"
            else:
                rw, rk, ru = "B1", "C0", "C1"
            psw, psk, psu = (psH[rw][:, 0:Tp], psH[rk][:, 0:Tp],
                            psH[ru][:, 0:Tp])
            pe_claim(rw, rk, ru)
            for k in range(ND):
                P.op("pe", lambda e, j=j, k=k, psw=psw: e.matmul(
                    psw, t.Wp_sb[:, k, j * 128:(j + 1) * 128], t.xn_f[:, k, :],
                    start=(k == 0), stop=(k == ND - 1)))
            for k in range(ND):
                last = (k == ND - 1)
                P.op("pe", lambda e, j=j, k=k, psk=psk: e.matmul(
                    psk, t.Wp_sb[:, k, 1024 + j * 128:1024 + (j + 1) * 128],
                    t.xn_f[:, k, :], start=(k == 0), stop=(k == ND - 1)),
                    inc="spe" if last else None,
                    event=("wk", ph, j) if last else None)
            for k in range(ND):
                last = (k == ND - 1)
                P.op("pe", lambda e, j=j, k=k, psu=psu: e.matmul(
                    psu, t.Wu_sb[:, k, j * 128:(j + 1) * 128], t.xn_f[:, k, :],
                    start=(k == 0), stop=(k == ND - 1)),
                    inc="spe" if last else None,
                    event=("wku", ph, j) if last else None)

            if j == 0:
                P.op("se", lambda s: _load_tab(s, TAB_SIG))
            P.wait("se", ("wk", ph, j))
            if j >= 2:
                P.wait("se", ("absc", ph, j - 2))
            elif ph >= 1:
                P.wait("se", ("absc", ph - 1, j + 6))
            P.drain("se")
            P.op("se", lambda s, j=j, psw=psw: s.activation(
                t.g_sb[:, j % 2, :], psw, AF.Sigmoid, bias=t.bpw[:, j:j + 1]))
            P.op("se", lambda s, j=j, psk=psk: s.activation(
                t.s2_sb[:, j % 2, :], psk, AF.Sigmoid, bias=t.bpk2[:, j:j + 1],
                scale=-2.0), inc="sse", event=("gs", ph, j))
            release(rw, ("gs", ph, j))
            release(rk, ("gs", ph, j))
            if ph + 1 < NPH and j < TPT:
                stats_tile(ph + 1, j)
            if ph + 1 < NPH and j == TPT:
                rstd_window(ph + 1)

            P.wait("ve", ("gs", ph, j))
            P.wait("ve", ("wku", ph, j))
            if ph >= 1 and j == 0:
                P.wait("ve", ("womm", ph - 1, TPT - 1))
            pp = j % 2
            P.drain("ve")
            P.op("ve", lambda v, pp=pp: v.scalar_tensor_tensor(
                t.a_sb[:], t.g_sb[:, pp, :], -2.0, t.s2_sb[:, pp, :],
                ALU.mult, ALU.mult))
            P.op("ve", lambda v, pp=pp, psu=psu, j=j: v.scalar_tensor_tensor(
                t.b_sb[:], psu, t.bu_c[:, j:j + 1], t.g_sb[:, pp, :],
                ALU.add, ALU.mult), inc="sve", event=("absc", ph, j))
            release(ru, ("absc", ph, j))
            P.drain("ve")
            P.op("ve", lambda v, pp=pp: v.tensor_scalar(
                t.a_sb[:], t.a_sb[:], 1.0, None, ALU.add))
            P.drain("ve")
            init = 0.0 if ph == 0 else t.ys_bf[:, j, Tp - 1:Tp]
            P.op("ve", lambda v, j=j, init=init: v.tensor_tensor_scan(
                t.ys_bf[:, j, :], t.a_sb[:], t.b_sb[:], init,
                ALU.mult, ALU.add),
                inc="sve", event=("ysd", ph, j))
            if ph + 1 < NPH and j == TPT - 1:
                var_chain(ph + 1)
            if ph + 1 < NPH and j == ND - 1:
                ln_apply(ph + 1)

    def h_logits(ph):
        P.wait("pe", ("xnf", ph, ND - 1))
        hr = ["A0", "A1", "B0", "B1"]
        for jh in range(NH):
            reg = hr[jh]
            ps = psH[reg][:, 0:Tp]
            pe_claim(reg)
            for k in range(ND):
                last = (k == ND - 1)
                P.op("pe", lambda e, jh=jh, k=k, ps=ps: e.matmul(
                    ps, t.w1t_sb[:, k, jh * 128:(jh + 1) * 128], t.xn_f[:, k, :],
                    start=(k == 0), stop=(k == ND - 1)),
                    inc="spe" if last else None,
                    event=("hmm", ph, jh) if last else None)
            P.wait("se", ("hmm", ph, jh))
            if jh == 0:
                if ph >= 1:
                    P.wait("se", ("lmm", ph - 1))
                P.op("se", lambda s: _load_tab(s, TAB_GELU))
            P.drain("se")
            P.op("se", lambda s, jh=jh, ps=ps: s.activation(
                t.h_f[:, jh, :], ps, AF.Gelu, bias=t.bias_h[:, jh:jh + 1]),
                inc="sse", event=("hf", ph, jh))
            release(reg, ("hf", ph, jh))
        P.wait("pe", ("hf", ph, NH - 1))
        pe_claim("C0")
        for i in range(TPT):
            for kh in range(NH):
                P.op("pe", lambda e, i=i, kh=kh: e.matmul(
                    t.psC[:, i * 64:(i + 1) * 64],
                    t.h_f[:, kh, i * 128:(i + 1) * 128], t.w2_sb[:, kh, :],
                    start=(kh == 0), stop=False))
            last = (i == TPT - 1)
            P.op("pe", lambda e, i=i: e.matmul(
                t.psC[:, i * 64:(i + 1) * 64], t.ones_r[:], t.b2_r[:],
                start=False, stop=True),
                inc="spe" if last else None,
                event=("lmm", ph) if last else None)
        P.wait("se", ("lmm", ph))
        if ph >= 1:
            P.wait("se", ("alpha", ph - 1))
        P.drain("se")
        P.op("se", lambda s: s.activation(
            t.l_sb[:], t.psC[:, 0:TPT * 64], AF.Identity, scale=t.invc[:]),
            inc="sse", event=("lev", ph))
        release("C0", ("lev", ph))
        P.wait("ve", ("lev", ph))
        for i in range(TPT):
            P.op("ve", lambda v, i=i: v.max(out=t.m18[:, i, :],
                                            in_=t.l_sb[:, i, :]))
        P.drain("ve")
        P.op("ve", lambda v: v.tensor_scalar(
            t.m1n[:], t.m18[:, :, 0], -1.0, None, ALU.mult),
            inc="sve", event=("max8", ph))

    def mid_weight_loads():
        def ihl(jt, h):
            P.dma("gp", lambda g, jt=jt, h=h: g.dma_start(
                out=t.WT[:, jt, h * 4:(h + 1) * 4],
                in_=t.i_Wih_pk.rearrange("p (jt k f) -> p jt k f", jt=6, k=8)
                    [:, jt, h * 4:(h + 1) * 4]), "sg", event=("ihl", jt, h))

        def whl(jt, h):
            P.dma("gp", lambda g, jt=jt, h=h: g.dma_start(
                out=t.whh_pp[:, jt % 2, h * 4:(h + 1) * 4],
                in_=t.i_Whh_pk.rearrange("p (jt k f) -> p jt k f", jt=6, k=8)
                    [:, jt, h * 4:(h + 1) * 4]), "sg", event=("whl", jt, h))

        P.wait("gp", ("wku", NPH - 1, ND - 1))
        P.wait_all_dma("gp", ("pstore", NPH - 1, NHF - 1), "sp")
        P.wait("gp", ("slotev", NPH - 1))
        P.wait("gp", ("exps", NPH - 1))
        P.wait("gp", ("slotmm", NPH - 1))
        P.dma("gp", lambda g: g.dma_start(out=t.s0_f32[:], in_=t.i_slot[:]), "sg")
        P.dma("gp", lambda g: g.dma_start(out=t.bih_r[:], in_=t.i_bih[None, :]), "sg")
        P.dma("gp", lambda g: g.dma_start(out=t.bhh_r[:], in_=t.i_bhh[None, :]), "sg")
        P.dma("gp", lambda g: g.dma_start(out=t.bv_r[:], in_=t.i_bv[None, :]), "sg")
        P.dma("gp", lambda g: g.dma_start(out=t.bvp_r[:], in_=t.i_bvp[None, :]), "sg",
              event=("brows",))
        P.wait("gp", ("womm", NPH - 1, TPT - 1))
        for jt in range(2):
            ihl(jt, 0)
            whl(jt, 0)
            ihl(jt, 1)
            whl(jt, 1)
        for jt in range(2, 6):
            ihl(jt, 0)
            ihl(jt, 1)

    for i in range(TPT):
        stats_tile(0, i)
    var_chain(0)
    rstd_window(0)
    ln_apply(0)

    P.wait_all_dma("pe", ("cload",), "sx")
    P.wait_all_dma("pe", ("wload",), "sg")
    pe_claim("T0")
    for k in range(ND):
        P.op("pe", lambda e, k=k: e.transpose(
            t.psT0[:, k * 64:(k + 1) * 64],
            t.s0_bf[0:64, k * 128:(k + 1) * 128], t.ident[0:64, 0:64]),
            inc="spe" if k == ND - 1 else None,
            event=("s0T",) if k == ND - 1 else None)
    P.wait("ve", ("s0T",))
    for k in range(ND):
        P.op("ve", lambda v, k=k: v.tensor_copy(
            t.si0f[:, k, :], t.psT0[:, k * 64:(k + 1) * 64]),
            inc="sve" if k == ND - 1 else None,
            event=("si0f",) if k == ND - 1 else None)
    release("T0", ("si0f",))
    P.drain("ve")
    for k in range(ND):
        P.op("ve", lambda v, k=k: v.reduce_sum(
            t.sm_f[:, k:k + 1], t.si0f[:, k, :], axis=AX.X))
    P.drain("ve")
    P.op("ve", lambda v: v.tensor_scalar(t.sm_fb[:], t.sm_f[:], 1.0 / 64.0, None,
                                         ALU.mult), inc="sve", event=("smf",))
    regs4 = ["A0", "A1", "B0", "B1"]
    pe_claim(*regs4)
    ps4 = [psH[r] for r in regs4]
    for k in range(ND):
        if k >= 2:
            P.wait("gp", ("smwk", k - 2))
        P.dma("gp", lambda g, k=k: g.dma_start(
            out=t.w1s[:, k % 2, :],
            in_=t.i_w1[1024 + k * 128:1024 + (k + 1) * 128, :]), "sg",
            event=("w1s", k))
        P.wait("pe", ("w1s", k))
        if k == 0:
            P.wait("pe", ("smf",))
        for jh in range(NH):
            P.op("pe", lambda e, jh=jh, k=k: e.matmul(
                ps4[jh][:, 0:1], t.w1s[:, k % 2, jh * 128:(jh + 1) * 128],
                t.sm_fb[:, k:k + 1], start=(k == 0), stop=(k == ND - 1)),
                inc="spe" if jh == NH - 1 else None,
                event=("smwk", k) if jh == NH - 1 else None)
    P.wait("ve", ("smwk", ND - 1))
    for jh in range(NH):
        P.op("ve", lambda v, jh=jh: v.tensor_add(
            t.bias_h[:, jh:jh + 1], t.b1_c[:, jh:jh + 1], ps4[jh][:, 0:1]),
            inc="sve" if jh == NH - 1 else None,
            event=("biash",) if jh == NH - 1 else None)
    for r in regs4:
        release(r, ("biash",))
    for i, fn in enumerate(gload2):
        P.dma("gp", fn, "sg", event=("wload2",) if i == len(gload2) - 1 else None)

    for ph in range(NPH + 1):
        window(ph)
        if ph >= 1:
            alpha_fin(ph - 1)
        if ph >= 1:
            wo_block(ph - 1)
            if ph + 1 < NPH:
                for h in range(NHF):
                    P.wait("sy", ("pstore", ph - 1, h))
                    P.dma("sy", lambda s, ph=ph, h=h: s.dma_start(
                        out=t.x_sl[:, xslot(ph + 1, h), :, :],
                        in_=t.i_x[(ph + 1) * Tp + h * 256:
                                  (ph + 1) * Tp + (h + 1) * 256, :]
                            .rearrange("(n p) d -> p n d", p=128)), "sx",
                        event=("xload", ph + 1, h))
        if ph >= 1:
            slot_pe(ph - 1)
            slot_ve(ph - 1)
        if ph < NPH:
            xn_transposes(ph)
            h_logits(ph)
            wku_loop(ph)
        if ph == NPH:
            mid_weight_loads()

    P.wait("ve", ("slotev", NPH - 1))
    P.wait("ve", ("exps", NPH - 1))
    P.op("ve", lambda v: v.tensor_scalar(t.dent[:], t.den_acc[:], 1e-8, None,
                                         ALU.add))
    P.drain("ve")
    P.op("ve", lambda v: v.reciprocal(t.dent[:], t.dent[:]))
    P.drain("ve")
    P.op("ve", lambda v: v.tensor_scalar(t.si_bf[:], t.sl_acc[:], t.dent[:],
                                         None, ALU.mult), inc="sve",
         event=("si",))
    P.wait("pe", ("si",))
    pe_claim("T0")
    for k in range(ND):
        P.op("pe", lambda e, k=k: e.transpose(
            t.psT0[:, k * 64:(k + 1) * 64], t.si_bf[0:64, k * 128:(k + 1) * 128],
            t.ident[0:64, 0:64]),
            inc="spe" if k == ND - 1 else None,
            event=("siT",) if k == ND - 1 else None)
    P.wait("ve", ("siT",))
    for k in range(ND):
        P.op("ve", lambda v, k=k: v.tensor_copy(
            t.si_f[:, k, :], t.psT0[:, k * 64:(k + 1) * 64]),
            inc="sve" if k == ND - 1 else None,
            event=("sif",) if k == ND - 1 else None)
    release("T0", ("sif",))

    whhbuf = t.whh_pp
    P.wait("pe", ("sif",))
    P.wait_all_dma("pe", ("brows",), "sg")
    mregs = ["A0", "A1", "B0", "B1", "C0", "C1"]
    for jt in range(6):
        reg = mregs[jt]
        ps = psH[reg]
        pe_claim(reg)
        for k in range(ND):
            if k % 4 == 0:
                P.wait("pe", ("ihl", jt, k // 4))
            P.op("pe", lambda e, k=k, jt=jt, ps=ps: e.matmul(
                ps[0:64, :], t.si_f[:, k, :], t.WT[:, jt, k, :],
                start=(k == 0), stop=False))
        P.op("pe", lambda e, jt=jt, ps=ps: e.matmul(
            ps[0:64, :], t.ones_r[0:1, 0:64], t.bih_r[:, jt * 512:(jt + 1) * 512],
            start=False, stop=True), inc="spe", event=("gmm", 0, jt))
        P.wait("se", ("gmm", 0, jt))
        P.op("se", lambda s, jt=jt, ps=ps: s.activation(
            t.gi_sb[:, jt * 512:(jt + 1) * 512], ps[0:64, :], AF.Identity),
            inc="sse", event=("gev", 0, jt))
        release(reg, ("gev", 0, jt))
        reg = mregs[jt]
        ps = psH[reg]
        pe_claim(reg)
        for k in range(ND):
            if k % 4 == 0:
                P.wait("pe", ("whl", jt, k // 4))
            P.op("pe", lambda e, k=k, jt=jt, ps=ps: e.matmul(
                ps[0:64, :], t.si0f[:, k, :],
                t.whh_pp[:, jt % 2, k, :], start=(k == 0), stop=False))
        P.op("pe", lambda e, jt=jt, ps=ps: e.matmul(
            ps[0:64, :], t.ones_r[0:1, 0:64], t.bhh_r[:, jt * 512:(jt + 1) * 512],
            start=False, stop=True), inc="spe", event=("ghmm", jt))
        if jt + 2 < 6:
            P.wait("gp", ("ghmm", jt))
            for h in range(2):
                P.dma("gp", lambda g, jt=jt, h=h: g.dma_start(
                    out=t.whh_pp[:, jt % 2, h * 4:(h + 1) * 4],
                    in_=t.i_Whh_pk.rearrange("p (jt k f) -> p jt k f",
                                             jt=6, k=8)
                        [:, jt + 2, h * 4:(h + 1) * 4]), "sg",
                    event=("whl", jt + 2, h))
        if jt < 4:
            P.wait("ve", ("ghmm", jt))
            P.wait("ve", ("gev", 0, jt))
            P.op("ve", lambda v, jt=jt, ps=ps: v.tensor_add(
                t.gi_sb[:, jt * 512:(jt + 1) * 512],
                t.gi_sb[:, jt * 512:(jt + 1) * 512], ps[0:64, :]),
                inc="sve", event=("gev", 1, jt))
        else:
            P.wait("ve", ("ghmm", jt))
            P.op("ve", lambda v, jt=jt, ps=ps: v.tensor_copy(
                t.gh_n[:, (jt - 4) * 512:(jt - 3) * 512], ps[0:64, :]),
                inc="sve", event=("gev", 1, jt))
        release(reg, ("gev", 1, jt))
    P.dma("gp", lambda g: g.dma_start(
        out=t.wv_sb[:], in_=t.i_wv_pk.rearrange("p (k f) -> p k f", k=8)), "sg")
    P.dma("gp", lambda g: g.dma_start(
        out=t.wvp_sb[:], in_=t.i_wvp_pk.rearrange("p (k f) -> p k f", k=8)), "sg",
        event=("w2load",))
    P.wait("gp", ("svmm",))
    P.dma("gp", lambda g: g.dma_start(
        out=t.wo2_sb[:], in_=t.i_wo2_pk.rearrange("p (k f) -> p k f", k=8)), "sg",
        event=("w3load",))

    P.wait("se", ("gev", 1, 3))
    P.op("se", lambda s: _load_tab(s, TAB_SIG))
    P.op("se", lambda s: s.activation(t.r_g[:], t.gi_sb[:, 0:1024], AF.Sigmoid))
    P.op("se", lambda s: s.activation(t.z_g[:], t.gi_sb[:, 1024:2048],
                                      AF.Sigmoid), inc="sse", event=("rz",))
    P.wait("ve", ("rz",))
    P.drain("ve")
    P.op("ve", lambda v: v.tensor_mul(t.t64a[:], t.r_g[:], t.gh_n[:, 0:1024]))
    P.drain("ve")
    P.op("ve", lambda v: v.tensor_add(t.t64a[:], t.t64a[:],
                                      t.gi_sb[:, 2048:3072]),
         inc="sve", event=("n_in",))
    P.wait("se", ("n_in",))
    P.op("se", lambda s: s.activation(t.n_g[:], t.t64a[:], AF.Tanh),
         inc="sse", event=("n_g",))
    P.wait("ve", ("n_g",))
    P.op("ve", lambda v: v.tensor_sub(t.t64a[:], t.s0_f32[:], t.n_g[:]))
    P.drain("ve")
    P.op("ve", lambda v: v.tensor_mul(t.t64a[:], t.t64a[:], t.z_g[:]))
    P.drain("ve")
    P.op("ve", lambda v: v.tensor_add(t.snew[:], t.n_g[:], t.t64a[:]))
    P.drain("ve")
    P.op("ve", lambda v: v.tensor_copy(t.snew_bf[:], t.snew[:]), inc="sve",
         event=("snew",))
    P.wait("pe", ("snew",))
    pe_claim("T0")
    for k in range(ND):
        P.op("pe", lambda e, k=k: e.transpose(
            t.psT0[:, k * 64:(k + 1) * 64],
            t.snew_bf[0:64, k * 128:(k + 1) * 128], t.ident[0:64, 0:64]),
            inc="spe" if k == ND - 1 else None,
            event=("snT",) if k == ND - 1 else None)
    P.wait("ve", ("snT",))
    for k in range(ND):
        P.op("ve", lambda v, k=k: v.tensor_copy(
            t.snew_f[:, k, :], t.psT0[:, k * 64:(k + 1) * 64]),
            inc="sve" if k == ND - 1 else None,
            event=("snf",) if k == ND - 1 else None)
    release("T0", ("snf",))
    P.drain("ve")
    P.op("ve", lambda v: v.tensor_scalar(t.bq_c[:], t.bq_c[:], DQS, None,
                                         ALU.mult), inc="sve", event=("bqs",))

    P.wait("pe", ("snf",))
    P.wait("pe", ("bqs",))
    pe_claim("A0", "A1")
    for m in range(2):
        for dqc in range(2):
            for k in range(ND):
                last = (m == 1 and dqc == 1 and k == ND - 1)
                P.op("pe", lambda e, m=m, dqc=dqc, k=k: e.matmul(
                    (t.psA[:, dqc * 64:(dqc + 1) * 64] if m == 0
                     else t.psA[:, 512 + dqc * 64:512 + (dqc + 1) * 64]),
                    (t.wq_sb if m == 0 else t.wk_sb)[:, k, dqc * 128:(dqc + 1) * 128],
                    t.snew_f[:, k, :], start=(k == 0), stop=(k == ND - 1)),
                    inc="spe" if last else None,
                    event=("qkmm",) if last else None)
    P.wait("se", ("qkmm",))
    for dqc in range(2):
        P.op("se", lambda s, dqc=dqc: s.activation(
            t.q_bf[:, dqc, :], t.psA[:, dqc * 64:(dqc + 1) * 64], AF.Identity,
            bias=t.bq_c[:, dqc:dqc + 1], scale=DQS))
        P.op("se", lambda s, dqc=dqc: s.activation(
            t.k_bf[:, dqc, :], t.psA[:, 512 + dqc * 64:512 + (dqc + 1) * 64],
            AF.Identity, bias=t.bk_c[:, dqc:dqc + 1]),
            inc="sse" if dqc == 1 else None,
            event=("qk",) if dqc == 1 else None)
    release("A0", ("qk",))
    release("A1", ("qk",))

    P.wait("pe", ("qk",))
    pe_claim("B0")
    for dqc in range(2):
        P.op("pe", lambda e, dqc=dqc: e.matmul(
            t.psB[0:64, 0:64], t.q_bf[:, dqc, :], t.k_bf[:, dqc, :],
            start=(dqc == 0), stop=(dqc == 1)),
            inc="spe" if dqc == 1 else None,
            event=("scmm",) if dqc == 1 else None)
    P.wait("se", ("scmm",))
    P.op("se", lambda s: s.activation(t.sc_sb[:], t.psB[0:64, 0:64], AF.Identity),
         inc="sse", event=("scev",))
    release("B0", ("scev",))
    P.wait("ve", ("scev",))
    P.op("ve", lambda v: v.reduce_max(t.scm[:], t.sc_sb[:], axis=AX.X))
    P.drain("ve")
    P.op("ve", lambda v: v.tensor_scalar(t.scm[:], t.scm[:], -1.0, None,
                                         ALU.mult), inc="sve", event=("scmax",))
    P.wait("se", ("scmax",))
    P.op("se", lambda s: _load_tab(s, TAB_LNEXP))
    P.drain("se")
    P.op("se", lambda s: s.activation(t.sc_sb[:], t.sc_sb[:], AF.Exp,
                                      bias=t.scm[:], accum_out=t.scs[:]),
         inc="sse", event=("scexp",))
    P.wait("ve", ("scexp",))
    P.op("ve", lambda v: v.reciprocal(t.scs[:], t.scs[:]))
    P.drain("ve")
    P.op("ve", lambda v: v.tensor_scalar(t.attn_bf[:], t.sc_sb[:], t.scs[:],
                                         None, ALU.mult), inc="sve",
         event=("attn",))
    P.wait("pe", ("attn",))
    pe_claim("T0")
    P.op("pe", lambda e: e.transpose(t.psT0[0:64, 0:64], t.attn_bf[0:64, 0:64],
                                     t.ident[0:64, 0:64]),
         inc="spe", event=("atT",))
    P.wait("ve", ("atT",))
    P.op("ve", lambda v: v.tensor_copy(t.attnT_bf[:], t.psT0[0:64, 0:64]),
         inc="sve", event=("atTe",))
    release("T0", ("atTe",))

    P.wait_all_dma("pe", ("w2load",), "sg")
    pe_claim("B0", "B1")
    for jt in range(2):
        ps = psH["B0" if jt == 0 else "B1"]
        for k in range(ND):
            P.op("pe", lambda e, jt=jt, k=k, ps=ps: e.matmul(
                ps[0:64, :], t.snew_f[:, k, :],
                t.wv_sb[:, k, jt * 512:(jt + 1) * 512],
                start=(k == 0), stop=False))
        P.op("pe", lambda e, jt=jt, ps=ps: e.matmul(
            ps[0:64, :], t.ones_r[0:1, 0:64], t.bv_r[:, jt * 512:(jt + 1) * 512],
            start=False, stop=True),
            inc="spe" if jt == 1 else None,
            event=("vmm",) if jt == 1 else None)
    P.wait("se", ("vmm",))
    P.op("se", lambda s: s.activation(t.v_bf[:, 0:512], t.psB[0:64, 0:512],
                                      AF.Identity))
    P.op("se", lambda s: s.activation(t.v_bf[:, 512:1024],
                                      t.psB[0:64, 512:1024], AF.Identity),
         inc="sse", event=("vev",))
    release("B0", ("vev",))
    release("B1", ("vev",))

    P.wait("pe", ("atTe",))
    P.wait("pe", ("vev",))
    pe_claim("C0", "C1")
    for jt in range(2):
        ps = psH["C0" if jt == 0 else "C1"]
        P.op("pe", lambda e, jt=jt, ps=ps: e.matmul(
            ps[0:64, :], t.attnT_bf[:], t.v_bf[0:64, jt * 512:(jt + 1) * 512],
            start=True, stop=True),
            inc="spe" if jt == 1 else None,
            event=("cmm",) if jt == 1 else None)
    P.wait("ve", ("cmm",))
    P.op("ve", lambda v: v.tensor_add(t.y_sb[:, 0:512], t.psC[0:64, 0:512],
                                      t.snew[:, 0:512]))
    P.op("ve", lambda v: v.tensor_add(t.y_sb[:, 512:1024],
                                      t.psC[0:64, 512:1024],
                                      t.snew[:, 512:1024]),
         inc="sve", event=("y",))
    release("C0", ("y",))
    release("C1", ("y",))

    P.wait("se", ("y",))
    P.drain("se")
    P.op("se", lambda s: s.activation(t.dump[0:64, :], t.y_sb[:], AF.Identity,
                                      accum_out=t.ysum[:]))
    P.drain("se")
    P.op("se", lambda s: s.activation(t.dump[0:64, :], t.y_sb[:], AF.Square,
                                      accum_out=t.yssq[:]),
         inc="sse", event=("ysums",))
    P.wait("ve", ("ysums",))
    P.op("ve", lambda v: v.tensor_scalar(t.ymu[:], t.ysum[:], 1.0 / 1024.0,
                                         None, ALU.mult))
    P.op("ve", lambda v: v.tensor_scalar(t.yvar[:], t.yssq[:], 1.0 / 1024.0,
                                         1e-5, ALU.mult, ALU.add))
    P.drain("ve")
    P.op("ve", lambda v: v.tensor_mul(t.ytmp[:], t.ymu[:], t.ymu[:]))
    P.drain("ve")
    P.op("ve", lambda v: v.tensor_sub(t.yvar[:], t.yvar[:], t.ytmp[:]),
         inc="sve", event=("yvar",))
    P.wait("se", ("yvar",))
    P.op("se", lambda s: s.activation(t.ytmp[:], t.yvar[:], AF.Ln))
    P.drain("se")
    P.op("se", lambda s: s.activation(t.yrst[:], t.ytmp[:], AF.Exp, scale=-0.5),
         inc="sse", event=("yrst",))
    P.wait("ve", ("yrst",))
    P.op("ve", lambda v: v.tensor_mul(t.ymu[:], t.ymu[:], t.yrst[:]))
    P.drain("ve")
    P.op("ve", lambda v: v.tensor_scalar(t.ymu[:], t.ymu[:], -1.0, None,
                                         ALU.mult))
    P.drain("ve")
    P.op("ve", lambda v: v.tensor_scalar(t.y0_bf[:], t.y_sb[:], t.yrst[:],
                                         t.ymu[:], ALU.mult, ALU.add),
         inc="sve", event=("y0",))
    P.wait("pe", ("y0",))
    pe_claim("T0")
    for k in range(ND):
        P.op("pe", lambda e, k=k: e.transpose(
            t.psT0[:, k * 64:(k + 1) * 64],
            t.y0_bf[0:64, k * 128:(k + 1) * 128], t.ident[0:64, 0:64]),
            inc="spe" if k == ND - 1 else None,
            event=("yT",) if k == ND - 1 else None)
    P.wait("ve", ("yT",))
    for k in range(ND):
        P.op("ve", lambda v, k=k: v.tensor_scalar(
            t.yln_f[:, k, :], t.psT0[:, k * 64:(k + 1) * 64],
            t.lng_c[:, k:k + 1], t.lnb_c[:, k:k + 1], ALU.mult, ALU.add),
            inc="sve" if k == ND - 1 else None,
            event=("ylnf",) if k == ND - 1 else None)
    release("T0", ("ylnf",))

    P.wait("pe", ("ylnf",))
    pe_claim("A0", "A1")
    for jt in range(2):
        ps = psH["A0" if jt == 0 else "A1"]
        for k in range(ND):
            P.op("pe", lambda e, jt=jt, k=k, ps=ps: e.matmul(
                ps[0:64, :], t.yln_f[:, k, :],
                t.wvp_sb[:, k, jt * 512:(jt + 1) * 512],
                start=(k == 0), stop=False))
        P.op("pe", lambda e, jt=jt, ps=ps: e.matmul(
            ps[0:64, :], t.ones_r[0:1, 0:64],
            t.bvp_r[:, jt * 512:(jt + 1) * 512], start=False, stop=True),
            inc="spe" if jt == 1 else None,
            event=("svmm",) if jt == 1 else None)
    P.wait("se", ("svmm",))
    P.op("se", lambda s: s.activation(t.sv_bf[:, 0:512], t.psA[0:64, 0:512],
                                      AF.Identity))
    P.op("se", lambda s: s.activation(t.sv_bf[:, 512:1024],
                                      t.psA[0:64, 512:1024], AF.Identity),
         inc="sse", event=("sv",))
    release("A0", ("sv",))
    release("A1", ("sv",))

    zregs = ["C0", "C1", "B0", "B1", "A0", "A1", "C0", "C1"]

    def zmm_phase(q):
        if q == 0:
            P.wait("pe", ("sv",))
            P.wait("pe", ("botot",))
            P.wait_all_dma("pe", ("w3load",), "sg")
        for k in range(ND):
            reg = zregs[k]
            pe_claim(reg)
            ps = psH[reg][:, 0:Tp]
            P.op("pe", lambda e, k=k, q=q, ps=ps: e.matmul(
                ps, t.sv_bf[0:64, k * 128:(k + 1) * 128],
                t.alpha_T[0:64, q * Tp:(q + 1) * Tp],
                start=True, stop=True), inc="spe", event=("zmm", q, k))
            P.wait("se", ("zmm", q, k))
            if q >= 2 and k == 0:
                P.wait("se", ("omm", q - 2, TPT - 1))
                P.drain("se")
            P.op("se", lambda s, k=k, q=q, ps=ps: s.activation(
                t.z_f[:, q % 2, k, :], ps, AF.Identity),
                inc="sse", event=("zev", q, k))
            release(reg, ("zev", q, k))

    if not cfg.get("skip_pass2"):
        zmm_phase(0)
    for p2 in ([] if cfg.get("skip_pass2") else range(NPH)):
        pr = p2 % 2
        if p2 == 0:
            P.wait_all_dma("sy", ("pstore", NPH - 1, NHF - 1), "sp")
            P.wait("sy", ("sv",))
        if p2 >= 2:
            P.wait("sy", ("fadd", p2 - 2, TPT - 1))
            P.wait("sy", ("ostore", p2 - 2, NHF - 1))
        P.dma("sy", lambda s, p2=p2, pr=pr: s.dma_start(
            out=t.prd[:, pr, :, :],
            in_=t.o_out[p2 * Tp:(p2 + 1) * Tp, :]
                .rearrange("(n p) d -> p n d", p=128)), "sx",
            event=("pread", p2))
        if p2 + 1 < NPH:
            zmm_phase(p2 + 1)
        oregs = [("A0", "A1", t.psA), ("B0", "B1", t.psB), ("C0", "C1", t.psC),
                 ("A0", "A1", t.psA)]
        for i in range(TPT):
            r0, r1, ps = oregs[i]
            pe_claim(r0, r1)
            for k in range(ND):
                if i == 0:
                    P.wait("pe", ("zev", p2, k))
                for f in range(2):
                    P.op("pe", lambda e, i=i, f=f, k=k, ps=ps, p2=p2: e.matmul(
                        ps[:, f * 512:(f + 1) * 512],
                        t.z_f[:, p2 % 2, k, i * 128:(i + 1) * 128],
                        t.wo2_sb[:, k, f * 512:(f + 1) * 512],
                        start=(k == 0), stop=False))
            for f in range(2):
                last = (f == 1)
                P.op("pe", lambda e, i=i, f=f, ps=ps: e.matmul(
                    ps[:, f * 512:(f + 1) * 512], t.ones_r[:],
                    t.bo_tot[:, f * 512:(f + 1) * 512], start=False, stop=True),
                    inc="spe" if last else None,
                    event=("omm", p2, i) if last else None)
            P.wait("ve", ("omm", p2, i))
            if i == 0:
                P.wait("ve", ("pread", p2))
            P.op("ve", lambda v, ps=ps, pr=pr, i=i: v.tensor_add(
                t.prd[:, pr, i, :], ps[:, 0:1024], t.prd[:, pr, i, :]),
                inc="sve", event=("fadd", p2, i))
            release(r0, ("fadd", p2, i))
            release(r1, ("fadd", p2, i))
        for h in range(NHF):
            P.wait("sy", ("fadd", p2, 2 * h + 1))
            P.dma("sy", lambda s, p2=p2, h=h, pr=pr: s.dma_start(
                out=t.o_out[p2 * Tp + h * 256:p2 * Tp + (h + 1) * 256, :]
                    .rearrange("(n p) d -> p n d", p=128),
                in_=t.prd[:, pr, 2 * h:2 * h + 2, :]), "so",
                event=("ostore", p2, h))

    if cfg.get("taps"):
        P.wait("sy", ("sv",))
        P.wait("sy", ("slotev", NPH - 1))
        P.wait("sy", ("ysd", NPH - 1, ND - 1))
        P.wait("sy", ("biash",))
        P.dma("sy", lambda s: s.dma_start(out=t.o_alpha[:], in_=t.alpha_T[:]), "so")
        P.dma("sy", lambda s: s.dma_start(out=t.o_slacc[:], in_=t.sl_acc[:]), "so")
        P.dma("sy", lambda s: s.dma_start(out=t.o_den[:], in_=t.den_acc[:]), "so")
        P.dma("sy", lambda s: s.dma_start(out=t.o_sv[:], in_=t.sv_bf[:]), "so")
        P.dma("sy", lambda s: s.dma_start(out=t.o_carry[:], in_=t.carry[:]), "so")
        P.dma("sy", lambda s: s.dma_start(out=t.o_biash[:], in_=t.bias_h[:]), "so")
        P.dma("sy", lambda s: s.dma_start(out=t.o_snew[:], in_=t.snew[:]), "so")
        P.dma("sy", lambda s: s.dma_start(out=t.o_s0[:], in_=t.s0_f32[:]), "so")
        P.dma("sy", lambda s: s.dma_start(out=t.o_ghn[:], in_=t.gh_n[:]), "so")
        P.dma("sy", lambda s: s.dma_start(out=t.o_gi[:], in_=t.gi_sb[:]), "so")
        P.dma("sy", lambda s: s.dma_start(out=t.o_ssum[:], in_=t.ssum[:, (NPH - 1) % 2]), "so")
        P.dma("sy", lambda s: s.dma_start(out=t.o_ssq[:], in_=t.ssq[:, (NPH - 1) % 2]), "so")
        P.dma("sy", lambda s: s.dma_start(out=t.o_sr[:], in_=t.sr[:, (NPH - 1) % 2]), "so")
        P.dma("sy", lambda s: s.dma_start(out=t.o_xn[:], in_=t.xn_tm[:, (NPH - 1) % 2]), "so")
        P.dma("sy", lambda s: s.dma_start(out=t.o_xnf[:], in_=t.xn_f[:]), "so")
        P.dma("sy", lambda s: s.dma_start(out=t.o_ysbf[:], in_=t.ys_bf[:]), "so")
        P.dma("sy", lambda s: s.dma_start(out=t.o_lsb[:], in_=t.l_sb[:]), "so")

    if P.mode == "sy":
        _, k = DMA_POOLS["so"]
        n = P.pool_n["so"]
        for i in range(k):
            cnt = 16 * ((n - i - 1) // k + 1) if n > i else 0
            if cnt:
                P.eng.wait_ge(P.sems[f"so{i}"], cnt)


def build_nc(cfg):
    T = cfg["T"]; Tp = cfg["Tp"]
    TPT = Tp // 128
    nc = bass.Bass("TRN2", target_bir_lowering=False)
    cfg = dict(cfg)
    cfg["tab_lnexp"] = _act_set_idx(nc, {AF.Ln, AF.Exp, AF.Identity, AF.Square})
    cfg["tab_sig"] = _act_set_idx(nc, {AF.Sigmoid, AF.Tanh, AF.Identity,
                                       AF.Square, AF.Abs})
    cfg["tab_gelu"] = _act_set_idx(nc, {AF.Gelu, AF.Identity})

    class TB:
        pass
    t = TB()

    def din(name, shape, dtype=F32):
        return nc.dram_tensor(name, shape, dtype, kind="ExternalInput").ap()

    t.i_x = din("x", [T, 1024])
    t.i_Wp_pk = din("Wp_pk", [128, 8 * 2048])
    t.i_bp0_pk = din("bp0_pk", [128, 8])
    t.i_bpk_pk = din("bpk_pk", [128, 8])
    t.i_Wu_pk = din("Wu_pk", [128, 8 * 1024])
    t.i_bu_pk = din("bu_pk", [128, 8])
    t.i_Wo_pk = din("Wo_pk", [128, 8 * 1024])
    t.i_bo = din("bo", [1024])
    t.i_slot = din("slot_init", [64, 1024])
    t.i_w1 = din("w1", [2048, 512])
    t.i_w1t_pk = din("w1t_pk", [128, 8 * 512])
    t.i_b1_pk = din("b1_pk", [128, 4])
    t.i_w2_pk = din("w2_pk", [128, 4 * 64])
    t.i_b2 = din("b2", [64])
    t.i_tau_b = din("tau_b", [128, 1])
    t.i_Wih_pk = din("Wih_pk", [128, 6 * 8 * 512])
    t.i_Whh_pk = din("Whh_pk", [128, 6 * 8 * 512])
    t.i_bih = din("bih", [3072])
    t.i_bhh = din("bhh", [3072])
    t.i_wq_pk = din("wq_pk", [128, 8 * 256])
    t.i_bq_pk = din("bq_pk", [128, 2])
    t.i_wk_pk = din("wk_pk", [128, 8 * 256])
    t.i_bk_pk = din("bk_pk", [128, 2])
    t.i_wv_pk = din("wv_pk", [128, 8 * 1024])
    t.i_bv = din("bv", [1024])
    t.i_lng_pk = din("lng_pk", [128, 8])
    t.i_lnb_pk = din("lnb_pk", [128, 8])
    t.i_wvp_pk = din("wvp_pk", [128, 8 * 1024])
    t.i_bvp = din("bvp", [1024])
    t.i_wo2_pk = din("wo2_pk", [128, 8 * 1024])
    t.i_bof = din("bof", [1024])
    t.i_ident = din("ident", [128, 128], BF16)
    t.i_ones_r = din("ones_r", [1, 128], BF16)
    t.i_ones_c = din("ones_c", [128, 1], BF16)
    t.o_out = nc.dram_tensor("out", [T, 1024], F32, kind="ExternalOutput").ap()
    if cfg.get("taps"):
        t.o_alpha = nc.dram_tensor("o_alpha", [64, T], BF16, kind="ExternalOutput").ap()
        t.o_slacc = nc.dram_tensor("o_slacc", [64, 1024], F32, kind="ExternalOutput").ap()
        t.o_den = nc.dram_tensor("o_den", [64, 1], F32, kind="ExternalOutput").ap()
        t.o_sv = nc.dram_tensor("o_sv", [64, 1024], BF16, kind="ExternalOutput").ap()
        t.o_carry = nc.dram_tensor("o_carry", [128, 8], F32, kind="ExternalOutput").ap()
        t.o_biash = nc.dram_tensor("o_biash", [128, 4], F32, kind="ExternalOutput").ap()
        t.o_snew = nc.dram_tensor("o_snew", [64, 1024], F32, kind="ExternalOutput").ap()
        t.o_gi = nc.dram_tensor("o_gi", [64, 3072], F32, kind="ExternalOutput").ap()
        t.o_s0 = nc.dram_tensor("o_s0", [64, 1024], F32, kind="ExternalOutput").ap()
        t.o_ghn = nc.dram_tensor("o_ghn", [64, 1024], F32, kind="ExternalOutput").ap()
        t.o_ssum = nc.dram_tensor("o_ssum", [128, TPT], F32, kind="ExternalOutput").ap()
        t.o_ssq = nc.dram_tensor("o_ssq", [128, TPT], F32, kind="ExternalOutput").ap()
        t.o_sr = nc.dram_tensor("o_sr", [128, TPT], F32, kind="ExternalOutput").ap()
        t.o_xn = nc.dram_tensor("o_xn", [128, TPT, 1024], BF16, kind="ExternalOutput").ap()
        t.o_xnf = nc.dram_tensor("o_xnf", [128, 8, Tp], BF16, kind="ExternalOutput").ap()
        t.o_ysbf = nc.dram_tensor("o_ysbf", [128, 8, Tp], BF16, kind="ExternalOutput").ap()
        t.o_lsb = nc.dram_tensor("o_lsb", [128, TPT, 64], F32, kind="ExternalOutput").ap()

    with ExitStack() as ctx:
        def sb(name, shape, dtype):
            return ctx.enter_context(nc.sbuf_tensor("sb_" + name, shape, dtype))

        t.ident = sb("ident", [128, 128], BF16)
        t.ones_r = sb("ones_r", [1, 128], BF16)
        t.ones_c = sb("ones_c", [128, 1], BF16)
        t.tau_b = sb("tau_b", [128, 1], F32)
        t.invc = sb("invc", [128, 1], F32)
        t.alpha_T = sb("alpha_T", [64, T], BF16)
        t.sl_acc = sb("sl_acc", [64, 1024], F32)
        t.den_acc = sb("den_acc", [64, 1], F32)
        t.carry = sb("carry", [128, 8], F32)
        t.si0f = sb("si0f", [128, 8, 64], BF16)
        t.sm_f = sb("sm_f", [128, 8], F32)
        t.sm_fb = sb("sm_fb", [128, 8], BF16)
        t.bpw = sb("bpw", [128, 8], F32)
        t.bpk2 = sb("bpk2", [128, 8], F32)
        t.bu_c = sb("bu_c", [128, 8], F32)
        t.b1_c = sb("b1_c", [128, 4], F32)
        t.bias_h = sb("bias_h", [128, 4], F32)
        t.b2_r = sb("b2_r", [1, 64], BF16)
        t.bq_c = sb("bq_c", [128, 2], F32)
        t.bk_c = sb("bk_c", [128, 2], F32)
        t.lng_c = sb("lng_c", [128, 8], F32)
        t.lnb_c = sb("lnb_c", [128, 8], F32)
        t.bo_tot = sb("bo_tot", [1, 1024], BF16)
        t.wq_sb = sb("wq_sb", [128, 8, 256], BF16)
        t.wk_sb = sb("wk_sb", [128, 8, 256], BF16)
        t.w2_sb = sb("w2_sb", [128, 4, 64], BF16)
        t.sv_bf = sb("sv_bf", [64, 1024], BF16)
        t.dump = sb("dump", [128, 1024], BF16)
        s1 = ExitStack()

        def sb(name, shape, dtype, _s=s1):
            return _s.enter_context(nc.sbuf_tensor("sb_" + name, shape, dtype))

        t.s0_bf = sb("s0_bf", [64, 1024], BF16)
        t.bo1_r = sb("bo1_r", [8, 128], F32)
        t.bo2_r = sb("bo2_r", [8, 128], F32)
        t.Wp_sb = sb("Wp_sb", [128, 8, 2048], BF16)
        t.Wu_sb = sb("Wu_sb", [128, 8, 1024], BF16)
        t.Wo_sb = sb("Wo_sb", [128, 8, 1024], BF16)
        t.w1t_sb = sb("w1t_sb", [128, 8, 512], BF16)
        t.w1s = sb("w1s", [128, 2, 512], BF16)
        t.x_sl = sb("x_sl", [128, 4, 2, 1024], F32)
        t.xn_tm = sb("xn_tm", [128, 2, TPT, 1024], BF16)
        t.xn_f = sb("xn_f", [128, 8, Tp], BF16)
        t.g_sb = sb("g_sb", [128, 2, Tp], F32)
        t.s2_sb = sb("s2_sb", [128, 2, Tp], F32)
        t.a_sb = sb("a_sb", [128, Tp], F32)
        t.b_sb = sb("b_sb", [128, Tp], F32)
        t.ys_bf = sb("ys_bf", [128, 8, Tp], BF16)
        t.h_f = sb("h_f", [128, 4, Tp], BF16)
        t.l_sb = sb("l_sb", [128, TPT, 64], F32)
        t.e_al = sb("e_al", [128, TPT, 64], F32)
        t.al_bf = sb("al_bf", [128, TPT, 64], BF16)
        t.m18 = sb("m18", [128, TPT, 8], F32)
        t.m1n = sb("m1n", [128, TPT], F32)
        t.alns = sb("alns", [128, TPT], F32)
        t.ssum = sb("ssum", [128, 2, TPT], F32)
        t.ssq = sb("ssq", [128, 2, TPT], F32)
        t.smn = sb("smn", [128, 2, TPT], F32)
        t.svar = sb("svar", [128, 2, TPT], F32)
        t.sr = sb("sr", [128, 2, TPT], F32)
        t.smrn = sb("smrn", [128, 2, TPT], F32)
        t.stmp = sb("stmp", [128, 2, TPT], F32)
        s1.close()
        s2 = ExitStack()

        def sb(name, shape, dtype, _s=s2):
            return _s.enter_context(nc.sbuf_tensor("sb_" + name, shape, dtype))

        t.WT = sb("WT", [128, 6, 8, 512], BF16)
        t.whh_pp = sb("whh_pp", [128, 2, 8, 512], BF16)
        t.s0_f32 = sb("s0_f32", [64, 1024], F32)
        t.bih_r = sb("bih_r", [1, 3072], BF16)
        t.bhh_r = sb("bhh_r", [1, 3072], BF16)
        t.bv_r = sb("bv_r", [1, 1024], BF16)
        t.bvp_r = sb("bvp_r", [1, 1024], BF16)
        t.wv_sb = sb("wv_sb", [128, 8, 1024], BF16)
        t.wvp_sb = sb("wvp_sb", [128, 8, 1024], BF16)
        t.si_bf = sb("si_bf", [64, 1024], BF16)
        t.si_f = sb("si_f", [128, 8, 64], BF16)
        t.gi_sb = sb("gi_sb", [64, 3072], F32)
        t.gh_n = sb("gh_n", [64, 1024], F32)
        t.t64a = sb("t64a", [64, 1024], F32)
        t.r_g = sb("r_g", [64, 1024], F32)
        t.z_g = sb("z_g", [64, 1024], F32)
        t.n_g = sb("n_g", [64, 1024], F32)
        t.snew = sb("snew", [64, 1024], F32)
        t.snew_bf = sb("snew_bf", [64, 1024], BF16)
        t.snew_f = sb("snew_f", [128, 8, 64], BF16)
        t.dent = sb("dent", [64, 1], F32)
        t.q_bf = sb("q_bf", [128, 2, 64], BF16)
        t.k_bf = sb("k_bf", [128, 2, 64], BF16)
        t.sc_sb = sb("sc_sb", [64, 64], F32)
        t.scm = sb("scm", [64, 1], F32)
        t.scs = sb("scs", [64, 1], F32)
        t.attn_bf = sb("attn_bf", [64, 64], BF16)
        t.attnT_bf = sb("attnT_bf", [64, 64], BF16)
        t.v_bf = sb("v_bf", [64, 1024], BF16)
        t.y_sb = sb("y_sb", [64, 1024], F32)
        t.ysum = sb("ysum", [64, 1], F32)
        t.yssq = sb("yssq", [64, 1], F32)
        t.ymu = sb("ymu", [64, 1], F32)
        t.yvar = sb("yvar", [64, 1], F32)
        t.ytmp = sb("ytmp", [64, 1], F32)
        t.yrst = sb("yrst", [64, 1], F32)
        t.y0_bf = sb("y0_bf", [64, 1024], BF16)
        t.yln_f = sb("yln_f", [128, 8, 64], BF16)
        s2.close()
        s3 = ExitStack()

        def sb(name, shape, dtype, _s=s3):
            return _s.enter_context(nc.sbuf_tensor("sb_" + name, shape, dtype))

        t.wo2_sb = sb("wo2_sb", [128, 8, 1024], BF16)
        t.z_f = sb("z_f", [128, 2, 8, Tp], BF16)
        t.prd = sb("prd", [128, 2, TPT, 1024], F32)
        s3.close()

        with ExitStack() as pctx:
            e2 = pctx.enter_context
            t.psA = e2(nc.psum_tensor("psA", [128, 1024], F32))
            t.psB = e2(nc.psum_tensor("psB", [128, 1024], F32))
            t.psC = e2(nc.psum_tensor("psC", [128, 1024], F32))
            t.psT0 = e2(nc.psum_tensor("psT0", [128, 1024], BF16))
            t.psT1 = e2(nc.psum_tensor("psT1", [128, 1024], BF16))
            sems = {k: e2(nc.semaphore(k)) for k in SEMS}

            P = Plan()
            P.begin("dry")
            build_walk(cfg, t, P)

            with nc.Block() as block:
                @block.sync
                def _(eng):
                    P.begin("sy", eng, sems)
                    build_walk(cfg, t, P)

                @block.gpsimd
                def _(eng):
                    P.begin("gp", eng, sems)
                    build_walk(cfg, t, P)

                @block.tensor
                def _(eng):
                    P.begin("pe", eng, sems)
                    build_walk(cfg, t, P)

                @block.vector
                def _(eng):
                    P.begin("ve", eng, sems)
                    build_walk(cfg, t, P)

                @block.scalar
                def _(eng):
                    P.begin("se", eng, sems)
                    build_walk(cfg, t, P)
    return nc


def make_in_maps(inputs, cfg, n_cores=8):
    T = cfg["T"]
    x = np.ascontiguousarray(np.asarray(inputs["L0_attn_x"], dtype=np.float32))
    ident = np.eye(128, dtype=ml_dtypes.bfloat16)
    ones_r = np.ones((1, 128), dtype=ml_dtypes.bfloat16)
    ones_c = np.ones((128, 1), dtype=ml_dtypes.bfloat16)
    tau_b = np.ascontiguousarray(
        np.broadcast_to(np.asarray(inputs["L0_ffn_tau"], np.float32).reshape(1, 1),
                        (128, 1)))
    names = {
        "bo": "L0_attn_bo", "slot_init": "L0_ffn_slot_init",
        "w1": "L0_ffn_w1", "b2": "L0_ffn_b2",
        "bih": "L0_ffn_bih", "bhh": "L0_ffn_bhh", "bv": "L0_ffn_bv",
        "bvp": "L0_ffn_bvp", "bof": "L0_ffn_bo",
    }
    common = {k: np.ascontiguousarray(np.asarray(inputs[v], dtype=np.float32))
              for k, v in names.items()}
    common.update({"tau_b": tau_b, "ident": ident, "ones_r": ones_r,
                   "ones_c": ones_c})

    def f32(v):
        return np.asarray(inputs[v], np.float32)

    def pk(a, k):
        return np.ascontiguousarray(
            a.reshape(k, 128, -1).transpose(1, 0, 2).reshape(128, -1))

    def cpk(a, k):
        return np.ascontiguousarray(a.reshape(k, 128).T)

    def jtpk(a):
        return np.ascontiguousarray(
            a.reshape(8, 128, 6, 512).transpose(1, 2, 0, 3).reshape(128, -1))

    bp = f32("L0_attn_bp")
    common.update({
        "Wp_pk": pk(f32("L0_attn_Wp")[:, 0:2048], 8),
        "bp0_pk": cpk(bp[0:1024], 8), "bpk_pk": cpk(bp[1024:2048], 8),
        "Wu_pk": pk(f32("L0_attn_Wu"), 8), "bu_pk": cpk(f32("L0_attn_bu"), 8),
        "Wo_pk": pk(f32("L0_attn_Wo"), 8),
        "w1t_pk": pk(f32("L0_ffn_w1")[0:1024], 8),
        "b1_pk": cpk(f32("L0_ffn_b1"), 4),
        "w2_pk": pk(f32("L0_ffn_w2"), 4),
        "Wih_pk": jtpk(np.ascontiguousarray(f32("L0_ffn_Wih").T)),
        "Whh_pk": jtpk(np.ascontiguousarray(f32("L0_ffn_Whh").T)),
        "wq_pk": pk(f32("L0_ffn_wq"), 8), "bq_pk": cpk(f32("L0_ffn_bq"), 2),
        "wk_pk": pk(f32("L0_ffn_wk"), 8), "bk_pk": cpk(f32("L0_ffn_bk"), 2),
        "wv_pk": pk(f32("L0_ffn_wv"), 8),
        "lng_pk": cpk(f32("L0_ffn_ln_g"), 8), "lnb_pk": cpk(f32("L0_ffn_ln_b"), 8),
        "wvp_pk": pk(f32("L0_ffn_wvp"), 8),
        "wo2_pk": pk(f32("L0_ffn_wo"), 8),
    })
    in_maps = []
    for c in range(n_cores):
        m = dict(common)
        m["x"] = np.ascontiguousarray(x[c, :T, :])
        in_maps.append(m)
    return in_maps


CFG = {"T": 8192, "Tp": 512}


def kernel(**inputs):
    cfg = CFG
    nc = build_nc(cfg)
    in_maps = make_in_maps(inputs, cfg, n_cores=8)
    res = run_bass_kernel_spmd(nc, in_maps, core_ids=list(range(8)))
    out = np.stack([r["out"] for r in res.results], axis=0)
    return out.astype(np.float32)


# version 34
# speedup vs baseline: 1.0519x; 1.0052x over previous
import numpy as np
import ml_dtypes
from contextlib import ExitStack

import concourse.bass as bass
import concourse.mybir as mybir
from concourse.bass_utils import run_bass_kernel_spmd
from concourse.hw_specs import get_activation_tables


def _act_set_idx(nc, funcs):
    for idx, fs in enumerate(get_activation_tables(nc.m.arch).values()):
        if all(f in fs for f in funcs):
            return idx
    raise ValueError(f"no activation table set contains {funcs}")


def _load_tab(eng, idx):
    return eng.add_instruction(mybir.InstLoadActFuncSet(
        name=eng.bass.get_next_instruction_name(),
        act_func_set_id=idx, ins=[], outs=[]))

F32 = mybir.dt.float32
BF16 = mybir.dt.bfloat16
AF = mybir.ActivationFunctionType
ALU = mybir.AluOpType
AX = mybir.AxisListType

DMA_POOLS = {"sx": ("sy", 6), "sp": ("sy", 3), "so": ("sy", 3), "sg": ("gp", 3)}
SEMS = {"spe": "pe", "sve": "ve", "sse": "se"}
for _p, (_e, _k) in DMA_POOLS.items():
    for _i in range(_k):
        SEMS[f"{_p}{_i}"] = _e


class Plan:

    def __init__(self):
        self.events = {}
        self.dry = True
        self.mode = None
        self.eng = None
        self.sems = None
        self.counts = {k: 0 for k in SEMS}
        self.pool_n = {p: 0 for p in DMA_POOLS}

    def begin(self, mode, eng=None, sems=None):
        self.dry = (mode == "dry")
        self.mode = mode
        self.eng = eng
        self.sems = sems
        self.counts = {k: 0 for k in SEMS}
        self.pool_n = {p: 0 for p in DMA_POOLS}

    def op(self, engine, fn, inc=None, inc_n=1, event=None):
        inst = None
        if self.mode == engine:
            inst = fn(self.eng)
        if inc is not None:
            assert SEMS[inc] == engine, f"sem {inc} not produced by {engine}"
            self.counts[inc] += inc_n
            if inst is not None:
                inst.then_inc(self.sems[inc], inc_n)
            if event is not None:
                val = (inc, self.counts[inc], dict(self.counts))
                if self.dry:
                    assert event not in self.events, f"dup event {event}"
                    self.events[event] = val
                else:
                    assert self.events.get(event)[:2] == val[:2], \
                        f"event drift {event}: {self.events.get(event)[:2]} vs {val[:2]}"
        else:
            assert event is None
        return inst

    def dma(self, engine, fn, pool, event=None):
        eng_of, k = DMA_POOLS[pool]
        assert eng_of == engine
        i = self.pool_n[pool]
        self.pool_n[pool] = i + 1
        sem = f"{pool}{i % k}"
        if i >= k and self.mode == engine:
            self.eng.wait_ge(self.sems[sem], 16 * (i // k))
        return self.op(engine, fn, inc=sem, inc_n=16, event=event)

    def wait(self, engine, event):
        if self.mode == engine:
            sem_name, val, _ = self.events[event]
            self.eng.wait_ge(self.sems[sem_name], val)

    def wait_all_dma(self, engine, event, pool):
        if self.mode == engine:
            _, _, snap = self.events[event]
            _, k = DMA_POOLS[pool]
            for i in range(k):
                sem = f"{pool}{i}"
                if snap.get(sem, 0) > 0:
                    self.eng.wait_ge(self.sems[sem], snap[sem])

    def drain(self, engine):
        if self.mode == engine:
            self.eng.drain()


def build_walk(cfg, t, P):
    TAB_LNEXP = cfg["tab_lnexp"]
    TAB_SIG = cfg["tab_sig"]
    TAB_GELU = cfg["tab_gelu"]
    T = cfg["T"]; Tp = cfg["Tp"]
    NPH = T // Tp
    TPT = Tp // 128
    NHF = TPT // 2
    ND = 8
    NH = 4
    SQ2I = float(1.0 / np.sqrt(2.0))
    DQS = float(256 ** -0.5)

    psH = {
        "A0": t.psA[:, 0:512], "A1": t.psA[:, 512:1024],
        "B0": t.psB[:, 0:512], "B1": t.psB[:, 512:1024],
        "C0": t.psC[:, 0:512], "C1": t.psC[:, 512:1024],
        "T0": t.psT0[:, :], "T1": t.psT1[:, :],
    }
    bank_free = {}

    def pe_claim(*regions):
        for r in regions:
            ev = bank_free.pop(r, None)
            if ev is not None:
                P.wait("pe", ev)

    def release(region, event):
        bank_free[region] = event

    def xslot(ph, h):
        return (2 * ph + h) % 4

    gload = [
        lambda g: g.dma_start(out=t.s0_bf[:], in_=t.i_slot[:]),
        lambda g: g.dma_start(out=t.b2_r[:], in_=t.i_b2[None, :]),
        lambda g: g.dma_start(out=t.w1t_sb[:],
            in_=t.i_w1t_pk.rearrange("p (k f) -> p k f", k=8)),
        lambda g: g.dma_start(out=t.w2_sb[:],
            in_=t.i_w2_pk.rearrange("p (k f) -> p k f", k=4)),
        lambda g: g.dma_start(out=t.Wp_sb[:],
            in_=t.i_Wp_pk.rearrange("p (k f) -> p k f", k=8)),
        lambda g: g.dma_start(out=t.Wu_sb[:],
            in_=t.i_Wu_pk.rearrange("p (k f) -> p k f", k=8)),
    ]
    for i, fn in enumerate(gload):
        P.dma("gp", fn, "sg", event=("wload",) if i == len(gload) - 1 else None)
    gload2 = [
        lambda g: g.dma_start(out=t.Wo_sb[:],
            in_=t.i_Wo_pk.rearrange("p (k f) -> p k f", k=8)),
        lambda g: g.dma_start(out=t.wq_sb[:],
            in_=t.i_wq_pk.rearrange("p (k f) -> p k f", k=8)),
        lambda g: g.dma_start(out=t.wk_sb[:],
            in_=t.i_wk_pk.rearrange("p (k f) -> p k f", k=8)),
    ]

    sload = [
        lambda s: s.dma_start(out=t.ident[:], in_=t.i_ident[:]),
        lambda s: s.dma_start(out=t.ones_r[:], in_=t.i_ones_r[:]),
        lambda s: s.dma_start(out=t.ones_c[:], in_=t.i_ones_c[:]),
        lambda s: s.dma_start(out=t.tau_b[:], in_=t.i_tau_b[:]),
        lambda s: s.dma_start(out=t.bpw[:], in_=t.i_bp0_pk[:]),
        lambda s: s.dma_start(out=t.bpk2[:], in_=t.i_bpk_pk[:]),
        lambda s: s.dma_start(out=t.bu_c[:], in_=t.i_bu_pk[:]),
        lambda s: s.dma_start(out=t.b1_c[:], in_=t.i_b1_pk[:]),
        lambda s: s.dma_start(out=t.bq_c[:], in_=t.i_bq_pk[:]),
        lambda s: s.dma_start(out=t.bk_c[:], in_=t.i_bk_pk[:]),
        lambda s: s.dma_start(out=t.lng_c[:], in_=t.i_lng_pk[:]),
        lambda s: s.dma_start(out=t.lnb_c[:], in_=t.i_lnb_pk[:]),
        lambda s: s.dma_start(out=t.bo1_r[:],
            in_=t.i_bo.rearrange("(a b) -> a b", a=8)),
        lambda s: s.dma_start(out=t.bo2_r[:],
            in_=t.i_bof.rearrange("(a b) -> a b", a=8)),
    ]
    for i, fn in enumerate(sload):
        P.dma("sy", fn, "sx", event=("cload",) if i == len(sload) - 1 else None)

    for ph in range(min(2, NPH)):
        for h in range(NHF):
            P.dma("sy", lambda s, ph=ph, h=h: s.dma_start(
                out=t.x_sl[:, xslot(ph, h), :, :],
                in_=t.i_x[ph * Tp + h * 256: ph * Tp + (h + 1) * 256, :]
                    .rearrange("(n p) d -> p n d", p=128)), "sx",
                event=("xload", ph, h))

    P.op("se", lambda s: _load_tab(s, TAB_SIG))
    P.wait_all_dma("se", ("cload",), "sx")
    P.op("se", lambda s: s.activation(t.invc[:], t.tau_b[:], AF.Abs),
         inc="sse", event=("tauabs",))
    P.wait("ve", ("tauabs",))
    P.wait_all_dma("ve", ("cload",), "sx")
    P.op("ve", lambda v: v.tensor_scalar(t.invc[:], t.invc[:], 0.1, None, ALU.add))
    P.drain("ve")
    P.op("ve", lambda v: v.reciprocal(t.invc[:], t.invc[:]))
    P.drain("ve")
    P.op("ve", lambda v: v.tensor_scalar(t.bpk2[:], t.bpk2[:], -2.0, None, ALU.mult),
         inc="sve", event=("bpk2",))
    P.op("ve", lambda v: v.tensor_add(t.bo1_r[:], t.bo1_r[:], t.bo2_r[:]),
         inc="sve", event=("bosum",))
    P.wait("gp", ("bosum",))
    P.dma("gp", lambda g: g.dma_start(
        out=t.bo_tot[:], in_=t.bo1_r[:]), "sg", event=("botot",))

    def stats_tile(ph, i):
        pp = ph % 2
        sl, ii = xslot(ph, i // 2), i % 2
        if i % 2 == 0:
            P.wait("se", ("xload", ph, i // 2))
        if i == 0 and ph >= 2:
            P.wait("se", ("var", ph - 2))
        P.drain("se")
        P.op("se", lambda s, sl=sl, ii=ii, i=i, pp=pp: s.activation(
            t.dump[:], t.x_sl[:, sl, ii, :], AF.Identity,
            accum_out=t.ssum[:, pp, i:i + 1]))
        P.drain("se")
        P.op("se", lambda s, sl=sl, ii=ii, i=i, pp=pp: s.activation(
            t.dump[:], t.x_sl[:, sl, ii, :], AF.Square,
            accum_out=t.ssq[:, pp, i:i + 1]),
            inc="sse" if i == TPT - 1 else None,
            event=("sums", ph) if i == TPT - 1 else None)

    def var_chain(ph):
        pp = ph % 2
        smn, svar, stmp = (t.smn[:, pp, :], t.svar[:, pp, :], t.stmp[:, pp, :])
        P.wait("ve", ("sums", ph))
        if ph >= 2:
            P.wait("ve", ("rstd", ph - 2))
        P.op("ve", lambda v, smn=smn, pp=pp: v.tensor_scalar(
            smn, t.ssum[:, pp, :], 1.0 / 1024.0, None, ALU.mult))
        P.op("ve", lambda v, svar=svar, pp=pp: v.tensor_scalar(
            svar, t.ssq[:, pp, :], 1.0 / 1024.0, 1e-5, ALU.mult, ALU.add))
        P.drain("ve")
        P.op("ve", lambda v, stmp=stmp, smn=smn: v.tensor_mul(stmp, smn, smn))
        P.drain("ve")
        P.op("ve", lambda v, svar=svar, stmp=stmp: v.tensor_sub(svar, svar, stmp),
             inc="sve", event=("var", ph))

    def rstd_window(ph):
        pp = ph % 2
        P.wait("se", ("var", ph))
        P.op("se", lambda s: _load_tab(s, TAB_LNEXP))
        P.op("se", lambda s, pp=pp: s.activation(
            t.stmp[:, pp, :], t.svar[:, pp, :], AF.Ln))
        P.drain("se")
        P.op("se", lambda s, pp=pp: s.activation(
            t.sr[:, pp, :], t.stmp[:, pp, :], AF.Exp, scale=-0.5),
            inc="sse", event=("rstd", ph))
        P.op("se", lambda s: _load_tab(s, TAB_SIG))

    def window(ph):
        if ph < 1:
            return
        P.wait("se", ("max8", ph - 1))
        if ph >= 2:
            P.wait("se", ("alpha", ph - 2))
        P.op("se", lambda s: _load_tab(s, TAB_LNEXP))
        for i in range(TPT):
            P.op("se", lambda s, i=i: s.activation(
                t.e_al[:, i, :], t.l_sb[:, i, :], AF.Exp,
                bias=t.m1n[:, i:i + 1]),
                inc="sse" if i == TPT - 1 else None,
                event=("exps", ph - 1) if i == TPT - 1 else None)

    def alpha_fin(ph):
        P.wait("ve", ("exps", ph))
        if ph >= 1:
            P.wait("ve", ("slotmm", ph - 1))
        for i in range(TPT):
            P.op("ve", lambda v, i=i: v.scalar_tensor_tensor(
                t.e_al[:, i, :], t.l_sb[:, i, :], t.m18[:, i, 7:8],
                t.e_al[:, i, :], ALU.is_ge, ALU.mult))
        P.drain("ve")
        for i in range(TPT):
            P.op("ve", lambda v, i=i: v.reduce_sum(
                t.alns[:, i:i + 1], t.e_al[:, i, :], axis=AX.X))
        P.drain("ve")
        P.op("ve", lambda v: v.reciprocal(t.alns[:], t.alns[:]))
        P.drain("ve")
        for i in range(TPT):
            P.op("ve", lambda v, i=i: v.tensor_scalar(
                t.al_bf[:, i, :], t.e_al[:, i, :], t.alns[:, i:i + 1], None,
                ALU.mult),
                inc="sve" if i == TPT - 1 else None,
                event=("alpha", ph) if i == TPT - 1 else None)

    def wo_block(ph):
        if ph == 0:
            P.wait_all_dma("pe", ("wload2",), "sg")
        regs = [("A0", "A1", t.psA), ("B0", "B1", t.psB), ("C0", "C1", t.psC),
                ("A0", "A1", t.psA)]

        def wo_tile_k(i, k, ps):
            for f in range(2):
                last = (f == 1 and k == ND - 1)
                P.op("pe", lambda e, i=i, f=f, k=k, ps=ps: e.matmul(
                    ps[:, f * 512:(f + 1) * 512],
                    t.ys_bf[:, k, i * 128:(i + 1) * 128],
                    t.Wo_sb[:, k, f * 512:(f + 1) * 512],
                    start=(k == 0), stop=(k == ND - 1)),
                    inc="spe" if last else None,
                    event=("womm", ph, i) if last else None)

        pe_claim(regs[0][0], regs[0][1])
        for k in range(ND - 1):
            P.wait("pe", ("ysd", ph, k))
            wo_tile_k(0, k, regs[0][2])
        pe_claim(regs[1][0], regs[1][1])
        for k in range(ND - 1):
            wo_tile_k(1, k, regs[1][2])
        P.wait("pe", ("ysd", ph, ND - 1))
        wo_tile_k(0, ND - 1, regs[0][2])
        wo_tile_k(1, ND - 1, regs[1][2])
        for i in range(TPT):
            r0, r1, ps = regs[i]
            if i >= 2:
                pe_claim(r0, r1)
                for k in range(ND):
                    wo_tile_k(i, k, ps)
            P.wait("ve", ("womm", ph, i))
            if i == 0:
                P.wait("ve", ("sums", ph))
            sl, ii = xslot(ph, i // 2), i % 2
            P.op("ve", lambda v, ps=ps, sl=sl, ii=ii: v.tensor_add(
                t.x_sl[:, sl, ii, :], ps[:, 0:1024], t.x_sl[:, sl, ii, :]),
                inc="sve", event=("part", ph, i))
            release(r0, ("part", ph, i))
            release(r1, ("part", ph, i))
        for h in range(NHF):
            P.wait("sy", ("part", ph, 2 * h + 1))
            P.dma("sy", lambda s, ph=ph, h=h: s.dma_start(
                out=t.o_out[ph * Tp + h * 256: ph * Tp + (h + 1) * 256, :]
                    .rearrange("(n p) d -> p n d", p=128),
                in_=t.x_sl[:, xslot(ph, h), :, :]), "sp",
                event=("pstore", ph, h))

    def slot_pe(ph):
        P.wait("pe", ("alpha", ph))
        pe_claim("B0", "B1", "C0", "T1")
        for i in range(TPT):
            P.op("pe", lambda e, i=i, pp=ph % 2: e.matmul(
                t.psB[0:64, 0:512], t.al_bf[:, i, :], t.xn_tm[:, pp, i, 0:512],
                start=(i == 0), stop=(i == TPT - 1)))
            P.op("pe", lambda e, i=i, pp=ph % 2: e.matmul(
                t.psB[0:64, 512:1024], t.al_bf[:, i, :],
                t.xn_tm[:, pp, i, 512:1024],
                start=(i == 0), stop=(i == TPT - 1)))
            P.op("pe", lambda e, i=i: e.matmul(
                t.psC[0:64, 0:1], t.al_bf[:, i, :], t.ones_c[:, 0:1],
                start=(i == 0), stop=(i == TPT - 1)))
        for i in range(TPT):
            P.op("pe", lambda e, i=i: e.transpose(
                t.psT1[0:64, i * 128:(i + 1) * 128],
                t.al_bf[:, i, :], t.ident[:]),
                inc="spe" if i == TPT - 1 else None,
                event=("slotmm", ph) if i == TPT - 1 else None)

    def slot_ve(ph):
        P.wait("ve", ("slotmm", ph))
        if ph == 0:
            P.op("ve", lambda v: v.tensor_copy(t.sl_acc[:, 0:512],
                                               t.psB[0:64, 0:512]))
            P.op("ve", lambda v: v.tensor_copy(t.sl_acc[:, 512:1024],
                                               t.psB[0:64, 512:1024]))
            P.op("ve", lambda v: v.tensor_copy(t.den_acc[:], t.psC[0:64, 0:1]))
        else:
            P.op("ve", lambda v: v.tensor_add(t.sl_acc[:, 0:512],
                                              t.sl_acc[:, 0:512], t.psB[0:64, 0:512]))
            P.op("ve", lambda v: v.tensor_add(t.sl_acc[:, 512:1024],
                                              t.sl_acc[:, 512:1024],
                                              t.psB[0:64, 512:1024]))
            P.op("ve", lambda v: v.tensor_add(t.den_acc[:], t.den_acc[:],
                                              t.psC[0:64, 0:1]))
        P.op("ve", lambda v, ph=ph: v.tensor_copy(
            t.alpha_T[0:64, ph * Tp:(ph + 1) * Tp], t.psT1[0:64, 0:Tp]),
            inc="sve", event=("slotev", ph))
        for r in ("B0", "B1", "C0", "T1"):
            release(r, ("slotev", ph))

    def ln_apply(ph):
        pp = ph % 2
        P.wait("ve", ("rstd", ph))
        if ph >= 2:
            P.wait("ve", ("slotmm", ph - 2))
        P.op("ve", lambda v, pp=pp: v.tensor_mul(
            t.smrn[:, pp, :], t.smn[:, pp, :], t.sr[:, pp, :]))
        P.drain("ve")
        P.op("ve", lambda v, pp=pp: v.tensor_scalar(
            t.smrn[:, pp, :], t.smrn[:, pp, :], -1.0, None, ALU.mult))
        P.drain("ve")
        for i in range(TPT):
            sl, ii = xslot(ph, i // 2), i % 2
            P.op("ve", lambda v, i=i, sl=sl, ii=ii, pp=pp: v.tensor_scalar(
                t.xn_tm[:, pp, i, :], t.x_sl[:, sl, ii, :],
                t.sr[:, pp, i:i + 1], t.smrn[:, pp, i:i + 1],
                ALU.mult, ALU.add),
                inc="sve" if i == TPT - 1 else None,
                event=("ln", ph) if i == TPT - 1 else None)

    def xn_transposes(ph):
        P.wait("pe", ("ln", ph))
        if ph >= 1:
            pass
        for k in range(ND):
            reg = "T0" if k % 2 == 0 else "T1"
            psT = psH[reg]
            pe_claim(reg)
            for i in range(TPT):
                P.op("pe", lambda e, k=k, i=i, psT=psT, pp=ph % 2: e.transpose(
                    psT[:, i * 128:(i + 1) * 128],
                    t.xn_tm[:, pp, i, k * 128:(k + 1) * 128], t.ident[:]),
                    inc="spe" if i == TPT - 1 else None,
                    event=("xnT", ph, k) if i == TPT - 1 else None)
            P.wait("ve", ("xnT", ph, k))
            if ph >= 1 and k < 2:
                P.wait("ve", ("wku", ph - 1, ND - 1))
                P.wait("ve", ("hmm", ph - 1, NH - 1))
            P.op("ve", lambda v, k=k, psT=psT: v.tensor_copy(
                t.xn_f[:, k, :], psT[:, 0:Tp]), inc="sve", event=("xnf", ph, k))
            release(reg, ("xnf", ph, k))

    def wku_loop(ph):
        P.wait("pe", ("xnf", ph, ND - 1))
        for j in range(ND):
            par = j % 2
            if par == 0:
                rw, rk, ru = "A0", "A1", "B0"
            else:
                rw, rk, ru = "B1", "C0", "C1"
            psw, psk, psu = (psH[rw][:, 0:Tp], psH[rk][:, 0:Tp],
                            psH[ru][:, 0:Tp])
            pe_claim(rw, rk, ru)
            for k in range(ND):
                P.op("pe", lambda e, j=j, k=k, psw=psw: e.matmul(
                    psw, t.Wp_sb[:, k, j * 128:(j + 1) * 128], t.xn_f[:, k, :],
                    start=(k == 0), stop=(k == ND - 1)))
            for k in range(ND):
                last = (k == ND - 1)
                P.op("pe", lambda e, j=j, k=k, psk=psk: e.matmul(
                    psk, t.Wp_sb[:, k, 1024 + j * 128:1024 + (j + 1) * 128],
                    t.xn_f[:, k, :], start=(k == 0), stop=(k == ND - 1)),
                    inc="spe" if last else None,
                    event=("wk", ph, j) if last else None)
            for k in range(ND):
                last = (k == ND - 1)
                P.op("pe", lambda e, j=j, k=k, psu=psu: e.matmul(
                    psu, t.Wu_sb[:, k, j * 128:(j + 1) * 128], t.xn_f[:, k, :],
                    start=(k == 0), stop=(k == ND - 1)),
                    inc="spe" if last else None,
                    event=("wku", ph, j) if last else None)

            if j == 0:
                P.op("se", lambda s: _load_tab(s, TAB_SIG))
            P.wait("se", ("wk", ph, j))
            if j >= 2:
                P.wait("se", ("absc", ph, j - 2))
            elif ph >= 1:
                P.wait("se", ("absc", ph - 1, j + 6))
            P.drain("se")
            P.op("se", lambda s, j=j, psw=psw: s.activation(
                t.g_sb[:, j % 2, :], psw, AF.Sigmoid, bias=t.bpw[:, j:j + 1]))
            P.op("se", lambda s, j=j, psk=psk: s.activation(
                t.s2_sb[:, j % 2, :], psk, AF.Sigmoid, bias=t.bpk2[:, j:j + 1],
                scale=-2.0), inc="sse", event=("gs", ph, j))
            release(rw, ("gs", ph, j))
            release(rk, ("gs", ph, j))
            if ph + 1 < NPH and j < TPT:
                stats_tile(ph + 1, j)
            if ph + 1 < NPH and j == TPT:
                rstd_window(ph + 1)

            P.wait("ve", ("gs", ph, j))
            P.wait("ve", ("wku", ph, j))
            if ph >= 1 and j == 0:
                P.wait("ve", ("womm", ph - 1, TPT - 1))
            pp = j % 2
            P.drain("ve")
            P.op("ve", lambda v, pp=pp: v.scalar_tensor_tensor(
                t.a_sb[:], t.g_sb[:, pp, :], -2.0, t.s2_sb[:, pp, :],
                ALU.mult, ALU.mult))
            P.op("ve", lambda v, pp=pp, psu=psu, j=j: v.scalar_tensor_tensor(
                t.b_sb[:], psu, t.bu_c[:, j:j + 1], t.g_sb[:, pp, :],
                ALU.add, ALU.mult), inc="sve", event=("absc", ph, j))
            release(ru, ("absc", ph, j))
            P.drain("ve")
            P.op("ve", lambda v, pp=pp: v.tensor_scalar(
                t.a_sb[:], t.a_sb[:], 1.0, None, ALU.add))
            P.drain("ve")
            init = 0.0 if ph == 0 else t.ys_bf[:, j, Tp - 1:Tp]
            P.op("ve", lambda v, j=j, init=init: v.tensor_tensor_scan(
                t.ys_bf[:, j, :], t.a_sb[:], t.b_sb[:], init,
                ALU.mult, ALU.add),
                inc="sve", event=("ysd", ph, j))
            if ph + 1 < NPH and j == TPT - 1:
                var_chain(ph + 1)
            if ph + 1 < NPH and j == ND - 1:
                ln_apply(ph + 1)

    def h_logits(ph):
        P.wait("pe", ("xnf", ph, ND - 1))
        hr = ["A0", "A1", "B0", "B1"]
        for jh in range(NH):
            reg = hr[jh]
            ps = psH[reg][:, 0:Tp]
            pe_claim(reg)
            for k in range(ND):
                last = (k == ND - 1)
                P.op("pe", lambda e, jh=jh, k=k, ps=ps: e.matmul(
                    ps, t.w1t_sb[:, k, jh * 128:(jh + 1) * 128], t.xn_f[:, k, :],
                    start=(k == 0), stop=(k == ND - 1)),
                    inc="spe" if last else None,
                    event=("hmm", ph, jh) if last else None)
            P.wait("se", ("hmm", ph, jh))
            if jh == 0:
                if ph >= 1:
                    P.wait("se", ("lmm", ph - 1))
                P.op("se", lambda s: _load_tab(s, TAB_GELU))
            P.drain("se")
            P.op("se", lambda s, jh=jh, ps=ps: s.activation(
                t.h_f[:, jh, :], ps, AF.Gelu, bias=t.bias_h[:, jh:jh + 1]),
                inc="sse", event=("hf", ph, jh))
            release(reg, ("hf", ph, jh))
        pe_claim("C0")
        for i in range(TPT):
            for kh in range(NH):
                if i == 0:
                    P.wait("pe", ("hf", ph, kh))
                P.op("pe", lambda e, i=i, kh=kh: e.matmul(
                    t.psC[:, i * 64:(i + 1) * 64],
                    t.h_f[:, kh, i * 128:(i + 1) * 128], t.w2_sb[:, kh, :],
                    start=(kh == 0), stop=False))
            last = (i == TPT - 1)
            P.op("pe", lambda e, i=i: e.matmul(
                t.psC[:, i * 64:(i + 1) * 64], t.ones_r[:], t.b2_r[:],
                start=False, stop=True),
                inc="spe" if last else None,
                event=("lmm", ph) if last else None)
        P.wait("se", ("lmm", ph))
        if ph >= 1:
            P.wait("se", ("alpha", ph - 1))
        P.drain("se")
        P.op("se", lambda s: s.activation(
            t.l_sb[:], t.psC[:, 0:TPT * 64], AF.Identity, scale=t.invc[:]),
            inc="sse", event=("lev", ph))
        release("C0", ("lev", ph))
        P.wait("ve", ("lev", ph))
        for i in range(TPT):
            P.op("ve", lambda v, i=i: v.max(out=t.m18[:, i, :],
                                            in_=t.l_sb[:, i, :]))
        P.drain("ve")
        P.op("ve", lambda v: v.tensor_scalar(
            t.m1n[:], t.m18[:, :, 0], -1.0, None, ALU.mult),
            inc="sve", event=("max8", ph))

    def mid_weight_loads():
        def ihl(jt, h):
            P.dma("gp", lambda g, jt=jt, h=h: g.dma_start(
                out=t.WT[:, jt, h * 4:(h + 1) * 4],
                in_=t.i_Wih_pk.rearrange("p (jt k f) -> p jt k f", jt=6, k=8)
                    [:, jt, h * 4:(h + 1) * 4]), "sg", event=("ihl", jt, h))

        def whl(jt, h):
            P.dma("gp", lambda g, jt=jt, h=h: g.dma_start(
                out=t.whh_pp[:, jt % 2, h * 4:(h + 1) * 4],
                in_=t.i_Whh_pk.rearrange("p (jt k f) -> p jt k f", jt=6, k=8)
                    [:, jt, h * 4:(h + 1) * 4]), "sg", event=("whl", jt, h))

        P.wait("gp", ("wku", NPH - 1, ND - 1))
        P.wait_all_dma("gp", ("pstore", NPH - 1, NHF - 1), "sp")
        P.wait("gp", ("slotev", NPH - 1))
        P.wait("gp", ("exps", NPH - 1))
        P.wait("gp", ("slotmm", NPH - 1))
        P.dma("gp", lambda g: g.dma_start(out=t.s0_f32[:], in_=t.i_slot[:]), "sg")
        P.dma("gp", lambda g: g.dma_start(out=t.bih_r[:], in_=t.i_bih[None, :]), "sg")
        P.dma("gp", lambda g: g.dma_start(out=t.bhh_r[:], in_=t.i_bhh[None, :]), "sg")
        P.dma("gp", lambda g: g.dma_start(out=t.bv_r[:], in_=t.i_bv[None, :]), "sg")
        P.dma("gp", lambda g: g.dma_start(out=t.bvp_r[:], in_=t.i_bvp[None, :]), "sg",
              event=("brows",))
        P.wait("gp", ("womm", NPH - 1, TPT - 1))
        for jt in range(2):
            ihl(jt, 0)
            whl(jt, 0)
            ihl(jt, 1)
            whl(jt, 1)
        for jt in range(2, 6):
            ihl(jt, 0)
            ihl(jt, 1)

    for i in range(TPT):
        stats_tile(0, i)
    var_chain(0)
    rstd_window(0)
    ln_apply(0)

    P.wait_all_dma("pe", ("cload",), "sx")
    P.wait_all_dma("pe", ("wload",), "sg")
    pe_claim("T0")
    for k in range(ND):
        P.op("pe", lambda e, k=k: e.transpose(
            t.psT0[:, k * 64:(k + 1) * 64],
            t.s0_bf[0:64, k * 128:(k + 1) * 128], t.ident[0:64, 0:64]),
            inc="spe" if k == ND - 1 else None,
            event=("s0T",) if k == ND - 1 else None)
    P.wait("ve", ("s0T",))
    for k in range(ND):
        P.op("ve", lambda v, k=k: v.tensor_copy(
            t.si0f[:, k, :], t.psT0[:, k * 64:(k + 1) * 64]),
            inc="sve" if k == ND - 1 else None,
            event=("si0f",) if k == ND - 1 else None)
    release("T0", ("si0f",))
    P.drain("ve")
    for k in range(ND):
        P.op("ve", lambda v, k=k: v.reduce_sum(
            t.sm_f[:, k:k + 1], t.si0f[:, k, :], axis=AX.X))
    P.drain("ve")
    P.op("ve", lambda v: v.tensor_scalar(t.sm_fb[:], t.sm_f[:], 1.0 / 64.0, None,
                                         ALU.mult), inc="sve", event=("smf",))
    regs4 = ["A0", "A1", "B0", "B1"]
    pe_claim(*regs4)
    ps4 = [psH[r] for r in regs4]
    for k in range(ND):
        if k >= 2:
            P.wait("gp", ("smwk", k - 2))
        P.dma("gp", lambda g, k=k: g.dma_start(
            out=t.w1s[:, k % 2, :],
            in_=t.i_w1[1024 + k * 128:1024 + (k + 1) * 128, :]), "sg",
            event=("w1s", k))
        P.wait("pe", ("w1s", k))
        if k == 0:
            P.wait("pe", ("smf",))
        for jh in range(NH):
            P.op("pe", lambda e, jh=jh, k=k: e.matmul(
                ps4[jh][:, 0:1], t.w1s[:, k % 2, jh * 128:(jh + 1) * 128],
                t.sm_fb[:, k:k + 1], start=(k == 0), stop=(k == ND - 1)),
                inc="spe" if jh == NH - 1 else None,
                event=("smwk", k) if jh == NH - 1 else None)
    P.wait("ve", ("smwk", ND - 1))
    for jh in range(NH):
        P.op("ve", lambda v, jh=jh: v.tensor_add(
            t.bias_h[:, jh:jh + 1], t.b1_c[:, jh:jh + 1], ps4[jh][:, 0:1]),
            inc="sve" if jh == NH - 1 else None,
            event=("biash",) if jh == NH - 1 else None)
    for r in regs4:
        release(r, ("biash",))
    for i, fn in enumerate(gload2):
        P.dma("gp", fn, "sg", event=("wload2",) if i == len(gload2) - 1 else None)

    for ph in range(NPH + 1):
        window(ph)
        if ph >= 1:
            alpha_fin(ph - 1)
        if ph >= 1:
            wo_block(ph - 1)
            if ph + 1 < NPH:
                for h in range(NHF):
                    P.wait("sy", ("pstore", ph - 1, h))
                    P.dma("sy", lambda s, ph=ph, h=h: s.dma_start(
                        out=t.x_sl[:, xslot(ph + 1, h), :, :],
                        in_=t.i_x[(ph + 1) * Tp + h * 256:
                                  (ph + 1) * Tp + (h + 1) * 256, :]
                            .rearrange("(n p) d -> p n d", p=128)), "sx",
                        event=("xload", ph + 1, h))
        if ph >= 1:
            slot_pe(ph - 1)
            slot_ve(ph - 1)
        if ph < NPH:
            xn_transposes(ph)
            h_logits(ph)
            wku_loop(ph)
        if ph == NPH:
            mid_weight_loads()

    P.wait("ve", ("slotev", NPH - 1))
    P.wait("ve", ("exps", NPH - 1))
    P.op("ve", lambda v: v.tensor_scalar(t.dent[:], t.den_acc[:], 1e-8, None,
                                         ALU.add))
    P.drain("ve")
    P.op("ve", lambda v: v.reciprocal(t.dent[:], t.dent[:]))
    P.drain("ve")
    P.op("ve", lambda v: v.tensor_scalar(t.si_bf[:], t.sl_acc[:], t.dent[:],
                                         None, ALU.mult), inc="sve",
         event=("si",))
    P.wait("pe", ("si",))
    pe_claim("T0")
    for k in range(ND):
        P.op("pe", lambda e, k=k: e.transpose(
            t.psT0[:, k * 64:(k + 1) * 64], t.si_bf[0:64, k * 128:(k + 1) * 128],
            t.ident[0:64, 0:64]),
            inc="spe" if k == ND - 1 else None,
            event=("siT",) if k == ND - 1 else None)
    P.wait("ve", ("siT",))
    for k in range(ND):
        P.op("ve", lambda v, k=k: v.tensor_copy(
            t.si_f[:, k, :], t.psT0[:, k * 64:(k + 1) * 64]),
            inc="sve" if k == ND - 1 else None,
            event=("sif",) if k == ND - 1 else None)
    release("T0", ("sif",))

    whhbuf = t.whh_pp
    P.wait("pe", ("sif",))
    P.wait_all_dma("pe", ("brows",), "sg")
    mregs = ["A0", "A1", "B0", "B1", "C0", "C1"]
    for jt in range(6):
        reg = mregs[jt]
        ps = psH[reg]
        pe_claim(reg)
        for k in range(ND):
            if k % 4 == 0:
                P.wait("pe", ("ihl", jt, k // 4))
            P.op("pe", lambda e, k=k, jt=jt, ps=ps: e.matmul(
                ps[0:64, :], t.si_f[:, k, :], t.WT[:, jt, k, :],
                start=(k == 0), stop=False))
        P.op("pe", lambda e, jt=jt, ps=ps: e.matmul(
            ps[0:64, :], t.ones_r[0:1, 0:64], t.bih_r[:, jt * 512:(jt + 1) * 512],
            start=False, stop=True), inc="spe", event=("gmm", 0, jt))
        P.wait("se", ("gmm", 0, jt))
        P.op("se", lambda s, jt=jt, ps=ps: s.activation(
            t.gi_sb[:, jt * 512:(jt + 1) * 512], ps[0:64, :], AF.Identity),
            inc="sse", event=("gev", 0, jt))
        release(reg, ("gev", 0, jt))
        reg = mregs[jt]
        ps = psH[reg]
        pe_claim(reg)
        for k in range(ND):
            if k % 4 == 0:
                P.wait("pe", ("whl", jt, k // 4))
            P.op("pe", lambda e, k=k, jt=jt, ps=ps: e.matmul(
                ps[0:64, :], t.si0f[:, k, :],
                t.whh_pp[:, jt % 2, k, :], start=(k == 0), stop=False))
        P.op("pe", lambda e, jt=jt, ps=ps: e.matmul(
            ps[0:64, :], t.ones_r[0:1, 0:64], t.bhh_r[:, jt * 512:(jt + 1) * 512],
            start=False, stop=True), inc="spe", event=("ghmm", jt))
        if jt + 2 < 6:
            P.wait("gp", ("ghmm", jt))
            for h in range(2):
                P.dma("gp", lambda g, jt=jt, h=h: g.dma_start(
                    out=t.whh_pp[:, jt % 2, h * 4:(h + 1) * 4],
                    in_=t.i_Whh_pk.rearrange("p (jt k f) -> p jt k f",
                                             jt=6, k=8)
                        [:, jt + 2, h * 4:(h + 1) * 4]), "sg",
                    event=("whl", jt + 2, h))
        if jt < 4:
            P.wait("ve", ("ghmm", jt))
            P.wait("ve", ("gev", 0, jt))
            P.op("ve", lambda v, jt=jt, ps=ps: v.tensor_add(
                t.gi_sb[:, jt * 512:(jt + 1) * 512],
                t.gi_sb[:, jt * 512:(jt + 1) * 512], ps[0:64, :]),
                inc="sve", event=("gev", 1, jt))
        else:
            P.wait("ve", ("ghmm", jt))
            P.op("ve", lambda v, jt=jt, ps=ps: v.tensor_copy(
                t.gh_n[:, (jt - 4) * 512:(jt - 3) * 512], ps[0:64, :]),
                inc="sve", event=("gev", 1, jt))
        release(reg, ("gev", 1, jt))
    P.dma("gp", lambda g: g.dma_start(
        out=t.wv_sb[:], in_=t.i_wv_pk.rearrange("p (k f) -> p k f", k=8)), "sg")
    P.dma("gp", lambda g: g.dma_start(
        out=t.wvp_sb[:], in_=t.i_wvp_pk.rearrange("p (k f) -> p k f", k=8)), "sg",
        event=("w2load",))
    P.wait("gp", ("svmm",))
    P.dma("gp", lambda g: g.dma_start(
        out=t.wo2_sb[:], in_=t.i_wo2_pk.rearrange("p (k f) -> p k f", k=8)), "sg",
        event=("w3load",))

    P.wait("se", ("gev", 1, 3))
    P.op("se", lambda s: _load_tab(s, TAB_SIG))
    P.op("se", lambda s: s.activation(t.r_g[:], t.gi_sb[:, 0:1024], AF.Sigmoid))
    P.op("se", lambda s: s.activation(t.z_g[:], t.gi_sb[:, 1024:2048],
                                      AF.Sigmoid), inc="sse", event=("rz",))
    P.wait("ve", ("rz",))
    P.drain("ve")
    P.op("ve", lambda v: v.tensor_mul(t.t64a[:], t.r_g[:], t.gh_n[:, 0:1024]))
    P.drain("ve")
    P.op("ve", lambda v: v.tensor_add(t.t64a[:], t.t64a[:],
                                      t.gi_sb[:, 2048:3072]),
         inc="sve", event=("n_in",))
    P.wait("se", ("n_in",))
    P.op("se", lambda s: s.activation(t.n_g[:], t.t64a[:], AF.Tanh),
         inc="sse", event=("n_g",))
    P.wait("ve", ("n_g",))
    P.op("ve", lambda v: v.tensor_sub(t.t64a[:], t.s0_f32[:], t.n_g[:]))
    P.drain("ve")
    P.op("ve", lambda v: v.tensor_mul(t.t64a[:], t.t64a[:], t.z_g[:]))
    P.drain("ve")
    P.op("ve", lambda v: v.tensor_add(t.snew[:], t.n_g[:], t.t64a[:]))
    P.drain("ve")
    P.op("ve", lambda v: v.tensor_copy(t.snew_bf[:], t.snew[:]), inc="sve",
         event=("snew",))
    P.wait("pe", ("snew",))
    pe_claim("T0")
    for k in range(ND):
        P.op("pe", lambda e, k=k: e.transpose(
            t.psT0[:, k * 64:(k + 1) * 64],
            t.snew_bf[0:64, k * 128:(k + 1) * 128], t.ident[0:64, 0:64]),
            inc="spe" if k == ND - 1 else None,
            event=("snT",) if k == ND - 1 else None)
    P.wait("ve", ("snT",))
    for k in range(ND):
        P.op("ve", lambda v, k=k: v.tensor_copy(
            t.snew_f[:, k, :], t.psT0[:, k * 64:(k + 1) * 64]),
            inc="sve" if k == ND - 1 else None,
            event=("snf",) if k == ND - 1 else None)
    release("T0", ("snf",))
    P.drain("ve")
    P.op("ve", lambda v: v.tensor_scalar(t.bq_c[:], t.bq_c[:], DQS, None,
                                         ALU.mult), inc="sve", event=("bqs",))

    P.wait("pe", ("snf",))
    P.wait("pe", ("bqs",))
    pe_claim("A0", "A1")
    for m in range(2):
        for dqc in range(2):
            for k in range(ND):
                last = (m == 1 and dqc == 1 and k == ND - 1)
                P.op("pe", lambda e, m=m, dqc=dqc, k=k: e.matmul(
                    (t.psA[:, dqc * 64:(dqc + 1) * 64] if m == 0
                     else t.psA[:, 512 + dqc * 64:512 + (dqc + 1) * 64]),
                    (t.wq_sb if m == 0 else t.wk_sb)[:, k, dqc * 128:(dqc + 1) * 128],
                    t.snew_f[:, k, :], start=(k == 0), stop=(k == ND - 1)),
                    inc="spe" if last else None,
                    event=("qkmm",) if last else None)
    P.wait("se", ("qkmm",))
    for dqc in range(2):
        P.op("se", lambda s, dqc=dqc: s.activation(
            t.q_bf[:, dqc, :], t.psA[:, dqc * 64:(dqc + 1) * 64], AF.Identity,
            bias=t.bq_c[:, dqc:dqc + 1], scale=DQS))
        P.op("se", lambda s, dqc=dqc: s.activation(
            t.k_bf[:, dqc, :], t.psA[:, 512 + dqc * 64:512 + (dqc + 1) * 64],
            AF.Identity, bias=t.bk_c[:, dqc:dqc + 1]),
            inc="sse" if dqc == 1 else None,
            event=("qk",) if dqc == 1 else None)
    release("A0", ("qk",))
    release("A1", ("qk",))

    P.wait("pe", ("qk",))
    pe_claim("B0")
    for dqc in range(2):
        P.op("pe", lambda e, dqc=dqc: e.matmul(
            t.psB[0:64, 0:64], t.q_bf[:, dqc, :], t.k_bf[:, dqc, :],
            start=(dqc == 0), stop=(dqc == 1)),
            inc="spe" if dqc == 1 else None,
            event=("scmm",) if dqc == 1 else None)
    P.wait("se", ("scmm",))
    P.op("se", lambda s: s.activation(t.sc_sb[:], t.psB[0:64, 0:64], AF.Identity),
         inc="sse", event=("scev",))
    release("B0", ("scev",))
    P.wait("ve", ("scev",))
    P.op("ve", lambda v: v.reduce_max(t.scm[:], t.sc_sb[:], axis=AX.X))
    P.drain("ve")
    P.op("ve", lambda v: v.tensor_scalar(t.scm[:], t.scm[:], -1.0, None,
                                         ALU.mult), inc="sve", event=("scmax",))
    P.wait("se", ("scmax",))
    P.op("se", lambda s: _load_tab(s, TAB_LNEXP))
    P.drain("se")
    P.op("se", lambda s: s.activation(t.sc_sb[:], t.sc_sb[:], AF.Exp,
                                      bias=t.scm[:], accum_out=t.scs[:]),
         inc="sse", event=("scexp",))
    P.wait("ve", ("scexp",))
    P.op("ve", lambda v: v.reciprocal(t.scs[:], t.scs[:]))
    P.drain("ve")
    P.op("ve", lambda v: v.tensor_scalar(t.attn_bf[:], t.sc_sb[:], t.scs[:],
                                         None, ALU.mult), inc="sve",
         event=("attn",))
    P.wait("pe", ("attn",))
    pe_claim("T0")
    P.op("pe", lambda e: e.transpose(t.psT0[0:64, 0:64], t.attn_bf[0:64, 0:64],
                                     t.ident[0:64, 0:64]),
         inc="spe", event=("atT",))
    P.wait("ve", ("atT",))
    P.op("ve", lambda v: v.tensor_copy(t.attnT_bf[:], t.psT0[0:64, 0:64]),
         inc="sve", event=("atTe",))
    release("T0", ("atTe",))

    P.wait_all_dma("pe", ("w2load",), "sg")
    pe_claim("B0", "B1")
    for jt in range(2):
        ps = psH["B0" if jt == 0 else "B1"]
        for k in range(ND):
            P.op("pe", lambda e, jt=jt, k=k, ps=ps: e.matmul(
                ps[0:64, :], t.snew_f[:, k, :],
                t.wv_sb[:, k, jt * 512:(jt + 1) * 512],
                start=(k == 0), stop=False))
        P.op("pe", lambda e, jt=jt, ps=ps: e.matmul(
            ps[0:64, :], t.ones_r[0:1, 0:64], t.bv_r[:, jt * 512:(jt + 1) * 512],
            start=False, stop=True),
            inc="spe" if jt == 1 else None,
            event=("vmm",) if jt == 1 else None)
    P.wait("se", ("vmm",))
    P.op("se", lambda s: s.activation(t.v_bf[:, 0:512], t.psB[0:64, 0:512],
                                      AF.Identity))
    P.op("se", lambda s: s.activation(t.v_bf[:, 512:1024],
                                      t.psB[0:64, 512:1024], AF.Identity),
         inc="sse", event=("vev",))
    release("B0", ("vev",))
    release("B1", ("vev",))

    P.wait("pe", ("atTe",))
    P.wait("pe", ("vev",))
    pe_claim("C0", "C1")
    for jt in range(2):
        ps = psH["C0" if jt == 0 else "C1"]
        P.op("pe", lambda e, jt=jt, ps=ps: e.matmul(
            ps[0:64, :], t.attnT_bf[:], t.v_bf[0:64, jt * 512:(jt + 1) * 512],
            start=True, stop=True),
            inc="spe" if jt == 1 else None,
            event=("cmm",) if jt == 1 else None)
    P.wait("ve", ("cmm",))
    P.op("ve", lambda v: v.tensor_add(t.y_sb[:, 0:512], t.psC[0:64, 0:512],
                                      t.snew[:, 0:512]))
    P.op("ve", lambda v: v.tensor_add(t.y_sb[:, 512:1024],
                                      t.psC[0:64, 512:1024],
                                      t.snew[:, 512:1024]),
         inc="sve", event=("y",))
    release("C0", ("y",))
    release("C1", ("y",))

    P.wait("se", ("y",))
    P.drain("se")
    P.op("se", lambda s: s.activation(t.dump[0:64, :], t.y_sb[:], AF.Identity,
                                      accum_out=t.ysum[:]))
    P.drain("se")
    P.op("se", lambda s: s.activation(t.dump[0:64, :], t.y_sb[:], AF.Square,
                                      accum_out=t.yssq[:]),
         inc="sse", event=("ysums",))
    P.wait("ve", ("ysums",))
    P.op("ve", lambda v: v.tensor_scalar(t.ymu[:], t.ysum[:], 1.0 / 1024.0,
                                         None, ALU.mult))
    P.op("ve", lambda v: v.tensor_scalar(t.yvar[:], t.yssq[:], 1.0 / 1024.0,
                                         1e-5, ALU.mult, ALU.add))
    P.drain("ve")
    P.op("ve", lambda v: v.tensor_mul(t.ytmp[:], t.ymu[:], t.ymu[:]))
    P.drain("ve")
    P.op("ve", lambda v: v.tensor_sub(t.yvar[:], t.yvar[:], t.ytmp[:]),
         inc="sve", event=("yvar",))
    P.wait("se", ("yvar",))
    P.op("se", lambda s: s.activation(t.ytmp[:], t.yvar[:], AF.Ln))
    P.drain("se")
    P.op("se", lambda s: s.activation(t.yrst[:], t.ytmp[:], AF.Exp, scale=-0.5),
         inc="sse", event=("yrst",))
    P.wait("ve", ("yrst",))
    P.op("ve", lambda v: v.tensor_mul(t.ymu[:], t.ymu[:], t.yrst[:]))
    P.drain("ve")
    P.op("ve", lambda v: v.tensor_scalar(t.ymu[:], t.ymu[:], -1.0, None,
                                         ALU.mult))
    P.drain("ve")
    P.op("ve", lambda v: v.tensor_scalar(t.y0_bf[:], t.y_sb[:], t.yrst[:],
                                         t.ymu[:], ALU.mult, ALU.add),
         inc="sve", event=("y0",))
    P.wait("pe", ("y0",))
    pe_claim("T0")
    for k in range(ND):
        P.op("pe", lambda e, k=k: e.transpose(
            t.psT0[:, k * 64:(k + 1) * 64],
            t.y0_bf[0:64, k * 128:(k + 1) * 128], t.ident[0:64, 0:64]),
            inc="spe" if k == ND - 1 else None,
            event=("yT",) if k == ND - 1 else None)
    P.wait("ve", ("yT",))
    for k in range(ND):
        P.op("ve", lambda v, k=k: v.tensor_scalar(
            t.yln_f[:, k, :], t.psT0[:, k * 64:(k + 1) * 64],
            t.lng_c[:, k:k + 1], t.lnb_c[:, k:k + 1], ALU.mult, ALU.add),
            inc="sve" if k == ND - 1 else None,
            event=("ylnf",) if k == ND - 1 else None)
    release("T0", ("ylnf",))

    P.wait("pe", ("ylnf",))
    pe_claim("A0", "A1")
    for jt in range(2):
        ps = psH["A0" if jt == 0 else "A1"]
        for k in range(ND):
            P.op("pe", lambda e, jt=jt, k=k, ps=ps: e.matmul(
                ps[0:64, :], t.yln_f[:, k, :],
                t.wvp_sb[:, k, jt * 512:(jt + 1) * 512],
                start=(k == 0), stop=False))
        P.op("pe", lambda e, jt=jt, ps=ps: e.matmul(
            ps[0:64, :], t.ones_r[0:1, 0:64],
            t.bvp_r[:, jt * 512:(jt + 1) * 512], start=False, stop=True),
            inc="spe" if jt == 1 else None,
            event=("svmm",) if jt == 1 else None)
    P.wait("se", ("svmm",))
    P.op("se", lambda s: s.activation(t.sv_bf[:, 0:512], t.psA[0:64, 0:512],
                                      AF.Identity))
    P.op("se", lambda s: s.activation(t.sv_bf[:, 512:1024],
                                      t.psA[0:64, 512:1024], AF.Identity),
         inc="sse", event=("sv",))
    release("A0", ("sv",))
    release("A1", ("sv",))

    zregs = ["C0", "C1", "B0", "B1", "A0", "A1", "C0", "C1"]

    def zmm_phase(q):
        if q == 0:
            P.wait("pe", ("sv",))
            P.wait("pe", ("botot",))
            P.wait_all_dma("pe", ("w3load",), "sg")
        for k in range(ND):
            reg = zregs[k]
            pe_claim(reg)
            ps = psH[reg][:, 0:Tp]
            P.op("pe", lambda e, k=k, q=q, ps=ps: e.matmul(
                ps, t.sv_bf[0:64, k * 128:(k + 1) * 128],
                t.alpha_T[0:64, q * Tp:(q + 1) * Tp],
                start=True, stop=True), inc="spe", event=("zmm", q, k))
            P.wait("se", ("zmm", q, k))
            if q >= 2 and k == 0:
                P.wait("se", ("omm", q - 2, TPT - 1))
                P.drain("se")
            P.op("se", lambda s, k=k, q=q, ps=ps: s.activation(
                t.z_f[:, q % 2, k, :], ps, AF.Identity),
                inc="sse", event=("zev", q, k))
            release(reg, ("zev", q, k))

    if not cfg.get("skip_pass2"):
        zmm_phase(0)
    for p2 in ([] if cfg.get("skip_pass2") else range(NPH)):
        pr = p2 % 2
        if p2 == 0:
            P.wait_all_dma("sy", ("pstore", NPH - 1, NHF - 1), "sp")
            P.wait("sy", ("sv",))
        if p2 >= 2:
            P.wait("sy", ("fadd", p2 - 2, TPT - 1))
            P.wait("sy", ("ostore", p2 - 2, NHF - 1))
        P.dma("sy", lambda s, p2=p2, pr=pr: s.dma_start(
            out=t.prd[:, pr, :, :],
            in_=t.o_out[p2 * Tp:(p2 + 1) * Tp, :]
                .rearrange("(n p) d -> p n d", p=128)), "sx",
            event=("pread", p2))
        if p2 + 1 < NPH:
            zmm_phase(p2 + 1)
        oregs = [("A0", "A1", t.psA), ("B0", "B1", t.psB), ("C0", "C1", t.psC),
                 ("A0", "A1", t.psA)]
        for i in range(TPT):
            r0, r1, ps = oregs[i]
            pe_claim(r0, r1)
            for k in range(ND):
                if i == 0:
                    P.wait("pe", ("zev", p2, k))
                for f in range(2):
                    P.op("pe", lambda e, i=i, f=f, k=k, ps=ps, p2=p2: e.matmul(
                        ps[:, f * 512:(f + 1) * 512],
                        t.z_f[:, p2 % 2, k, i * 128:(i + 1) * 128],
                        t.wo2_sb[:, k, f * 512:(f + 1) * 512],
                        start=(k == 0), stop=False))
            for f in range(2):
                last = (f == 1)
                P.op("pe", lambda e, i=i, f=f, ps=ps: e.matmul(
                    ps[:, f * 512:(f + 1) * 512], t.ones_r[:],
                    t.bo_tot[:, f * 512:(f + 1) * 512], start=False, stop=True),
                    inc="spe" if last else None,
                    event=("omm", p2, i) if last else None)
            P.wait("ve", ("omm", p2, i))
            if i == 0:
                P.wait("ve", ("pread", p2))
            P.op("ve", lambda v, ps=ps, pr=pr, i=i: v.tensor_add(
                t.prd[:, pr, i, :], ps[:, 0:1024], t.prd[:, pr, i, :]),
                inc="sve", event=("fadd", p2, i))
            release(r0, ("fadd", p2, i))
            release(r1, ("fadd", p2, i))
        for h in range(NHF):
            P.wait("sy", ("fadd", p2, 2 * h + 1))
            P.dma("sy", lambda s, p2=p2, h=h, pr=pr: s.dma_start(
                out=t.o_out[p2 * Tp + h * 256:p2 * Tp + (h + 1) * 256, :]
                    .rearrange("(n p) d -> p n d", p=128),
                in_=t.prd[:, pr, 2 * h:2 * h + 2, :]), "so",
                event=("ostore", p2, h))

    if cfg.get("taps"):
        P.wait("sy", ("sv",))
        P.wait("sy", ("slotev", NPH - 1))
        P.wait("sy", ("ysd", NPH - 1, ND - 1))
        P.wait("sy", ("biash",))
        P.dma("sy", lambda s: s.dma_start(out=t.o_alpha[:], in_=t.alpha_T[:]), "so")
        P.dma("sy", lambda s: s.dma_start(out=t.o_slacc[:], in_=t.sl_acc[:]), "so")
        P.dma("sy", lambda s: s.dma_start(out=t.o_den[:], in_=t.den_acc[:]), "so")
        P.dma("sy", lambda s: s.dma_start(out=t.o_sv[:], in_=t.sv_bf[:]), "so")
        P.dma("sy", lambda s: s.dma_start(out=t.o_carry[:], in_=t.carry[:]), "so")
        P.dma("sy", lambda s: s.dma_start(out=t.o_biash[:], in_=t.bias_h[:]), "so")
        P.dma("sy", lambda s: s.dma_start(out=t.o_snew[:], in_=t.snew[:]), "so")
        P.dma("sy", lambda s: s.dma_start(out=t.o_s0[:], in_=t.s0_f32[:]), "so")
        P.dma("sy", lambda s: s.dma_start(out=t.o_ghn[:], in_=t.gh_n[:]), "so")
        P.dma("sy", lambda s: s.dma_start(out=t.o_gi[:], in_=t.gi_sb[:]), "so")
        P.dma("sy", lambda s: s.dma_start(out=t.o_ssum[:], in_=t.ssum[:, (NPH - 1) % 2]), "so")
        P.dma("sy", lambda s: s.dma_start(out=t.o_ssq[:], in_=t.ssq[:, (NPH - 1) % 2]), "so")
        P.dma("sy", lambda s: s.dma_start(out=t.o_sr[:], in_=t.sr[:, (NPH - 1) % 2]), "so")
        P.dma("sy", lambda s: s.dma_start(out=t.o_xn[:], in_=t.xn_tm[:, (NPH - 1) % 2]), "so")
        P.dma("sy", lambda s: s.dma_start(out=t.o_xnf[:], in_=t.xn_f[:]), "so")
        P.dma("sy", lambda s: s.dma_start(out=t.o_ysbf[:], in_=t.ys_bf[:]), "so")
        P.dma("sy", lambda s: s.dma_start(out=t.o_lsb[:], in_=t.l_sb[:]), "so")

    if P.mode == "sy":
        _, k = DMA_POOLS["so"]
        n = P.pool_n["so"]
        for i in range(k):
            cnt = 16 * ((n - i - 1) // k + 1) if n > i else 0
            if cnt:
                P.eng.wait_ge(P.sems[f"so{i}"], cnt)


def build_nc(cfg):
    T = cfg["T"]; Tp = cfg["Tp"]
    TPT = Tp // 128
    nc = bass.Bass("TRN2", target_bir_lowering=False)
    cfg = dict(cfg)
    cfg["tab_lnexp"] = _act_set_idx(nc, {AF.Ln, AF.Exp, AF.Identity, AF.Square})
    cfg["tab_sig"] = _act_set_idx(nc, {AF.Sigmoid, AF.Tanh, AF.Identity,
                                       AF.Square, AF.Abs})
    cfg["tab_gelu"] = _act_set_idx(nc, {AF.Gelu, AF.Identity})

    class TB:
        pass
    t = TB()

    def din(name, shape, dtype=F32):
        return nc.dram_tensor(name, shape, dtype, kind="ExternalInput").ap()

    t.i_x = din("x", [T, 1024])
    t.i_Wp_pk = din("Wp_pk", [128, 8 * 2048])
    t.i_bp0_pk = din("bp0_pk", [128, 8])
    t.i_bpk_pk = din("bpk_pk", [128, 8])
    t.i_Wu_pk = din("Wu_pk", [128, 8 * 1024])
    t.i_bu_pk = din("bu_pk", [128, 8])
    t.i_Wo_pk = din("Wo_pk", [128, 8 * 1024])
    t.i_bo = din("bo", [1024])
    t.i_slot = din("slot_init", [64, 1024])
    t.i_w1 = din("w1", [2048, 512])
    t.i_w1t_pk = din("w1t_pk", [128, 8 * 512])
    t.i_b1_pk = din("b1_pk", [128, 4])
    t.i_w2_pk = din("w2_pk", [128, 4 * 64])
    t.i_b2 = din("b2", [64])
    t.i_tau_b = din("tau_b", [128, 1])
    t.i_Wih_pk = din("Wih_pk", [128, 6 * 8 * 512])
    t.i_Whh_pk = din("Whh_pk", [128, 6 * 8 * 512])
    t.i_bih = din("bih", [3072])
    t.i_bhh = din("bhh", [3072])
    t.i_wq_pk = din("wq_pk", [128, 8 * 256])
    t.i_bq_pk = din("bq_pk", [128, 2])
    t.i_wk_pk = din("wk_pk", [128, 8 * 256])
    t.i_bk_pk = din("bk_pk", [128, 2])
    t.i_wv_pk = din("wv_pk", [128, 8 * 1024])
    t.i_bv = din("bv", [1024])
    t.i_lng_pk = din("lng_pk", [128, 8])
    t.i_lnb_pk = din("lnb_pk", [128, 8])
    t.i_wvp_pk = din("wvp_pk", [128, 8 * 1024])
    t.i_bvp = din("bvp", [1024])
    t.i_wo2_pk = din("wo2_pk", [128, 8 * 1024])
    t.i_bof = din("bof", [1024])
    t.i_ident = din("ident", [128, 128], BF16)
    t.i_ones_r = din("ones_r", [1, 128], BF16)
    t.i_ones_c = din("ones_c", [128, 1], BF16)
    t.o_out = nc.dram_tensor("out", [T, 1024], F32, kind="ExternalOutput").ap()
    if cfg.get("taps"):
        t.o_alpha = nc.dram_tensor("o_alpha", [64, T], BF16, kind="ExternalOutput").ap()
        t.o_slacc = nc.dram_tensor("o_slacc", [64, 1024], F32, kind="ExternalOutput").ap()
        t.o_den = nc.dram_tensor("o_den", [64, 1], F32, kind="ExternalOutput").ap()
        t.o_sv = nc.dram_tensor("o_sv", [64, 1024], BF16, kind="ExternalOutput").ap()
        t.o_carry = nc.dram_tensor("o_carry", [128, 8], F32, kind="ExternalOutput").ap()
        t.o_biash = nc.dram_tensor("o_biash", [128, 4], F32, kind="ExternalOutput").ap()
        t.o_snew = nc.dram_tensor("o_snew", [64, 1024], F32, kind="ExternalOutput").ap()
        t.o_gi = nc.dram_tensor("o_gi", [64, 3072], F32, kind="ExternalOutput").ap()
        t.o_s0 = nc.dram_tensor("o_s0", [64, 1024], F32, kind="ExternalOutput").ap()
        t.o_ghn = nc.dram_tensor("o_ghn", [64, 1024], F32, kind="ExternalOutput").ap()
        t.o_ssum = nc.dram_tensor("o_ssum", [128, TPT], F32, kind="ExternalOutput").ap()
        t.o_ssq = nc.dram_tensor("o_ssq", [128, TPT], F32, kind="ExternalOutput").ap()
        t.o_sr = nc.dram_tensor("o_sr", [128, TPT], F32, kind="ExternalOutput").ap()
        t.o_xn = nc.dram_tensor("o_xn", [128, TPT, 1024], BF16, kind="ExternalOutput").ap()
        t.o_xnf = nc.dram_tensor("o_xnf", [128, 8, Tp], BF16, kind="ExternalOutput").ap()
        t.o_ysbf = nc.dram_tensor("o_ysbf", [128, 8, Tp], BF16, kind="ExternalOutput").ap()
        t.o_lsb = nc.dram_tensor("o_lsb", [128, TPT, 64], F32, kind="ExternalOutput").ap()

    with ExitStack() as ctx:
        def sb(name, shape, dtype):
            return ctx.enter_context(nc.sbuf_tensor("sb_" + name, shape, dtype))

        t.ident = sb("ident", [128, 128], BF16)
        t.ones_r = sb("ones_r", [1, 128], BF16)
        t.ones_c = sb("ones_c", [128, 1], BF16)
        t.tau_b = sb("tau_b", [128, 1], F32)
        t.invc = sb("invc", [128, 1], F32)
        t.alpha_T = sb("alpha_T", [64, T], BF16)
        t.sl_acc = sb("sl_acc", [64, 1024], F32)
        t.den_acc = sb("den_acc", [64, 1], F32)
        t.carry = sb("carry", [128, 8], F32)
        t.si0f = sb("si0f", [128, 8, 64], BF16)
        t.sm_f = sb("sm_f", [128, 8], F32)
        t.sm_fb = sb("sm_fb", [128, 8], BF16)
        t.bpw = sb("bpw", [128, 8], F32)
        t.bpk2 = sb("bpk2", [128, 8], F32)
        t.bu_c = sb("bu_c", [128, 8], F32)
        t.b1_c = sb("b1_c", [128, 4], F32)
        t.bias_h = sb("bias_h", [128, 4], F32)
        t.b2_r = sb("b2_r", [1, 64], BF16)
        t.bq_c = sb("bq_c", [128, 2], F32)
        t.bk_c = sb("bk_c", [128, 2], F32)
        t.lng_c = sb("lng_c", [128, 8], F32)
        t.lnb_c = sb("lnb_c", [128, 8], F32)
        t.bo_tot = sb("bo_tot", [1, 1024], BF16)
        t.wq_sb = sb("wq_sb", [128, 8, 256], BF16)
        t.wk_sb = sb("wk_sb", [128, 8, 256], BF16)
        t.w2_sb = sb("w2_sb", [128, 4, 64], BF16)
        t.sv_bf = sb("sv_bf", [64, 1024], BF16)
        t.dump = sb("dump", [128, 1024], BF16)
        s1 = ExitStack()

        def sb(name, shape, dtype, _s=s1):
            return _s.enter_context(nc.sbuf_tensor("sb_" + name, shape, dtype))

        t.s0_bf = sb("s0_bf", [64, 1024], BF16)
        t.bo1_r = sb("bo1_r", [8, 128], F32)
        t.bo2_r = sb("bo2_r", [8, 128], F32)
        t.Wp_sb = sb("Wp_sb", [128, 8, 2048], BF16)
        t.Wu_sb = sb("Wu_sb", [128, 8, 1024], BF16)
        t.Wo_sb = sb("Wo_sb", [128, 8, 1024], BF16)
        t.w1t_sb = sb("w1t_sb", [128, 8, 512], BF16)
        t.w1s = sb("w1s", [128, 2, 512], BF16)
        t.x_sl = sb("x_sl", [128, 4, 2, 1024], F32)
        t.xn_tm = sb("xn_tm", [128, 2, TPT, 1024], BF16)
        t.xn_f = sb("xn_f", [128, 8, Tp], BF16)
        t.g_sb = sb("g_sb", [128, 2, Tp], F32)
        t.s2_sb = sb("s2_sb", [128, 2, Tp], F32)
        t.a_sb = sb("a_sb", [128, Tp], F32)
        t.b_sb = sb("b_sb", [128, Tp], F32)
        t.ys_bf = sb("ys_bf", [128, 8, Tp], BF16)
        t.h_f = sb("h_f", [128, 4, Tp], BF16)
        t.l_sb = sb("l_sb", [128, TPT, 64], F32)
        t.e_al = sb("e_al", [128, TPT, 64], F32)
        t.al_bf = sb("al_bf", [128, TPT, 64], BF16)
        t.m18 = sb("m18", [128, TPT, 8], F32)
        t.m1n = sb("m1n", [128, TPT], F32)
        t.alns = sb("alns", [128, TPT], F32)
        t.ssum = sb("ssum", [128, 2, TPT], F32)
        t.ssq = sb("ssq", [128, 2, TPT], F32)
        t.smn = sb("smn", [128, 2, TPT], F32)
        t.svar = sb("svar", [128, 2, TPT], F32)
        t.sr = sb("sr", [128, 2, TPT], F32)
        t.smrn = sb("smrn", [128, 2, TPT], F32)
        t.stmp = sb("stmp", [128, 2, TPT], F32)
        s1.close()
        s2 = ExitStack()

        def sb(name, shape, dtype, _s=s2):
            return _s.enter_context(nc.sbuf_tensor("sb_" + name, shape, dtype))

        t.WT = sb("WT", [128, 6, 8, 512], BF16)
        t.whh_pp = sb("whh_pp", [128, 2, 8, 512], BF16)
        t.s0_f32 = sb("s0_f32", [64, 1024], F32)
        t.bih_r = sb("bih_r", [1, 3072], BF16)
        t.bhh_r = sb("bhh_r", [1, 3072], BF16)
        t.bv_r = sb("bv_r", [1, 1024], BF16)
        t.bvp_r = sb("bvp_r", [1, 1024], BF16)
        t.wv_sb = sb("wv_sb", [128, 8, 1024], BF16)
        t.wvp_sb = sb("wvp_sb", [128, 8, 1024], BF16)
        t.si_bf = sb("si_bf", [64, 1024], BF16)
        t.si_f = sb("si_f", [128, 8, 64], BF16)
        t.gi_sb = sb("gi_sb", [64, 3072], F32)
        t.gh_n = sb("gh_n", [64, 1024], F32)
        t.t64a = sb("t64a", [64, 1024], F32)
        t.r_g = sb("r_g", [64, 1024], F32)
        t.z_g = sb("z_g", [64, 1024], F32)
        t.n_g = sb("n_g", [64, 1024], F32)
        t.snew = sb("snew", [64, 1024], F32)
        t.snew_bf = sb("snew_bf", [64, 1024], BF16)
        t.snew_f = sb("snew_f", [128, 8, 64], BF16)
        t.dent = sb("dent", [64, 1], F32)
        t.q_bf = sb("q_bf", [128, 2, 64], BF16)
        t.k_bf = sb("k_bf", [128, 2, 64], BF16)
        t.sc_sb = sb("sc_sb", [64, 64], F32)
        t.scm = sb("scm", [64, 1], F32)
        t.scs = sb("scs", [64, 1], F32)
        t.attn_bf = sb("attn_bf", [64, 64], BF16)
        t.attnT_bf = sb("attnT_bf", [64, 64], BF16)
        t.v_bf = sb("v_bf", [64, 1024], BF16)
        t.y_sb = sb("y_sb", [64, 1024], F32)
        t.ysum = sb("ysum", [64, 1], F32)
        t.yssq = sb("yssq", [64, 1], F32)
        t.ymu = sb("ymu", [64, 1], F32)
        t.yvar = sb("yvar", [64, 1], F32)
        t.ytmp = sb("ytmp", [64, 1], F32)
        t.yrst = sb("yrst", [64, 1], F32)
        t.y0_bf = sb("y0_bf", [64, 1024], BF16)
        t.yln_f = sb("yln_f", [128, 8, 64], BF16)
        s2.close()
        s3 = ExitStack()

        def sb(name, shape, dtype, _s=s3):
            return _s.enter_context(nc.sbuf_tensor("sb_" + name, shape, dtype))

        t.wo2_sb = sb("wo2_sb", [128, 8, 1024], BF16)
        t.z_f = sb("z_f", [128, 2, 8, Tp], BF16)
        t.prd = sb("prd", [128, 2, TPT, 1024], F32)
        s3.close()

        with ExitStack() as pctx:
            e2 = pctx.enter_context
            t.psA = e2(nc.psum_tensor("psA", [128, 1024], F32))
            t.psB = e2(nc.psum_tensor("psB", [128, 1024], F32))
            t.psC = e2(nc.psum_tensor("psC", [128, 1024], F32))
            t.psT0 = e2(nc.psum_tensor("psT0", [128, 1024], BF16))
            t.psT1 = e2(nc.psum_tensor("psT1", [128, 1024], BF16))
            sems = {k: e2(nc.semaphore(k)) for k in SEMS}

            P = Plan()
            P.begin("dry")
            build_walk(cfg, t, P)

            with nc.Block() as block:
                @block.sync
                def _(eng):
                    P.begin("sy", eng, sems)
                    build_walk(cfg, t, P)

                @block.gpsimd
                def _(eng):
                    P.begin("gp", eng, sems)
                    build_walk(cfg, t, P)

                @block.tensor
                def _(eng):
                    P.begin("pe", eng, sems)
                    build_walk(cfg, t, P)

                @block.vector
                def _(eng):
                    P.begin("ve", eng, sems)
                    build_walk(cfg, t, P)

                @block.scalar
                def _(eng):
                    P.begin("se", eng, sems)
                    build_walk(cfg, t, P)
    return nc


def make_in_maps(inputs, cfg, n_cores=8):
    T = cfg["T"]
    x = np.ascontiguousarray(np.asarray(inputs["L0_attn_x"], dtype=np.float32))
    ident = np.eye(128, dtype=ml_dtypes.bfloat16)
    ones_r = np.ones((1, 128), dtype=ml_dtypes.bfloat16)
    ones_c = np.ones((128, 1), dtype=ml_dtypes.bfloat16)
    tau_b = np.ascontiguousarray(
        np.broadcast_to(np.asarray(inputs["L0_ffn_tau"], np.float32).reshape(1, 1),
                        (128, 1)))
    names = {
        "bo": "L0_attn_bo", "slot_init": "L0_ffn_slot_init",
        "w1": "L0_ffn_w1", "b2": "L0_ffn_b2",
        "bih": "L0_ffn_bih", "bhh": "L0_ffn_bhh", "bv": "L0_ffn_bv",
        "bvp": "L0_ffn_bvp", "bof": "L0_ffn_bo",
    }
    common = {k: np.ascontiguousarray(np.asarray(inputs[v], dtype=np.float32))
              for k, v in names.items()}
    common.update({"tau_b": tau_b, "ident": ident, "ones_r": ones_r,
                   "ones_c": ones_c})

    def f32(v):
        return np.asarray(inputs[v], np.float32)

    def pk(a, k):
        return np.ascontiguousarray(
            a.reshape(k, 128, -1).transpose(1, 0, 2).reshape(128, -1))

    def cpk(a, k):
        return np.ascontiguousarray(a.reshape(k, 128).T)

    def jtpk(a):
        return np.ascontiguousarray(
            a.reshape(8, 128, 6, 512).transpose(1, 2, 0, 3).reshape(128, -1))

    bp = f32("L0_attn_bp")
    common.update({
        "Wp_pk": pk(f32("L0_attn_Wp")[:, 0:2048], 8),
        "bp0_pk": cpk(bp[0:1024], 8), "bpk_pk": cpk(bp[1024:2048], 8),
        "Wu_pk": pk(f32("L0_attn_Wu"), 8), "bu_pk": cpk(f32("L0_attn_bu"), 8),
        "Wo_pk": pk(f32("L0_attn_Wo"), 8),
        "w1t_pk": pk(f32("L0_ffn_w1")[0:1024], 8),
        "b1_pk": cpk(f32("L0_ffn_b1"), 4),
        "w2_pk": pk(f32("L0_ffn_w2"), 4),
        "Wih_pk": jtpk(np.ascontiguousarray(f32("L0_ffn_Wih").T)),
        "Whh_pk": jtpk(np.ascontiguousarray(f32("L0_ffn_Whh").T)),
        "wq_pk": pk(f32("L0_ffn_wq"), 8), "bq_pk": cpk(f32("L0_ffn_bq"), 2),
        "wk_pk": pk(f32("L0_ffn_wk"), 8), "bk_pk": cpk(f32("L0_ffn_bk"), 2),
        "wv_pk": pk(f32("L0_ffn_wv"), 8),
        "lng_pk": cpk(f32("L0_ffn_ln_g"), 8), "lnb_pk": cpk(f32("L0_ffn_ln_b"), 8),
        "wvp_pk": pk(f32("L0_ffn_wvp"), 8),
        "wo2_pk": pk(f32("L0_ffn_wo"), 8),
    })
    in_maps = []
    for c in range(n_cores):
        m = dict(common)
        m["x"] = np.ascontiguousarray(x[c, :T, :])
        in_maps.append(m)
    return in_maps


CFG = {"T": 8192, "Tp": 512}


def kernel(**inputs):
    cfg = CFG
    nc = build_nc(cfg)
    in_maps = make_in_maps(inputs, cfg, n_cores=8)
    res = run_bass_kernel_spmd(nc, in_maps, core_ids=list(range(8)))
    out = np.stack([r["out"] for r in res.results], axis=0)
    return out.astype(np.float32)
